# version 18
# baseline (speedup 1.0000x reference)
import numpy as np
import ml_dtypes
from contextlib import ExitStack

import concourse.bass as bass
import concourse.mybir as mybir
from concourse.bass_utils import run_bass_kernel_spmd

F32 = mybir.dt.float32
BF16 = mybir.dt.bfloat16
AF = mybir.ActivationFunctionType
ALU = mybir.AluOpType
AX = mybir.AxisListType.X

C = 1024
T = 4096
SEQ = 2048
NH = 16
HD = 64
BUCKET = 64
FF_A = 4096
FF_E = 2048
NEXP = 8
KC = C // 128
NTILES = T // 128
EPS = 1e-5
SCALE_B = float(HD ** -0.25)
SCALE_A = 0.125
V65W = NH * (HD + 1)

ENGINES = ("PE", "DVE", "ACT", "POOL", "SP")


class Op:
    __slots__ = ("idx", "eng", "emit", "reads", "writes", "dma", "deps",
                 "marked", "inc_val", "tag", "_qslot", "_qseq", "fence",
                 "_fence_snap")

    def __init__(self, idx, eng, emit, reads, writes, dma, tag):
        self.idx = idx; self.eng = eng; self.emit = emit
        self.reads = reads; self.writes = writes; self.dma = dma
        self.deps = []; self.marked = False; self.inc_val = 0; self.tag = tag
        self.fence = False; self._fence_snap = None


class Prog:

    def __init__(self, relax_same_engine=False):
        self.ops = []
        self.last_write = {}
        self.readers = {}
        self.relax = relax_same_engine
        self.fence_n = 0

    def op(self, eng, emit, reads=(), writes=(), dma=False, tag=""):
        o = Op(len(self.ops), eng, emit, tuple(reads), tuple(writes), dma, tag)
        deps = set()
        for r in o.reads:
            w = self.last_write.get(r)
            if w is not None:
                deps.add(w)
        for w in o.writes:
            lw = self.last_write.get(w)
            if lw is not None:
                deps.add(lw)
            deps.update(self.readers.get(w, ()))
        deps.discard(o.idx)
        o.deps = sorted(deps)
        self.ops.append(o)
        for r in o.reads:
            self.readers.setdefault(r, []).append(o.idx)
        for w in o.writes:
            self.last_write[w] = o.idx
            self.readers[w] = []
        return o.idx

    def fence(self, tag="fence"):
        n = self.fence_n
        self.fence_n += 1
        for E in ENGINES:
            o = Op(len(self.ops), E, None, (), (), False, f"{tag}{n}_{E}")
            o.fence = True
            self.ops.append(o)

    def _needs_wait(self, ci, pi):
        p, c = self.ops[pi], self.ops[ci]
        if p.eng == c.eng and not p.dma:
            if p.eng == "PE" and not c.dma:
                return False
            if self.relax and not c.dma:
                return False
            return True
        return True

    NQ = 12

    def build(self, nc, es):
        sems = {e: es.enter_context(nc.semaphore(f"s_{e}")) for e in ENGINES}
        qsems = {e: [es.enter_context(nc.semaphore(f"q_{e}{i}"))
                     for i in range(self.NQ)] for e in ("SP", "ACT", "POOL")}

        last_by_eng = {}
        for o in self.ops:
            if o.fence:
                for le in last_by_eng.values():
                    le.marked = True
            else:
                last_by_eng[o.eng] = o
        for o in self.ops:
            if o.dma:
                o.marked = True
            for d in o.deps:
                if self._needs_wait(o.idx, d):
                    self.ops[d].marked = True

        counts = {}
        dma_seq = {}
        for o in self.ops:
            if o.fence:
                snap = dict(counts)
                qsnap = {}
                for e2, nn in dma_seq.items():
                    for i in range(min(nn, self.NQ)):
                        qsnap[(e2, i)] = ((nn - 1 - i) // self.NQ + 1) * 16
                o._fence_snap = (snap, qsnap)
                continue
            if o.marked:
                if o.dma:
                    j = dma_seq.get(o.eng, 0)
                    dma_seq[o.eng] = j + 1
                    o.inc_val = (j // self.NQ + 1) * 16
                    o._qslot = j % self.NQ
                    o._qseq = j
                else:
                    key = (o.eng, False)
                    counts[key] = counts.get(key, 0) + 1
                    o.inc_val = counts[key]

        per_eng = {e: [] for e in ENGINES}
        for o in self.ops:
            per_eng[o.eng].append(o)
        final_q = {}
        for e in ("SP", "ACT", "POOL"):
            n = dma_seq.get(e, 0)
            final_q[e] = [(n - 1 - i) // self.NQ + 1 for i in range(self.NQ)] \
                if n else []

        def body(eng_obj, eng_name):
            waited = {}

            def wait(key, sem, val):
                if waited.get(key, 0) >= val:
                    return
                eng_obj.wait_ge(sem, val)
                waited[key] = val

            for o in per_eng[eng_name]:
                if o.fence:
                    snap, qsnap = o._fence_snap
                    for (e2, dm), v in snap.items():
                        if v and not (e2 == eng_name and not dm):
                            wait(("s", e2), sems[e2], v)
                    for (e2, i), v in qsnap.items():
                        wait(("q", e2, i), qsems[e2][i], v)
                    continue
                if o.dma and o._qseq >= self.NQ:
                    wait(("q", o.eng, o._qslot), qsems[o.eng][o._qslot],
                         o.inc_val - 16)
                for d in o.deps:
                    if not self._needs_wait(o.idx, d):
                        continue
                    pp = self.ops[d]
                    if not pp.marked:
                        continue
                    if pp.dma:
                        wait(("q", pp.eng, pp._qslot), qsems[pp.eng][pp._qslot],
                             pp.inc_val)
                    else:
                        wait(("s", pp.eng), sems[pp.eng], pp.inc_val)
                inst = o.emit(eng_obj)
                if o.marked:
                    if isinstance(inst, (list, tuple)):
                        inst = inst[-1]
                    assert inst is not None, f"op {o.idx} ({o.tag}) returned None"
                    if o.dma:
                        inst.then_inc(qsems[o.eng][o._qslot], 16)
                    else:
                        inst.then_inc(sems[o.eng], 1)
            if eng_name == "POOL":
                for e in ("SP", "ACT", "POOL"):
                    n = dma_seq.get(e, 0)
                    for i in range(min(n, self.NQ)):
                        nslot = (n - 1 - i) // self.NQ + 1
                        wait(("q", e, i), qsems[e][i], nslot * 16)

        blk = es.enter_context(nc.Block())
        blk.sync(lambda e: body(e, "SP"))
        blk.tensor(lambda e: body(e, "PE"))
        blk.vector(lambda e: body(e, "DVE"))
        blk.scalar(lambda e: body(e, "ACT"))
        blk.gpsimd(lambda e: body(e, "POOL"))


class Arena:
    def __init__(self, nc, es, prog, nbytes=204800):
        self.t = es.enter_context(nc.sbuf_tensor("arena", [128, nbytes],
                                                 mybir.dt.uint8))
        self.nbytes = nbytes
        self.off = 0
        self.stack = []
        self.hi = 0
        self.prog = prog

    def alloc(self, cols, dtype):
        size = cols * (2 if dtype == BF16 else 4)
        size = (size + 63) & ~63
        assert self.off + size <= self.nbytes, \
            f"arena overflow: {self.off}+{size} > {self.nbytes}"
        ap = self.t[:][:, self.off:self.off + size].bitcast(dtype)[:, 0:cols]
        self.off += size
        self.hi = max(self.hi, self.off)
        return ap

    def push(self):
        self.stack.append(self.off)

    def pop(self):
        self.off = self.stack.pop()
        self.prog.fence()



def build_program(nc, stages=("aa", "f1", "ab", "f2"), relax=False):
    p = Prog(relax_same_engine=relax)
    es = ExitStack()

    def din(name, shape, dt=F32):
        return nc.dram_tensor(name, shape, dt, kind="ExternalInput").ap()

    xin = din("xin", [T, C])
    aa_wqkv = din("aa_wqkv", [C, 3 * C]); aa_bqkv = din("aa_bqkv", [3 * C])
    aa_wout = din("aa_wout", [C, C]); aa_bout = din("aa_bout", [C])
    aa_g1 = din("aa_g1", [C]); aa_b1g = din("aa_b1g", [C])
    aa_g2 = din("aa_g2", [C]); aa_b2g = din("aa_b2g", [C])
    aa_w1 = din("aa_w1", [C, FF_A]); aa_b1 = din("aa_b1", [FF_A])
    aa_w2 = din("aa_w2", [FF_A, C]); aa_b2 = din("aa_b2", [C])
    ab_wq = din("ab_wq", [C, C]); ab_wk = din("ab_wk", [C, C])
    ab_wv = din("ab_wv", [C, C]); ab_wo = din("ab_wo", [C, C])
    ab_bo = din("ab_bo", [C])
    f_rw = din("f_rw", [NEXP, C])
    f_eg = din("f_eg", [NEXP, C, FF_E]); f_eu = din("f_eu", [NEXP, C, FF_E])
    f_ed = din("f_ed", [NEXP, FF_E, C])
    f_sg = din("f_sg", [C, FF_E]); f_su = din("f_su", [C, FF_E])
    f_sd = din("f_sd", [FF_E, C])
    id_f = din("id_f", [128, 128])
    id_b = din("id_b", [128, 128], BF16)
    hmaskin = din("hmask", [128, 2 * NH * BUCKET])
    iota256 = din("iota256", [128, NTILES * NEXP])
    iota32 = din("iota32", [128, NTILES])
    ltri = din("ltri", [128, 128])

    out = nc.dram_tensor("out", [T, C], F32, kind="ExternalOutput").ap()

    def scr(name, shape, dt=F32):
        return nc.dram_tensor(name, shape, dt, kind="Internal").ap()

    ys = scr("ys", [T, C])
    y2s = scr("y2s", [T, C])
    r1 = scr("r1", [T, C])
    r2 = scr("r2", [T, C])
    r3 = scr("r3", [T, C])
    xTs = scr("xTs", [C, T], BF16)
    CAP = 1024
    NSLOT = NEXP * CAP + NTILES
    g1s = scr("g1s", [NSLOT, C], BF16)
    g1eo = scr("g1eo", [NSLOT, C])
    g2s = scr("g2s", [NSLOT, C], BF16)
    g2eo = scr("g2eo", [NSLOT, C])
    sh1s = scr("sh1s", [T, C])
    sh2s = scr("sh2s", [T, C])
    qTs = scr("qTs", [C, T], BF16)
    oTs = scr("oTs", [C, T], BF16)

    ar = Arena(nc, es, p)

    PSA = es.enter_context(nc.psum_tensor("psa", [128, 2048], F32))[:]
    PSB = es.enter_context(nc.psum_tensor("psb", [128, 1024], F32))[:]
    PSC = es.enter_context(nc.psum_tensor("psc", [128, 512], F32))[:]
    PST = es.enter_context(nc.psum_tensor("pst", [128, 1024], BF16))[:]
    PSAK = ["psa0", "psa1", "psa2", "psa3"]
    PSBK = ["psb0", "psb1"]

    def dma(eng, dst, src, reads, writes, tag=""):
        p.op(eng, lambda e, d=dst, s=src: e.dma_start(out=d, in_=s),
             reads=reads, writes=writes, dma=True, tag=tag)

    sb_idf = ar.alloc(128, F32)
    sb_idb = ar.alloc(128, BF16)
    sb_ones = ar.alloc(128, F32)
    sb_onesb = ar.alloc(128, BF16)
    sb_hmask = ar.alloc(2 * NH * BUCKET, BF16)
    sb_iota256 = ar.alloc(NTILES * NEXP, F32)
    sb_iota32 = ar.alloc(NTILES, F32)
    rwT = ar.alloc(8 * KC, F32)
    sb_ltri = ar.alloc(128, F32)
    sb_onesf = ar.alloc(4, F32)
    sb_zt = ar.alloc(1024, BF16)
    sb_onesr = ar.alloc(128, BF16)
    ROWBUF = ar.alloc(1024, F32)
    STG = ar.alloc(128, F32)
    STT = ar.alloc(64, F32)

    dma("SP", sb_idf[:, :], id_f, ["H:idf"], ["idf"], "idf")
    dma("SP", sb_idb[:, :], id_b, ["H:idb"], ["idb"], "idb")
    dma("SP", sb_iota256[:, :], iota256, ["H:iota256"], ["iota256"], "iota256")
    dma("SP", sb_iota32[:, :], iota32, ["H:iota32"], ["iota32"], "iota32")
    dma("POOL", sb_hmask[:, :], hmaskin, ["H:hmask"], ["hmask"], "hmask")
    dma("SP", sb_ltri[:, :], ltri, ["H:ltri"], ["ltri"], "ltri")
    p.op("DVE", lambda e: e.memset(sb_onesf[:, 0:1], 1.0), writes=["onesf"], tag="onesf")
    p.op("DVE", lambda e: e.memset(sb_zt[:, :], 0.0), writes=["zt"], tag="zt")
    p.op("DVE", lambda e: e.memset(sb_ones[0:1, :], 1.0), writes=["ones"], tag="ones")
    p.op("DVE", lambda e: e.memset(sb_onesb[:, 0:1], 1.0), writes=["onesb"], tag="onesb")
    p.op("DVE", lambda e: e.memset(sb_onesr[0:1, :], 1.0), writes=["onesr"], tag="onesr")
    KCOL = ar.alloc(8, F32)
    _kvals = [EPS, -0.5, SCALE_A, SCALE_B, -1.0]
    for _i, _v in enumerate(_kvals):
        p.op("DVE", lambda e, _i=_i, _v=_v: e.memset(KCOL[:, _i:_i + 1], _v),
             writes=["Kc"], tag=f"kc{_i}")
        nc.const_aps.register(F32, _v, KCOL[:, _i:_i + 1])

    for k in range(KC):
        dma("SP", STG[0:8, 0:128], f_rw[:, k * 128:(k + 1) * 128], ["H:rw"],
            ["stg"], f"ld_rw{k}")
        p.op("PE", lambda e: e.transpose(PSB[0:128, 0:8], STG[0:8, 0:128],
                                         sb_idf[0:8, 0:8]),
             reads=["stg", "idf"], writes=["psb0"], tag=f"tr_rw{k}")
        p.op("DVE", lambda e, k=k: e.tensor_copy(rwT[:, k * 8:(k + 1) * 8], PSB[:, 0:8]),
             reads=["psb0"], writes=["C:rwT", "stg"], tag=f"ev_rw{k}")

    def mk_bcast(dst, src_row_ap, n, tag):
        row = ROWBUF[0:1, 0:n]
        dma("SP", row, src_row_ap, [f"H:{tag}"], ["rowbuf"], f"ld_{tag}")
        nch = (n + 511) // 512

        def mm(e):
            inst = None
            for i in range(nch):
                s = i * 512
                w = min(512, n - s)
                inst = e.matmul(PSA[:, s:s + w], sb_ones[0:1, :], row[:, s:s + w],
                                start=True, stop=True)
            return inst
        p.op("PE", mm, reads=["ones", "rowbuf"],
             writes=[PSAK[i] for i in range(nch)], tag=f"bc_{tag}")
        p.op("DVE", lambda e: e.tensor_copy(dst[:, 0:n], PSA[:, 0:n]),
             reads=[PSAK[i] for i in range(nch)],
             writes=[f"C:{tag}", "rowbuf"], tag=f"ev_{tag}")

    def mk_cols(dst, src_vec, ntile, tag):
        stg = STG[0:ntile, 0:128]
        dma("SP", stg, src_vec.rearrange("(t p) -> t p", p=128),
            [f"H:{tag}"], ["stg"], f"ld_{tag}")
        p.op("PE", lambda e, stg=stg, ntile=ntile: e.transpose(
            PSA[:, 0:ntile], stg, sb_idf[0:ntile, 0:ntile]),
            reads=["stg", "idf"], writes=["psa0"], tag=f"tr_{tag}")
        p.op("DVE", lambda e, ntile=ntile: e.tensor_copy(dst[:, 0:ntile],
                                                         PSA[:, 0:ntile]),
             reads=["psa0"], writes=[f"C:{tag}", "stg"], tag=f"ev_{tag}")

    PK = {}

    def alloc_pack():
        PK["T"] = [ar.alloc(1024, F32) for _ in range(2)]
        PK["T2"] = [ar.alloc(1024, F32) for _ in range(2)]
        PK["Y"] = [ar.alloc(1024, F32) for _ in range(2)]
        PK["H"] = [ar.alloc(1024, BF16) for _ in range(2)]
        PK["SQ"] = [ar.alloc(1024, F32) for _ in range(2)]

    _ln_ctr = [0]

    def ln_core(src_ap, src_keys, y_ap, y_key, kt):
        st = STT
        sc = _ln_ctr[0] % 6
        _ln_ctr[0] += 1
        c0 = sc * 8
        sq = PK["SQ"][sc % 2]
        p.op("ACT", lambda e: e.activation(sq[:, :], src_ap, AF.Square,
                                           accum_out=st[:, c0:c0 + 1]),
             reads=src_keys, writes=[f"sqd{sc % 2}", f"st{c0}"], tag=f"{kt}_sq")
        p.op("DVE", lambda e: e.reduce_sum(st[:, c0 + 1:c0 + 2], src_ap, axis=AX),
             reads=src_keys, writes=[f"st{c0}b"], tag=f"{kt}_sm")
        p.op("DVE", lambda e: e.tensor_scalar(st[:, c0 + 2:c0 + 3],
                                              st[:, c0 + 1:c0 + 2],
                                              1.0 / C, None, ALU.mult),
             reads=[f"st{c0}b"], writes=[f"st{c0}c"], tag=f"{kt}_mu")
        p.op("DVE", lambda e: e.tensor_scalar(st[:, c0 + 3:c0 + 4], st[:, c0:c0 + 1],
                                              1.0 / C, None, ALU.mult),
             reads=[f"st{c0}"], writes=[f"st{c0}d"], tag=f"{kt}_e2")
        p.op("DVE", lambda e: e.tensor_mul(st[:, c0 + 4:c0 + 5], st[:, c0 + 2:c0 + 3],
                                           st[:, c0 + 2:c0 + 3]),
             reads=[f"st{c0}c"], writes=[f"st{c0}e"], tag=f"{kt}_m2")
        p.op("DVE", lambda e: e.tensor_sub(st[:, c0 + 5:c0 + 6], st[:, c0 + 3:c0 + 4],
                                           st[:, c0 + 4:c0 + 5]),
             reads=[f"st{c0}d", f"st{c0}e"], writes=[f"st{c0}f"], tag=f"{kt}_va")
        p.op("ACT", lambda e: e.activation(st[:, c0 + 6:c0 + 7], st[:, c0 + 5:c0 + 6],
                                           AF.Ln, bias=EPS),
             reads=[f"st{c0}f", "Kc"], writes=[f"st{c0}g"], tag=f"{kt}_ln")
        p.op("ACT", lambda e: e.activation(st[:, c0 + 7:c0 + 8], st[:, c0 + 6:c0 + 7],
                                           AF.Exp, scale=-0.5),
             reads=[f"st{c0}g", "Kc"], writes=[f"st{c0}h"], tag=f"{kt}_rs")
        p.op("DVE", lambda e: e.tensor_scalar(y_ap, src_ap, st[:, c0 + 2:c0 + 3],
                                              st[:, c0 + 7:c0 + 8],
                                              ALU.subtract, ALU.mult),
             reads=list(src_keys) + [f"st{c0}c", f"st{c0}h"], writes=[y_key],
             tag=f"{kt}_ap")

    def transpose8(src_ap, src_keys, kt):
        def mm(e):
            inst = None
            for k in range(KC):
                inst = e.transpose(PST[:, k * 128:(k + 1) * 128],
                                   src_ap[:, k * 128:(k + 1) * 128], sb_idb[:, :])
            return inst
        p.op("PE", mm, reads=list(src_keys) + ["idb"], writes=["pst"], tag=kt)

    def wchunks(slot, skey, w_ap, kdim, n, tag):
        kch = kdim // 128
        step = max(1, min(kch, (2 * 1024 * 1024) // (n * 512)))
        for k0 in range(0, kch, step):
            k1 = min(k0 + step, kch)
            dma("POOL",
                slot[:, :].rearrange("p (k n) -> p k n", k=kch)[:, k0:k1, :],
                w_ap.rearrange("(k p) n -> p k n", p=128)[:, k0:k1, :],
                [f"H:{tag}"], [skey], f"{tag}_{k0}")

    if "aa" in stages:
        ar.push()
        alloc_pack()
        cb_g1 = ar.alloc(C, F32)
        cb_b1g = ar.alloc(C, F32)
        cb_bqv = ar.alloc(C, F32)
        cb_bout = ar.alloc(C, F32)
        col_bqkv = ar.alloc(24, F32)
        mk_bcast(cb_g1, aa_g1.rearrange("(o n) -> o n", o=1), C, "g1")
        mk_bcast(cb_b1g, aa_b1g.rearrange("(o n) -> o n", o=1), C, "b1g")
        mk_bcast(cb_bqv, aa_bqkv.rearrange("(o n) -> o n", o=1)[:, 2 * C:3 * C], C, "bqv")
        mk_bcast(cb_bout, aa_bout.rearrange("(o n) -> o n", o=1), C, "bout")
        mk_cols(col_bqkv, aa_bqkv, 24, "bqkv")

        wqkv_sb = ar.alloc(KC * 3 * C, BF16)
        wout_sb = ar.alloc(KC * C, BF16)
        hTc = ar.alloc(KC * 512, BF16)
        HGRP = 512
        NG = T // HGRP
        GT = HGRP // 128
        GB = HGRP // BUCKET
        GPB = SEQ // HGRP
        qT_g = ar.alloc(KC * HGRP, BF16)
        kT_g = ar.alloc(KC * HGRP, BF16)
        kT_c = ar.alloc(KC * BUCKET, BF16)
        v65_g = [ar.alloc(V65W, BF16) for _ in range(GT)]
        v65_c = ar.alloc(V65W, BF16)
        zv65 = ar.alloc(V65W, BF16)
        zkT = ar.alloc(64, BF16)
        oT_g = ar.alloc(KC * HGRP, BF16)
        orw = [ar.alloc(512, F32) for _ in range(2)]
        rrow = [ar.alloc(1024, BF16) for _ in range(3)]
        lna = ar.alloc(2048, F32)
        expb = [ar.alloc(NH * BUCKET, BF16) for _ in range(2)]

        wchunks(wqkv_sb, "W:qkv", aa_wqkv, C, 3 * C, "aaqkv")
        wchunks(wout_sb, "W:aout", aa_wout, C, C, "aaout")
        p.op("DVE", lambda e: e.memset(zv65[:, :], 0.0), writes=["zv65"], tag="zv0")
        p.op("DVE", lambda e: e.memset(
            zv65[:, :].rearrange("p (h w) -> p h w", h=NH)[:, :, HD:HD + 1], 1.0),
            reads=["zv65"], writes=["zv65"], tag="zv1")
        p.op("DVE", lambda e: e.memset(zkT[:, :], 0.0), writes=["zkT"], tag="zk")

        for g in range(NG):
            gk = f"g{g}"
            for i in range(GT):
                ti = g * GT + i
                b = i % 2
                t, t2, y, h = PK["T"][b], PK["T2"][b], PK["Y"][b], PK["H"][b]
                kt = f"aa{ti}"
                dma("SP", t[:, :], xin[ti * 128:(ti + 1) * 128, :],
                    [f"H:xin:{ti}"], [f"tb{b}"], kt)
                ln_core(t[:, :], [f"tb{b}"], y[:, :], f"yb{b}", kt)
                dma("ACT", ys[ti * 128:(ti + 1) * 128, :], y[:, :],
                    [f"yb{b}"], [f"ys:{ti}"], f"{kt}_sy")
                ln_core(y[:, :], [f"yb{b}"], t2[:, :], f"t2b{b}", kt + "i")
                p.op("DVE", lambda e, t2=t2: e.tensor_mul(t2[:, :], t2[:, :],
                                                          cb_g1[:, :]),
                     reads=[f"t2b{b}", "C:g1"], writes=[f"t2b{b}"], tag=f"{kt}_g1")
                p.op("DVE", lambda e, t2=t2, h=h: e.tensor_add(h[:, :], t2[:, :],
                                                               cb_b1g[:, :]),
                     reads=[f"t2b{b}", "C:b1g"], writes=[f"hb{b}"], tag=f"{kt}_b1")
                transpose8(h[:, :], [f"hb{b}"], f"{kt}_tr")
                p.op("DVE", lambda e, i=i: e.tensor_copy(
                    hTc[:, :].rearrange("p (k n) -> p k n", k=KC)[:, :, i * 128:(i + 1) * 128],
                    PST[:, :].rearrange("p (k n) -> p k n", k=KC)),
                    reads=["pst"], writes=[f"hTc:{i}"], tag=f"{kt}_ev")
            hck = [f"hTc:{i}" for i in range(GT)]
            for m in range(16):
                bank = PSAK[m % 4]
                dst = qT_g if m < 8 else kT_g
                dk = ("qT" if m < 8 else "kT") + f"{gk}:{m % 8}"

                def qkmm(e, m=m):
                    inst = None
                    for k in range(KC):
                        inst = e.matmul(
                            PSA[:, (m % 4) * 512:(m % 4) * 512 + 512],
                            wqkv_sb[:, k * 3 * C + m * 128:k * 3 * C + (m + 1) * 128],
                            hTc[:, k * 512:(k + 1) * 512],
                            start=(k == 0), stop=(k == KC - 1))
                    return inst
                p.op("PE", qkmm, reads=hck + ["W:qkv"], writes=[bank], tag=f"{gk}qk{m}")
                p.op("ACT", lambda e, m=m, dst=dst: e.activation(
                    dst[:, (m % 8) * HGRP:(m % 8) * HGRP + 512],
                    PSA[:, (m % 4) * 512:(m % 4) * 512 + 512],
                    AF.Identity, bias=col_bqkv[:, m:m + 1]),
                    reads=[bank, "C:bqkv"], writes=[dk], tag=f"{gk}qke{m}")
            for i in range(GT):
                def vmm(e, i=i):
                    inst = None
                    for k in range(KC):
                        for nh_ in range(2):
                            inst = e.matmul(
                                PSB[:, nh_ * 512:(nh_ + 1) * 512],
                                hTc[:, k * 512 + i * 128:k * 512 + (i + 1) * 128],
                                wqkv_sb[:, k * 3 * C + 2 * C + nh_ * 512:
                                        k * 3 * C + 2 * C + (nh_ + 1) * 512],
                                start=(k == 0), stop=(k == KC - 1))
                    return inst
                p.op("PE", vmm, reads=hck + ["W:qkv"], writes=PSBK, tag=f"{gk}v{i}")
                p.op("DVE", lambda e, i=i: e.memset(
                    v65_g[i][:, :].rearrange("p (h w) -> p h w", h=NH)[:, :, HD:HD + 1],
                    1.0), writes=[f"v65:{i}"], tag=f"{gk}vo{i}")
                p.op("DVE", lambda e, i=i: e.scalar_tensor_tensor(
                    v65_g[i][:, :].rearrange("p (h w) -> p h w", h=NH)[:, :, 0:HD],
                    PSB[:, :].rearrange("p (h w) -> p h w", h=NH),
                    1.0, cb_bqv[:, :].rearrange("p (h w) -> p h w", h=NH),
                    ALU.mult, ALU.add),
                    reads=PSBK + ["C:bqv", f"v65:{i}"],
                    writes=[f"v65:{i}"], tag=f"{gk}ve{i}")
            qkkeys = [f"qT{gk}:{m}" for m in range(8)] + \
                     [f"kT{gk}:{m}" for m in range(8)]
            vkeys = [f"v65:{i}" for i in range(GT)]
            def hcol(h):
                return (h // 2) * 64 + (h % 2) * 512

            def SS(b):
                bi = b
                first = (g % GPB == 0) and bi == 0
                bk = f"{gk}b{bi}"
                pc = (bi % 2) * 64
                pp = 64 - pc

                def smm(e, bi=bi, first=first, pc=pc, pp=pp):
                    inst = None
                    for h in [0, 2, 4, 6, 8, 10, 12, 14, 1, 3, 5, 7, 9, 11, 13, 15]:
                        sI = h // 2
                        r = (h % 2) * 64
                        qsl = qT_g[r:r + 64, sI * HGRP + bi * 64:sI * HGRP + (bi + 1) * 64]
                        if first:
                            kp = zkT[r:r + 64, 0:64]
                        elif bi == 0:
                            kp = kT_c[r:r + 64, sI * BUCKET:(sI + 1) * BUCKET]
                        else:
                            kp = kT_g[r:r + 64, sI * HGRP + (bi - 1) * 64:sI * HGRP + bi * 64]
                        kc = kT_g[r:r + 64, sI * HGRP + bi * 64:sI * HGRP + (bi + 1) * 64]
                        inst = e.matmul(PSB[pp:pp + 64, hcol(h):hcol(h) + 64], kp, qsl,
                                        start=True, stop=True)
                        inst = e.matmul(PSB[pc:pc + 64, hcol(h):hcol(h) + 64], kc, qsl,
                                        start=True, stop=True)
                    return inst
                p.op("PE", smm, reads=qkkeys + ["zkT", "kTc"], writes=PSBK,
                     tag=f"{bk}_s")
                p.op("ACT", lambda e, b=b: e.activation(expb[b % 2][:, :], PSB[:, :],
                                                        AF.Exp, scale=SCALE_A),
                     reads=PSBK + ["Kc"], writes=[f"expb{b % 2}"], tag=f"{bk}_exp")
                p.op("DVE", lambda e, bi=bi, b=b: e.tensor_mul(
                    expb[b % 2][:, :], expb[b % 2][:, :],
                    sb_hmask[:, (bi % 2) * NH * BUCKET:(bi % 2 + 1) * NH * BUCKET]),
                    reads=[f"expb{b % 2}", "hmask"], writes=[f"expb{b % 2}"],
                    tag=f"{bk}_msk")

                def dmm(e, b=b):
                    e.matmul(PSB[0:1, 0:512], sb_onesb[:, 0:1], expb[b % 2][:, 0:512],
                             start=True, stop=True)
                    return e.matmul(PSB[0:1, 512:1024], sb_onesb[:, 0:1],
                                    expb[b % 2][:, 512:1024], start=True, stop=True)
                p.op("PE", dmm, reads=[f"expb{b % 2}", "onesb"], writes=PSBK,
                     tag=f"{bk}_d")
                p.op("DVE", lambda e: e.tensor_copy(lna[0:1, 0:1024], PSB[0:1, :]),
                     reads=PSBK, writes=["lnA"], tag=f"{bk}_rc0")
                p.op("ACT", lambda e: e.activation(lna[0:1, 1024:2048],
                                                   lna[0:1, 0:1024], AF.Ln),
                     reads=["lnA"], writes=["lnA2"], tag=f"{bk}_rc1")

                def rcf(e, b=b):
                    with nc.allow_low_precision(
                            reason="recip feeds a bf16-operand matmul; rounding "
                                   "here equals the matmul's own operand cast"):
                        return e.activation(rrow[b % 3][0:1, :],
                                            lna[0:1, 1024:2048], AF.Exp,
                                            scale=-1.0)
                p.op("ACT", rcf, reads=["lnA2", "Kc"], writes=[f"rrow{b % 3}"],
                     tag=f"{bk}_rc")

            def OM(b):
                bi = b
                first = (g % GPB == 0) and bi == 0
                bk = f"{gk}b{bi}"
                pc = (bi % 2) * 64
                pp = 64 - pc
                bko = (b % 2) * 2

                def omm(e, bi=bi, first=first, pc=pc, pp=pp, b=b, bko=bko):
                    inst = None
                    for grp in range(2):
                        for h in range(NH):
                            sI = h // 2
                            ro = (h % 2) * 64
                            co = (bko + grp) * 512 + sI * 64
                            if grp == 0:
                                if first:
                                    v = zv65[pp:pp + 64, h * 65:h * 65 + 64]
                                elif bi == 0:
                                    v = v65_c[64:128, h * 65:h * 65 + 64]
                                else:
                                    v = v65_g[(bi - 1) // 2][pp:pp + 64,
                                                             h * 65:h * 65 + 64]
                                rh = expb[b % 2][pp:pp + 64, hcol(h):hcol(h) + 64]
                            else:
                                v = v65_g[bi // 2][pc:pc + 64, h * 65:h * 65 + 64]
                                rh = expb[b % 2][pc:pc + 64, hcol(h):hcol(h) + 64]
                            inst = e.matmul(PSA[ro:ro + 64, co:co + 64], v, rh,
                                            start=True, stop=True)
                    return inst
                p.op("PE", omm, reads=[f"expb{b % 2}", "zv65", "v65c"] + vkeys,
                     writes=[PSAK[bko], PSAK[bko + 1]], tag=f"{bk}_o")

            def BM(b):
                bi = b
                bk = f"{gk}b{bi}"
                bko = (b % 2) * 2

                def bmm(e, b=b):
                    e.matmul(PSB[:, 0:512], sb_onesr[0:1, :],
                             rrow[b % 3][0:1, 0:512], start=True, stop=True)
                    return e.matmul(PSB[:, 512:1024], sb_onesr[0:1, :],
                                    rrow[b % 3][0:1, 512:1024], start=True, stop=True)
                p.op("PE", bmm, reads=[f"rrow{b % 3}", "onesr"],
                     writes=PSBK, tag=f"{bk}_bc")
                p.op("ACT", lambda e, b=b, bko=bko: e.activation(
                    orw[b % 2][0:64, 0:512], PSA[0:64, bko * 512:bko * 512 + 512],
                    AF.Copy),
                    reads=[PSAK[bko]], writes=[f"orwE{b % 2}"], tag=f"{bk}_ee")
                p.op("DVE", lambda e, b=b, bko=bko: e.scalar_tensor_tensor(
                    orw[b % 2][0:64, 0:512], PSA[0:64, (bko + 1) * 512:(bko + 2) * 512],
                    1.0, orw[b % 2][0:64, 0:512], ALU.mult, ALU.add),
                    reads=[PSAK[bko + 1], f"orwE{b % 2}"], writes=[f"orwE{b % 2}"],
                    tag=f"{bk}_ee2")
                p.op("ACT", lambda e, b=b, bko=bko: e.activation(
                    orw[b % 2][64:128, 0:512], PSA[64:128, bko * 512:bko * 512 + 512],
                    AF.Copy),
                    reads=[PSAK[bko]], writes=[f"orwO{b % 2}"], tag=f"{bk}_eo")
                p.op("DVE", lambda e, b=b, bko=bko: e.scalar_tensor_tensor(
                    orw[b % 2][64:128, 0:512], PSA[64:128, (bko + 1) * 512:(bko + 2) * 512],
                    1.0, orw[b % 2][64:128, 0:512], ALU.mult, ALU.add),
                    reads=[PSAK[bko + 1], f"orwO{b % 2}"], writes=[f"orwO{b % 2}"],
                    tag=f"{bk}_eo2")
                p.op("DVE", lambda e, bi=bi, b=b: e.tensor_tensor(
                    oT_g[0:64, :].rearrange("p (s n) -> p s n", s=KC)[:, :, bi * 64:(bi + 1) * 64],
                    orw[b % 2][0:64, 0:512].rearrange("p (s n) -> p s n", s=8),
                    PSB[0:64, 0:512].rearrange("p (s n) -> p s n", s=8),
                    ALU.mult),
                    reads=[f"orwE{b % 2}"] + PSBK, writes=[f"oT{gk}"],
                    tag=f"{bk}_ne")
                p.op("DVE", lambda e, bi=bi, b=b: e.tensor_tensor(
                    oT_g[64:128, :].rearrange("p (s n) -> p s n", s=KC)[:, :, bi * 64:(bi + 1) * 64],
                    orw[b % 2][64:128, 0:512].rearrange("p (s n) -> p s n", s=8),
                    PSB[64:128, 512:1024].rearrange("p (s n) -> p s n", s=8),
                    ALU.mult),
                    reads=[f"orwO{b % 2}"] + PSBK, writes=[f"oT{gk}"],
                    tag=f"{bk}_no")

            for b in range(GB + 2):
                if b < GB:
                    SS(b)
                if 1 <= b <= GB:
                    OM(b - 1)
                if b >= 2:
                    BM(b - 2)
            p.op("DVE", lambda e: e.tensor_copy(
                kT_c[:, :].rearrange("p (s n) -> p s n", s=KC),
                kT_g[:, :].rearrange("p (s n) -> p s n", s=KC)[:, :, HGRP - 64:HGRP]),
                reads=[f"kT{gk}:{m}" for m in range(8)], writes=["kTc"],
                tag=f"{gk}_kc")
            p.op("DVE", lambda e: e.tensor_copy(v65_c[64:128, :],
                                                v65_g[GT - 1][64:128, :]),
                 reads=[f"v65:{GT - 1}"], writes=["v65c"], tag=f"{gk}_vc")
            for i in range(GT):
                ti = g * GT + i
                b = i % 2
                kt = f"{gk}z{i}"

                def zmm(e, i=i):
                    inst = None
                    for k in range(KC):
                        for nh_ in range(2):
                            inst = e.matmul(
                                PSB[:, nh_ * 512:(nh_ + 1) * 512],
                                oT_g[:, k * HGRP + i * 128:k * HGRP + (i + 1) * 128],
                                wout_sb[:, k * C + nh_ * 512:k * C + (nh_ + 1) * 512],
                                start=(k == 0), stop=(k == KC - 1))
                    return inst
                p.op("PE", zmm, reads=[f"oT{gk}", "W:aout"], writes=PSBK, tag=kt)
                tb_, t2b_ = PK["T"][b], PK["T2"][b]
                dma("SP", tb_[:, :], ys[ti * 128:(ti + 1) * 128, :],
                    [f"ys:{ti}"], [f"tb{b}"], f"{kt}_ly")
                p.op("DVE", lambda e, t2b_=t2b_: e.scalar_tensor_tensor(
                    t2b_[:, :], PSB[:, :], 1.0, cb_bout[:, :],
                    ALU.mult, ALU.add),
                    reads=PSBK + ["C:bout"], writes=[f"t2b{b}"], tag=f"{kt}_b")
                p.op("DVE", lambda e, t2b_=t2b_, tb_=tb_: e.tensor_add(
                    t2b_[:, :], t2b_[:, :], tb_[:, :]),
                    reads=[f"t2b{b}", f"tb{b}"], writes=[f"t2b{b}"], tag=f"{kt}_y")
                dma("ACT", y2s[ti * 128:(ti + 1) * 128, :], t2b_[:, :],
                    [f"t2b{b}"], [f"y2s:{ti}"], f"{kt}_st")
        ar.pop()

        ar.push()
        alloc_pack()
        cb_g2 = ar.alloc(C, F32)
        cb_b2g = ar.alloc(C, F32)
        cb_b2 = ar.alloc(C, F32)
        col_b1 = ar.alloc(32, F32)
        mk_bcast(cb_g2, aa_g2.rearrange("(o n) -> o n", o=1), C, "g2")
        mk_bcast(cb_b2g, aa_b2g.rearrange("(o n) -> o n", o=1), C, "b2g")
        mk_bcast(cb_b2, aa_b2.rearrange("(o n) -> o n", o=1), C, "b2")
        mk_cols(col_b1, aa_b1, 32, "b1")
        w2_sb = ar.alloc((FF_A // 128) * C, BF16)
        w1p = [ar.alloc(KC * 128, BF16) for _ in range(4)]
        h2T = ar.alloc(KC * 512, BF16)
        m1T = ar.alloc((FF_A // 128) * 512, BF16)
        wchunks(w2_sb, "W:w2", aa_w2, FF_A, C, "aaw2")
        NF = FF_A // 128
        for ch in range(T // 512):
            ck = f"a4c{ch}"
            for i in range(4):
                ti = ch * 4 + i
                b = i % 2
                t, t2, h = PK["T"][b], PK["T2"][b], PK["H"][b]
                kt = f"a4{ti}"
                dma("SP", t[:, :], y2s[ti * 128:(ti + 1) * 128, :],
                    [f"y2s:{ti}"], [f"tb{b}"], kt)
                ln_core(t[:, :], [f"tb{b}"], t2[:, :], f"t2b{b}", kt)
                p.op("DVE", lambda e, t2=t2: e.tensor_mul(t2[:, :], t2[:, :],
                                                          cb_g2[:, :]),
                     reads=[f"t2b{b}", "C:g2"], writes=[f"t2b{b}"], tag=f"{kt}_g2")
                p.op("DVE", lambda e, t2=t2, h=h: e.tensor_add(h[:, :], t2[:, :],
                                                               cb_b2g[:, :]),
                     reads=[f"t2b{b}", "C:b2g"], writes=[f"hb{b}"], tag=f"{kt}_b2")
                transpose8(h[:, :], [f"hb{b}"], f"{kt}_tr")
                p.op("DVE", lambda e, i=i: e.tensor_copy(
                    h2T[:, :].rearrange("p (k n) -> p k n", k=KC)[:, :, i * 128:(i + 1) * 128],
                    PST[:, :].rearrange("p (k n) -> p k n", k=KC)),
                    reads=["pst"], writes=[f"h2T:{i}"], tag=f"{kt}_ev")
            hk = [f"h2T:{i}" for i in range(4)]
            for f in range(NF):
                wb = w1p[f % 4]
                wk = f"w1p{f % 4}"
                dma("POOL", wb[:, :].rearrange("p (k n) -> p k n", k=KC),
                    aa_w1.rearrange("(k p) n -> p k n", p=128)[:, :, f * 128:(f + 1) * 128],
                    ["H:w1"], [wk], f"{ck}w1_{f}")
                bank = PSAK[f % 4]

                def m1mm(e, f=f, wb=wb):
                    inst = None
                    for k in range(KC):
                        inst = e.matmul(PSA[:, (f % 4) * 512:(f % 4) * 512 + 512],
                                        wb[:, k * 128:(k + 1) * 128],
                                        h2T[:, k * 512:(k + 1) * 512],
                                        start=(k == 0), stop=(k == KC - 1))
                    return inst
                p.op("PE", m1mm, reads=hk + [wk], writes=[bank], tag=f"{ck}m1_{f}")
                p.op("ACT", lambda e, f=f: e.activation(
                    m1T[:, f * 512:(f + 1) * 512],
                    PSA[:, (f % 4) * 512:(f % 4) * 512 + 512],
                    AF.Gelu, bias=col_b1[:, f:f + 1]),
                    reads=[bank, "C:b1"], writes=[f"m1T:{f}"], tag=f"{ck}ge_{f}")
            m1k = [f"m1T:{f}" for f in range(NF)]
            for i in range(4):
                ti = ch * 4 + i
                b = i % 2

                def m2mm(e, i=i):
                    inst = None
                    for k in range(NF):
                        for nh_ in range(2):
                            inst = e.matmul(
                                PSB[:, nh_ * 512:(nh_ + 1) * 512],
                                m1T[:, k * 512 + i * 128:k * 512 + (i + 1) * 128],
                                w2_sb[:, k * C + nh_ * 512:k * C + (nh_ + 1) * 512],
                                start=(k == 0), stop=(k == NF - 1))
                    return inst
                kt = f"{ck}m2_{i}"
                p.op("PE", m2mm, reads=m1k + ["W:w2"], writes=PSBK, tag=kt)
                tb_, t2b_, yb_ = PK["T"][b], PK["T2"][b], PK["Y"][b]
                dma("SP", tb_[:, :], y2s[ti * 128:(ti + 1) * 128, :],
                    [f"y2s:{ti}"], [f"tb{b}"], f"{kt}_l2")
                dma("SP", yb_[:, :], xin[ti * 128:(ti + 1) * 128, :],
                    [f"H:xin:{ti}"], [f"yb{b}"], f"{kt}_l0")
                p.op("DVE", lambda e, t2b_=t2b_: e.scalar_tensor_tensor(
                    t2b_[:, :], PSB[:, :], 1.0, cb_b2[:, :],
                    ALU.mult, ALU.add),
                    reads=PSBK + ["C:b2"], writes=[f"t2b{b}"], tag=f"{kt}_b")
                p.op("DVE", lambda e, t2b_=t2b_, tb_=tb_: e.tensor_add(
                    t2b_[:, :], t2b_[:, :], tb_[:, :]),
                    reads=[f"t2b{b}", f"tb{b}"], writes=[f"t2b{b}"], tag=f"{kt}_y2")
                p.op("DVE", lambda e, t2b_=t2b_, yb_=yb_: e.tensor_add(
                    t2b_[:, :], t2b_[:, :], yb_[:, :]),
                    reads=[f"t2b{b}", f"yb{b}"], writes=[f"t2b{b}"], tag=f"{kt}_x0")
                dma("ACT", r1[ti * 128:(ti + 1) * 128, :], t2b_[:, :],
                    [f"t2b{b}"], [f"r1:{ti}"], f"{kt}_st")
        ar.pop()

    def ffn(tag, rin, rin_name, rout, rout_name, G, Geo, SHS):
        ar.push()
        nzi = (NSLOT + 127) // 128
        for zi in range(nzi):
            r0 = zi * 128
            nr = min(128, NSLOT - r0)
            dma("ACT", G[r0:r0 + nr, :], sb_zt[0:nr, :], ["zt"], ["Ginit"],
                f"{tag}gz{zi}")
        ztf = sb_zt[:, :].bitcast(F32)
        dma("ACT", Geo[NEXP * CAP:NSLOT, 0:512], ztf[0:NTILES, 0:512],
            ["zt"], ["Geoinit"], f"{tag}ez0")
        dma("ACT", Geo[NEXP * CAP:NSLOT, 512:1024], ztf[0:NTILES, 0:512],
            ["zt"], ["Geoinit"], f"{tag}ez1")
        fmask = ar.alloc(NTILES * NEXP, F32)
        rankw = ar.alloc(NTILES * NEXP, F32)
        desti = ar.alloc(NTILES, mybir.dt.int32)
        cntrow = ar.alloc(NTILES * NEXP, F32)
        baserow = ar.alloc(NTILES * NEXP, F32)
        cmat = ar.alloc(NEXP, F32)
        lgtall = ar.alloc(NTILES * NEXP, F32)
        mb0 = ar.alloc(NTILES * NEXP, F32)
        mb8 = ar.alloc(NTILES * NEXP, F32)
        mxsc = ar.alloc(2 * NTILES, F32)
        dtmp = ar.alloc(8 * NTILES, F32)
        PSCK2 = ["psc0", "psc1"]
        sl01 = [ar.alloc(16 * 1024, BF16) for _ in range(2)]
        wchunks(sl01[0], "slot0", f_sg, C, FF_E, f"{tag}shg")
        wchunks(sl01[1], "slot1", f_su, C, FF_E, f"{tag}shu")
        ar.push()
        xsall = ar.alloc(NTILES * C, BF16)
        alloc_pack()
        xstg = [ar.alloc(1024, BF16) for _ in range(2)]
        xtf = [ar.alloc(1024, F32) for _ in range(2)]
        for ti in range(NTILES):
            b = ti % 2
            t, y, h = PK["T"][b], PK["Y"][b], PK["H"][b]
            kt = f"{tag}L{ti}"
            xt = xtf[b]
            po = 1024 * b
            pk2 = [f"psa{2 * b}", f"psa{2 * b + 1}"]
            dma("SP", t[:, :], rin[ti * 128:(ti + 1) * 128, :],
                [f"{rin_name}:{ti}"], [f"tb{b}"], kt)
            ln_core(t[:, :], [f"tb{b}"], y[:, :], f"yb{b}", kt)
            p.op("PE", lambda e, y=y, po=po: [e.transpose(
                PSA[:, po + k * 128:po + (k + 1) * 128],
                y[:, k * 128:(k + 1) * 128], sb_idf[:, :]) for k in range(KC)][-1],
                reads=[f"yb{b}", "idf"], writes=pk2, tag=f"{kt}_ftr")
            p.op("DVE", lambda e, xt=xt, po=po: e.tensor_copy(
                xt[:, :], PSA[:, po:po + 1024]),
                reads=pk2, writes=[f"xtf{b}"], tag=f"{kt}_fev")

            def rmm(e, xt=xt, b=b):
                inst = None
                for k in range(KC):
                    inst = e.matmul(PSC[:, b * 8:b * 8 + 8],
                                    xt[:, k * 128:(k + 1) * 128],
                                    rwT[:, k * 8:(k + 1) * 8],
                                    start=(k == 0), stop=(k == KC - 1))
                return inst
            p.op("PE", rmm, reads=[f"xtf{b}", "C:rwT"], writes=[f"psc{b}"],
                 tag=f"{kt}_rt")
            p.op("DVE", lambda e, ti=ti, b=b: e.tensor_copy(
                lgtall[:, ti * 8:(ti + 1) * 8], PSC[:, b * 8:b * 8 + 8]),
                reads=[f"psc{b}"], writes=["lga"], tag=f"{kt}_lg")
            p.op("ACT", lambda e, y=y, ti=ti: e.activation(
                xsall[:, ti * C:(ti + 1) * C], y[:, :], AF.Copy),
                reads=[f"yb{b}"], writes=[f"xsa:{ti}"], tag=f"{kt}_cx")
            p.op("ACT", lambda e, y=y, h=h: e.activation(h[:, :], y[:, :], AF.Copy),
                 reads=[f"yb{b}"], writes=[f"hb{b}"], tag=f"{kt}_cy")
            transpose8(h[:, :], [f"hb{b}"], f"{kt}_t1")
            p.op("DVE", lambda e, b=b: e.tensor_copy(xstg[b][:, :], PST[:, :]),
                 reads=["pst"], writes=[f"xstg{b}"], tag=f"{kt}_e1")
            dma("ACT", xTs.rearrange("(k p) n -> p k n", p=128)[:, :, ti * 128:(ti + 1) * 128],
                xstg[b][:, :].rearrange("p (k n) -> p k n", k=KC),
                [f"xstg{b}"], [f"xTs:{ti}"], f"{kt}_s1")
        lg3 = lgtall[:, 0:NTILES * NEXP].rearrange("p (t e) -> p t e", e=NEXP)
        mx3 = mxsc[:, 0:NTILES].rearrange("p (t o) -> p t o", o=1)
        p.op("DVE", lambda e: e.reduce_max(mx3, lg3, axis=AX),
             reads=["lga"], writes=["mxc"], tag=f"{tag}mx")
        p.op("ACT", lambda e: e.activation(mxsc[:, NTILES:2 * NTILES],
                                           mxsc[:, 0:NTILES], AF.Sigmoid),
             reads=["mxc"], writes=["scal"], tag=f"{tag}sg")
        for ti in range(NTILES):
            g0, g1 = ti * NEXP, (ti + 1) * NEXP
            p.op("DVE", lambda e, g0=g0, g1=g1, ti=ti: e.tensor_single_scalar(
                mb0[:, g0:g1], lgtall[:, g0:g1], mxsc[:, ti:ti + 1], ALU.is_equal),
                reads=["lga", "mxc"], writes=["m0"], tag=f"{tag}eq{ti}")
            p.op("DVE", lambda e, g0=g0, g1=g1: e.tensor_tensor_scan(
                mb8[:, g0:g1], mb0[:, g0:g1], mb0[:, g0:g1], 0.0,
                ALU.add, ALU.bypass),
                reads=["m0"], writes=["m8"], tag=f"{tag}cs{ti}")
        p.op("DVE", lambda e: e.tensor_scalar(mb8[:, :], mb8[:, :], 1.0, None,
                                              ALU.is_equal),
             reads=["m8"], writes=["m8"], tag=f"{tag}c1")
        p.op("DVE", lambda e: e.tensor_mul(fmask[:, :], mb0[:, :], mb8[:, :]),
             reads=["m0", "m8"], writes=["fma"], tag=f"{tag}fm")
        p.op("PE", lambda e: e.matmul(PSC[:, 256:512], sb_ltri[:, :], fmask[:, :],
                                      start=True, stop=True),
             reads=["fma", "ltri"], writes=PSCK2, tag=f"{tag}rk")
        p.op("PE", lambda e: e.matmul(PSC[0:1, 0:256], sb_onesf[:, 0:1], fmask[:, :],
                                      start=True, stop=True),
             reads=["fma", "onesf"], writes=PSCK2, tag=f"{tag}ct")
        p.op("DVE", lambda e: e.tensor_copy(rankw[:, :], PSC[:, 256:512]),
             reads=PSCK2, writes=["rka"], tag=f"{tag}rke")
        p.op("DVE", lambda e: e.tensor_copy(cntrow[0:1, :], PSC[0:1, 0:256]),
             reads=PSCK2, writes=["cnt"], tag=f"{tag}cte")
        for ti in range(NTILES):
            def xsf(e, ti=ti):
                with nc.allow_low_precision(
                        reason="xs feeds bf16-operand matmuls; rounding here "
                               "equals the matmul's own operand cast"):
                    return e.tensor_scalar(
                        xsall[:, ti * C:(ti + 1) * C], xsall[:, ti * C:(ti + 1) * C],
                        mxsc[:, NTILES + ti:NTILES + ti + 1], None, ALU.mult)
            p.op("DVE", xsf,
                 reads=[f"xsa:{ti}", "scal"], writes=[f"xsa:{ti}"], tag=f"{tag}xs{ti}")
        dma("SP", cmat[0:NTILES, 0:NEXP],
            cntrow[0:1, 0:NTILES * NEXP].rearrange("o (t e) -> o t e", e=NEXP),
            ["cnt"], ["cmat"], f"{tag}cm")
        p.op("PE", lambda e: e.matmul(PSC[0:NTILES, 8:16], sb_ltri[0:NTILES, 0:NTILES],
                                      cmat[0:NTILES, 0:NEXP], start=True, stop=True),
             reads=["cmat", "ltri"], writes=PSCK2, tag=f"{tag}bs")
        p.op("DVE", lambda e: e.tensor_copy(cmat[0:NTILES, 0:NEXP], PSC[0:NTILES, 8:16]),
             reads=PSCK2, writes=["cmat2"], tag=f"{tag}bse")
        dma("SP", baserow[0:1, 0:NTILES * NEXP].rearrange("o (t e) -> o t e", e=NEXP),
            cmat[0:NTILES, 0:NEXP], ["cmat2"], ["baserow"], f"{tag}br")
        p.op("PE", lambda e: e.matmul(PSC[:, 0:256], sb_ones[0:1, :],
                                      baserow[0:1, 0:256], start=True, stop=True),
             reads=["baserow", "ones"], writes=PSCK2, tag=f"{tag}bb")

        def d3(c):
            return dtmp[:, c * NTILES:(c + 1) * NTILES]

        def t3(buf):
            return buf[:, :].rearrange("p (t e) -> p t e", e=NEXP)

        def dr3(c):
            return d3(c).rearrange("p (t o) -> p t o", o=1)
        p.op("DVE", lambda e: e.tensor_mul(mb0[:, :], rankw[:, :], fmask[:, :]),
             reads=["rka", "fma"], writes=["m0"], tag=f"{tag}rs")
        p.op("DVE", lambda e: e.reduce_sum(dr3(0), t3(mb0), axis=AX),
             reads=["m0"], writes=["dR"], tag=f"{tag}rs2")
        p.op("DVE", lambda e: e.tensor_mul(mb8[:, :], PSC[:, 0:256], fmask[:, :]),
             reads=PSCK2 + ["fma"], writes=["m8"], tag=f"{tag}bsl")
        p.op("DVE", lambda e: e.reduce_sum(dr3(1), t3(mb8), axis=AX),
             reads=["m8"], writes=["dB"], tag=f"{tag}bsl2")
        p.op("DVE", lambda e: e.tensor_mul(mb0[:, :], sb_iota256[:, :], fmask[:, :]),
             reads=["iota256", "fma"], writes=["m0"], tag=f"{tag}cp")
        p.op("DVE", lambda e: e.reduce_sum(dr3(2), t3(mb0), axis=AX),
             reads=["m0"], writes=["dE"], tag=f"{tag}cp2")
        p.op("DVE", lambda e: e.tensor_scalar(d3(2), d3(2), float(CAP), None,
                                              ALU.mult),
             reads=["dE"], writes=["dE"], tag=f"{tag}cpm")
        p.op("DVE", lambda e: e.tensor_add(d3(3), d3(0), d3(1)),
             reads=["dR", "dB"], writes=["dS"], tag=f"{tag}rb")
        p.op("DVE", lambda e: e.tensor_single_scalar(d3(4), d3(3), float(CAP),
                                                     ALU.is_ge),
             reads=["dS"], writes=["dO"], tag=f"{tag}ov")
        p.op("DVE", lambda e: e.tensor_add(d3(5), d3(3), d3(2)),
             reads=["dS", "dE"], writes=["dF"], tag=f"{tag}df")
        p.op("DVE", lambda e: e.tensor_scalar(d3(6), d3(5), -1.0, float(NEXP * CAP),
                                              ALU.mult, ALU.add),
             reads=["dF"], writes=["dT"], tag=f"{tag}tr")
        p.op("DVE", lambda e: e.tensor_add(d3(6), d3(6), sb_iota32[:, 0:NTILES]),
             reads=["dT", "iota32"], writes=["dT"], tag=f"{tag}tr2")
        p.op("DVE", lambda e: e.tensor_mul(d3(4), d3(4), d3(6)),
             reads=["dO", "dT"], writes=["dO"], tag=f"{tag}cl")
        p.op("DVE", lambda e: e.tensor_add(d3(7), d3(4), d3(5)),
             reads=["dO", "dF"], writes=["dPre"], tag=f"{tag}cl2")
        p.op("DVE", lambda e: e.tensor_copy(desti[:, 0:NTILES], d3(7)),
             reads=["dPre"], writes=["diall"], tag=f"{tag}ci")
        for ti in range(NTILES):
            p.op("POOL", lambda e, ti=ti: e.indirect_dma_start(
                G, bass.IndirectOffsetOnAxis(ap=desti[:, ti:ti + 1], axis=0),
                xsall[:, ti * C:(ti + 1) * C], None),
                reads=["diall", f"xsa:{ti}", "Ginit"], writes=[f"H:G:{ti}"],
                dma=True, tag=f"{tag}sc{ti}")
        ar.pop()

        slots = sl01 + [ar.alloc(16 * 1024, BF16)]
        hT = ar.alloc(16 * 512, BF16)
        xch = ar.alloc(KC * 512, BF16)
        stile = [ar.alloc(512, BF16) for _ in range(2)]
        ebuf = [ar.alloc(1024, F32) for _ in range(2)]
        xg = [ar.alloc(1024, BF16) for _ in range(4)]

        def expert_block(en, wg, wu, wd, nch, load_x, out_rows, out_name,
                         skip_gu=False):
            ek = f"{tag}{en}"
            sg, su, sd = slots
            kg, ku, kd = "slot0", "slot1", "slot2"
            if not skip_gu:
                wchunks(sg, kg, wg, C, FF_E, f"{ek}g")
                wchunks(su, ku, wu, C, FF_E, f"{ek}u")
            wchunks(sd, kd, wd, FF_E, C, f"{ek}d")
            for ch in range(nch):
                load_x(ek, ch)
                for f in range(16):
                    st = stile[f % 2]
                    stk = f"stile{f % 2}"
                    gco = (f % 2) * 1024
                    uco = (f % 2) * 1024 + 512
                    gk2 = PSAK[(f % 2) * 2]
                    uk2 = PSAK[(f % 2) * 2 + 1]

                    def gmm(e, f=f, sg=sg, gco=gco):
                        inst = None
                        for k in range(KC):
                            inst = e.matmul(
                                PSA[:, gco:gco + 512],
                                sg[:, k * FF_E + f * 128:k * FF_E + (f + 1) * 128],
                                xch[:, k * 512:(k + 1) * 512],
                                start=(k == 0), stop=(k == KC - 1))
                        return inst

                    def umm(e, f=f, su=su, uco=uco):
                        inst = None
                        for k in range(KC):
                            inst = e.matmul(
                                PSA[:, uco:uco + 512],
                                su[:, k * FF_E + f * 128:k * FF_E + (f + 1) * 128],
                                xch[:, k * 512:(k + 1) * 512],
                                start=(k == 0), stop=(k == KC - 1))
                        return inst
                    p.op("PE", gmm, reads=[kg, "xch"], writes=[gk2], tag=f"{ek}g{ch}_{f}")
                    p.op("PE", umm, reads=[ku, "xch"], writes=[uk2], tag=f"{ek}u{ch}_{f}")
                    p.op("ACT", lambda e, st=st, gco=gco: e.activation(
                        st[:, :], PSA[:, gco:gco + 512], AF.Silu),
                        reads=[gk2], writes=[stk], tag=f"{ek}s{ch}_{f}")
                    p.op("DVE", lambda e, f=f, st=st, uco=uco: e.tensor_tensor(
                        hT[:, f * 512:(f + 1) * 512], st[:, :],
                        PSA[:, uco:uco + 512], ALU.mult),
                        reads=[stk, uk2], writes=[f"hT:{f}"], tag=f"{ek}h{ch}_{f}")
                hk = [f"hT:{f}" for f in range(16)]
                for it in range(4):
                    eb2 = ebuf[it % 2]
                    ebk = f"ebuf{it % 2}"
                    for nh_ in range(2):
                        bkk = PSBK[nh_]

                        def dnmm(e, it=it, sd=sd, nh_=nh_):
                            inst = None
                            for k in range(16):
                                inst = e.matmul(
                                    PSB[:, nh_ * 512:(nh_ + 1) * 512],
                                    hT[:, k * 512 + it * 128:k * 512 + (it + 1) * 128],
                                    sd[:, k * C + nh_ * 512:k * C + (nh_ + 1) * 512],
                                    start=(k == 0), stop=(k == 15))
                            return inst
                        p.op("PE", dnmm, reads=hk + [kd], writes=[bkk],
                             tag=f"{ek}d{ch}_{it}_{nh_}")
                        p.op("DVE", lambda e, nh_=nh_, eb2=eb2: e.tensor_copy(
                            eb2[:, nh_ * 512:(nh_ + 1) * 512],
                            PSB[:, nh_ * 512:(nh_ + 1) * 512]),
                            reads=[bkk], writes=[ebk + f":{nh_}"],
                            tag=f"{ek}e{ch}_{it}_{nh_}")
                    r0 = out_rows(ch, it)
                    dma("ACT", r0, eb2[:, :], [ebk + ":0", ebk + ":1"],
                        [f"{out_name}:{en}:{ch}:{it}"], f"{ek}o{ch}_{it}")

        def shared_loadx(ek, ch):
            dma("SP", xch[:, :].rearrange("p (k n) -> p k n", k=KC),
                xTs.rearrange("(k p) n -> p k n", p=128)[:, :, ch * 512:(ch + 1) * 512],
                [f"xTs:{tt}" for tt in range(ch * 4, ch * 4 + 4)],
                ["xch"], f"{ek}x{ch}")
        expert_block("sh", f_sg, f_su, f_sd, 8, shared_loadx,
                     lambda ch, it: SHS[(ch * 4 + it) * 128:(ch * 4 + it + 1) * 128, :],
                     "H:SHS", skip_gu=True)
        shs_keys = [f"H:SHS:sh:{ch}:{it}" for ch in range(8) for it in range(4)]

        for x in range(NEXP):
            def exp_loadx(ek, ch, x=x):
                for i in range(4):
                    b = i % 2
                    row0 = x * CAP + ch * 512 + i * 128
                    dma("SP", xg[b][:, :], G[row0:row0 + 128, :],
                        [f"H:G:{tt}" for tt in range(NTILES)], [f"xg{b}"],
                        f"{ek}xl{ch}_{i}")
                    transpose8(xg[b][:, :], [f"xg{b}"], f"{ek}xt{ch}_{i}")
                    p.op("DVE", lambda e, i=i: e.tensor_copy(
                        xch[:, :].rearrange("p (k n) -> p k n", k=KC)[:, :, i * 128:(i + 1) * 128],
                        PST[:, :].rearrange("p (k n) -> p k n", k=KC)),
                        reads=["pst"], writes=["xch"], tag=f"{ek}xe{ch}_{i}")
            expert_block(f"e{x}", f_eg[x], f_eu[x], f_ed[x], CAP // 512, exp_loadx,
                         lambda ch, it, x=x: Geo[x * CAP + (ch * 4 + it) * 128:
                                                 x * CAP + (ch * 4 + it + 1) * 128, :],
                         "H:Geo")
        geo_keys = [f"H:Geo:e{x}:{ch}:{it}" for x in range(NEXP)
                    for ch in range(CAP // 512) for it in range(4)]

        ar.push()
        eb = [ar.alloc(1024, F32) for _ in range(4)]
        mg = [ar.alloc(1024, F32) for _ in range(2)]
        for ti in range(NTILES):
            b = ti % 2
            kt = f"{tag}E{ti}"
            dma("SP", eb[b][:, :], rin[ti * 128:(ti + 1) * 128, :],
                [f"{rin_name}:{ti}"], [f"ebl{b}"], kt)
            dma("SP", eb[2 + b][:, :], SHS[ti * 128:(ti + 1) * 128, :],
                shs_keys, [f"ebs{b}"], f"{kt}_sh")
            p.op("POOL", lambda e, ti=ti, b=b: e.indirect_dma_start(
                mg[b][:, :], None, Geo,
                bass.IndirectOffsetOnAxis(ap=desti[:, ti:ti + 1], axis=0)),
                reads=geo_keys + ["Geoinit", "diall"], writes=[f"mg{b}"],
                dma=True, tag=f"{kt}_gt")
            p.op("DVE", lambda e, b=b: e.tensor_add(eb[b][:, :], eb[b][:, :],
                                                    eb[2 + b][:, :]),
                 reads=[f"ebl{b}", f"ebs{b}"], writes=[f"ebl{b}"], tag=f"{kt}_a1")
            p.op("DVE", lambda e, b=b: e.tensor_add(eb[b][:, :], eb[b][:, :],
                                                    mg[b][:, :]),
                 reads=[f"ebl{b}", f"mg{b}"], writes=[f"ebl{b}"], tag=f"{kt}_a2")
            dma("ACT", rout[ti * 128:(ti + 1) * 128, :], eb[b][:, :],
                [f"ebl{b}"], [f"{rout_name}:{ti}"], f"{kt}_st")
        ar.pop()
        ar.pop()

    if "f1" in stages:
        ffn("f1", r1, "r1", r2, "r2", g1s, g1eo, sh1s)

    if "ab" in stages:
        ar.push()
        kT_b = ar.alloc(KC * SEQ, BF16)
        v65_b = [ar.alloc(V65W, BF16) for _ in range(SEQ // 128)]
        for bb in range(2):
            bt0 = bb * 16
            bk0 = f"ab{bb}"
            vkeys = [f"v65b:{t}" for t in range(16)]
            kkeys = [f"kTb:{m}" for m in range(8)]
            ar.push()
            alloc_pack()
            xTb = ar.alloc(KC * 512, BF16)
            qstg = ar.alloc(KC * 512, BF16)
            wring = [ar.alloc(KC * C, BF16) for _ in range(2)]
            wseq = 0
            for ch in range(4):
                for i in range(4):
                    ti = bt0 + ch * 4 + i
                    b = i % 2
                    t, y, h = PK["T"][b], PK["Y"][b], PK["H"][b]
                    kt = f"{bk0}p{ti}"
                    dma("SP", t[:, :], r2[ti * 128:(ti + 1) * 128, :],
                        [f"r2:{ti}"], [f"tb{b}"], kt)
                    ln_core(t[:, :], [f"tb{b}"], y[:, :], f"yb{b}", kt)
                    p.op("ACT", lambda e, y=y, h=h: e.activation(h[:, :], y[:, :],
                                                                 AF.Copy),
                         reads=[f"yb{b}"], writes=[f"hb{b}"], tag=f"{kt}_c")
                    transpose8(h[:, :], [f"hb{b}"], f"{kt}_tr")
                    p.op("DVE", lambda e, i=i: e.tensor_copy(
                        xTb[:, :].rearrange("p (k n) -> p k n", k=KC)[:, :, i * 128:(i + 1) * 128],
                        PST[:, :].rearrange("p (k n) -> p k n", k=KC)),
                        reads=["pst"], writes=[f"xTb:{i}"], tag=f"{kt}_ev")
                xbk = [f"xTb:{i}" for i in range(4)]
                for wsrc, wtag in [(ab_wq, "q"), (ab_wk, "k"), (ab_wv, "v")]:
                    wb = wring[wseq % 2]
                    wk = f"wring{wseq % 2}"
                    wseq += 1
                    wchunks(wb, wk, wsrc, C, C, f"{bk0}w{wtag}{ch}")
                    if wtag != "v":
                        for m in range(8):
                            def pmm(e, m=m, wb=wb):
                                inst = None
                                for k in range(KC):
                                    inst = e.matmul(
                                        PSA[:, (m % 4) * 512:(m % 4) * 512 + 512],
                                        wb[:, k * C + m * 128:k * C + (m + 1) * 128],
                                        xTb[:, k * 512:(k + 1) * 512],
                                        start=(k == 0), stop=(k == KC - 1))
                                return inst
                            p.op("PE", pmm, reads=xbk + [wk], writes=[PSAK[m % 4]],
                                 tag=f"{bk0}{wtag}{ch}_{m}")
                            if wtag == "q":
                                p.op("ACT", lambda e, m=m: e.activation(
                                    qstg[:, m * 512:(m + 1) * 512],
                                    PSA[:, (m % 4) * 512:(m % 4) * 512 + 512],
                                    AF.Copy, scale=SCALE_B),
                                    reads=[PSAK[m % 4], "Kc"], writes=[f"qstg:{m}"],
                                    tag=f"{bk0}qe{ch}_{m}")
                            else:
                                p.op("ACT", lambda e, m=m, ch=ch: e.activation(
                                    kT_b[:, m * SEQ + ch * 512:m * SEQ + ch * 512 + 512],
                                    PSA[:, (m % 4) * 512:(m % 4) * 512 + 512],
                                    AF.Copy, scale=SCALE_B),
                                    reads=[PSAK[m % 4], "Kc"], writes=[f"kTb:{m}"],
                                    tag=f"{bk0}ke{ch}_{m}")
                        if wtag == "q":
                            dma("ACT", qTs.rearrange("(k p) n -> p k n", p=128)
                                [:, :, bt0 * 128 + ch * 512:bt0 * 128 + (ch + 1) * 512],
                                qstg[:, :].rearrange("p (k n) -> p k n", k=KC),
                                [f"qstg:{m}" for m in range(8)], ["qstg_out"],
                                f"{bk0}qs{ch}")
                    else:
                        for i in range(4):
                            vt = ch * 4 + i

                            def vmmb(e, i=i, wb=wb):
                                inst = None
                                for k in range(KC):
                                    for nh_ in range(2):
                                        inst = e.matmul(
                                            PSB[:, nh_ * 512:(nh_ + 1) * 512],
                                            xTb[:, k * 512 + i * 128:k * 512 + (i + 1) * 128],
                                            wb[:, k * C + nh_ * 512:k * C + (nh_ + 1) * 512],
                                            start=(k == 0), stop=(k == KC - 1))
                                return inst
                            p.op("PE", vmmb, reads=xbk + [wk], writes=PSBK,
                                 tag=f"{bk0}v{vt}")
                            p.op("DVE", lambda e, vt=vt: e.memset(
                                v65_b[vt][:, :].rearrange("p (h w) -> p h w", h=NH)
                                [:, :, HD:HD + 1], 1.0),
                                writes=[f"v65b:{vt}"], tag=f"{bk0}vo{vt}")
                            p.op("DVE", lambda e, vt=vt: e.tensor_copy(
                                v65_b[vt][:, :].rearrange("p (h w) -> p h w", h=NH)[:, :, 0:HD],
                                PSB[:, :].rearrange("p (h w) -> p h w", h=NH)),
                                reads=PSBK + [f"v65b:{vt}"], writes=[f"v65b:{vt}"],
                                tag=f"{bk0}ve{vt}")
            ar.pop()
            ar.push()
            qTc = [ar.alloc(KC * 512, BF16) for _ in range(2)]
            oTc = [ar.alloc(KC * 512, BF16) for _ in range(2)]
            expt = [ar.alloc(16 * 512, BF16) for _ in range(2)]
            orwB = [ar.alloc(512, F32) for _ in range(2)]
            rrowB = [ar.alloc(512, F32) for _ in range(2)]
            lnb = ar.alloc(1024, F32)
            shft = [ar.alloc(512, BF16) for _ in range(2)]
            J = 4 * NH

            def qload(ic):
                dma("SP", qTc[ic % 2][:, :].rearrange("p (k n) -> p k n", k=KC),
                    qTs.rearrange("(k p) n -> p k n", p=128)
                    [:, :, bt0 * 128 + ic * 512:bt0 * 128 + (ic + 1) * 512],
                    ["qstg_out"], [f"qTc{ic % 2}"], f"{bk0}lq{ic}")

            def S(j):
                ic, h = j // NH, j % NH
                sI = h // 2
                r = (h % 2) * 64
                akt = f"{bk0}i{ic}h{h}"
                qsl = qTc[ic % 2][r:r + 64, sI * 512:(sI + 1) * 512]
                for w in range(8):
                    pk2 = [PSAK[(w % 2) * 2], PSAK[(w % 2) * 2 + 1]]

                    def swv(e, w=w, r=r, sI=sI, qsl=qsl):
                        inst = None
                        for jj in range(2):
                            jt = w * 2 + jj
                            co = (w % 2) * 1024 + jj * 512
                            inst = e.matmul(
                                PSA[:, co:co + 512],
                                kT_b[r:r + 64, sI * SEQ + jt * 128:sI * SEQ + (jt + 1) * 128],
                                qsl, start=True, stop=True)
                        return inst
                    p.op("PE", swv, reads=kkeys + [f"qTc{ic % 2}"], writes=pk2,
                         tag=f"{akt}_s{w}")

                    def ewv(e, w=w, j=j):
                        inst = None
                        for jj in range(2):
                            jt = w * 2 + jj
                            co = (w % 2) * 1024 + jj * 512
                            inst = e.activation(
                                expt[j % 2][:, jt * 512:(jt + 1) * 512],
                                PSA[:, co:co + 512], AF.Exp)
                        return inst
                    p.op("ACT", ewv, reads=pk2, writes=[f"ex{j % 2}:{w}"],
                         tag=f"{akt}_e{w}")

            def O(j):
                ic, h = j // NH, j % NH
                akt = f"{bk0}i{ic}h{h}"
                ek = [f"ex{j % 2}:{w}" for w in range(8)]
                pco = (j % 2) * 512

                def ommb(e, h=h, j=j, pco=pco):
                    inst = None
                    for jt in range(16):
                        inst = e.matmul(PSB[0:65, pco:pco + 512],
                                        v65_b[jt][:, h * 65:h * 65 + 65],
                                        expt[j % 2][:, jt * 512:(jt + 1) * 512],
                                        start=(jt == 0), stop=(jt == 15))
                    return inst
                p.op("PE", ommb, reads=ek + vkeys, writes=[PSBK[j % 2]],
                     tag=f"{akt}_o")
                p.op("DVE", lambda e, pco=pco: e.tensor_copy(
                    lnb[0:1, 0:512], PSB[64:65, pco:pco + 512]),
                    reads=[PSBK[j % 2]], writes=["lnB"], tag=f"{akt}_rc0")
                p.op("ACT", lambda e: e.activation(lnb[0:1, 512:1024],
                                                   lnb[0:1, 0:512], AF.Ln),
                     reads=["lnB"], writes=["lnB2"], tag=f"{akt}_rc1")
                p.op("ACT", lambda e, j=j: e.activation(
                    rrowB[j % 2][0:1, :], lnb[0:1, 512:1024], AF.Exp, scale=-1.0),
                    reads=["lnB2", "Kc"], writes=[f"rrB{j % 2}"], tag=f"{akt}_rc")

            def B(j):
                ic, h = j // NH, j % NH
                sI = h // 2
                akt = f"{bk0}i{ic}h{h}"
                pco = (j % 2) * 512
                p.op("PE", lambda e, j=j: e.matmul(
                    PSC[0:64, :], sb_ones[0:1, 0:64], rrowB[j % 2][0:1, :],
                    start=True, stop=True),
                    reads=[f"rrB{j % 2}", "ones"], writes=["psc"], tag=f"{akt}_bc")
                p.op("ACT", lambda e, j=j, pco=pco: e.activation(
                    orwB[j % 2][0:64, :], PSB[0:64, pco:pco + 512], AF.Copy),
                    reads=[PSBK[j % 2]], writes=[f"orB{j % 2}"], tag=f"{akt}_ev")
                if h % 2 == 0:
                    p.op("DVE", lambda e, sI=sI, j=j, ic=ic: e.tensor_tensor(
                        oTc[ic % 2][0:64, sI * 512:(sI + 1) * 512],
                        orwB[j % 2][0:64, :], PSC[0:64, :], ALU.mult),
                        reads=[f"orB{j % 2}", "psc"], writes=[f"oTc{ic % 2}:{sI}:0"],
                        tag=f"{akt}_n")
                else:
                    p.op("DVE", lambda e, j=j: e.tensor_tensor(
                        shft[j % 2][0:64, :], orwB[j % 2][0:64, :], PSC[0:64, :],
                        ALU.mult),
                        reads=[f"orB{j % 2}", "psc"], writes=[f"shft{j % 2}"],
                        tag=f"{akt}_n")
                    dma("SP", oTc[ic % 2][64:128, sI * 512:(sI + 1) * 512],
                        shft[j % 2][0:64, :], [f"shft{j % 2}"],
                        [f"oTc{ic % 2}:{sI}:1"], f"{akt}_sh")

            for j in range(J + 2):
                if j < J and j % NH == 0:
                    qload(j // NH)
                if j < J:
                    S(j)
                if j >= 1:
                    O(j - 1)
                if j >= 2:
                    B(j - 2)
                    if (j - 2) % NH == NH - 1:
                        ic = (j - 2) // NH
                        okeys = [f"oTc{ic % 2}:{ss}:{par}" for ss in range(8)
                                 for par in range(2)]
                        dma("ACT", oTs.rearrange("(k p) n -> p k n", p=128)
                            [:, :, bt0 * 128 + ic * 512:bt0 * 128 + (ic + 1) * 512],
                            oTc[ic % 2][:, :].rearrange("p (k n) -> p k n", k=KC),
                            okeys, ["oTs_out"], f"{bk0}so{ic}")
            ar.pop()
            ar.push()
            cb_bo = ar.alloc(C, F32)
            mk_bcast(cb_bo, ab_bo.rearrange("(o n) -> o n", o=1), C, "bo")
            wo_sb = ar.alloc(KC * C, BF16)
            oTt = [ar.alloc(KC * 128, BF16) for _ in range(2)]
            zb = [ar.alloc(1024, F32) for _ in range(4)]
            wchunks(wo_sb, "W:bwo", ab_wo, C, C, "abwo")
            for i in range(16):
                ti = bt0 + i
                b = i % 2
                kt = f"{bk0}z{i}"
                dma("SP", oTt[b][:, :].rearrange("p (k n) -> p k n", k=KC),
                    oTs.rearrange("(k p) n -> p k n", p=128)
                    [:, :, ti * 128:(ti + 1) * 128],
                    ["oTs_out"], [f"oTt{b}"], f"{kt}_lo")

                def zmmb(e, b=b):
                    inst = None
                    for k in range(KC):
                        for nh_ in range(2):
                            inst = e.matmul(
                                PSB[:, nh_ * 512:(nh_ + 1) * 512],
                                oTt[b][:, k * 128:(k + 1) * 128],
                                wo_sb[:, k * C + nh_ * 512:k * C + (nh_ + 1) * 512],
                                start=(k == 0), stop=(k == KC - 1))
                    return inst
                p.op("PE", zmmb, reads=[f"oTt{b}", "W:bwo"], writes=PSBK, tag=kt)
                dma("SP", zb[b][:, :], r2[ti * 128:(ti + 1) * 128, :],
                    [f"r2:{ti}"], [f"zbl{b}"], f"{kt}_l")
                p.op("DVE", lambda e, b=b: e.scalar_tensor_tensor(
                    zb[2 + b][:, :], PSB[:, :], 1.0, cb_bo[:, :], ALU.mult, ALU.add),
                    reads=PSBK + ["C:bo"], writes=[f"zbo{b}"], tag=f"{kt}_b")
                p.op("DVE", lambda e, b=b: e.tensor_add(zb[2 + b][:, :],
                                                        zb[2 + b][:, :], zb[b][:, :]),
                     reads=[f"zbo{b}", f"zbl{b}"], writes=[f"zbo{b}"], tag=f"{kt}_r")
                dma("ACT", r3[ti * 128:(ti + 1) * 128, :], zb[2 + b][:, :],
                    [f"zbo{b}"], [f"r3:{ti}"], f"{kt}_st")
            ar.pop()
        ar.pop()

    if "f2" in stages:
        ffn("f2", r3, "r3", out, "out", g2s, g2eo, sh2s)
    else:
        last = {"ab": (r3, "r3"), "f1": (r2, "r2"), "aa": (r1, "r1")}
        for stg_ in ("ab", "f1", "aa"):
            if stg_ in stages:
                src, sname = last[stg_]
                ar.push()
                cp = [ar.alloc(1024, F32) for _ in range(2)]
                for ti in range(NTILES):
                    b = ti % 2
                    dma("SP", cp[b][:, :], src[ti * 128:(ti + 1) * 128, :],
                        [f"{sname}:{ti}"], [f"cp{b}"], f"cp{ti}")
                    dma("ACT", out[ti * 128:(ti + 1) * 128, :], cp[b][:, :],
                        [f"cp{b}"], [f"out:{ti}"], f"cp{ti}o")
                ar.pop()
                break

    p.build(nc, es)
    es.close()
    return p



def make_inputs(inputs, core):
    x = inputs["L0_attn_a_x"]
    shard = np.ascontiguousarray(x[2 * core:2 * core + 2]).reshape(T, C)
    ident_f = np.eye(128, dtype=np.float32)
    ident_b = np.eye(128, dtype=ml_dtypes.bfloat16)
    j = np.arange(64)[:, None]
    i = np.arange(64)[None, :]
    causal = (j <= i).astype(np.float32)
    ones = np.ones((64, 64), np.float32)
    mA = np.concatenate([causal, ones], axis=0)
    mB = np.concatenate([ones, causal], axis=0)
    hmask = np.concatenate([np.tile(mA, (1, NH)), np.tile(mB, (1, NH))], axis=1)
    iota256 = np.tile(np.tile(np.arange(NEXP, dtype=np.float32), 32)[None, :],
                      (128, 1))
    iota32 = np.tile(np.arange(32, dtype=np.float32)[None, :], (128, 1))
    ltri = np.triu(np.ones((128, 128), np.float32), k=1)
    return {
        "xin": shard,
        "aa_wqkv": inputs["L0_attn_a_w_qkv"], "aa_bqkv": inputs["L0_attn_a_b_qkv"],
        "aa_wout": inputs["L0_attn_a_w_out"], "aa_bout": inputs["L0_attn_a_b_out"],
        "aa_g1": inputs["L0_attn_a_ln1_g"], "aa_b1g": inputs["L0_attn_a_ln1_b"],
        "aa_g2": inputs["L0_attn_a_ln2_g"], "aa_b2g": inputs["L0_attn_a_ln2_b"],
        "aa_w1": inputs["L0_attn_a_w1"], "aa_b1": inputs["L0_attn_a_b1"],
        "aa_w2": inputs["L0_attn_a_w2"], "aa_b2": inputs["L0_attn_a_b2"],
        "ab_wq": inputs["L0_attn_b_Wq"], "ab_wk": inputs["L0_attn_b_Wk"],
        "ab_wv": inputs["L0_attn_b_Wv"], "ab_wo": inputs["L0_attn_b_Wo"],
        "ab_bo": inputs["L0_attn_b_bo"],
        "f_rw": inputs["L0_ffn_router_w"],
        "f_eg": inputs["L0_ffn_expert_gate"], "f_eu": inputs["L0_ffn_expert_up"],
        "f_ed": inputs["L0_ffn_expert_down"],
        "f_sg": inputs["L0_ffn_shared_gate"], "f_su": inputs["L0_ffn_shared_up"],
        "f_sd": inputs["L0_ffn_shared_down"],
        "id_f": ident_f, "id_b": ident_b, "hmask": hmask, "iota256": iota256,
        "iota32": iota32, "ltri": ltri,
    }


def kernel(**inputs):
    nc = bass.Bass("TRN2", target_bir_lowering=False)
    build_program(nc)
    nc.finalize()
    in_maps = [make_inputs(inputs, c) for c in range(8)]
    res = run_bass_kernel_spmd(nc, in_maps, core_ids=list(range(8)))
    outs = [np.asarray(res.results[c]["out"]).reshape(2, SEQ, C) for c in range(8)]
    return np.concatenate(outs, axis=0).astype(np.float32)


# version 19 (final)
# speedup vs baseline: 1.0416x; 1.0416x over previous
import numpy as np
import ml_dtypes
from contextlib import ExitStack

import concourse.bass as bass
import concourse.mybir as mybir
from concourse.bass_utils import run_bass_kernel_spmd

F32 = mybir.dt.float32
BF16 = mybir.dt.bfloat16
AF = mybir.ActivationFunctionType
ALU = mybir.AluOpType
AX = mybir.AxisListType.X

C = 1024
T = 4096
SEQ = 2048
NH = 16
HD = 64
BUCKET = 64
FF_A = 4096
FF_E = 2048
NEXP = 8
KC = C // 128
NTILES = T // 128
EPS = 1e-5
SCALE_B = float(HD ** -0.25)
SCALE_A = 0.125
V65W = NH * (HD + 1)

ENGINES = ("PE", "DVE", "ACT", "POOL", "SP")


class Op:
    __slots__ = ("idx", "eng", "emit", "reads", "writes", "dma", "deps",
                 "marked", "inc_val", "tag", "_qslot", "_qseq", "fence",
                 "_fence_snap")

    def __init__(self, idx, eng, emit, reads, writes, dma, tag):
        self.idx = idx; self.eng = eng; self.emit = emit
        self.reads = reads; self.writes = writes; self.dma = dma
        self.deps = []; self.marked = False; self.inc_val = 0; self.tag = tag
        self.fence = False; self._fence_snap = None


class Prog:

    def __init__(self, relax_same_engine=False):
        self.ops = []
        self.last_write = {}
        self.readers = {}
        self.relax = relax_same_engine
        self.fence_n = 0

    def op(self, eng, emit, reads=(), writes=(), dma=False, tag=""):
        o = Op(len(self.ops), eng, emit, tuple(reads), tuple(writes), dma, tag)
        deps = set()
        for r in o.reads:
            w = self.last_write.get(r)
            if w is not None:
                deps.add(w)
        for w in o.writes:
            lw = self.last_write.get(w)
            if lw is not None:
                deps.add(lw)
            deps.update(self.readers.get(w, ()))
        deps.discard(o.idx)
        o.deps = sorted(deps)
        self.ops.append(o)
        for r in o.reads:
            self.readers.setdefault(r, []).append(o.idx)
        for w in o.writes:
            self.last_write[w] = o.idx
            self.readers[w] = []
        return o.idx

    def fence(self, tag="fence"):
        n = self.fence_n
        self.fence_n += 1
        for E in ENGINES:
            o = Op(len(self.ops), E, None, (), (), False, f"{tag}{n}_{E}")
            o.fence = True
            self.ops.append(o)

    def _needs_wait(self, ci, pi):
        p, c = self.ops[pi], self.ops[ci]
        if p.eng == c.eng and not p.dma:
            if p.eng == "PE" and not c.dma:
                return False
            if self.relax and not c.dma:
                return False
            return True
        return True

    NQ = 12

    def build(self, nc, es):
        sems = {e: es.enter_context(nc.semaphore(f"s_{e}")) for e in ENGINES}
        qsems = {e: [es.enter_context(nc.semaphore(f"q_{e}{i}"))
                     for i in range(self.NQ)] for e in ("SP", "ACT", "POOL")}

        last_by_eng = {}
        for o in self.ops:
            if o.fence:
                for le in last_by_eng.values():
                    le.marked = True
            else:
                last_by_eng[o.eng] = o
        for o in self.ops:
            if o.dma:
                o.marked = True
            for d in o.deps:
                if self._needs_wait(o.idx, d):
                    self.ops[d].marked = True

        counts = {}
        dma_seq = {}
        for o in self.ops:
            if o.fence:
                snap = dict(counts)
                qsnap = {}
                for e2, nn in dma_seq.items():
                    for i in range(min(nn, self.NQ)):
                        qsnap[(e2, i)] = ((nn - 1 - i) // self.NQ + 1) * 16
                o._fence_snap = (snap, qsnap)
                continue
            if o.marked:
                if o.dma:
                    j = dma_seq.get(o.eng, 0)
                    dma_seq[o.eng] = j + 1
                    o.inc_val = (j // self.NQ + 1) * 16
                    o._qslot = j % self.NQ
                    o._qseq = j
                else:
                    key = (o.eng, False)
                    counts[key] = counts.get(key, 0) + 1
                    o.inc_val = counts[key]

        per_eng = {e: [] for e in ENGINES}
        for o in self.ops:
            per_eng[o.eng].append(o)
        final_q = {}
        for e in ("SP", "ACT", "POOL"):
            n = dma_seq.get(e, 0)
            final_q[e] = [(n - 1 - i) // self.NQ + 1 for i in range(self.NQ)] \
                if n else []

        def body(eng_obj, eng_name):
            waited = {}

            def wait(key, sem, val):
                if waited.get(key, 0) >= val:
                    return
                eng_obj.wait_ge(sem, val)
                waited[key] = val

            for o in per_eng[eng_name]:
                if o.fence:
                    snap, qsnap = o._fence_snap
                    for (e2, dm), v in snap.items():
                        if v and not (e2 == eng_name and not dm):
                            wait(("s", e2), sems[e2], v)
                    for (e2, i), v in qsnap.items():
                        wait(("q", e2, i), qsems[e2][i], v)
                    continue
                if o.dma and o._qseq >= self.NQ:
                    wait(("q", o.eng, o._qslot), qsems[o.eng][o._qslot],
                         o.inc_val - 16)
                for d in o.deps:
                    if not self._needs_wait(o.idx, d):
                        continue
                    pp = self.ops[d]
                    if not pp.marked:
                        continue
                    if pp.dma:
                        wait(("q", pp.eng, pp._qslot), qsems[pp.eng][pp._qslot],
                             pp.inc_val)
                    else:
                        wait(("s", pp.eng), sems[pp.eng], pp.inc_val)
                inst = o.emit(eng_obj)
                if o.marked:
                    if isinstance(inst, (list, tuple)):
                        inst = inst[-1]
                    assert inst is not None, f"op {o.idx} ({o.tag}) returned None"
                    if o.dma:
                        inst.then_inc(qsems[o.eng][o._qslot], 16)
                    else:
                        inst.then_inc(sems[o.eng], 1)
            if eng_name == "POOL":
                for e in ("SP", "ACT", "POOL"):
                    n = dma_seq.get(e, 0)
                    for i in range(min(n, self.NQ)):
                        nslot = (n - 1 - i) // self.NQ + 1
                        wait(("q", e, i), qsems[e][i], nslot * 16)

        blk = es.enter_context(nc.Block())
        blk.sync(lambda e: body(e, "SP"))
        blk.tensor(lambda e: body(e, "PE"))
        blk.vector(lambda e: body(e, "DVE"))
        blk.scalar(lambda e: body(e, "ACT"))
        blk.gpsimd(lambda e: body(e, "POOL"))


class Arena:
    def __init__(self, nc, es, prog, nbytes=204800):
        self.t = es.enter_context(nc.sbuf_tensor("arena", [128, nbytes],
                                                 mybir.dt.uint8))
        self.nbytes = nbytes
        self.off = 0
        self.stack = []
        self.hi = 0
        self.prog = prog

    def alloc(self, cols, dtype):
        size = cols * (2 if dtype == BF16 else 4)
        size = (size + 63) & ~63
        assert self.off + size <= self.nbytes, \
            f"arena overflow: {self.off}+{size} > {self.nbytes}"
        ap = self.t[:][:, self.off:self.off + size].bitcast(dtype)[:, 0:cols]
        self.off += size
        self.hi = max(self.hi, self.off)
        return ap

    def push(self):
        self.stack.append(self.off)

    def pop(self):
        self.off = self.stack.pop()
        self.prog.fence()



def build_program(nc, stages=("aa", "f1", "ab", "f2"), relax=False):
    p = Prog(relax_same_engine=relax)
    es = ExitStack()

    def din(name, shape, dt=F32):
        return nc.dram_tensor(name, shape, dt, kind="ExternalInput").ap()

    xin = din("xin", [T, C])
    aa_wqkv = din("aa_wqkv", [C, 3 * C]); aa_bqkv = din("aa_bqkv", [3 * C])
    aa_wout = din("aa_wout", [C, C]); aa_bout = din("aa_bout", [C])
    aa_g1 = din("aa_g1", [C]); aa_b1g = din("aa_b1g", [C])
    aa_g2 = din("aa_g2", [C]); aa_b2g = din("aa_b2g", [C])
    aa_w1 = din("aa_w1", [C, FF_A]); aa_b1 = din("aa_b1", [FF_A])
    aa_w2 = din("aa_w2", [FF_A, C]); aa_b2 = din("aa_b2", [C])
    ab_wq = din("ab_wq", [C, C]); ab_wk = din("ab_wk", [C, C])
    ab_wv = din("ab_wv", [C, C]); ab_wo = din("ab_wo", [C, C])
    ab_bo = din("ab_bo", [C])
    f_rw = din("f_rw", [NEXP, C])
    f_eg = din("f_eg", [NEXP, C, FF_E]); f_eu = din("f_eu", [NEXP, C, FF_E])
    f_ed = din("f_ed", [NEXP, FF_E, C])
    f_sg = din("f_sg", [C, FF_E]); f_su = din("f_su", [C, FF_E])
    f_sd = din("f_sd", [FF_E, C])
    id_f = din("id_f", [128, 128])
    id_b = din("id_b", [128, 128], BF16)
    hmaskin = din("hmask", [128, 2 * NH * BUCKET])
    iota256 = din("iota256", [128, NTILES * NEXP])
    iota32 = din("iota32", [128, NTILES])
    ltri = din("ltri", [128, 128])

    out = nc.dram_tensor("out", [T, C], F32, kind="ExternalOutput").ap()

    def scr(name, shape, dt=F32):
        return nc.dram_tensor(name, shape, dt, kind="Internal").ap()

    ys = scr("ys", [T, C])
    y2s = scr("y2s", [T, C])
    r1 = scr("r1", [T, C])
    r2 = scr("r2", [T, C])
    r3 = scr("r3", [T, C])
    xTs = scr("xTs", [C, T], BF16)
    CAP = 1024
    NSLOT = NEXP * CAP + NTILES
    g1s = scr("g1s", [NSLOT, C], BF16)
    g1eo = scr("g1eo", [NSLOT, C], BF16)
    g2s = scr("g2s", [NSLOT, C], BF16)
    g2eo = scr("g2eo", [NSLOT, C], BF16)
    sh1s = scr("sh1s", [T, C], BF16)
    sh2s = scr("sh2s", [T, C], BF16)
    qTs = scr("qTs", [C, T], BF16)
    oTs = scr("oTs", [C, T], BF16)

    ar = Arena(nc, es, p)

    PSA = es.enter_context(nc.psum_tensor("psa", [128, 2048], F32))[:]
    PSB = es.enter_context(nc.psum_tensor("psb", [128, 1024], F32))[:]
    PSC = es.enter_context(nc.psum_tensor("psc", [128, 512], F32))[:]
    PST = es.enter_context(nc.psum_tensor("pst", [128, 1024], BF16))[:]
    PSAK = ["psa0", "psa1", "psa2", "psa3"]
    PSBK = ["psb0", "psb1"]

    def dma(eng, dst, src, reads, writes, tag=""):
        p.op(eng, lambda e, d=dst, s=src: e.dma_start(out=d, in_=s),
             reads=reads, writes=writes, dma=True, tag=tag)

    sb_idf = ar.alloc(128, F32)
    sb_idb = ar.alloc(128, BF16)
    sb_ones = ar.alloc(128, F32)
    sb_onesb = ar.alloc(128, BF16)
    sb_hmask = ar.alloc(2 * NH * BUCKET, BF16)
    sb_iota256 = ar.alloc(NTILES * NEXP, F32)
    sb_iota32 = ar.alloc(NTILES, F32)
    rwT = ar.alloc(8 * KC, F32)
    sb_ltri = ar.alloc(128, F32)
    sb_onesf = ar.alloc(4, F32)
    sb_zt = ar.alloc(1024, BF16)
    sb_onesr = ar.alloc(128, BF16)
    ROWBUF = ar.alloc(1024, F32)
    STG = ar.alloc(128, F32)
    STT = ar.alloc(64, F32)

    dma("SP", sb_idf[:, :], id_f, ["H:idf"], ["idf"], "idf")
    dma("SP", sb_idb[:, :], id_b, ["H:idb"], ["idb"], "idb")
    dma("SP", sb_iota256[:, :], iota256, ["H:iota256"], ["iota256"], "iota256")
    dma("SP", sb_iota32[:, :], iota32, ["H:iota32"], ["iota32"], "iota32")
    dma("POOL", sb_hmask[:, :], hmaskin, ["H:hmask"], ["hmask"], "hmask")
    dma("SP", sb_ltri[:, :], ltri, ["H:ltri"], ["ltri"], "ltri")
    p.op("DVE", lambda e: e.memset(sb_onesf[:, 0:1], 1.0), writes=["onesf"], tag="onesf")
    p.op("DVE", lambda e: e.memset(sb_zt[:, :], 0.0), writes=["zt"], tag="zt")
    p.op("DVE", lambda e: e.memset(sb_ones[0:1, :], 1.0), writes=["ones"], tag="ones")
    p.op("DVE", lambda e: e.memset(sb_onesb[:, 0:1], 1.0), writes=["onesb"], tag="onesb")
    p.op("DVE", lambda e: e.memset(sb_onesr[0:1, :], 1.0), writes=["onesr"], tag="onesr")
    KCOL = ar.alloc(8, F32)
    _kvals = [EPS, -0.5, SCALE_A, SCALE_B, -1.0]
    for _i, _v in enumerate(_kvals):
        p.op("DVE", lambda e, _i=_i, _v=_v: e.memset(KCOL[:, _i:_i + 1], _v),
             writes=["Kc"], tag=f"kc{_i}")
        nc.const_aps.register(F32, _v, KCOL[:, _i:_i + 1])

    GZP = {}

    def prep_gz(tag, G, Geo):
        ops = []
        nzi = (NSLOT + 127) // 128
        for zi in range(nzi):
            r0 = zi * 128
            nr = min(128, NSLOT - r0)
            ops.append((G[r0:r0 + nr, :], sb_zt[0:nr, :], f"{tag}gz{zi}",
                        ["Ginit"]))
        ops.append((Geo[NEXP * CAP:NSLOT, :], sb_zt[0:NTILES, :], f"{tag}ez0",
                    ["Geoinit"]))
        GZP[tag] = ops

    def emit_gz(tag, n, queue):
        ops = GZP.get(tag)
        if not ops:
            return
        for _ in range(min(n, len(ops))):
            dst, src, tg, wk = ops.pop(0)
            dma(queue, dst, src, ["zt"], wk, tg)

    prep_gz("f1", g1s, g1eo)
    prep_gz("f2", g2s, g2eo)

    for k in range(KC):
        dma("SP", STG[0:8, 0:128], f_rw[:, k * 128:(k + 1) * 128], ["H:rw"],
            ["stg"], f"ld_rw{k}")
        p.op("PE", lambda e: e.transpose(PSB[0:128, 0:8], STG[0:8, 0:128],
                                         sb_idf[0:8, 0:8]),
             reads=["stg", "idf"], writes=["psb0"], tag=f"tr_rw{k}")
        p.op("DVE", lambda e, k=k: e.tensor_copy(rwT[:, k * 8:(k + 1) * 8], PSB[:, 0:8]),
             reads=["psb0"], writes=["C:rwT", "stg"], tag=f"ev_rw{k}")

    def mk_bcast(dst, src_row_ap, n, tag):
        row = ROWBUF[0:1, 0:n]
        dma("SP", row, src_row_ap, [f"H:{tag}"], ["rowbuf"], f"ld_{tag}")
        nch = (n + 511) // 512

        def mm(e):
            inst = None
            for i in range(nch):
                s = i * 512
                w = min(512, n - s)
                inst = e.matmul(PSA[:, s:s + w], sb_ones[0:1, :], row[:, s:s + w],
                                start=True, stop=True)
            return inst
        p.op("PE", mm, reads=["ones", "rowbuf"],
             writes=[PSAK[i] for i in range(nch)], tag=f"bc_{tag}")
        p.op("DVE", lambda e: e.tensor_copy(dst[:, 0:n], PSA[:, 0:n]),
             reads=[PSAK[i] for i in range(nch)],
             writes=[f"C:{tag}", "rowbuf"], tag=f"ev_{tag}")

    def mk_cols(dst, src_vec, ntile, tag):
        stg = STG[0:ntile, 0:128]
        dma("SP", stg, src_vec.rearrange("(t p) -> t p", p=128),
            [f"H:{tag}"], ["stg"], f"ld_{tag}")
        p.op("PE", lambda e, stg=stg, ntile=ntile: e.transpose(
            PSA[:, 0:ntile], stg, sb_idf[0:ntile, 0:ntile]),
            reads=["stg", "idf"], writes=["psa0"], tag=f"tr_{tag}")
        p.op("DVE", lambda e, ntile=ntile: e.tensor_copy(dst[:, 0:ntile],
                                                         PSA[:, 0:ntile]),
             reads=["psa0"], writes=[f"C:{tag}", "stg"], tag=f"ev_{tag}")

    PK = {}

    def alloc_pack():
        PK["T"] = [ar.alloc(1024, F32) for _ in range(2)]
        PK["T2"] = [ar.alloc(1024, F32) for _ in range(2)]
        PK["Y"] = [ar.alloc(1024, F32) for _ in range(2)]
        PK["H"] = [ar.alloc(1024, BF16) for _ in range(2)]
        PK["SQ"] = [ar.alloc(1024, F32) for _ in range(2)]

    _ln_ctr = [0]

    def ln_core(src_ap, src_keys, y_ap, y_key, kt):
        st = STT
        sc = _ln_ctr[0] % 6
        _ln_ctr[0] += 1
        c0 = sc * 8
        sq = PK["SQ"][sc % 2]
        p.op("ACT", lambda e: e.activation(sq[:, :], src_ap, AF.Square,
                                           accum_out=st[:, c0:c0 + 1]),
             reads=src_keys, writes=[f"sqd{sc % 2}", f"st{c0}"], tag=f"{kt}_sq")
        p.op("DVE", lambda e: e.reduce_sum(st[:, c0 + 1:c0 + 2], src_ap, axis=AX),
             reads=src_keys, writes=[f"st{c0}b"], tag=f"{kt}_sm")
        p.op("DVE", lambda e: e.tensor_scalar(st[:, c0 + 2:c0 + 3],
                                              st[:, c0 + 1:c0 + 2],
                                              1.0 / C, None, ALU.mult),
             reads=[f"st{c0}b"], writes=[f"st{c0}c"], tag=f"{kt}_mu")
        p.op("DVE", lambda e: e.tensor_scalar(st[:, c0 + 3:c0 + 4], st[:, c0:c0 + 1],
                                              1.0 / C, None, ALU.mult),
             reads=[f"st{c0}"], writes=[f"st{c0}d"], tag=f"{kt}_e2")
        p.op("DVE", lambda e: e.tensor_mul(st[:, c0 + 4:c0 + 5], st[:, c0 + 2:c0 + 3],
                                           st[:, c0 + 2:c0 + 3]),
             reads=[f"st{c0}c"], writes=[f"st{c0}e"], tag=f"{kt}_m2")
        p.op("DVE", lambda e: e.tensor_sub(st[:, c0 + 5:c0 + 6], st[:, c0 + 3:c0 + 4],
                                           st[:, c0 + 4:c0 + 5]),
             reads=[f"st{c0}d", f"st{c0}e"], writes=[f"st{c0}f"], tag=f"{kt}_va")
        p.op("ACT", lambda e: e.activation(st[:, c0 + 6:c0 + 7], st[:, c0 + 5:c0 + 6],
                                           AF.Ln, bias=EPS),
             reads=[f"st{c0}f", "Kc"], writes=[f"st{c0}g"], tag=f"{kt}_ln")
        p.op("ACT", lambda e: e.activation(st[:, c0 + 7:c0 + 8], st[:, c0 + 6:c0 + 7],
                                           AF.Exp, scale=-0.5),
             reads=[f"st{c0}g", "Kc"], writes=[f"st{c0}h"], tag=f"{kt}_rs")
        p.op("DVE", lambda e: e.tensor_scalar(y_ap, src_ap, st[:, c0 + 2:c0 + 3],
                                              st[:, c0 + 7:c0 + 8],
                                              ALU.subtract, ALU.mult),
             reads=list(src_keys) + [f"st{c0}c", f"st{c0}h"], writes=[y_key],
             tag=f"{kt}_ap")

    def transpose8(src_ap, src_keys, kt):
        def mm(e):
            inst = None
            for k in range(KC):
                inst = e.transpose(PST[:, k * 128:(k + 1) * 128],
                                   src_ap[:, k * 128:(k + 1) * 128], sb_idb[:, :])
            return inst
        p.op("PE", mm, reads=list(src_keys) + ["idb"], writes=["pst"], tag=kt)

    def wchunks(slot, skey, w_ap, kdim, n, tag):
        kch = kdim // 128
        step = max(1, min(kch, (2 * 1024 * 1024) // (n * 512)))
        for k0 in range(0, kch, step):
            k1 = min(k0 + step, kch)
            dma("POOL",
                slot[:, :].rearrange("p (k n) -> p k n", k=kch)[:, k0:k1, :],
                w_ap.rearrange("(k p) n -> p k n", p=128)[:, k0:k1, :],
                [f"H:{tag}"], [skey], f"{tag}_{k0}")

    if "aa" in stages:
        ar.push()
        alloc_pack()
        cb_g1 = ar.alloc(C, F32)
        cb_b1g = ar.alloc(C, F32)
        cb_bqv = ar.alloc(C, F32)
        cb_bout = ar.alloc(C, F32)
        col_bqkv = ar.alloc(24, F32)
        mk_bcast(cb_g1, aa_g1.rearrange("(o n) -> o n", o=1), C, "g1")
        mk_bcast(cb_b1g, aa_b1g.rearrange("(o n) -> o n", o=1), C, "b1g")
        mk_bcast(cb_bqv, aa_bqkv.rearrange("(o n) -> o n", o=1)[:, 2 * C:3 * C], C, "bqv")
        mk_bcast(cb_bout, aa_bout.rearrange("(o n) -> o n", o=1), C, "bout")
        mk_cols(col_bqkv, aa_bqkv, 24, "bqkv")

        wqkv_sb = ar.alloc(KC * 3 * C, BF16)
        wout_sb = ar.alloc(KC * C, BF16)
        hTc = ar.alloc(KC * 512, BF16)
        HGRP = 512
        NG = T // HGRP
        GT = HGRP // 128
        GB = HGRP // BUCKET
        GPB = SEQ // HGRP
        qT_g = ar.alloc(KC * HGRP, BF16)
        kT_g = ar.alloc(KC * HGRP, BF16)
        kT_c = ar.alloc(KC * BUCKET, BF16)
        v65_g = [ar.alloc(V65W, BF16) for _ in range(GT)]
        v65_c = ar.alloc(V65W, BF16)
        zv65 = ar.alloc(V65W, BF16)
        zkT = ar.alloc(64, BF16)
        oT_g = ar.alloc(KC * HGRP, BF16)
        orw = [ar.alloc(512, F32) for _ in range(2)]
        rrow = [ar.alloc(1024, BF16) for _ in range(3)]
        lna = ar.alloc(2048, F32)
        expb = [ar.alloc(NH * BUCKET, BF16) for _ in range(2)]

        wchunks(wqkv_sb, "W:qkv", aa_wqkv, C, 3 * C, "aaqkv")
        wchunks(wout_sb, "W:aout", aa_wout, C, C, "aaout")
        p.op("DVE", lambda e: e.memset(zv65[:, :], 0.0), writes=["zv65"], tag="zv0")
        p.op("DVE", lambda e: e.memset(
            zv65[:, :].rearrange("p (h w) -> p h w", h=NH)[:, :, HD:HD + 1], 1.0),
            reads=["zv65"], writes=["zv65"], tag="zv1")
        p.op("DVE", lambda e: e.memset(zkT[:, :], 0.0), writes=["zkT"], tag="zk")

        for g in range(NG):
            gk = f"g{g}"
            for i in range(GT):
                ti = g * GT + i
                b = i % 2
                t, t2, y, h = PK["T"][b], PK["T2"][b], PK["Y"][b], PK["H"][b]
                kt = f"aa{ti}"
                dma("SP", t[:, :], xin[ti * 128:(ti + 1) * 128, :],
                    [f"H:xin:{ti}"], [f"tb{b}"], kt)
                ln_core(t[:, :], [f"tb{b}"], y[:, :], f"yb{b}", kt)
                dma("ACT", ys[ti * 128:(ti + 1) * 128, :], y[:, :],
                    [f"yb{b}"], [f"ys:{ti}"], f"{kt}_sy")
                ln_core(y[:, :], [f"yb{b}"], t2[:, :], f"t2b{b}", kt + "i")
                p.op("DVE", lambda e, t2=t2: e.tensor_mul(t2[:, :], t2[:, :],
                                                          cb_g1[:, :]),
                     reads=[f"t2b{b}", "C:g1"], writes=[f"t2b{b}"], tag=f"{kt}_g1")
                p.op("DVE", lambda e, t2=t2, h=h: e.tensor_add(h[:, :], t2[:, :],
                                                               cb_b1g[:, :]),
                     reads=[f"t2b{b}", "C:b1g"], writes=[f"hb{b}"], tag=f"{kt}_b1")
                transpose8(h[:, :], [f"hb{b}"], f"{kt}_tr")
                p.op("DVE", lambda e, i=i: e.tensor_copy(
                    hTc[:, :].rearrange("p (k n) -> p k n", k=KC)[:, :, i * 128:(i + 1) * 128],
                    PST[:, :].rearrange("p (k n) -> p k n", k=KC)),
                    reads=["pst"], writes=[f"hTc:{i}"], tag=f"{kt}_ev")
            hck = [f"hTc:{i}" for i in range(GT)]
            for m in range(16):
                bank = PSAK[m % 4]
                dst = qT_g if m < 8 else kT_g
                dk = ("qT" if m < 8 else "kT") + f"{gk}:{m % 8}"

                def qkmm(e, m=m):
                    inst = None
                    for k in range(KC):
                        inst = e.matmul(
                            PSA[:, (m % 4) * 512:(m % 4) * 512 + 512],
                            wqkv_sb[:, k * 3 * C + m * 128:k * 3 * C + (m + 1) * 128],
                            hTc[:, k * 512:(k + 1) * 512],
                            start=(k == 0), stop=(k == KC - 1))
                    return inst
                p.op("PE", qkmm, reads=hck + ["W:qkv"], writes=[bank], tag=f"{gk}qk{m}")
                p.op("ACT", lambda e, m=m, dst=dst: e.activation(
                    dst[:, (m % 8) * HGRP:(m % 8) * HGRP + 512],
                    PSA[:, (m % 4) * 512:(m % 4) * 512 + 512],
                    AF.Identity, bias=col_bqkv[:, m:m + 1]),
                    reads=[bank, "C:bqkv"], writes=[dk], tag=f"{gk}qke{m}")
            for i in range(GT):
                def vmm(e, i=i):
                    inst = None
                    for k in range(KC):
                        for nh_ in range(2):
                            inst = e.matmul(
                                PSB[:, nh_ * 512:(nh_ + 1) * 512],
                                hTc[:, k * 512 + i * 128:k * 512 + (i + 1) * 128],
                                wqkv_sb[:, k * 3 * C + 2 * C + nh_ * 512:
                                        k * 3 * C + 2 * C + (nh_ + 1) * 512],
                                start=(k == 0), stop=(k == KC - 1))
                    return inst
                p.op("PE", vmm, reads=hck + ["W:qkv"], writes=PSBK, tag=f"{gk}v{i}")
                p.op("DVE", lambda e, i=i: e.memset(
                    v65_g[i][:, :].rearrange("p (h w) -> p h w", h=NH)[:, :, HD:HD + 1],
                    1.0), writes=[f"v65:{i}"], tag=f"{gk}vo{i}")
                p.op("DVE", lambda e, i=i: e.scalar_tensor_tensor(
                    v65_g[i][:, :].rearrange("p (h w) -> p h w", h=NH)[:, :, 0:HD],
                    PSB[:, :].rearrange("p (h w) -> p h w", h=NH),
                    1.0, cb_bqv[:, :].rearrange("p (h w) -> p h w", h=NH),
                    ALU.mult, ALU.add),
                    reads=PSBK + ["C:bqv", f"v65:{i}"],
                    writes=[f"v65:{i}"], tag=f"{gk}ve{i}")
            qkkeys = [f"qT{gk}:{m}" for m in range(8)] + \
                     [f"kT{gk}:{m}" for m in range(8)]
            vkeys = [f"v65:{i}" for i in range(GT)]
            def hcol(h):
                return (h // 2) * 64 + (h % 2) * 512

            def SS(b):
                bi = b
                first = (g % GPB == 0) and bi == 0
                bk = f"{gk}b{bi}"
                pc = (bi % 2) * 64
                pp = 64 - pc

                def smm(e, bi=bi, first=first, pc=pc, pp=pp):
                    inst = None
                    for h in [0, 2, 4, 6, 8, 10, 12, 14, 1, 3, 5, 7, 9, 11, 13, 15]:
                        sI = h // 2
                        r = (h % 2) * 64
                        qsl = qT_g[r:r + 64, sI * HGRP + bi * 64:sI * HGRP + (bi + 1) * 64]
                        if first:
                            kp = zkT[r:r + 64, 0:64]
                        elif bi == 0:
                            kp = kT_c[r:r + 64, sI * BUCKET:(sI + 1) * BUCKET]
                        else:
                            kp = kT_g[r:r + 64, sI * HGRP + (bi - 1) * 64:sI * HGRP + bi * 64]
                        kc = kT_g[r:r + 64, sI * HGRP + bi * 64:sI * HGRP + (bi + 1) * 64]
                        inst = e.matmul(PSB[pp:pp + 64, hcol(h):hcol(h) + 64], kp, qsl,
                                        start=True, stop=True)
                        inst = e.matmul(PSB[pc:pc + 64, hcol(h):hcol(h) + 64], kc, qsl,
                                        start=True, stop=True)
                    return inst
                p.op("PE", smm, reads=qkkeys + ["zkT", "kTc"], writes=PSBK,
                     tag=f"{bk}_s")
                p.op("ACT", lambda e, b=b: e.activation(expb[b % 2][:, :], PSB[:, :],
                                                        AF.Exp, scale=SCALE_A),
                     reads=PSBK + ["Kc"], writes=[f"expb{b % 2}"], tag=f"{bk}_exp")
                p.op("DVE", lambda e, bi=bi, b=b: e.tensor_mul(
                    expb[b % 2][:, :], expb[b % 2][:, :],
                    sb_hmask[:, (bi % 2) * NH * BUCKET:(bi % 2 + 1) * NH * BUCKET]),
                    reads=[f"expb{b % 2}", "hmask"], writes=[f"expb{b % 2}"],
                    tag=f"{bk}_msk")

                def dmm(e, b=b):
                    e.matmul(PSB[0:1, 0:512], sb_onesb[:, 0:1], expb[b % 2][:, 0:512],
                             start=True, stop=True)
                    return e.matmul(PSB[0:1, 512:1024], sb_onesb[:, 0:1],
                                    expb[b % 2][:, 512:1024], start=True, stop=True)
                p.op("PE", dmm, reads=[f"expb{b % 2}", "onesb"], writes=PSBK,
                     tag=f"{bk}_d")
                p.op("DVE", lambda e: e.tensor_copy(lna[0:1, 0:1024], PSB[0:1, :]),
                     reads=PSBK, writes=["lnA"], tag=f"{bk}_rc0")
                p.op("ACT", lambda e: e.activation(lna[0:1, 1024:2048],
                                                   lna[0:1, 0:1024], AF.Ln),
                     reads=["lnA"], writes=["lnA2"], tag=f"{bk}_rc1")

                def rcf(e, b=b):
                    with nc.allow_low_precision(
                            reason="recip feeds a bf16-operand matmul; rounding "
                                   "here equals the matmul's own operand cast"):
                        return e.activation(rrow[b % 3][0:1, :],
                                            lna[0:1, 1024:2048], AF.Exp,
                                            scale=-1.0)
                p.op("ACT", rcf, reads=["lnA2", "Kc"], writes=[f"rrow{b % 3}"],
                     tag=f"{bk}_rc")

            def OM(b):
                bi = b
                first = (g % GPB == 0) and bi == 0
                bk = f"{gk}b{bi}"
                pc = (bi % 2) * 64
                pp = 64 - pc
                bko = (b % 2) * 2

                def omm(e, bi=bi, first=first, pc=pc, pp=pp, b=b, bko=bko):
                    inst = None
                    for grp in range(2):
                        for h in range(NH):
                            sI = h // 2
                            ro = (h % 2) * 64
                            co = (bko + grp) * 512 + sI * 64
                            if grp == 0:
                                if first:
                                    v = zv65[pp:pp + 64, h * 65:h * 65 + 64]
                                elif bi == 0:
                                    v = v65_c[64:128, h * 65:h * 65 + 64]
                                else:
                                    v = v65_g[(bi - 1) // 2][pp:pp + 64,
                                                             h * 65:h * 65 + 64]
                                rh = expb[b % 2][pp:pp + 64, hcol(h):hcol(h) + 64]
                            else:
                                v = v65_g[bi // 2][pc:pc + 64, h * 65:h * 65 + 64]
                                rh = expb[b % 2][pc:pc + 64, hcol(h):hcol(h) + 64]
                            inst = e.matmul(PSA[ro:ro + 64, co:co + 64], v, rh,
                                            start=True, stop=True)
                    return inst
                p.op("PE", omm, reads=[f"expb{b % 2}", "zv65", "v65c"] + vkeys,
                     writes=[PSAK[bko], PSAK[bko + 1]], tag=f"{bk}_o")

            def BM(b):
                bi = b
                bk = f"{gk}b{bi}"
                bko = (b % 2) * 2

                def bmm(e, b=b):
                    e.matmul(PSB[:, 0:512], sb_onesr[0:1, :],
                             rrow[b % 3][0:1, 0:512], start=True, stop=True)
                    return e.matmul(PSB[:, 512:1024], sb_onesr[0:1, :],
                                    rrow[b % 3][0:1, 512:1024], start=True, stop=True)
                p.op("PE", bmm, reads=[f"rrow{b % 3}", "onesr"],
                     writes=PSBK, tag=f"{bk}_bc")
                p.op("ACT", lambda e, b=b, bko=bko: e.activation(
                    orw[b % 2][0:64, 0:512], PSA[0:64, bko * 512:bko * 512 + 512],
                    AF.Copy),
                    reads=[PSAK[bko]], writes=[f"orwE{b % 2}"], tag=f"{bk}_ee")
                p.op("DVE", lambda e, b=b, bko=bko: e.scalar_tensor_tensor(
                    orw[b % 2][0:64, 0:512], PSA[0:64, (bko + 1) * 512:(bko + 2) * 512],
                    1.0, orw[b % 2][0:64, 0:512], ALU.mult, ALU.add),
                    reads=[PSAK[bko + 1], f"orwE{b % 2}"], writes=[f"orwE{b % 2}"],
                    tag=f"{bk}_ee2")
                p.op("ACT", lambda e, b=b, bko=bko: e.activation(
                    orw[b % 2][64:128, 0:512], PSA[64:128, bko * 512:bko * 512 + 512],
                    AF.Copy),
                    reads=[PSAK[bko]], writes=[f"orwO{b % 2}"], tag=f"{bk}_eo")
                p.op("DVE", lambda e, b=b, bko=bko: e.scalar_tensor_tensor(
                    orw[b % 2][64:128, 0:512], PSA[64:128, (bko + 1) * 512:(bko + 2) * 512],
                    1.0, orw[b % 2][64:128, 0:512], ALU.mult, ALU.add),
                    reads=[PSAK[bko + 1], f"orwO{b % 2}"], writes=[f"orwO{b % 2}"],
                    tag=f"{bk}_eo2")
                p.op("DVE", lambda e, bi=bi, b=b: e.tensor_tensor(
                    oT_g[0:64, :].rearrange("p (s n) -> p s n", s=KC)[:, :, bi * 64:(bi + 1) * 64],
                    orw[b % 2][0:64, 0:512].rearrange("p (s n) -> p s n", s=8),
                    PSB[0:64, 0:512].rearrange("p (s n) -> p s n", s=8),
                    ALU.mult),
                    reads=[f"orwE{b % 2}"] + PSBK, writes=[f"oT{gk}"],
                    tag=f"{bk}_ne")
                p.op("DVE", lambda e, bi=bi, b=b: e.tensor_tensor(
                    oT_g[64:128, :].rearrange("p (s n) -> p s n", s=KC)[:, :, bi * 64:(bi + 1) * 64],
                    orw[b % 2][64:128, 0:512].rearrange("p (s n) -> p s n", s=8),
                    PSB[64:128, 512:1024].rearrange("p (s n) -> p s n", s=8),
                    ALU.mult),
                    reads=[f"orwO{b % 2}"] + PSBK, writes=[f"oT{gk}"],
                    tag=f"{bk}_no")

            for b in range(GB + 2):
                if b < GB:
                    SS(b)
                if 1 <= b <= GB:
                    OM(b - 1)
                if b >= 2:
                    BM(b - 2)
            p.op("DVE", lambda e: e.tensor_copy(
                kT_c[:, :].rearrange("p (s n) -> p s n", s=KC),
                kT_g[:, :].rearrange("p (s n) -> p s n", s=KC)[:, :, HGRP - 64:HGRP]),
                reads=[f"kT{gk}:{m}" for m in range(8)], writes=["kTc"],
                tag=f"{gk}_kc")
            p.op("DVE", lambda e: e.tensor_copy(v65_c[64:128, :],
                                                v65_g[GT - 1][64:128, :]),
                 reads=[f"v65:{GT - 1}"], writes=["v65c"], tag=f"{gk}_vc")
            for i in range(GT):
                ti = g * GT + i
                b = i % 2
                kt = f"{gk}z{i}"

                def zmm(e, i=i):
                    inst = None
                    for k in range(KC):
                        for nh_ in range(2):
                            inst = e.matmul(
                                PSB[:, nh_ * 512:(nh_ + 1) * 512],
                                oT_g[:, k * HGRP + i * 128:k * HGRP + (i + 1) * 128],
                                wout_sb[:, k * C + nh_ * 512:k * C + (nh_ + 1) * 512],
                                start=(k == 0), stop=(k == KC - 1))
                    return inst
                p.op("PE", zmm, reads=[f"oT{gk}", "W:aout"], writes=PSBK, tag=kt)
                tb_, t2b_ = PK["T"][b], PK["T2"][b]
                dma("SP", tb_[:, :], ys[ti * 128:(ti + 1) * 128, :],
                    [f"ys:{ti}"], [f"tb{b}"], f"{kt}_ly")
                p.op("DVE", lambda e, t2b_=t2b_: e.scalar_tensor_tensor(
                    t2b_[:, :], PSB[:, :], 1.0, cb_bout[:, :],
                    ALU.mult, ALU.add),
                    reads=PSBK + ["C:bout"], writes=[f"t2b{b}"], tag=f"{kt}_b")
                p.op("DVE", lambda e, t2b_=t2b_, tb_=tb_: e.tensor_add(
                    t2b_[:, :], t2b_[:, :], tb_[:, :]),
                    reads=[f"t2b{b}", f"tb{b}"], writes=[f"t2b{b}"], tag=f"{kt}_y")
                dma("ACT", y2s[ti * 128:(ti + 1) * 128, :], t2b_[:, :],
                    [f"t2b{b}"], [f"y2s:{ti}"], f"{kt}_st")
        ar.pop()

        ar.push()
        alloc_pack()
        cb_g2 = ar.alloc(C, F32)
        cb_b2g = ar.alloc(C, F32)
        cb_b2 = ar.alloc(C, F32)
        col_b1 = ar.alloc(32, F32)
        mk_bcast(cb_g2, aa_g2.rearrange("(o n) -> o n", o=1), C, "g2")
        mk_bcast(cb_b2g, aa_b2g.rearrange("(o n) -> o n", o=1), C, "b2g")
        mk_bcast(cb_b2, aa_b2.rearrange("(o n) -> o n", o=1), C, "b2")
        mk_cols(col_b1, aa_b1, 32, "b1")
        w2_sb = ar.alloc((FF_A // 128) * C, BF16)
        w1p = [ar.alloc(KC * 128, BF16) for _ in range(4)]
        h2T = ar.alloc(KC * 512, BF16)
        m1T = ar.alloc((FF_A // 128) * 512, BF16)
        wchunks(w2_sb, "W:w2", aa_w2, FF_A, C, "aaw2")
        NF = FF_A // 128
        for ch in range(T // 512):
            ck = f"a4c{ch}"
            if "f1" in stages:
                emit_gz("f1", 9, "ACT")
            for i in range(4):
                ti = ch * 4 + i
                b = i % 2
                t, t2, h = PK["T"][b], PK["T2"][b], PK["H"][b]
                kt = f"a4{ti}"
                dma("SP", t[:, :], y2s[ti * 128:(ti + 1) * 128, :],
                    [f"y2s:{ti}"], [f"tb{b}"], kt)
                ln_core(t[:, :], [f"tb{b}"], t2[:, :], f"t2b{b}", kt)
                p.op("DVE", lambda e, t2=t2: e.tensor_mul(t2[:, :], t2[:, :],
                                                          cb_g2[:, :]),
                     reads=[f"t2b{b}", "C:g2"], writes=[f"t2b{b}"], tag=f"{kt}_g2")
                p.op("DVE", lambda e, t2=t2, h=h: e.tensor_add(h[:, :], t2[:, :],
                                                               cb_b2g[:, :]),
                     reads=[f"t2b{b}", "C:b2g"], writes=[f"hb{b}"], tag=f"{kt}_b2")
                transpose8(h[:, :], [f"hb{b}"], f"{kt}_tr")
                p.op("DVE", lambda e, i=i: e.tensor_copy(
                    h2T[:, :].rearrange("p (k n) -> p k n", k=KC)[:, :, i * 128:(i + 1) * 128],
                    PST[:, :].rearrange("p (k n) -> p k n", k=KC)),
                    reads=["pst"], writes=[f"h2T:{i}"], tag=f"{kt}_ev")
            hk = [f"h2T:{i}" for i in range(4)]
            for f in range(NF):
                wb = w1p[f % 4]
                wk = f"w1p{f % 4}"
                dma("POOL", wb[:, :].rearrange("p (k n) -> p k n", k=KC),
                    aa_w1.rearrange("(k p) n -> p k n", p=128)[:, :, f * 128:(f + 1) * 128],
                    ["H:w1"], [wk], f"{ck}w1_{f}")
                bank = PSAK[f % 4]

                def m1mm(e, f=f, wb=wb):
                    inst = None
                    for k in range(KC):
                        inst = e.matmul(PSA[:, (f % 4) * 512:(f % 4) * 512 + 512],
                                        wb[:, k * 128:(k + 1) * 128],
                                        h2T[:, k * 512:(k + 1) * 512],
                                        start=(k == 0), stop=(k == KC - 1))
                    return inst
                p.op("PE", m1mm, reads=hk + [wk], writes=[bank], tag=f"{ck}m1_{f}")
                p.op("ACT", lambda e, f=f: e.activation(
                    m1T[:, f * 512:(f + 1) * 512],
                    PSA[:, (f % 4) * 512:(f % 4) * 512 + 512],
                    AF.Gelu, bias=col_b1[:, f:f + 1]),
                    reads=[bank, "C:b1"], writes=[f"m1T:{f}"], tag=f"{ck}ge_{f}")
            m1k = [f"m1T:{f}" for f in range(NF)]
            for i in range(4):
                ti = ch * 4 + i
                b = i % 2

                def m2mm(e, i=i):
                    inst = None
                    for k in range(NF):
                        for nh_ in range(2):
                            inst = e.matmul(
                                PSB[:, nh_ * 512:(nh_ + 1) * 512],
                                m1T[:, k * 512 + i * 128:k * 512 + (i + 1) * 128],
                                w2_sb[:, k * C + nh_ * 512:k * C + (nh_ + 1) * 512],
                                start=(k == 0), stop=(k == NF - 1))
                    return inst
                kt = f"{ck}m2_{i}"
                p.op("PE", m2mm, reads=m1k + ["W:w2"], writes=PSBK, tag=kt)
                tb_, t2b_, yb_ = PK["T"][b], PK["T2"][b], PK["Y"][b]
                dma("SP", tb_[:, :], y2s[ti * 128:(ti + 1) * 128, :],
                    [f"y2s:{ti}"], [f"tb{b}"], f"{kt}_l2")
                dma("SP", yb_[:, :], xin[ti * 128:(ti + 1) * 128, :],
                    [f"H:xin:{ti}"], [f"yb{b}"], f"{kt}_l0")
                p.op("DVE", lambda e, t2b_=t2b_: e.scalar_tensor_tensor(
                    t2b_[:, :], PSB[:, :], 1.0, cb_b2[:, :],
                    ALU.mult, ALU.add),
                    reads=PSBK + ["C:b2"], writes=[f"t2b{b}"], tag=f"{kt}_b")
                p.op("DVE", lambda e, t2b_=t2b_, tb_=tb_: e.tensor_add(
                    t2b_[:, :], t2b_[:, :], tb_[:, :]),
                    reads=[f"t2b{b}", f"tb{b}"], writes=[f"t2b{b}"], tag=f"{kt}_y2")
                p.op("DVE", lambda e, t2b_=t2b_, yb_=yb_: e.tensor_add(
                    t2b_[:, :], t2b_[:, :], yb_[:, :]),
                    reads=[f"t2b{b}", f"yb{b}"], writes=[f"t2b{b}"], tag=f"{kt}_x0")
                dma("ACT", r1[ti * 128:(ti + 1) * 128, :], t2b_[:, :],
                    [f"t2b{b}"], [f"r1:{ti}"], f"{kt}_st")
        ar.pop()

    def ffn(tag, rin, rin_name, rout, rout_name, G, Geo, SHS):
        ar.push()
        emit_gz(tag, 10 ** 9, "ACT")
        fmask = ar.alloc(NTILES * NEXP, F32)
        rankw = ar.alloc(NTILES * NEXP, F32)
        desti = ar.alloc(NTILES, mybir.dt.int32)
        cntrow = ar.alloc(NTILES * NEXP, F32)
        baserow = ar.alloc(NTILES * NEXP, F32)
        cmat = ar.alloc(NEXP, F32)
        lgtall = ar.alloc(NTILES * NEXP, F32)
        mb0 = ar.alloc(NTILES * NEXP, F32)
        mb8 = ar.alloc(NTILES * NEXP, F32)
        mxsc = ar.alloc(2 * NTILES, F32)
        dtmp = ar.alloc(8 * NTILES, F32)
        PSCK2 = ["psc0", "psc1"]
        sl01 = [ar.alloc(16 * 1024, BF16) for _ in range(2)]
        wchunks(sl01[0], "slot0", f_sg, C, FF_E, f"{tag}shg")
        wchunks(sl01[1], "slot1", f_su, C, FF_E, f"{tag}shu")
        ar.push()
        xsall = ar.alloc(NTILES * C, BF16)
        alloc_pack()
        xstg = [ar.alloc(1024, BF16) for _ in range(2)]
        xtf = [ar.alloc(1024, F32) for _ in range(2)]
        for ti in range(NTILES):
            b = ti % 2
            t, y, h = PK["T"][b], PK["Y"][b], PK["H"][b]
            kt = f"{tag}L{ti}"
            xt = xtf[b]
            po = 1024 * b
            pk2 = [f"psa{2 * b}", f"psa{2 * b + 1}"]
            dma("SP", t[:, :], rin[ti * 128:(ti + 1) * 128, :],
                [f"{rin_name}:{ti}"], [f"tb{b}"], kt)
            ln_core(t[:, :], [f"tb{b}"], y[:, :], f"yb{b}", kt)
            p.op("PE", lambda e, y=y, po=po: [e.transpose(
                PSA[:, po + k * 128:po + (k + 1) * 128],
                y[:, k * 128:(k + 1) * 128], sb_idf[:, :]) for k in range(KC)][-1],
                reads=[f"yb{b}", "idf"], writes=pk2, tag=f"{kt}_ftr")
            p.op("DVE", lambda e, xt=xt, po=po: e.tensor_copy(
                xt[:, :], PSA[:, po:po + 1024]),
                reads=pk2, writes=[f"xtf{b}"], tag=f"{kt}_fev")

            def rmm(e, xt=xt, b=b):
                inst = None
                for k in range(KC):
                    inst = e.matmul(PSC[:, b * 8:b * 8 + 8],
                                    xt[:, k * 128:(k + 1) * 128],
                                    rwT[:, k * 8:(k + 1) * 8],
                                    start=(k == 0), stop=(k == KC - 1))
                return inst
            p.op("PE", rmm, reads=[f"xtf{b}", "C:rwT"], writes=[f"psc{b}"],
                 tag=f"{kt}_rt")
            p.op("DVE", lambda e, ti=ti, b=b: e.tensor_copy(
                lgtall[:, ti * 8:(ti + 1) * 8], PSC[:, b * 8:b * 8 + 8]),
                reads=[f"psc{b}"], writes=["lga"], tag=f"{kt}_lg")
            p.op("ACT", lambda e, y=y, ti=ti: e.activation(
                xsall[:, ti * C:(ti + 1) * C], y[:, :], AF.Copy),
                reads=[f"yb{b}"], writes=[f"xsa:{ti}"], tag=f"{kt}_cx")
            p.op("ACT", lambda e, y=y, h=h: e.activation(h[:, :], y[:, :], AF.Copy),
                 reads=[f"yb{b}"], writes=[f"hb{b}"], tag=f"{kt}_cy")
            transpose8(h[:, :], [f"hb{b}"], f"{kt}_t1")
            p.op("DVE", lambda e, b=b: e.tensor_copy(xstg[b][:, :], PST[:, :]),
                 reads=["pst"], writes=[f"xstg{b}"], tag=f"{kt}_e1")
            dma("ACT", xTs.rearrange("(k p) n -> p k n", p=128)[:, :, ti * 128:(ti + 1) * 128],
                xstg[b][:, :].rearrange("p (k n) -> p k n", k=KC),
                [f"xstg{b}"], [f"xTs:{ti}"], f"{kt}_s1")
        lg3 = lgtall[:, 0:NTILES * NEXP].rearrange("p (t e) -> p t e", e=NEXP)
        mx3 = mxsc[:, 0:NTILES].rearrange("p (t o) -> p t o", o=1)
        p.op("DVE", lambda e: e.reduce_max(mx3, lg3, axis=AX),
             reads=["lga"], writes=["mxc"], tag=f"{tag}mx")
        p.op("ACT", lambda e: e.activation(mxsc[:, NTILES:2 * NTILES],
                                           mxsc[:, 0:NTILES], AF.Sigmoid),
             reads=["mxc"], writes=["scal"], tag=f"{tag}sg")
        for ti in range(NTILES):
            g0, g1 = ti * NEXP, (ti + 1) * NEXP
            p.op("DVE", lambda e, g0=g0, g1=g1, ti=ti: e.tensor_single_scalar(
                mb0[:, g0:g1], lgtall[:, g0:g1], mxsc[:, ti:ti + 1], ALU.is_equal),
                reads=["lga", "mxc"], writes=["m0"], tag=f"{tag}eq{ti}")
            p.op("DVE", lambda e, g0=g0, g1=g1: e.tensor_tensor_scan(
                mb8[:, g0:g1], mb0[:, g0:g1], mb0[:, g0:g1], 0.0,
                ALU.add, ALU.bypass),
                reads=["m0"], writes=["m8"], tag=f"{tag}cs{ti}")
        p.op("DVE", lambda e: e.tensor_scalar(mb8[:, :], mb8[:, :], 1.0, None,
                                              ALU.is_equal),
             reads=["m8"], writes=["m8"], tag=f"{tag}c1")
        p.op("DVE", lambda e: e.tensor_mul(fmask[:, :], mb0[:, :], mb8[:, :]),
             reads=["m0", "m8"], writes=["fma"], tag=f"{tag}fm")
        p.op("PE", lambda e: e.matmul(PSC[:, 256:512], sb_ltri[:, :], fmask[:, :],
                                      start=True, stop=True),
             reads=["fma", "ltri"], writes=PSCK2, tag=f"{tag}rk")
        p.op("PE", lambda e: e.matmul(PSC[0:1, 0:256], sb_onesf[:, 0:1], fmask[:, :],
                                      start=True, stop=True),
             reads=["fma", "onesf"], writes=PSCK2, tag=f"{tag}ct")
        p.op("DVE", lambda e: e.tensor_copy(rankw[:, :], PSC[:, 256:512]),
             reads=PSCK2, writes=["rka"], tag=f"{tag}rke")
        p.op("DVE", lambda e: e.tensor_copy(cntrow[0:1, :], PSC[0:1, 0:256]),
             reads=PSCK2, writes=["cnt"], tag=f"{tag}cte")
        for ti in range(NTILES):
            def xsf(e, ti=ti):
                with nc.allow_low_precision(
                        reason="xs feeds bf16-operand matmuls; rounding here "
                               "equals the matmul's own operand cast"):
                    return e.tensor_scalar(
                        xsall[:, ti * C:(ti + 1) * C], xsall[:, ti * C:(ti + 1) * C],
                        mxsc[:, NTILES + ti:NTILES + ti + 1], None, ALU.mult)
            p.op("DVE", xsf,
                 reads=[f"xsa:{ti}", "scal"], writes=[f"xsa:{ti}"], tag=f"{tag}xs{ti}")
        dma("SP", cmat[0:NTILES, 0:NEXP],
            cntrow[0:1, 0:NTILES * NEXP].rearrange("o (t e) -> o t e", e=NEXP),
            ["cnt"], ["cmat"], f"{tag}cm")
        p.op("PE", lambda e: e.matmul(PSC[0:NTILES, 8:16], sb_ltri[0:NTILES, 0:NTILES],
                                      cmat[0:NTILES, 0:NEXP], start=True, stop=True),
             reads=["cmat", "ltri"], writes=PSCK2, tag=f"{tag}bs")
        p.op("DVE", lambda e: e.tensor_copy(cmat[0:NTILES, 0:NEXP], PSC[0:NTILES, 8:16]),
             reads=PSCK2, writes=["cmat2"], tag=f"{tag}bse")
        dma("SP", baserow[0:1, 0:NTILES * NEXP].rearrange("o (t e) -> o t e", e=NEXP),
            cmat[0:NTILES, 0:NEXP], ["cmat2"], ["baserow"], f"{tag}br")
        p.op("PE", lambda e: e.matmul(PSC[:, 0:256], sb_ones[0:1, :],
                                      baserow[0:1, 0:256], start=True, stop=True),
             reads=["baserow", "ones"], writes=PSCK2, tag=f"{tag}bb")

        def d3(c):
            return dtmp[:, c * NTILES:(c + 1) * NTILES]

        def t3(buf):
            return buf[:, :].rearrange("p (t e) -> p t e", e=NEXP)

        def dr3(c):
            return d3(c).rearrange("p (t o) -> p t o", o=1)
        p.op("DVE", lambda e: e.tensor_mul(mb0[:, :], rankw[:, :], fmask[:, :]),
             reads=["rka", "fma"], writes=["m0"], tag=f"{tag}rs")
        p.op("DVE", lambda e: e.reduce_sum(dr3(0), t3(mb0), axis=AX),
             reads=["m0"], writes=["dR"], tag=f"{tag}rs2")
        p.op("DVE", lambda e: e.tensor_mul(mb8[:, :], PSC[:, 0:256], fmask[:, :]),
             reads=PSCK2 + ["fma"], writes=["m8"], tag=f"{tag}bsl")
        p.op("DVE", lambda e: e.reduce_sum(dr3(1), t3(mb8), axis=AX),
             reads=["m8"], writes=["dB"], tag=f"{tag}bsl2")
        p.op("DVE", lambda e: e.tensor_mul(mb0[:, :], sb_iota256[:, :], fmask[:, :]),
             reads=["iota256", "fma"], writes=["m0"], tag=f"{tag}cp")
        p.op("DVE", lambda e: e.reduce_sum(dr3(2), t3(mb0), axis=AX),
             reads=["m0"], writes=["dE"], tag=f"{tag}cp2")
        p.op("DVE", lambda e: e.tensor_scalar(d3(2), d3(2), float(CAP), None,
                                              ALU.mult),
             reads=["dE"], writes=["dE"], tag=f"{tag}cpm")
        p.op("DVE", lambda e: e.tensor_add(d3(3), d3(0), d3(1)),
             reads=["dR", "dB"], writes=["dS"], tag=f"{tag}rb")
        p.op("DVE", lambda e: e.tensor_single_scalar(d3(4), d3(3), float(CAP),
                                                     ALU.is_ge),
             reads=["dS"], writes=["dO"], tag=f"{tag}ov")
        p.op("DVE", lambda e: e.tensor_add(d3(5), d3(3), d3(2)),
             reads=["dS", "dE"], writes=["dF"], tag=f"{tag}df")
        p.op("DVE", lambda e: e.tensor_scalar(d3(6), d3(5), -1.0, float(NEXP * CAP),
                                              ALU.mult, ALU.add),
             reads=["dF"], writes=["dT"], tag=f"{tag}tr")
        p.op("DVE", lambda e: e.tensor_add(d3(6), d3(6), sb_iota32[:, 0:NTILES]),
             reads=["dT", "iota32"], writes=["dT"], tag=f"{tag}tr2")
        p.op("DVE", lambda e: e.tensor_mul(d3(4), d3(4), d3(6)),
             reads=["dO", "dT"], writes=["dO"], tag=f"{tag}cl")
        p.op("DVE", lambda e: e.tensor_add(d3(7), d3(4), d3(5)),
             reads=["dO", "dF"], writes=["dPre"], tag=f"{tag}cl2")
        p.op("DVE", lambda e: e.tensor_copy(desti[:, 0:NTILES], d3(7)),
             reads=["dPre"], writes=["diall"], tag=f"{tag}ci")
        for ti in range(NTILES):
            p.op("POOL", lambda e, ti=ti: e.indirect_dma_start(
                G, bass.IndirectOffsetOnAxis(ap=desti[:, ti:ti + 1], axis=0),
                xsall[:, ti * C:(ti + 1) * C], None),
                reads=["diall", f"xsa:{ti}", "Ginit"], writes=[f"H:G:{ti}"],
                dma=True, tag=f"{tag}sc{ti}")
        ar.pop()

        slots = sl01 + [ar.alloc(16 * 1024, BF16)]
        hT = ar.alloc(16 * 512, BF16)
        xch = ar.alloc(KC * 512, BF16)
        stile = [ar.alloc(512, BF16) for _ in range(2)]
        ebuf = [ar.alloc(1024, BF16) for _ in range(2)]
        xg = [ar.alloc(1024, BF16) for _ in range(4)]

        def expert_block(en, wg, wu, wd, nch, load_x, out_rows, out_name,
                         skip_gu=False):
            ek = f"{tag}{en}"
            sg, su, sd = slots
            kg, ku, kd = "slot0", "slot1", "slot2"
            if not skip_gu:
                wchunks(sg, kg, wg, C, FF_E, f"{ek}g")
                wchunks(su, ku, wu, C, FF_E, f"{ek}u")
            wchunks(sd, kd, wd, FF_E, C, f"{ek}d")
            for ch in range(nch):
                load_x(ek, ch)
                for f in range(16):
                    st = stile[f % 2]
                    stk = f"stile{f % 2}"
                    gco = (f % 2) * 1024
                    uco = (f % 2) * 1024 + 512
                    gk2 = PSAK[(f % 2) * 2]
                    uk2 = PSAK[(f % 2) * 2 + 1]

                    def gmm(e, f=f, sg=sg, gco=gco):
                        inst = None
                        for k in range(KC):
                            inst = e.matmul(
                                PSA[:, gco:gco + 512],
                                sg[:, k * FF_E + f * 128:k * FF_E + (f + 1) * 128],
                                xch[:, k * 512:(k + 1) * 512],
                                start=(k == 0), stop=(k == KC - 1))
                        return inst

                    def umm(e, f=f, su=su, uco=uco):
                        inst = None
                        for k in range(KC):
                            inst = e.matmul(
                                PSA[:, uco:uco + 512],
                                su[:, k * FF_E + f * 128:k * FF_E + (f + 1) * 128],
                                xch[:, k * 512:(k + 1) * 512],
                                start=(k == 0), stop=(k == KC - 1))
                        return inst
                    p.op("PE", gmm, reads=[kg, "xch"], writes=[gk2], tag=f"{ek}g{ch}_{f}")
                    p.op("PE", umm, reads=[ku, "xch"], writes=[uk2], tag=f"{ek}u{ch}_{f}")
                    p.op("ACT", lambda e, st=st, gco=gco: e.activation(
                        st[:, :], PSA[:, gco:gco + 512], AF.Silu),
                        reads=[gk2], writes=[stk], tag=f"{ek}s{ch}_{f}")
                    p.op("DVE", lambda e, f=f, st=st, uco=uco: e.tensor_tensor(
                        hT[:, f * 512:(f + 1) * 512], st[:, :],
                        PSA[:, uco:uco + 512], ALU.mult),
                        reads=[stk, uk2], writes=[f"hT:{f}"], tag=f"{ek}h{ch}_{f}")
                hk = [f"hT:{f}" for f in range(16)]
                for it in range(4):
                    eb2 = ebuf[it % 2]
                    ebk = f"ebuf{it % 2}"
                    for nh_ in range(2):
                        bkk = PSBK[nh_]

                        def dnmm(e, it=it, sd=sd, nh_=nh_):
                            inst = None
                            for k in range(16):
                                inst = e.matmul(
                                    PSB[:, nh_ * 512:(nh_ + 1) * 512],
                                    hT[:, k * 512 + it * 128:k * 512 + (it + 1) * 128],
                                    sd[:, k * C + nh_ * 512:k * C + (nh_ + 1) * 512],
                                    start=(k == 0), stop=(k == 15))
                            return inst
                        p.op("PE", dnmm, reads=hk + [kd], writes=[bkk],
                             tag=f"{ek}d{ch}_{it}_{nh_}")
                        def dnev(e, nh_=nh_, eb2=eb2):
                            with nc.allow_low_precision(
                                    reason="ffn branch outputs bounce via HBM "
                                           "in bf16; same rounding class as "
                                           "the bf16 matmul operands"):
                                return e.tensor_copy(
                                    eb2[:, nh_ * 512:(nh_ + 1) * 512],
                                    PSB[:, nh_ * 512:(nh_ + 1) * 512])
                        p.op("DVE", dnev,
                             reads=[bkk], writes=[ebk + f":{nh_}"],
                             tag=f"{ek}e{ch}_{it}_{nh_}")
                    r0 = out_rows(ch, it)
                    dma("ACT", r0, eb2[:, :], [ebk + ":0", ebk + ":1"],
                        [f"{out_name}:{en}:{ch}:{it}"], f"{ek}o{ch}_{it}")

        def shared_loadx(ek, ch):
            dma("SP", xch[:, :].rearrange("p (k n) -> p k n", k=KC),
                xTs.rearrange("(k p) n -> p k n", p=128)[:, :, ch * 512:(ch + 1) * 512],
                [f"xTs:{tt}" for tt in range(ch * 4, ch * 4 + 4)],
                ["xch"], f"{ek}x{ch}")
        expert_block("sh", f_sg, f_su, f_sd, 8, shared_loadx,
                     lambda ch, it: SHS[(ch * 4 + it) * 128:(ch * 4 + it + 1) * 128, :],
                     "H:SHS", skip_gu=True)
        shs_keys = [f"H:SHS:sh:{ch}:{it}" for ch in range(8) for it in range(4)]

        for x in range(NEXP):
            def exp_loadx(ek, ch, x=x):
                for i in range(4):
                    b = i % 2
                    row0 = x * CAP + ch * 512 + i * 128
                    dma("SP", xg[b][:, :], G[row0:row0 + 128, :],
                        [f"H:G:{tt}" for tt in range(NTILES)], [f"xg{b}"],
                        f"{ek}xl{ch}_{i}")
                    transpose8(xg[b][:, :], [f"xg{b}"], f"{ek}xt{ch}_{i}")
                    p.op("DVE", lambda e, i=i: e.tensor_copy(
                        xch[:, :].rearrange("p (k n) -> p k n", k=KC)[:, :, i * 128:(i + 1) * 128],
                        PST[:, :].rearrange("p (k n) -> p k n", k=KC)),
                        reads=["pst"], writes=["xch"], tag=f"{ek}xe{ch}_{i}")
            expert_block(f"e{x}", f_eg[x], f_eu[x], f_ed[x], CAP // 512, exp_loadx,
                         lambda ch, it, x=x: Geo[x * CAP + (ch * 4 + it) * 128:
                                                 x * CAP + (ch * 4 + it + 1) * 128, :],
                         "H:Geo")
        geo_keys = [f"H:Geo:e{x}:{ch}:{it}" for x in range(NEXP)
                    for ch in range(CAP // 512) for it in range(4)]

        ar.push()
        eb = [ar.alloc(1024, F32) for _ in range(2)] + \
            [ar.alloc(1024, BF16) for _ in range(2)]
        mg = [ar.alloc(1024, BF16) for _ in range(2)]
        for ti in range(NTILES):
            b = ti % 2
            kt = f"{tag}E{ti}"
            dma("SP", eb[b][:, :], rin[ti * 128:(ti + 1) * 128, :],
                [f"{rin_name}:{ti}"], [f"ebl{b}"], kt)
            dma("SP", eb[2 + b][:, :], SHS[ti * 128:(ti + 1) * 128, :],
                shs_keys, [f"ebs{b}"], f"{kt}_sh")
            p.op("POOL", lambda e, ti=ti, b=b: e.indirect_dma_start(
                mg[b][:, :], None, Geo,
                bass.IndirectOffsetOnAxis(ap=desti[:, ti:ti + 1], axis=0)),
                reads=geo_keys + ["Geoinit", "diall"], writes=[f"mg{b}"],
                dma=True, tag=f"{kt}_gt")
            p.op("DVE", lambda e, b=b: e.tensor_add(eb[b][:, :], eb[b][:, :],
                                                    eb[2 + b][:, :]),
                 reads=[f"ebl{b}", f"ebs{b}"], writes=[f"ebl{b}"], tag=f"{kt}_a1")
            p.op("DVE", lambda e, b=b: e.tensor_add(eb[b][:, :], eb[b][:, :],
                                                    mg[b][:, :]),
                 reads=[f"ebl{b}", f"mg{b}"], writes=[f"ebl{b}"], tag=f"{kt}_a2")
            dma("ACT", rout[ti * 128:(ti + 1) * 128, :], eb[b][:, :],
                [f"ebl{b}"], [f"{rout_name}:{ti}"], f"{kt}_st")
        ar.pop()
        ar.pop()

    if "f1" in stages:
        ffn("f1", r1, "r1", r2, "r2", g1s, g1eo, sh1s)

    if "ab" in stages:
        ar.push()
        kT_b = ar.alloc(KC * SEQ, BF16)
        v65_b = [ar.alloc(V65W, BF16) for _ in range(SEQ // 128)]
        for bb in range(2):
            bt0 = bb * 16
            bk0 = f"ab{bb}"
            vkeys = [f"v65b:{t}" for t in range(16)]
            kkeys = [f"kTb:{m}" for m in range(8)]
            ar.push()
            alloc_pack()
            xTb = ar.alloc(KC * 512, BF16)
            qstg = ar.alloc(KC * 512, BF16)
            wring = [ar.alloc(KC * C, BF16) for _ in range(2)]
            wseq = 0
            for ch in range(4):
                for i in range(4):
                    ti = bt0 + ch * 4 + i
                    b = i % 2
                    t, y, h = PK["T"][b], PK["Y"][b], PK["H"][b]
                    kt = f"{bk0}p{ti}"
                    dma("SP", t[:, :], r2[ti * 128:(ti + 1) * 128, :],
                        [f"r2:{ti}"], [f"tb{b}"], kt)
                    ln_core(t[:, :], [f"tb{b}"], y[:, :], f"yb{b}", kt)
                    p.op("ACT", lambda e, y=y, h=h: e.activation(h[:, :], y[:, :],
                                                                 AF.Copy),
                         reads=[f"yb{b}"], writes=[f"hb{b}"], tag=f"{kt}_c")
                    transpose8(h[:, :], [f"hb{b}"], f"{kt}_tr")
                    p.op("DVE", lambda e, i=i: e.tensor_copy(
                        xTb[:, :].rearrange("p (k n) -> p k n", k=KC)[:, :, i * 128:(i + 1) * 128],
                        PST[:, :].rearrange("p (k n) -> p k n", k=KC)),
                        reads=["pst"], writes=[f"xTb:{i}"], tag=f"{kt}_ev")
                xbk = [f"xTb:{i}" for i in range(4)]
                for wsrc, wtag in [(ab_wq, "q"), (ab_wk, "k"), (ab_wv, "v")]:
                    wb = wring[wseq % 2]
                    wk = f"wring{wseq % 2}"
                    wseq += 1
                    wchunks(wb, wk, wsrc, C, C, f"{bk0}w{wtag}{ch}")
                    if wtag != "v":
                        for m in range(8):
                            def pmm(e, m=m, wb=wb):
                                inst = None
                                for k in range(KC):
                                    inst = e.matmul(
                                        PSA[:, (m % 4) * 512:(m % 4) * 512 + 512],
                                        wb[:, k * C + m * 128:k * C + (m + 1) * 128],
                                        xTb[:, k * 512:(k + 1) * 512],
                                        start=(k == 0), stop=(k == KC - 1))
                                return inst
                            p.op("PE", pmm, reads=xbk + [wk], writes=[PSAK[m % 4]],
                                 tag=f"{bk0}{wtag}{ch}_{m}")
                            if wtag == "q":
                                p.op("ACT", lambda e, m=m: e.activation(
                                    qstg[:, m * 512:(m + 1) * 512],
                                    PSA[:, (m % 4) * 512:(m % 4) * 512 + 512],
                                    AF.Copy, scale=SCALE_B),
                                    reads=[PSAK[m % 4], "Kc"], writes=[f"qstg:{m}"],
                                    tag=f"{bk0}qe{ch}_{m}")
                            else:
                                p.op("ACT", lambda e, m=m, ch=ch: e.activation(
                                    kT_b[:, m * SEQ + ch * 512:m * SEQ + ch * 512 + 512],
                                    PSA[:, (m % 4) * 512:(m % 4) * 512 + 512],
                                    AF.Copy, scale=SCALE_B),
                                    reads=[PSAK[m % 4], "Kc"], writes=[f"kTb:{m}"],
                                    tag=f"{bk0}ke{ch}_{m}")
                        if wtag == "q":
                            dma("ACT", qTs.rearrange("(k p) n -> p k n", p=128)
                                [:, :, bt0 * 128 + ch * 512:bt0 * 128 + (ch + 1) * 512],
                                qstg[:, :].rearrange("p (k n) -> p k n", k=KC),
                                [f"qstg:{m}" for m in range(8)], ["qstg_out"],
                                f"{bk0}qs{ch}")
                    else:
                        for i in range(4):
                            vt = ch * 4 + i

                            def vmmb(e, i=i, wb=wb):
                                inst = None
                                for k in range(KC):
                                    for nh_ in range(2):
                                        inst = e.matmul(
                                            PSB[:, nh_ * 512:(nh_ + 1) * 512],
                                            xTb[:, k * 512 + i * 128:k * 512 + (i + 1) * 128],
                                            wb[:, k * C + nh_ * 512:k * C + (nh_ + 1) * 512],
                                            start=(k == 0), stop=(k == KC - 1))
                                return inst
                            p.op("PE", vmmb, reads=xbk + [wk], writes=PSBK,
                                 tag=f"{bk0}v{vt}")
                            p.op("DVE", lambda e, vt=vt: e.memset(
                                v65_b[vt][:, :].rearrange("p (h w) -> p h w", h=NH)
                                [:, :, HD:HD + 1], 1.0),
                                writes=[f"v65b:{vt}"], tag=f"{bk0}vo{vt}")
                            p.op("DVE", lambda e, vt=vt: e.tensor_copy(
                                v65_b[vt][:, :].rearrange("p (h w) -> p h w", h=NH)[:, :, 0:HD],
                                PSB[:, :].rearrange("p (h w) -> p h w", h=NH)),
                                reads=PSBK + [f"v65b:{vt}"], writes=[f"v65b:{vt}"],
                                tag=f"{bk0}ve{vt}")
            ar.pop()
            ar.push()
            qTc = [ar.alloc(KC * 512, BF16) for _ in range(2)]
            oTc = [ar.alloc(KC * 512, BF16) for _ in range(2)]
            expt = [ar.alloc(16 * 512, BF16) for _ in range(2)]
            orwB = [ar.alloc(512, F32) for _ in range(2)]
            rrowB = [ar.alloc(512, F32) for _ in range(2)]
            lnb = ar.alloc(1024, F32)
            shft = [ar.alloc(512, BF16) for _ in range(2)]
            J = 4 * NH

            def qload(ic):
                dma("SP", qTc[ic % 2][:, :].rearrange("p (k n) -> p k n", k=KC),
                    qTs.rearrange("(k p) n -> p k n", p=128)
                    [:, :, bt0 * 128 + ic * 512:bt0 * 128 + (ic + 1) * 512],
                    ["qstg_out"], [f"qTc{ic % 2}"], f"{bk0}lq{ic}")

            def S(j):
                ic, h = j // NH, j % NH
                sI = h // 2
                r = (h % 2) * 64
                akt = f"{bk0}i{ic}h{h}"
                qsl = qTc[ic % 2][r:r + 64, sI * 512:(sI + 1) * 512]
                for w in range(8):
                    pk2 = [PSAK[(w % 2) * 2], PSAK[(w % 2) * 2 + 1]]

                    def swv(e, w=w, r=r, sI=sI, qsl=qsl):
                        inst = None
                        for jj in range(2):
                            jt = w * 2 + jj
                            co = (w % 2) * 1024 + jj * 512
                            inst = e.matmul(
                                PSA[:, co:co + 512],
                                kT_b[r:r + 64, sI * SEQ + jt * 128:sI * SEQ + (jt + 1) * 128],
                                qsl, start=True, stop=True)
                        return inst
                    p.op("PE", swv, reads=kkeys + [f"qTc{ic % 2}"], writes=pk2,
                         tag=f"{akt}_s{w}")

                    def ewv(e, w=w, j=j):
                        inst = None
                        for jj in range(2):
                            jt = w * 2 + jj
                            co = (w % 2) * 1024 + jj * 512
                            inst = e.activation(
                                expt[j % 2][:, jt * 512:(jt + 1) * 512],
                                PSA[:, co:co + 512], AF.Exp)
                        return inst
                    p.op("ACT", ewv, reads=pk2, writes=[f"ex{j % 2}:{w}"],
                         tag=f"{akt}_e{w}")

            def O(j):
                ic, h = j // NH, j % NH
                akt = f"{bk0}i{ic}h{h}"
                ek = [f"ex{j % 2}:{w}" for w in range(8)]
                pco = (j % 2) * 512

                def ommb(e, h=h, j=j, pco=pco):
                    inst = None
                    for jt in range(16):
                        inst = e.matmul(PSB[0:65, pco:pco + 512],
                                        v65_b[jt][:, h * 65:h * 65 + 65],
                                        expt[j % 2][:, jt * 512:(jt + 1) * 512],
                                        start=(jt == 0), stop=(jt == 15))
                    return inst
                p.op("PE", ommb, reads=ek + vkeys, writes=[PSBK[j % 2]],
                     tag=f"{akt}_o")
                p.op("DVE", lambda e, pco=pco: e.tensor_copy(
                    lnb[0:1, 0:512], PSB[64:65, pco:pco + 512]),
                    reads=[PSBK[j % 2]], writes=["lnB"], tag=f"{akt}_rc0")
                p.op("ACT", lambda e: e.activation(lnb[0:1, 512:1024],
                                                   lnb[0:1, 0:512], AF.Ln),
                     reads=["lnB"], writes=["lnB2"], tag=f"{akt}_rc1")
                p.op("ACT", lambda e, j=j: e.activation(
                    rrowB[j % 2][0:1, :], lnb[0:1, 512:1024], AF.Exp, scale=-1.0),
                    reads=["lnB2", "Kc"], writes=[f"rrB{j % 2}"], tag=f"{akt}_rc")

            def B(j):
                ic, h = j // NH, j % NH
                sI = h // 2
                akt = f"{bk0}i{ic}h{h}"
                pco = (j % 2) * 512
                p.op("PE", lambda e, j=j: e.matmul(
                    PSC[0:64, :], sb_ones[0:1, 0:64], rrowB[j % 2][0:1, :],
                    start=True, stop=True),
                    reads=[f"rrB{j % 2}", "ones"], writes=["psc"], tag=f"{akt}_bc")
                p.op("ACT", lambda e, j=j, pco=pco: e.activation(
                    orwB[j % 2][0:64, :], PSB[0:64, pco:pco + 512], AF.Copy),
                    reads=[PSBK[j % 2]], writes=[f"orB{j % 2}"], tag=f"{akt}_ev")
                if h % 2 == 0:
                    p.op("DVE", lambda e, sI=sI, j=j, ic=ic: e.tensor_tensor(
                        oTc[ic % 2][0:64, sI * 512:(sI + 1) * 512],
                        orwB[j % 2][0:64, :], PSC[0:64, :], ALU.mult),
                        reads=[f"orB{j % 2}", "psc"], writes=[f"oTc{ic % 2}:{sI}:0"],
                        tag=f"{akt}_n")
                else:
                    p.op("DVE", lambda e, j=j: e.tensor_tensor(
                        shft[j % 2][0:64, :], orwB[j % 2][0:64, :], PSC[0:64, :],
                        ALU.mult),
                        reads=[f"orB{j % 2}", "psc"], writes=[f"shft{j % 2}"],
                        tag=f"{akt}_n")
                    dma("SP", oTc[ic % 2][64:128, sI * 512:(sI + 1) * 512],
                        shft[j % 2][0:64, :], [f"shft{j % 2}"],
                        [f"oTc{ic % 2}:{sI}:1"], f"{akt}_sh")

            for j in range(J + 2):
                if "f2" in stages:
                    emit_gz("f2", 1, "SP")
                if j < J and j % NH == 0:
                    qload(j // NH)
                if j < J:
                    S(j)
                if j >= 1:
                    O(j - 1)
                if j >= 2:
                    B(j - 2)
                    if (j - 2) % NH == NH - 1:
                        ic = (j - 2) // NH
                        okeys = [f"oTc{ic % 2}:{ss}:{par}" for ss in range(8)
                                 for par in range(2)]
                        dma("ACT", oTs.rearrange("(k p) n -> p k n", p=128)
                            [:, :, bt0 * 128 + ic * 512:bt0 * 128 + (ic + 1) * 512],
                            oTc[ic % 2][:, :].rearrange("p (k n) -> p k n", k=KC),
                            okeys, ["oTs_out"], f"{bk0}so{ic}")
            ar.pop()
            ar.push()
            cb_bo = ar.alloc(C, F32)
            mk_bcast(cb_bo, ab_bo.rearrange("(o n) -> o n", o=1), C, "bo")
            wo_sb = ar.alloc(KC * C, BF16)
            oTt = [ar.alloc(KC * 128, BF16) for _ in range(2)]
            zb = [ar.alloc(1024, F32) for _ in range(4)]
            wchunks(wo_sb, "W:bwo", ab_wo, C, C, "abwo")
            for i in range(16):
                ti = bt0 + i
                b = i % 2
                kt = f"{bk0}z{i}"
                dma("SP", oTt[b][:, :].rearrange("p (k n) -> p k n", k=KC),
                    oTs.rearrange("(k p) n -> p k n", p=128)
                    [:, :, ti * 128:(ti + 1) * 128],
                    ["oTs_out"], [f"oTt{b}"], f"{kt}_lo")

                def zmmb(e, b=b):
                    inst = None
                    for k in range(KC):
                        for nh_ in range(2):
                            inst = e.matmul(
                                PSB[:, nh_ * 512:(nh_ + 1) * 512],
                                oTt[b][:, k * 128:(k + 1) * 128],
                                wo_sb[:, k * C + nh_ * 512:k * C + (nh_ + 1) * 512],
                                start=(k == 0), stop=(k == KC - 1))
                    return inst
                p.op("PE", zmmb, reads=[f"oTt{b}", "W:bwo"], writes=PSBK, tag=kt)
                dma("SP", zb[b][:, :], r2[ti * 128:(ti + 1) * 128, :],
                    [f"r2:{ti}"], [f"zbl{b}"], f"{kt}_l")
                p.op("DVE", lambda e, b=b: e.scalar_tensor_tensor(
                    zb[2 + b][:, :], PSB[:, :], 1.0, cb_bo[:, :], ALU.mult, ALU.add),
                    reads=PSBK + ["C:bo"], writes=[f"zbo{b}"], tag=f"{kt}_b")
                p.op("DVE", lambda e, b=b: e.tensor_add(zb[2 + b][:, :],
                                                        zb[2 + b][:, :], zb[b][:, :]),
                     reads=[f"zbo{b}", f"zbl{b}"], writes=[f"zbo{b}"], tag=f"{kt}_r")
                dma("ACT", r3[ti * 128:(ti + 1) * 128, :], zb[2 + b][:, :],
                    [f"zbo{b}"], [f"r3:{ti}"], f"{kt}_st")
            ar.pop()
        ar.pop()

    if "f2" in stages:
        ffn("f2", r3, "r3", out, "out", g2s, g2eo, sh2s)
    else:
        last = {"ab": (r3, "r3"), "f1": (r2, "r2"), "aa": (r1, "r1")}
        for stg_ in ("ab", "f1", "aa"):
            if stg_ in stages:
                src, sname = last[stg_]
                ar.push()
                cp = [ar.alloc(1024, F32) for _ in range(2)]
                for ti in range(NTILES):
                    b = ti % 2
                    dma("SP", cp[b][:, :], src[ti * 128:(ti + 1) * 128, :],
                        [f"{sname}:{ti}"], [f"cp{b}"], f"cp{ti}")
                    dma("ACT", out[ti * 128:(ti + 1) * 128, :], cp[b][:, :],
                        [f"cp{b}"], [f"out:{ti}"], f"cp{ti}o")
                ar.pop()
                break

    p.build(nc, es)
    es.close()
    return p



def make_inputs(inputs, core):
    x = inputs["L0_attn_a_x"]
    shard = np.ascontiguousarray(x[2 * core:2 * core + 2]).reshape(T, C)
    ident_f = np.eye(128, dtype=np.float32)
    ident_b = np.eye(128, dtype=ml_dtypes.bfloat16)
    j = np.arange(64)[:, None]
    i = np.arange(64)[None, :]
    causal = (j <= i).astype(np.float32)
    ones = np.ones((64, 64), np.float32)
    mA = np.concatenate([causal, ones], axis=0)
    mB = np.concatenate([ones, causal], axis=0)
    hmask = np.concatenate([np.tile(mA, (1, NH)), np.tile(mB, (1, NH))], axis=1)
    iota256 = np.tile(np.tile(np.arange(NEXP, dtype=np.float32), 32)[None, :],
                      (128, 1))
    iota32 = np.tile(np.arange(32, dtype=np.float32)[None, :], (128, 1))
    ltri = np.triu(np.ones((128, 128), np.float32), k=1)
    return {
        "xin": shard,
        "aa_wqkv": inputs["L0_attn_a_w_qkv"], "aa_bqkv": inputs["L0_attn_a_b_qkv"],
        "aa_wout": inputs["L0_attn_a_w_out"], "aa_bout": inputs["L0_attn_a_b_out"],
        "aa_g1": inputs["L0_attn_a_ln1_g"], "aa_b1g": inputs["L0_attn_a_ln1_b"],
        "aa_g2": inputs["L0_attn_a_ln2_g"], "aa_b2g": inputs["L0_attn_a_ln2_b"],
        "aa_w1": inputs["L0_attn_a_w1"], "aa_b1": inputs["L0_attn_a_b1"],
        "aa_w2": inputs["L0_attn_a_w2"], "aa_b2": inputs["L0_attn_a_b2"],
        "ab_wq": inputs["L0_attn_b_Wq"], "ab_wk": inputs["L0_attn_b_Wk"],
        "ab_wv": inputs["L0_attn_b_Wv"], "ab_wo": inputs["L0_attn_b_Wo"],
        "ab_bo": inputs["L0_attn_b_bo"],
        "f_rw": inputs["L0_ffn_router_w"],
        "f_eg": inputs["L0_ffn_expert_gate"], "f_eu": inputs["L0_ffn_expert_up"],
        "f_ed": inputs["L0_ffn_expert_down"],
        "f_sg": inputs["L0_ffn_shared_gate"], "f_su": inputs["L0_ffn_shared_up"],
        "f_sd": inputs["L0_ffn_shared_down"],
        "id_f": ident_f, "id_b": ident_b, "hmask": hmask, "iota256": iota256,
        "iota32": iota32, "ltri": ltri,
    }


def kernel(**inputs):
    nc = bass.Bass("TRN2", target_bir_lowering=False)
    build_program(nc)
    nc.finalize()
    in_maps = [make_inputs(inputs, c) for c in range(8)]
    res = run_bass_kernel_spmd(nc, in_maps, core_ids=list(range(8)))
    outs = [np.asarray(res.results[c]["out"]).reshape(2, SEQ, C) for c in range(8)]
    return np.concatenate(outs, axis=0).astype(np.float32)


# version 20
# speedup vs baseline: 1.0435x; 1.0018x over previous
import numpy as np
import ml_dtypes
from contextlib import ExitStack

import concourse.bass as bass
import concourse.mybir as mybir
from concourse.bass_utils import run_bass_kernel_spmd

F32 = mybir.dt.float32
BF16 = mybir.dt.bfloat16
AF = mybir.ActivationFunctionType
ALU = mybir.AluOpType
AX = mybir.AxisListType.X

C = 1024
T = 4096
SEQ = 2048
NH = 16
HD = 64
BUCKET = 64
FF_A = 4096
FF_E = 2048
NEXP = 8
KC = C // 128
NTILES = T // 128
EPS = 1e-5
SCALE_B = float(HD ** -0.25)
SCALE_A = 0.125
V65W = NH * (HD + 1)

ENGINES = ("PE", "DVE", "ACT", "POOL", "SP")


class Op:
    __slots__ = ("idx", "eng", "emit", "reads", "writes", "dma", "deps",
                 "marked", "inc_val", "tag", "_qslot", "_qseq", "fence",
                 "_fence_snap")

    def __init__(self, idx, eng, emit, reads, writes, dma, tag):
        self.idx = idx; self.eng = eng; self.emit = emit
        self.reads = reads; self.writes = writes; self.dma = dma
        self.deps = []; self.marked = False; self.inc_val = 0; self.tag = tag
        self.fence = False; self._fence_snap = None


class Prog:

    def __init__(self, relax_same_engine=False):
        self.ops = []
        self.last_write = {}
        self.readers = {}
        self.relax = relax_same_engine
        self.fence_n = 0

    def op(self, eng, emit, reads=(), writes=(), dma=False, tag=""):
        o = Op(len(self.ops), eng, emit, tuple(reads), tuple(writes), dma, tag)
        deps = set()
        for r in o.reads:
            w = self.last_write.get(r)
            if w is not None:
                deps.add(w)
        for w in o.writes:
            lw = self.last_write.get(w)
            if lw is not None:
                deps.add(lw)
            deps.update(self.readers.get(w, ()))
        deps.discard(o.idx)
        o.deps = sorted(deps)
        self.ops.append(o)
        for r in o.reads:
            self.readers.setdefault(r, []).append(o.idx)
        for w in o.writes:
            self.last_write[w] = o.idx
            self.readers[w] = []
        return o.idx

    def fence(self, tag="fence"):
        n = self.fence_n
        self.fence_n += 1
        for E in ENGINES:
            o = Op(len(self.ops), E, None, (), (), False, f"{tag}{n}_{E}")
            o.fence = True
            self.ops.append(o)

    def _needs_wait(self, ci, pi):
        p, c = self.ops[pi], self.ops[ci]
        if p.eng == c.eng and not p.dma:
            if p.eng == "PE" and not c.dma:
                return False
            if self.relax and not c.dma:
                return False
            return True
        return True

    NQ = 12

    def build(self, nc, es):
        sems = {e: es.enter_context(nc.semaphore(f"s_{e}")) for e in ENGINES}
        qsems = {e: [es.enter_context(nc.semaphore(f"q_{e}{i}"))
                     for i in range(self.NQ)] for e in ("SP", "ACT", "POOL")}

        last_by_eng = {}
        for o in self.ops:
            if o.fence:
                for le in last_by_eng.values():
                    le.marked = True
            else:
                last_by_eng[o.eng] = o
        for o in self.ops:
            if o.dma:
                o.marked = True
            for d in o.deps:
                if self._needs_wait(o.idx, d):
                    self.ops[d].marked = True

        counts = {}
        dma_seq = {}
        for o in self.ops:
            if o.fence:
                snap = dict(counts)
                qsnap = {}
                for e2, nn in dma_seq.items():
                    for i in range(min(nn, self.NQ)):
                        qsnap[(e2, i)] = ((nn - 1 - i) // self.NQ + 1) * 16
                o._fence_snap = (snap, qsnap)
                continue
            if o.marked:
                if o.dma:
                    j = dma_seq.get(o.eng, 0)
                    dma_seq[o.eng] = j + 1
                    o.inc_val = (j // self.NQ + 1) * 16
                    o._qslot = j % self.NQ
                    o._qseq = j
                else:
                    key = (o.eng, False)
                    counts[key] = counts.get(key, 0) + 1
                    o.inc_val = counts[key]

        per_eng = {e: [] for e in ENGINES}
        for o in self.ops:
            per_eng[o.eng].append(o)
        final_q = {}
        for e in ("SP", "ACT", "POOL"):
            n = dma_seq.get(e, 0)
            final_q[e] = [(n - 1 - i) // self.NQ + 1 for i in range(self.NQ)] \
                if n else []

        def body(eng_obj, eng_name):
            waited = {}

            def wait(key, sem, val):
                if waited.get(key, 0) >= val:
                    return
                eng_obj.wait_ge(sem, val)
                waited[key] = val

            for o in per_eng[eng_name]:
                if o.fence:
                    snap, qsnap = o._fence_snap
                    for (e2, dm), v in snap.items():
                        if v and not (e2 == eng_name and not dm):
                            wait(("s", e2), sems[e2], v)
                    for (e2, i), v in qsnap.items():
                        wait(("q", e2, i), qsems[e2][i], v)
                    continue
                if o.dma and o._qseq >= self.NQ:
                    wait(("q", o.eng, o._qslot), qsems[o.eng][o._qslot],
                         o.inc_val - 16)
                for d in o.deps:
                    if not self._needs_wait(o.idx, d):
                        continue
                    pp = self.ops[d]
                    if not pp.marked:
                        continue
                    if pp.dma:
                        wait(("q", pp.eng, pp._qslot), qsems[pp.eng][pp._qslot],
                             pp.inc_val)
                    else:
                        wait(("s", pp.eng), sems[pp.eng], pp.inc_val)
                inst = o.emit(eng_obj)
                if o.marked:
                    if isinstance(inst, (list, tuple)):
                        inst = inst[-1]
                    assert inst is not None, f"op {o.idx} ({o.tag}) returned None"
                    if o.dma:
                        inst.then_inc(qsems[o.eng][o._qslot], 16)
                    else:
                        inst.then_inc(sems[o.eng], 1)
            if eng_name == "POOL":
                for e in ("SP", "ACT", "POOL"):
                    n = dma_seq.get(e, 0)
                    for i in range(min(n, self.NQ)):
                        nslot = (n - 1 - i) // self.NQ + 1
                        wait(("q", e, i), qsems[e][i], nslot * 16)

        blk = es.enter_context(nc.Block())
        blk.sync(lambda e: body(e, "SP"))
        blk.tensor(lambda e: body(e, "PE"))
        blk.vector(lambda e: body(e, "DVE"))
        blk.scalar(lambda e: body(e, "ACT"))
        blk.gpsimd(lambda e: body(e, "POOL"))


class Arena:
    def __init__(self, nc, es, prog, nbytes=204800):
        self.t = es.enter_context(nc.sbuf_tensor("arena", [128, nbytes],
                                                 mybir.dt.uint8))
        self.nbytes = nbytes
        self.off = 0
        self.stack = []
        self.hi = 0
        self.prog = prog

    def alloc(self, cols, dtype):
        size = cols * (2 if dtype == BF16 else 4)
        size = (size + 63) & ~63
        assert self.off + size <= self.nbytes, \
            f"arena overflow: {self.off}+{size} > {self.nbytes}"
        ap = self.t[:][:, self.off:self.off + size].bitcast(dtype)[:, 0:cols]
        self.off += size
        self.hi = max(self.hi, self.off)
        return ap

    def push(self):
        self.stack.append(self.off)

    def pop(self):
        self.off = self.stack.pop()
        self.prog.fence()



def build_program(nc, stages=("aa", "f1", "ab", "f2"), relax=False):
    p = Prog(relax_same_engine=relax)
    es = ExitStack()

    def din(name, shape, dt=F32):
        return nc.dram_tensor(name, shape, dt, kind="ExternalInput").ap()

    xin = din("xin", [T, C])
    aa_wqkv = din("aa_wqkv", [C, 3 * C]); aa_bqkv = din("aa_bqkv", [3 * C])
    aa_wout = din("aa_wout", [C, C]); aa_bout = din("aa_bout", [C])
    aa_g1 = din("aa_g1", [C]); aa_b1g = din("aa_b1g", [C])
    aa_g2 = din("aa_g2", [C]); aa_b2g = din("aa_b2g", [C])
    aa_w1 = din("aa_w1", [C, FF_A]); aa_b1 = din("aa_b1", [FF_A])
    aa_w2 = din("aa_w2", [FF_A, C]); aa_b2 = din("aa_b2", [C])
    ab_wq = din("ab_wq", [C, C]); ab_wk = din("ab_wk", [C, C])
    ab_wv = din("ab_wv", [C, C]); ab_wo = din("ab_wo", [C, C])
    ab_bo = din("ab_bo", [C])
    f_rw = din("f_rw", [NEXP, C])
    f_eg = din("f_eg", [NEXP, C, FF_E]); f_eu = din("f_eu", [NEXP, C, FF_E])
    f_ed = din("f_ed", [NEXP, FF_E, C])
    f_sg = din("f_sg", [C, FF_E]); f_su = din("f_su", [C, FF_E])
    f_sd = din("f_sd", [FF_E, C])
    id_f = din("id_f", [128, 128])
    id_b = din("id_b", [128, 128], BF16)
    hmaskin = din("hmask", [128, 2 * NH * BUCKET])
    iota256 = din("iota256", [128, NTILES * NEXP])
    iota32 = din("iota32", [128, NTILES])
    ltri = din("ltri", [128, 128])

    out = nc.dram_tensor("out", [T, C], F32, kind="ExternalOutput").ap()

    def scr(name, shape, dt=F32):
        return nc.dram_tensor(name, shape, dt, kind="Internal").ap()

    ys = scr("ys", [T, C])
    y2s = scr("y2s", [T, C])
    r1 = scr("r1", [T, C])
    r2 = scr("r2", [T, C])
    r3 = scr("r3", [T, C])
    xTs = scr("xTs", [C, T], BF16)
    CAP = 1024
    NSLOT = NEXP * CAP + NTILES
    g1s = scr("g1s", [NSLOT, C], BF16)
    g1eo = scr("g1eo", [NSLOT, C], BF16)
    g2s = scr("g2s", [NSLOT, C], BF16)
    g2eo = scr("g2eo", [NSLOT, C], BF16)
    sh1s = scr("sh1s", [T, C], BF16)
    sh2s = scr("sh2s", [T, C], BF16)
    qTs = scr("qTs", [C, T], BF16)
    oTs = scr("oTs", [C, T], BF16)

    ar = Arena(nc, es, p)

    PSA = es.enter_context(nc.psum_tensor("psa", [128, 2048], F32))[:]
    PSB = es.enter_context(nc.psum_tensor("psb", [128, 1024], F32))[:]
    PSC = es.enter_context(nc.psum_tensor("psc", [128, 512], F32))[:]
    PST = es.enter_context(nc.psum_tensor("pst", [128, 1024], BF16))[:]
    PSAK = ["psa0", "psa1", "psa2", "psa3"]
    PSBK = ["psb0", "psb1"]

    def dma(eng, dst, src, reads, writes, tag=""):
        p.op(eng, lambda e, d=dst, s=src: e.dma_start(out=d, in_=s),
             reads=reads, writes=writes, dma=True, tag=tag)

    sb_idf = ar.alloc(128, F32)
    sb_idb = ar.alloc(128, BF16)
    sb_ones = ar.alloc(128, F32)
    sb_onesb = ar.alloc(128, BF16)
    sb_hmask = ar.alloc(2 * NH * BUCKET, BF16)
    sb_iota256 = ar.alloc(NTILES * NEXP, F32)
    sb_iota32 = ar.alloc(NTILES, F32)
    rwT = ar.alloc(8 * KC, F32)
    sb_ltri = ar.alloc(128, F32)
    sb_onesf = ar.alloc(4, F32)
    sb_zt = ar.alloc(1024, BF16)
    sb_onesr = ar.alloc(128, BF16)
    ROWBUF = ar.alloc(1024, F32)
    STG = ar.alloc(128, F32)
    STT = ar.alloc(64, F32)

    dma("SP", sb_idf[:, :], id_f, ["H:idf"], ["idf"], "idf")
    dma("SP", sb_idb[:, :], id_b, ["H:idb"], ["idb"], "idb")
    dma("SP", sb_iota256[:, :], iota256, ["H:iota256"], ["iota256"], "iota256")
    dma("SP", sb_iota32[:, :], iota32, ["H:iota32"], ["iota32"], "iota32")
    dma("POOL", sb_hmask[:, :], hmaskin, ["H:hmask"], ["hmask"], "hmask")
    dma("SP", sb_ltri[:, :], ltri, ["H:ltri"], ["ltri"], "ltri")
    p.op("DVE", lambda e: e.memset(sb_onesf[:, 0:1], 1.0), writes=["onesf"], tag="onesf")
    p.op("DVE", lambda e: e.memset(sb_zt[:, :], 0.0), writes=["zt"], tag="zt")
    p.op("DVE", lambda e: e.memset(sb_ones[0:1, :], 1.0), writes=["ones"], tag="ones")
    p.op("DVE", lambda e: e.memset(sb_onesb[:, 0:1], 1.0), writes=["onesb"], tag="onesb")
    p.op("DVE", lambda e: e.memset(sb_onesr[0:1, :], 1.0), writes=["onesr"], tag="onesr")
    KCOL = ar.alloc(8, F32)
    _kvals = [EPS, -0.5, SCALE_A, SCALE_B, -1.0]
    for _i, _v in enumerate(_kvals):
        p.op("DVE", lambda e, _i=_i, _v=_v: e.memset(KCOL[:, _i:_i + 1], _v),
             writes=["Kc"], tag=f"kc{_i}")
        nc.const_aps.register(F32, _v, KCOL[:, _i:_i + 1])

    GZP = {}

    def prep_gz(tag, G, Geo):
        ops = []
        nzi = (NSLOT + 127) // 128
        for zi in range(nzi):
            r0 = zi * 128
            nr = min(128, NSLOT - r0)
            ops.append((G[r0:r0 + nr, :], sb_zt[0:nr, :], f"{tag}gz{zi}",
                        ["Ginit"]))
        ops.append((Geo[NEXP * CAP:NSLOT, :], sb_zt[0:NTILES, :], f"{tag}ez0",
                    ["Geoinit"]))
        GZP[tag] = ops

    def emit_gz(tag, n, queue):
        ops = GZP.get(tag)
        if not ops:
            return
        for _ in range(min(n, len(ops))):
            dst, src, tg, wk = ops.pop(0)
            dma(queue, dst, src, ["zt"], wk, tg)

    prep_gz("f1", g1s, g1eo)
    prep_gz("f2", g2s, g2eo)

    for k in range(KC):
        dma("SP", STG[0:8, 0:128], f_rw[:, k * 128:(k + 1) * 128], ["H:rw"],
            ["stg"], f"ld_rw{k}")
        p.op("PE", lambda e: e.transpose(PSB[0:128, 0:8], STG[0:8, 0:128],
                                         sb_idf[0:8, 0:8]),
             reads=["stg", "idf"], writes=["psb0"], tag=f"tr_rw{k}")
        p.op("DVE", lambda e, k=k: e.tensor_copy(rwT[:, k * 8:(k + 1) * 8], PSB[:, 0:8]),
             reads=["psb0"], writes=["C:rwT", "stg"], tag=f"ev_rw{k}")

    def mk_bcast(dst, src_row_ap, n, tag):
        row = ROWBUF[0:1, 0:n]
        dma("SP", row, src_row_ap, [f"H:{tag}"], ["rowbuf"], f"ld_{tag}")
        nch = (n + 511) // 512

        def mm(e):
            inst = None
            for i in range(nch):
                s = i * 512
                w = min(512, n - s)
                inst = e.matmul(PSA[:, s:s + w], sb_ones[0:1, :], row[:, s:s + w],
                                start=True, stop=True)
            return inst
        p.op("PE", mm, reads=["ones", "rowbuf"],
             writes=[PSAK[i] for i in range(nch)], tag=f"bc_{tag}")
        p.op("DVE", lambda e: e.tensor_copy(dst[:, 0:n], PSA[:, 0:n]),
             reads=[PSAK[i] for i in range(nch)],
             writes=[f"C:{tag}", "rowbuf"], tag=f"ev_{tag}")

    def mk_cols(dst, src_vec, ntile, tag):
        stg = STG[0:ntile, 0:128]
        dma("SP", stg, src_vec.rearrange("(t p) -> t p", p=128),
            [f"H:{tag}"], ["stg"], f"ld_{tag}")
        p.op("PE", lambda e, stg=stg, ntile=ntile: e.transpose(
            PSA[:, 0:ntile], stg, sb_idf[0:ntile, 0:ntile]),
            reads=["stg", "idf"], writes=["psa0"], tag=f"tr_{tag}")
        p.op("DVE", lambda e, ntile=ntile: e.tensor_copy(dst[:, 0:ntile],
                                                         PSA[:, 0:ntile]),
             reads=["psa0"], writes=[f"C:{tag}", "stg"], tag=f"ev_{tag}")

    PK = {}

    def alloc_pack():
        PK["T"] = [ar.alloc(1024, F32) for _ in range(2)]
        PK["T2"] = [ar.alloc(1024, F32) for _ in range(2)]
        PK["Y"] = [ar.alloc(1024, F32) for _ in range(2)]
        PK["H"] = [ar.alloc(1024, BF16) for _ in range(2)]
        PK["SQ"] = [ar.alloc(1024, F32) for _ in range(2)]

    _ln_ctr = [0]

    def ln_core(src_ap, src_keys, y_ap, y_key, kt):
        st = STT
        sc = _ln_ctr[0] % 6
        _ln_ctr[0] += 1
        c0 = sc * 8
        sq = PK["SQ"][sc % 2]
        p.op("ACT", lambda e: e.activation(sq[:, :], src_ap, AF.Square,
                                           accum_out=st[:, c0:c0 + 1]),
             reads=src_keys, writes=[f"sqd{sc % 2}", f"st{c0}"], tag=f"{kt}_sq")
        p.op("DVE", lambda e: e.reduce_sum(st[:, c0 + 1:c0 + 2], src_ap, axis=AX),
             reads=src_keys, writes=[f"st{c0}b"], tag=f"{kt}_sm")
        p.op("DVE", lambda e: e.tensor_scalar(st[:, c0 + 2:c0 + 3],
                                              st[:, c0 + 1:c0 + 2],
                                              1.0 / C, None, ALU.mult),
             reads=[f"st{c0}b"], writes=[f"st{c0}c"], tag=f"{kt}_mu")
        p.op("DVE", lambda e: e.tensor_scalar(st[:, c0 + 3:c0 + 4], st[:, c0:c0 + 1],
                                              1.0 / C, None, ALU.mult),
             reads=[f"st{c0}"], writes=[f"st{c0}d"], tag=f"{kt}_e2")
        p.op("DVE", lambda e: e.tensor_mul(st[:, c0 + 4:c0 + 5], st[:, c0 + 2:c0 + 3],
                                           st[:, c0 + 2:c0 + 3]),
             reads=[f"st{c0}c"], writes=[f"st{c0}e"], tag=f"{kt}_m2")
        p.op("DVE", lambda e: e.tensor_sub(st[:, c0 + 5:c0 + 6], st[:, c0 + 3:c0 + 4],
                                           st[:, c0 + 4:c0 + 5]),
             reads=[f"st{c0}d", f"st{c0}e"], writes=[f"st{c0}f"], tag=f"{kt}_va")
        p.op("ACT", lambda e: e.activation(st[:, c0 + 6:c0 + 7], st[:, c0 + 5:c0 + 6],
                                           AF.Ln, bias=EPS),
             reads=[f"st{c0}f", "Kc"], writes=[f"st{c0}g"], tag=f"{kt}_ln")
        p.op("ACT", lambda e: e.activation(st[:, c0 + 7:c0 + 8], st[:, c0 + 6:c0 + 7],
                                           AF.Exp, scale=-0.5),
             reads=[f"st{c0}g", "Kc"], writes=[f"st{c0}h"], tag=f"{kt}_rs")
        p.op("DVE", lambda e: e.tensor_scalar(y_ap, src_ap, st[:, c0 + 2:c0 + 3],
                                              st[:, c0 + 7:c0 + 8],
                                              ALU.subtract, ALU.mult),
             reads=list(src_keys) + [f"st{c0}c", f"st{c0}h"], writes=[y_key],
             tag=f"{kt}_ap")

    def transpose8(src_ap, src_keys, kt):
        def mm(e):
            inst = None
            for k in range(KC):
                inst = e.transpose(PST[:, k * 128:(k + 1) * 128],
                                   src_ap[:, k * 128:(k + 1) * 128], sb_idb[:, :])
            return inst
        p.op("PE", mm, reads=list(src_keys) + ["idb"], writes=["pst"], tag=kt)

    def wchunks(slot, skey, w_ap, kdim, n, tag):
        kch = kdim // 128
        step = max(1, min(kch, (2 * 1024 * 1024) // (n * 512)))
        for k0 in range(0, kch, step):
            k1 = min(k0 + step, kch)
            dma("POOL",
                slot[:, :].rearrange("p (k n) -> p k n", k=kch)[:, k0:k1, :],
                w_ap.rearrange("(k p) n -> p k n", p=128)[:, k0:k1, :],
                [f"H:{tag}"], [skey], f"{tag}_{k0}")

    if "aa" in stages:
        ar.push()
        alloc_pack()
        cb_g1 = ar.alloc(C, F32)
        cb_b1g = ar.alloc(C, F32)
        cb_bqv = ar.alloc(C, F32)
        cb_bout = ar.alloc(C, F32)
        col_bqkv = ar.alloc(24, F32)
        mk_bcast(cb_g1, aa_g1.rearrange("(o n) -> o n", o=1), C, "g1")
        mk_bcast(cb_b1g, aa_b1g.rearrange("(o n) -> o n", o=1), C, "b1g")
        mk_bcast(cb_bqv, aa_bqkv.rearrange("(o n) -> o n", o=1)[:, 2 * C:3 * C], C, "bqv")
        mk_bcast(cb_bout, aa_bout.rearrange("(o n) -> o n", o=1), C, "bout")
        mk_cols(col_bqkv, aa_bqkv, 24, "bqkv")

        wqkv_sb = ar.alloc(KC * 3 * C, BF16)
        wout_sb = ar.alloc(KC * C, BF16)
        hTc = ar.alloc(KC * 512, BF16)
        HGRP = 512
        NG = T // HGRP
        GT = HGRP // 128
        GB = HGRP // BUCKET
        GPB = SEQ // HGRP
        qT_g = ar.alloc(KC * HGRP, BF16)
        kT_g = ar.alloc(KC * HGRP, BF16)
        kT_c = ar.alloc(KC * BUCKET, BF16)
        v65_g = [ar.alloc(V65W, BF16) for _ in range(GT)]
        v65_c = ar.alloc(V65W, BF16)
        zv65 = ar.alloc(V65W, BF16)
        zkT = ar.alloc(64, BF16)
        oT_g = ar.alloc(KC * HGRP, BF16)
        orw = [ar.alloc(512, F32) for _ in range(2)]
        rrow = [ar.alloc(1024, BF16) for _ in range(3)]
        lna = ar.alloc(2048, F32)
        expb = [ar.alloc(NH * BUCKET, BF16) for _ in range(2)]

        wchunks(wqkv_sb, "W:qkv", aa_wqkv, C, 3 * C, "aaqkv")
        wchunks(wout_sb, "W:aout", aa_wout, C, C, "aaout")
        p.op("DVE", lambda e: e.memset(zv65[:, :], 0.0), writes=["zv65"], tag="zv0")
        p.op("DVE", lambda e: e.memset(
            zv65[:, :].rearrange("p (h w) -> p h w", h=NH)[:, :, HD:HD + 1], 1.0),
            reads=["zv65"], writes=["zv65"], tag="zv1")
        p.op("DVE", lambda e: e.memset(zkT[:, :], 0.0), writes=["zkT"], tag="zk")

        for g in range(NG):
            gk = f"g{g}"
            for i in range(GT):
                ti = g * GT + i
                b = i % 2
                t, t2, y, h = PK["T"][b], PK["T2"][b], PK["Y"][b], PK["H"][b]
                kt = f"aa{ti}"
                dma("SP", t[:, :], xin[ti * 128:(ti + 1) * 128, :],
                    [f"H:xin:{ti}"], [f"tb{b}"], kt)
                ln_core(t[:, :], [f"tb{b}"], y[:, :], f"yb{b}", kt)
                dma("ACT", ys[ti * 128:(ti + 1) * 128, :], y[:, :],
                    [f"yb{b}"], [f"ys:{ti}"], f"{kt}_sy")
                ln_core(y[:, :], [f"yb{b}"], t2[:, :], f"t2b{b}", kt + "i")
                p.op("DVE", lambda e, t2=t2: e.tensor_mul(t2[:, :], t2[:, :],
                                                          cb_g1[:, :]),
                     reads=[f"t2b{b}", "C:g1"], writes=[f"t2b{b}"], tag=f"{kt}_g1")
                p.op("DVE", lambda e, t2=t2, h=h: e.tensor_add(h[:, :], t2[:, :],
                                                               cb_b1g[:, :]),
                     reads=[f"t2b{b}", "C:b1g"], writes=[f"hb{b}"], tag=f"{kt}_b1")
                transpose8(h[:, :], [f"hb{b}"], f"{kt}_tr")
                p.op("DVE", lambda e, i=i: e.tensor_copy(
                    hTc[:, :].rearrange("p (k n) -> p k n", k=KC)[:, :, i * 128:(i + 1) * 128],
                    PST[:, :].rearrange("p (k n) -> p k n", k=KC)),
                    reads=["pst"], writes=[f"hTc:{i}"], tag=f"{kt}_ev")
            hck = [f"hTc:{i}" for i in range(GT)]
            for m in range(16):
                bank = PSAK[m % 4]
                dst = qT_g if m < 8 else kT_g
                dk = ("qT" if m < 8 else "kT") + f"{gk}:{m % 8}"

                def qkmm(e, m=m):
                    inst = None
                    for k in range(KC):
                        inst = e.matmul(
                            PSA[:, (m % 4) * 512:(m % 4) * 512 + 512],
                            wqkv_sb[:, k * 3 * C + m * 128:k * 3 * C + (m + 1) * 128],
                            hTc[:, k * 512:(k + 1) * 512],
                            start=(k == 0), stop=(k == KC - 1))
                    return inst
                p.op("PE", qkmm, reads=hck + ["W:qkv"], writes=[bank], tag=f"{gk}qk{m}")
                p.op("ACT", lambda e, m=m, dst=dst: e.activation(
                    dst[:, (m % 8) * HGRP:(m % 8) * HGRP + 512],
                    PSA[:, (m % 4) * 512:(m % 4) * 512 + 512],
                    AF.Identity, bias=col_bqkv[:, m:m + 1]),
                    reads=[bank, "C:bqkv"], writes=[dk], tag=f"{gk}qke{m}")
            for i in range(GT):
                def vmm(e, i=i):
                    inst = None
                    for k in range(KC):
                        for nh_ in range(2):
                            inst = e.matmul(
                                PSB[:, nh_ * 512:(nh_ + 1) * 512],
                                hTc[:, k * 512 + i * 128:k * 512 + (i + 1) * 128],
                                wqkv_sb[:, k * 3 * C + 2 * C + nh_ * 512:
                                        k * 3 * C + 2 * C + (nh_ + 1) * 512],
                                start=(k == 0), stop=(k == KC - 1))
                    return inst
                p.op("PE", vmm, reads=hck + ["W:qkv"], writes=PSBK, tag=f"{gk}v{i}")
                p.op("DVE", lambda e, i=i: e.memset(
                    v65_g[i][:, :].rearrange("p (h w) -> p h w", h=NH)[:, :, HD:HD + 1],
                    1.0), writes=[f"v65:{i}"], tag=f"{gk}vo{i}")
                p.op("DVE", lambda e, i=i: e.scalar_tensor_tensor(
                    v65_g[i][:, :].rearrange("p (h w) -> p h w", h=NH)[:, :, 0:HD],
                    PSB[:, :].rearrange("p (h w) -> p h w", h=NH),
                    1.0, cb_bqv[:, :].rearrange("p (h w) -> p h w", h=NH),
                    ALU.mult, ALU.add),
                    reads=PSBK + ["C:bqv", f"v65:{i}"],
                    writes=[f"v65:{i}"], tag=f"{gk}ve{i}")
            qkkeys = [f"qT{gk}:{m}" for m in range(8)] + \
                     [f"kT{gk}:{m}" for m in range(8)]
            vkeys = [f"v65:{i}" for i in range(GT)]
            def hcol(h):
                return (h // 2) * 64 + (h % 2) * 512

            def SS(b):
                bi = b
                first = (g % GPB == 0) and bi == 0
                bk = f"{gk}b{bi}"
                pc = (bi % 2) * 64
                pp = 64 - pc

                def smm(e, bi=bi, first=first, pc=pc, pp=pp):
                    inst = None
                    for h in [0, 2, 4, 6, 8, 10, 12, 14, 1, 3, 5, 7, 9, 11, 13, 15]:
                        sI = h // 2
                        r = (h % 2) * 64
                        qsl = qT_g[r:r + 64, sI * HGRP + bi * 64:sI * HGRP + (bi + 1) * 64]
                        if first:
                            kp = zkT[r:r + 64, 0:64]
                        elif bi == 0:
                            kp = kT_c[r:r + 64, sI * BUCKET:(sI + 1) * BUCKET]
                        else:
                            kp = kT_g[r:r + 64, sI * HGRP + (bi - 1) * 64:sI * HGRP + bi * 64]
                        kc = kT_g[r:r + 64, sI * HGRP + bi * 64:sI * HGRP + (bi + 1) * 64]
                        inst = e.matmul(PSB[pp:pp + 64, hcol(h):hcol(h) + 64], kp, qsl,
                                        start=True, stop=True)
                        inst = e.matmul(PSB[pc:pc + 64, hcol(h):hcol(h) + 64], kc, qsl,
                                        start=True, stop=True)
                    return inst
                p.op("PE", smm, reads=qkkeys + ["zkT", "kTc"], writes=PSBK,
                     tag=f"{bk}_s")
                p.op("ACT", lambda e, b=b: e.activation(expb[b % 2][:, :], PSB[:, :],
                                                        AF.Exp, scale=SCALE_A),
                     reads=PSBK + ["Kc"], writes=[f"expb{b % 2}"], tag=f"{bk}_exp")
                p.op("DVE", lambda e, bi=bi, b=b: e.tensor_mul(
                    expb[b % 2][:, :], expb[b % 2][:, :],
                    sb_hmask[:, (bi % 2) * NH * BUCKET:(bi % 2 + 1) * NH * BUCKET]),
                    reads=[f"expb{b % 2}", "hmask"], writes=[f"expb{b % 2}"],
                    tag=f"{bk}_msk")

                def dmm(e, b=b):
                    e.matmul(PSB[0:1, 0:512], sb_onesb[:, 0:1], expb[b % 2][:, 0:512],
                             start=True, stop=True)
                    return e.matmul(PSB[0:1, 512:1024], sb_onesb[:, 0:1],
                                    expb[b % 2][:, 512:1024], start=True, stop=True)
                p.op("PE", dmm, reads=[f"expb{b % 2}", "onesb"], writes=PSBK,
                     tag=f"{bk}_d")
                p.op("DVE", lambda e: e.tensor_copy(lna[0:1, 0:1024], PSB[0:1, :]),
                     reads=PSBK, writes=["lnA"], tag=f"{bk}_rc0")
                p.op("ACT", lambda e: e.activation(lna[0:1, 1024:2048],
                                                   lna[0:1, 0:1024], AF.Ln),
                     reads=["lnA"], writes=["lnA2"], tag=f"{bk}_rc1")

                def rcf(e, b=b):
                    with nc.allow_low_precision(
                            reason="recip feeds a bf16-operand matmul; rounding "
                                   "here equals the matmul's own operand cast"):
                        return e.activation(rrow[b % 3][0:1, :],
                                            lna[0:1, 1024:2048], AF.Exp,
                                            scale=-1.0)
                p.op("ACT", rcf, reads=["lnA2", "Kc"], writes=[f"rrow{b % 3}"],
                     tag=f"{bk}_rc")

            def OM(b):
                bi = b
                first = (g % GPB == 0) and bi == 0
                bk = f"{gk}b{bi}"
                pc = (bi % 2) * 64
                pp = 64 - pc
                bko = (b % 2) * 2

                def omm(e, bi=bi, first=first, pc=pc, pp=pp, b=b, bko=bko):
                    inst = None
                    for grp in range(2):
                        for h in range(NH):
                            sI = h // 2
                            ro = (h % 2) * 64
                            co = (bko + grp) * 512 + sI * 64
                            if grp == 0:
                                if first:
                                    v = zv65[pp:pp + 64, h * 65:h * 65 + 64]
                                elif bi == 0:
                                    v = v65_c[64:128, h * 65:h * 65 + 64]
                                else:
                                    v = v65_g[(bi - 1) // 2][pp:pp + 64,
                                                             h * 65:h * 65 + 64]
                                rh = expb[b % 2][pp:pp + 64, hcol(h):hcol(h) + 64]
                            else:
                                v = v65_g[bi // 2][pc:pc + 64, h * 65:h * 65 + 64]
                                rh = expb[b % 2][pc:pc + 64, hcol(h):hcol(h) + 64]
                            inst = e.matmul(PSA[ro:ro + 64, co:co + 64], v, rh,
                                            start=True, stop=True)
                    return inst
                p.op("PE", omm, reads=[f"expb{b % 2}", "zv65", "v65c"] + vkeys,
                     writes=[PSAK[bko], PSAK[bko + 1]], tag=f"{bk}_o")

            def BM(b):
                bi = b
                bk = f"{gk}b{bi}"
                bko = (b % 2) * 2

                def bmm(e, b=b):
                    e.matmul(PSB[:, 0:512], sb_onesr[0:1, :],
                             rrow[b % 3][0:1, 0:512], start=True, stop=True)
                    return e.matmul(PSB[:, 512:1024], sb_onesr[0:1, :],
                                    rrow[b % 3][0:1, 512:1024], start=True, stop=True)
                p.op("PE", bmm, reads=[f"rrow{b % 3}", "onesr"],
                     writes=PSBK, tag=f"{bk}_bc")
                p.op("ACT", lambda e, b=b, bko=bko: e.activation(
                    orw[b % 2][0:64, 0:512], PSA[0:64, bko * 512:bko * 512 + 512],
                    AF.Copy),
                    reads=[PSAK[bko]], writes=[f"orwE{b % 2}"], tag=f"{bk}_ee")
                p.op("DVE", lambda e, b=b, bko=bko: e.scalar_tensor_tensor(
                    orw[b % 2][0:64, 0:512], PSA[0:64, (bko + 1) * 512:(bko + 2) * 512],
                    1.0, orw[b % 2][0:64, 0:512], ALU.mult, ALU.add),
                    reads=[PSAK[bko + 1], f"orwE{b % 2}"], writes=[f"orwE{b % 2}"],
                    tag=f"{bk}_ee2")
                p.op("ACT", lambda e, b=b, bko=bko: e.activation(
                    orw[b % 2][64:128, 0:512], PSA[64:128, bko * 512:bko * 512 + 512],
                    AF.Copy),
                    reads=[PSAK[bko]], writes=[f"orwO{b % 2}"], tag=f"{bk}_eo")
                p.op("DVE", lambda e, b=b, bko=bko: e.scalar_tensor_tensor(
                    orw[b % 2][64:128, 0:512], PSA[64:128, (bko + 1) * 512:(bko + 2) * 512],
                    1.0, orw[b % 2][64:128, 0:512], ALU.mult, ALU.add),
                    reads=[PSAK[bko + 1], f"orwO{b % 2}"], writes=[f"orwO{b % 2}"],
                    tag=f"{bk}_eo2")
                p.op("DVE", lambda e, bi=bi, b=b: e.tensor_tensor(
                    oT_g[0:64, :].rearrange("p (s n) -> p s n", s=KC)[:, :, bi * 64:(bi + 1) * 64],
                    orw[b % 2][0:64, 0:512].rearrange("p (s n) -> p s n", s=8),
                    PSB[0:64, 0:512].rearrange("p (s n) -> p s n", s=8),
                    ALU.mult),
                    reads=[f"orwE{b % 2}"] + PSBK, writes=[f"oT{gk}"],
                    tag=f"{bk}_ne")
                p.op("DVE", lambda e, bi=bi, b=b: e.tensor_tensor(
                    oT_g[64:128, :].rearrange("p (s n) -> p s n", s=KC)[:, :, bi * 64:(bi + 1) * 64],
                    orw[b % 2][64:128, 0:512].rearrange("p (s n) -> p s n", s=8),
                    PSB[64:128, 512:1024].rearrange("p (s n) -> p s n", s=8),
                    ALU.mult),
                    reads=[f"orwO{b % 2}"] + PSBK, writes=[f"oT{gk}"],
                    tag=f"{bk}_no")

            for b in range(GB + 2):
                if b < GB:
                    SS(b)
                if 1 <= b <= GB:
                    OM(b - 1)
                if b >= 2:
                    BM(b - 2)
            p.op("DVE", lambda e: e.tensor_copy(
                kT_c[:, :].rearrange("p (s n) -> p s n", s=KC),
                kT_g[:, :].rearrange("p (s n) -> p s n", s=KC)[:, :, HGRP - 64:HGRP]),
                reads=[f"kT{gk}:{m}" for m in range(8)], writes=["kTc"],
                tag=f"{gk}_kc")
            p.op("DVE", lambda e: e.tensor_copy(v65_c[64:128, :],
                                                v65_g[GT - 1][64:128, :]),
                 reads=[f"v65:{GT - 1}"], writes=["v65c"], tag=f"{gk}_vc")
            for i in range(GT):
                ti = g * GT + i
                b = i % 2
                kt = f"{gk}z{i}"

                def zmm(e, i=i):
                    inst = None
                    for k in range(KC):
                        for nh_ in range(2):
                            inst = e.matmul(
                                PSB[:, nh_ * 512:(nh_ + 1) * 512],
                                oT_g[:, k * HGRP + i * 128:k * HGRP + (i + 1) * 128],
                                wout_sb[:, k * C + nh_ * 512:k * C + (nh_ + 1) * 512],
                                start=(k == 0), stop=(k == KC - 1))
                    return inst
                p.op("PE", zmm, reads=[f"oT{gk}", "W:aout"], writes=PSBK, tag=kt)
                tb_, t2b_ = PK["T"][b], PK["T2"][b]
                dma("SP", tb_[:, :], ys[ti * 128:(ti + 1) * 128, :],
                    [f"ys:{ti}"], [f"tb{b}"], f"{kt}_ly")
                p.op("DVE", lambda e, t2b_=t2b_: e.scalar_tensor_tensor(
                    t2b_[:, :], PSB[:, :], 1.0, cb_bout[:, :],
                    ALU.mult, ALU.add),
                    reads=PSBK + ["C:bout"], writes=[f"t2b{b}"], tag=f"{kt}_b")
                p.op("DVE", lambda e, t2b_=t2b_, tb_=tb_: e.tensor_add(
                    t2b_[:, :], t2b_[:, :], tb_[:, :]),
                    reads=[f"t2b{b}", f"tb{b}"], writes=[f"t2b{b}"], tag=f"{kt}_y")
                dma("ACT", y2s[ti * 128:(ti + 1) * 128, :], t2b_[:, :],
                    [f"t2b{b}"], [f"y2s:{ti}"], f"{kt}_st")
        ar.pop()

        ar.push()
        alloc_pack()
        cb_g2 = ar.alloc(C, F32)
        cb_b2g = ar.alloc(C, F32)
        cb_b2 = ar.alloc(C, F32)
        col_b1 = ar.alloc(32, F32)
        mk_bcast(cb_g2, aa_g2.rearrange("(o n) -> o n", o=1), C, "g2")
        mk_bcast(cb_b2g, aa_b2g.rearrange("(o n) -> o n", o=1), C, "b2g")
        mk_bcast(cb_b2, aa_b2.rearrange("(o n) -> o n", o=1), C, "b2")
        mk_cols(col_b1, aa_b1, 32, "b1")
        w2_sb = ar.alloc((FF_A // 128) * C, BF16)
        w1p = [ar.alloc(KC * 128, BF16) for _ in range(4)]
        h2T = [ar.alloc(KC * 512, BF16) for _ in range(2)]
        m1T = ar.alloc((FF_A // 128) * 512, BF16)
        wchunks(w2_sb, "W:w2", aa_w2, FF_A, C, "aaw2")
        NF = FF_A // 128

        def a4_ln(ch):
            r = ch % 2
            for i in range(4):
                ti = ch * 4 + i
                b = i % 2
                t, t2, h = PK["T"][b], PK["T2"][b], PK["H"][b]
                kt = f"a4{ti}"
                dma("SP", t[:, :], y2s[ti * 128:(ti + 1) * 128, :],
                    [f"y2s:{ti}"], [f"tb{b}"], kt)
                ln_core(t[:, :], [f"tb{b}"], t2[:, :], f"t2b{b}", kt)
                p.op("DVE", lambda e, t2=t2: e.tensor_mul(t2[:, :], t2[:, :],
                                                          cb_g2[:, :]),
                     reads=[f"t2b{b}", "C:g2"], writes=[f"t2b{b}"], tag=f"{kt}_g2")
                p.op("DVE", lambda e, t2=t2, h=h: e.tensor_add(h[:, :], t2[:, :],
                                                               cb_b2g[:, :]),
                     reads=[f"t2b{b}", "C:b2g"], writes=[f"hb{b}"], tag=f"{kt}_b2")
                transpose8(h[:, :], [f"hb{b}"], f"{kt}_tr")
                p.op("DVE", lambda e, i=i, r=r: e.tensor_copy(
                    h2T[r][:, :].rearrange("p (k n) -> p k n", k=KC)[:, :, i * 128:(i + 1) * 128],
                    PST[:, :].rearrange("p (k n) -> p k n", k=KC)),
                    reads=["pst"], writes=[f"h2T{r}:{i}"], tag=f"{kt}_ev")

        def a4_m1(ch):
            ck = f"a4c{ch}"
            r = ch % 2
            hk = [f"h2T{r}:{i}" for i in range(4)]
            for f in range(NF):
                wb = w1p[f % 4]
                wk = f"w1p{f % 4}"
                dma("POOL", wb[:, :].rearrange("p (k n) -> p k n", k=KC),
                    aa_w1.rearrange("(k p) n -> p k n", p=128)[:, :, f * 128:(f + 1) * 128],
                    ["H:w1"], [wk], f"{ck}w1_{f}")
                bank = PSAK[f % 4]

                def m1mm(e, f=f, wb=wb, r=r):
                    inst = None
                    for k in range(KC):
                        inst = e.matmul(PSA[:, (f % 4) * 512:(f % 4) * 512 + 512],
                                        wb[:, k * 128:(k + 1) * 128],
                                        h2T[r][:, k * 512:(k + 1) * 512],
                                        start=(k == 0), stop=(k == KC - 1))
                    return inst
                p.op("PE", m1mm, reads=hk + [wk], writes=[bank], tag=f"{ck}m1_{f}")
                p.op("ACT", lambda e, f=f: e.activation(
                    m1T[:, f * 512:(f + 1) * 512],
                    PSA[:, (f % 4) * 512:(f % 4) * 512 + 512],
                    AF.Gelu, bias=col_b1[:, f:f + 1]),
                    reads=[bank, "C:b1"], writes=[f"m1T:{f}"], tag=f"{ck}ge_{f}")

        def a4_m2(ch):
            ck = f"a4c{ch}"
            m1k = [f"m1T:{f}" for f in range(NF)]
            for i in range(4):
                ti = ch * 4 + i
                b = i % 2

                def m2mm(e, i=i):
                    inst = None
                    for k in range(NF):
                        for nh_ in range(2):
                            inst = e.matmul(
                                PSB[:, nh_ * 512:(nh_ + 1) * 512],
                                m1T[:, k * 512 + i * 128:k * 512 + (i + 1) * 128],
                                w2_sb[:, k * C + nh_ * 512:k * C + (nh_ + 1) * 512],
                                start=(k == 0), stop=(k == NF - 1))
                    return inst
                kt = f"{ck}m2_{i}"
                p.op("PE", m2mm, reads=m1k + ["W:w2"], writes=PSBK, tag=kt)
                tb_, t2b_, yb_ = PK["T"][b], PK["T2"][b], PK["Y"][b]
                dma("SP", tb_[:, :], y2s[ti * 128:(ti + 1) * 128, :],
                    [f"y2s:{ti}"], [f"tb{b}"], f"{kt}_l2")
                dma("SP", yb_[:, :], xin[ti * 128:(ti + 1) * 128, :],
                    [f"H:xin:{ti}"], [f"yb{b}"], f"{kt}_l0")
                p.op("DVE", lambda e, t2b_=t2b_: e.scalar_tensor_tensor(
                    t2b_[:, :], PSB[:, :], 1.0, cb_b2[:, :],
                    ALU.mult, ALU.add),
                    reads=PSBK + ["C:b2"], writes=[f"t2b{b}"], tag=f"{kt}_b")
                p.op("DVE", lambda e, t2b_=t2b_, tb_=tb_: e.tensor_add(
                    t2b_[:, :], t2b_[:, :], tb_[:, :]),
                    reads=[f"t2b{b}", f"tb{b}"], writes=[f"t2b{b}"], tag=f"{kt}_y2")
                p.op("DVE", lambda e, t2b_=t2b_, yb_=yb_: e.tensor_add(
                    t2b_[:, :], t2b_[:, :], yb_[:, :]),
                    reads=[f"t2b{b}", f"yb{b}"], writes=[f"t2b{b}"], tag=f"{kt}_x0")
                dma("ACT", r1[ti * 128:(ti + 1) * 128, :], t2b_[:, :],
                    [f"t2b{b}"], [f"r1:{ti}"], f"{kt}_st")

        NCH = T // 512
        a4_ln(0)
        for ch in range(NCH):
            if "f1" in stages:
                emit_gz("f1", 9, "ACT")
            if ch + 1 < NCH:
                a4_ln(ch + 1)
            if ch > 0:
                a4_m2(ch - 1)
            a4_m1(ch)
        a4_m2(NCH - 1)
        ar.pop()

    def ffn(tag, rin, rin_name, rout, rout_name, G, Geo, SHS):
        ar.push()
        emit_gz(tag, 10 ** 9, "ACT")
        fmask = ar.alloc(NTILES * NEXP, F32)
        rankw = ar.alloc(NTILES * NEXP, F32)
        desti = ar.alloc(NTILES, mybir.dt.int32)
        cntrow = ar.alloc(NTILES * NEXP, F32)
        baserow = ar.alloc(NTILES * NEXP, F32)
        cmat = ar.alloc(NEXP, F32)
        lgtall = ar.alloc(NTILES * NEXP, F32)
        mb0 = ar.alloc(NTILES * NEXP, F32)
        mb8 = ar.alloc(NTILES * NEXP, F32)
        mxsc = ar.alloc(2 * NTILES, F32)
        dtmp = ar.alloc(8 * NTILES, F32)
        PSCK2 = ["psc0", "psc1"]
        sl01 = [ar.alloc(16 * 1024, BF16) for _ in range(2)]
        wchunks(sl01[0], "slot0", f_sg, C, FF_E, f"{tag}shg")
        wchunks(sl01[1], "slot1", f_su, C, FF_E, f"{tag}shu")
        ar.push()
        xsall = ar.alloc(NTILES * C, BF16)
        alloc_pack()
        xstg = [ar.alloc(1024, BF16) for _ in range(2)]
        xtf = [ar.alloc(1024, F32) for _ in range(2)]
        for ti in range(NTILES):
            b = ti % 2
            t, y, h = PK["T"][b], PK["Y"][b], PK["H"][b]
            kt = f"{tag}L{ti}"
            xt = xtf[b]
            po = 1024 * b
            pk2 = [f"psa{2 * b}", f"psa{2 * b + 1}"]
            dma("SP", t[:, :], rin[ti * 128:(ti + 1) * 128, :],
                [f"{rin_name}:{ti}"], [f"tb{b}"], kt)
            ln_core(t[:, :], [f"tb{b}"], y[:, :], f"yb{b}", kt)
            p.op("PE", lambda e, y=y, po=po: [e.transpose(
                PSA[:, po + k * 128:po + (k + 1) * 128],
                y[:, k * 128:(k + 1) * 128], sb_idf[:, :]) for k in range(KC)][-1],
                reads=[f"yb{b}", "idf"], writes=pk2, tag=f"{kt}_ftr")
            p.op("DVE", lambda e, xt=xt, po=po: e.tensor_copy(
                xt[:, :], PSA[:, po:po + 1024]),
                reads=pk2, writes=[f"xtf{b}"], tag=f"{kt}_fev")

            def rmm(e, xt=xt, b=b):
                inst = None
                for k in range(KC):
                    inst = e.matmul(PSC[:, b * 8:b * 8 + 8],
                                    xt[:, k * 128:(k + 1) * 128],
                                    rwT[:, k * 8:(k + 1) * 8],
                                    start=(k == 0), stop=(k == KC - 1))
                return inst
            p.op("PE", rmm, reads=[f"xtf{b}", "C:rwT"], writes=[f"psc{b}"],
                 tag=f"{kt}_rt")
            p.op("DVE", lambda e, ti=ti, b=b: e.tensor_copy(
                lgtall[:, ti * 8:(ti + 1) * 8], PSC[:, b * 8:b * 8 + 8]),
                reads=[f"psc{b}"], writes=["lga"], tag=f"{kt}_lg")
            p.op("ACT", lambda e, y=y, ti=ti: e.activation(
                xsall[:, ti * C:(ti + 1) * C], y[:, :], AF.Copy),
                reads=[f"yb{b}"], writes=[f"xsa:{ti}"], tag=f"{kt}_cx")
            p.op("ACT", lambda e, y=y, h=h: e.activation(h[:, :], y[:, :], AF.Copy),
                 reads=[f"yb{b}"], writes=[f"hb{b}"], tag=f"{kt}_cy")
            transpose8(h[:, :], [f"hb{b}"], f"{kt}_t1")
            p.op("DVE", lambda e, b=b: e.tensor_copy(xstg[b][:, :], PST[:, :]),
                 reads=["pst"], writes=[f"xstg{b}"], tag=f"{kt}_e1")
            dma("ACT", xTs.rearrange("(k p) n -> p k n", p=128)[:, :, ti * 128:(ti + 1) * 128],
                xstg[b][:, :].rearrange("p (k n) -> p k n", k=KC),
                [f"xstg{b}"], [f"xTs:{ti}"], f"{kt}_s1")
        lg3 = lgtall[:, 0:NTILES * NEXP].rearrange("p (t e) -> p t e", e=NEXP)
        mx3 = mxsc[:, 0:NTILES].rearrange("p (t o) -> p t o", o=1)
        p.op("DVE", lambda e: e.reduce_max(mx3, lg3, axis=AX),
             reads=["lga"], writes=["mxc"], tag=f"{tag}mx")
        p.op("ACT", lambda e: e.activation(mxsc[:, NTILES:2 * NTILES],
                                           mxsc[:, 0:NTILES], AF.Sigmoid),
             reads=["mxc"], writes=["scal"], tag=f"{tag}sg")
        for ti in range(NTILES):
            g0, g1 = ti * NEXP, (ti + 1) * NEXP
            p.op("DVE", lambda e, g0=g0, g1=g1, ti=ti: e.tensor_single_scalar(
                mb0[:, g0:g1], lgtall[:, g0:g1], mxsc[:, ti:ti + 1], ALU.is_equal),
                reads=["lga", "mxc"], writes=["m0"], tag=f"{tag}eq{ti}")
            p.op("DVE", lambda e, g0=g0, g1=g1: e.tensor_tensor_scan(
                mb8[:, g0:g1], mb0[:, g0:g1], mb0[:, g0:g1], 0.0,
                ALU.add, ALU.bypass),
                reads=["m0"], writes=["m8"], tag=f"{tag}cs{ti}")
        p.op("DVE", lambda e: e.tensor_scalar(mb8[:, :], mb8[:, :], 1.0, None,
                                              ALU.is_equal),
             reads=["m8"], writes=["m8"], tag=f"{tag}c1")
        p.op("DVE", lambda e: e.tensor_mul(fmask[:, :], mb0[:, :], mb8[:, :]),
             reads=["m0", "m8"], writes=["fma"], tag=f"{tag}fm")
        p.op("PE", lambda e: e.matmul(PSC[:, 256:512], sb_ltri[:, :], fmask[:, :],
                                      start=True, stop=True),
             reads=["fma", "ltri"], writes=PSCK2, tag=f"{tag}rk")
        p.op("PE", lambda e: e.matmul(PSC[0:1, 0:256], sb_onesf[:, 0:1], fmask[:, :],
                                      start=True, stop=True),
             reads=["fma", "onesf"], writes=PSCK2, tag=f"{tag}ct")
        p.op("DVE", lambda e: e.tensor_copy(rankw[:, :], PSC[:, 256:512]),
             reads=PSCK2, writes=["rka"], tag=f"{tag}rke")
        p.op("DVE", lambda e: e.tensor_copy(cntrow[0:1, :], PSC[0:1, 0:256]),
             reads=PSCK2, writes=["cnt"], tag=f"{tag}cte")
        for ti in range(NTILES):
            def xsf(e, ti=ti):
                with nc.allow_low_precision(
                        reason="xs feeds bf16-operand matmuls; rounding here "
                               "equals the matmul's own operand cast"):
                    return e.tensor_scalar(
                        xsall[:, ti * C:(ti + 1) * C], xsall[:, ti * C:(ti + 1) * C],
                        mxsc[:, NTILES + ti:NTILES + ti + 1], None, ALU.mult)
            p.op("DVE", xsf,
                 reads=[f"xsa:{ti}", "scal"], writes=[f"xsa:{ti}"], tag=f"{tag}xs{ti}")
        dma("SP", cmat[0:NTILES, 0:NEXP],
            cntrow[0:1, 0:NTILES * NEXP].rearrange("o (t e) -> o t e", e=NEXP),
            ["cnt"], ["cmat"], f"{tag}cm")
        p.op("PE", lambda e: e.matmul(PSC[0:NTILES, 8:16], sb_ltri[0:NTILES, 0:NTILES],
                                      cmat[0:NTILES, 0:NEXP], start=True, stop=True),
             reads=["cmat", "ltri"], writes=PSCK2, tag=f"{tag}bs")
        p.op("DVE", lambda e: e.tensor_copy(cmat[0:NTILES, 0:NEXP], PSC[0:NTILES, 8:16]),
             reads=PSCK2, writes=["cmat2"], tag=f"{tag}bse")
        dma("SP", baserow[0:1, 0:NTILES * NEXP].rearrange("o (t e) -> o t e", e=NEXP),
            cmat[0:NTILES, 0:NEXP], ["cmat2"], ["baserow"], f"{tag}br")
        p.op("PE", lambda e: e.matmul(PSC[:, 0:256], sb_ones[0:1, :],
                                      baserow[0:1, 0:256], start=True, stop=True),
             reads=["baserow", "ones"], writes=PSCK2, tag=f"{tag}bb")

        def d3(c):
            return dtmp[:, c * NTILES:(c + 1) * NTILES]

        def t3(buf):
            return buf[:, :].rearrange("p (t e) -> p t e", e=NEXP)

        def dr3(c):
            return d3(c).rearrange("p (t o) -> p t o", o=1)
        p.op("DVE", lambda e: e.tensor_mul(mb0[:, :], rankw[:, :], fmask[:, :]),
             reads=["rka", "fma"], writes=["m0"], tag=f"{tag}rs")
        p.op("DVE", lambda e: e.reduce_sum(dr3(0), t3(mb0), axis=AX),
             reads=["m0"], writes=["dR"], tag=f"{tag}rs2")
        p.op("DVE", lambda e: e.tensor_mul(mb8[:, :], PSC[:, 0:256], fmask[:, :]),
             reads=PSCK2 + ["fma"], writes=["m8"], tag=f"{tag}bsl")
        p.op("DVE", lambda e: e.reduce_sum(dr3(1), t3(mb8), axis=AX),
             reads=["m8"], writes=["dB"], tag=f"{tag}bsl2")
        p.op("DVE", lambda e: e.tensor_mul(mb0[:, :], sb_iota256[:, :], fmask[:, :]),
             reads=["iota256", "fma"], writes=["m0"], tag=f"{tag}cp")
        p.op("DVE", lambda e: e.reduce_sum(dr3(2), t3(mb0), axis=AX),
             reads=["m0"], writes=["dE"], tag=f"{tag}cp2")
        p.op("DVE", lambda e: e.tensor_scalar(d3(2), d3(2), float(CAP), None,
                                              ALU.mult),
             reads=["dE"], writes=["dE"], tag=f"{tag}cpm")
        p.op("DVE", lambda e: e.tensor_add(d3(3), d3(0), d3(1)),
             reads=["dR", "dB"], writes=["dS"], tag=f"{tag}rb")
        p.op("DVE", lambda e: e.tensor_single_scalar(d3(4), d3(3), float(CAP),
                                                     ALU.is_ge),
             reads=["dS"], writes=["dO"], tag=f"{tag}ov")
        p.op("DVE", lambda e: e.tensor_add(d3(5), d3(3), d3(2)),
             reads=["dS", "dE"], writes=["dF"], tag=f"{tag}df")
        p.op("DVE", lambda e: e.tensor_scalar(d3(6), d3(5), -1.0, float(NEXP * CAP),
                                              ALU.mult, ALU.add),
             reads=["dF"], writes=["dT"], tag=f"{tag}tr")
        p.op("DVE", lambda e: e.tensor_add(d3(6), d3(6), sb_iota32[:, 0:NTILES]),
             reads=["dT", "iota32"], writes=["dT"], tag=f"{tag}tr2")
        p.op("DVE", lambda e: e.tensor_mul(d3(4), d3(4), d3(6)),
             reads=["dO", "dT"], writes=["dO"], tag=f"{tag}cl")
        p.op("DVE", lambda e: e.tensor_add(d3(7), d3(4), d3(5)),
             reads=["dO", "dF"], writes=["dPre"], tag=f"{tag}cl2")
        p.op("DVE", lambda e: e.tensor_copy(desti[:, 0:NTILES], d3(7)),
             reads=["dPre"], writes=["diall"], tag=f"{tag}ci")
        for ti in range(NTILES):
            p.op("POOL", lambda e, ti=ti: e.indirect_dma_start(
                G, bass.IndirectOffsetOnAxis(ap=desti[:, ti:ti + 1], axis=0),
                xsall[:, ti * C:(ti + 1) * C], None),
                reads=["diall", f"xsa:{ti}", "Ginit"], writes=[f"H:G:{ti}"],
                dma=True, tag=f"{tag}sc{ti}")
        ar.pop()

        slots = sl01 + [ar.alloc(16 * 1024, BF16)]
        hT = ar.alloc(16 * 512, BF16)
        xch = ar.alloc(KC * 512, BF16)
        stile = [ar.alloc(512, BF16) for _ in range(2)]
        ebuf = [ar.alloc(1024, BF16) for _ in range(2)]
        xg = [ar.alloc(1024, BF16) for _ in range(4)]

        def expert_block(en, wg, wu, wd, nch, load_x, out_rows, out_name,
                         skip_gu=False):
            ek = f"{tag}{en}"
            sg, su, sd = slots
            kg, ku, kd = "slot0", "slot1", "slot2"
            if not skip_gu:
                wchunks(sg, kg, wg, C, FF_E, f"{ek}g")
                wchunks(su, ku, wu, C, FF_E, f"{ek}u")
            wchunks(sd, kd, wd, FF_E, C, f"{ek}d")
            for ch in range(nch):
                load_x(ek, ch)
                for f in range(16):
                    st = stile[f % 2]
                    stk = f"stile{f % 2}"
                    gco = (f % 2) * 1024
                    uco = (f % 2) * 1024 + 512
                    gk2 = PSAK[(f % 2) * 2]
                    uk2 = PSAK[(f % 2) * 2 + 1]

                    def gmm(e, f=f, sg=sg, gco=gco):
                        inst = None
                        for k in range(KC):
                            inst = e.matmul(
                                PSA[:, gco:gco + 512],
                                sg[:, k * FF_E + f * 128:k * FF_E + (f + 1) * 128],
                                xch[:, k * 512:(k + 1) * 512],
                                start=(k == 0), stop=(k == KC - 1))
                        return inst

                    def umm(e, f=f, su=su, uco=uco):
                        inst = None
                        for k in range(KC):
                            inst = e.matmul(
                                PSA[:, uco:uco + 512],
                                su[:, k * FF_E + f * 128:k * FF_E + (f + 1) * 128],
                                xch[:, k * 512:(k + 1) * 512],
                                start=(k == 0), stop=(k == KC - 1))
                        return inst
                    p.op("PE", gmm, reads=[kg, "xch"], writes=[gk2], tag=f"{ek}g{ch}_{f}")
                    p.op("PE", umm, reads=[ku, "xch"], writes=[uk2], tag=f"{ek}u{ch}_{f}")
                    p.op("ACT", lambda e, st=st, gco=gco: e.activation(
                        st[:, :], PSA[:, gco:gco + 512], AF.Silu),
                        reads=[gk2], writes=[stk], tag=f"{ek}s{ch}_{f}")
                    p.op("DVE", lambda e, f=f, st=st, uco=uco: e.tensor_tensor(
                        hT[:, f * 512:(f + 1) * 512], st[:, :],
                        PSA[:, uco:uco + 512], ALU.mult),
                        reads=[stk, uk2], writes=[f"hT:{f}"], tag=f"{ek}h{ch}_{f}")
                hk = [f"hT:{f}" for f in range(16)]
                for it in range(4):
                    eb2 = ebuf[it % 2]
                    ebk = f"ebuf{it % 2}"
                    for nh_ in range(2):
                        bkk = PSBK[nh_]

                        def dnmm(e, it=it, sd=sd, nh_=nh_):
                            inst = None
                            for k in range(16):
                                inst = e.matmul(
                                    PSB[:, nh_ * 512:(nh_ + 1) * 512],
                                    hT[:, k * 512 + it * 128:k * 512 + (it + 1) * 128],
                                    sd[:, k * C + nh_ * 512:k * C + (nh_ + 1) * 512],
                                    start=(k == 0), stop=(k == 15))
                            return inst
                        p.op("PE", dnmm, reads=hk + [kd], writes=[bkk],
                             tag=f"{ek}d{ch}_{it}_{nh_}")
                        def dnev(e, nh_=nh_, eb2=eb2):
                            with nc.allow_low_precision(
                                    reason="ffn branch outputs bounce via HBM "
                                           "in bf16; same rounding class as "
                                           "the bf16 matmul operands"):
                                return e.tensor_copy(
                                    eb2[:, nh_ * 512:(nh_ + 1) * 512],
                                    PSB[:, nh_ * 512:(nh_ + 1) * 512])
                        p.op("DVE", dnev,
                             reads=[bkk], writes=[ebk + f":{nh_}"],
                             tag=f"{ek}e{ch}_{it}_{nh_}")
                    r0 = out_rows(ch, it)
                    dma("ACT", r0, eb2[:, :], [ebk + ":0", ebk + ":1"],
                        [f"{out_name}:{en}:{ch}:{it}"], f"{ek}o{ch}_{it}")

        def shared_loadx(ek, ch):
            dma("SP", xch[:, :].rearrange("p (k n) -> p k n", k=KC),
                xTs.rearrange("(k p) n -> p k n", p=128)[:, :, ch * 512:(ch + 1) * 512],
                [f"xTs:{tt}" for tt in range(ch * 4, ch * 4 + 4)],
                ["xch"], f"{ek}x{ch}")
        expert_block("sh", f_sg, f_su, f_sd, 8, shared_loadx,
                     lambda ch, it: SHS[(ch * 4 + it) * 128:(ch * 4 + it + 1) * 128, :],
                     "H:SHS", skip_gu=True)
        shs_keys = [f"H:SHS:sh:{ch}:{it}" for ch in range(8) for it in range(4)]

        for x in range(NEXP):
            def exp_loadx(ek, ch, x=x):
                for i in range(4):
                    b = i % 2
                    row0 = x * CAP + ch * 512 + i * 128
                    dma("SP", xg[b][:, :], G[row0:row0 + 128, :],
                        [f"H:G:{tt}" for tt in range(NTILES)], [f"xg{b}"],
                        f"{ek}xl{ch}_{i}")
                    transpose8(xg[b][:, :], [f"xg{b}"], f"{ek}xt{ch}_{i}")
                    p.op("DVE", lambda e, i=i: e.tensor_copy(
                        xch[:, :].rearrange("p (k n) -> p k n", k=KC)[:, :, i * 128:(i + 1) * 128],
                        PST[:, :].rearrange("p (k n) -> p k n", k=KC)),
                        reads=["pst"], writes=["xch"], tag=f"{ek}xe{ch}_{i}")
            expert_block(f"e{x}", f_eg[x], f_eu[x], f_ed[x], CAP // 512, exp_loadx,
                         lambda ch, it, x=x: Geo[x * CAP + (ch * 4 + it) * 128:
                                                 x * CAP + (ch * 4 + it + 1) * 128, :],
                         "H:Geo")
        geo_keys = [f"H:Geo:e{x}:{ch}:{it}" for x in range(NEXP)
                    for ch in range(CAP // 512) for it in range(4)]

        ar.push()
        eb = [ar.alloc(1024, F32) for _ in range(2)] + \
            [ar.alloc(1024, BF16) for _ in range(2)]
        mg = [ar.alloc(1024, BF16) for _ in range(2)]
        for ti in range(NTILES):
            b = ti % 2
            kt = f"{tag}E{ti}"
            dma("SP", eb[b][:, :], rin[ti * 128:(ti + 1) * 128, :],
                [f"{rin_name}:{ti}"], [f"ebl{b}"], kt)
            dma("SP", eb[2 + b][:, :], SHS[ti * 128:(ti + 1) * 128, :],
                shs_keys, [f"ebs{b}"], f"{kt}_sh")
            p.op("POOL", lambda e, ti=ti, b=b: e.indirect_dma_start(
                mg[b][:, :], None, Geo,
                bass.IndirectOffsetOnAxis(ap=desti[:, ti:ti + 1], axis=0)),
                reads=geo_keys + ["Geoinit", "diall"], writes=[f"mg{b}"],
                dma=True, tag=f"{kt}_gt")
            p.op("DVE", lambda e, b=b: e.tensor_add(eb[b][:, :], eb[b][:, :],
                                                    eb[2 + b][:, :]),
                 reads=[f"ebl{b}", f"ebs{b}"], writes=[f"ebl{b}"], tag=f"{kt}_a1")
            p.op("DVE", lambda e, b=b: e.tensor_add(eb[b][:, :], eb[b][:, :],
                                                    mg[b][:, :]),
                 reads=[f"ebl{b}", f"mg{b}"], writes=[f"ebl{b}"], tag=f"{kt}_a2")
            dma("ACT", rout[ti * 128:(ti + 1) * 128, :], eb[b][:, :],
                [f"ebl{b}"], [f"{rout_name}:{ti}"], f"{kt}_st")
        ar.pop()
        ar.pop()

    if "f1" in stages:
        ffn("f1", r1, "r1", r2, "r2", g1s, g1eo, sh1s)

    if "ab" in stages:
        ar.push()
        kT_b = ar.alloc(KC * SEQ, BF16)
        v65_b = [ar.alloc(V65W, BF16) for _ in range(SEQ // 128)]
        for bb in range(2):
            bt0 = bb * 16
            bk0 = f"ab{bb}"
            vkeys = [f"v65b:{t}" for t in range(16)]
            kkeys = [f"kTb:{m}" for m in range(8)]
            ar.push()
            alloc_pack()
            xTb = ar.alloc(KC * 512, BF16)
            qstg = ar.alloc(KC * 512, BF16)
            wring = [ar.alloc(KC * C, BF16) for _ in range(2)]
            wseq = 0
            for ch in range(4):
                for i in range(4):
                    ti = bt0 + ch * 4 + i
                    b = i % 2
                    t, y, h = PK["T"][b], PK["Y"][b], PK["H"][b]
                    kt = f"{bk0}p{ti}"
                    dma("SP", t[:, :], r2[ti * 128:(ti + 1) * 128, :],
                        [f"r2:{ti}"], [f"tb{b}"], kt)
                    ln_core(t[:, :], [f"tb{b}"], y[:, :], f"yb{b}", kt)
                    p.op("ACT", lambda e, y=y, h=h: e.activation(h[:, :], y[:, :],
                                                                 AF.Copy),
                         reads=[f"yb{b}"], writes=[f"hb{b}"], tag=f"{kt}_c")
                    transpose8(h[:, :], [f"hb{b}"], f"{kt}_tr")
                    p.op("DVE", lambda e, i=i: e.tensor_copy(
                        xTb[:, :].rearrange("p (k n) -> p k n", k=KC)[:, :, i * 128:(i + 1) * 128],
                        PST[:, :].rearrange("p (k n) -> p k n", k=KC)),
                        reads=["pst"], writes=[f"xTb:{i}"], tag=f"{kt}_ev")
                xbk = [f"xTb:{i}" for i in range(4)]
                for wsrc, wtag in [(ab_wq, "q"), (ab_wk, "k"), (ab_wv, "v")]:
                    wb = wring[wseq % 2]
                    wk = f"wring{wseq % 2}"
                    wseq += 1
                    wchunks(wb, wk, wsrc, C, C, f"{bk0}w{wtag}{ch}")
                    if wtag != "v":
                        for m in range(8):
                            def pmm(e, m=m, wb=wb):
                                inst = None
                                for k in range(KC):
                                    inst = e.matmul(
                                        PSA[:, (m % 4) * 512:(m % 4) * 512 + 512],
                                        wb[:, k * C + m * 128:k * C + (m + 1) * 128],
                                        xTb[:, k * 512:(k + 1) * 512],
                                        start=(k == 0), stop=(k == KC - 1))
                                return inst
                            p.op("PE", pmm, reads=xbk + [wk], writes=[PSAK[m % 4]],
                                 tag=f"{bk0}{wtag}{ch}_{m}")
                            if wtag == "q":
                                p.op("ACT", lambda e, m=m: e.activation(
                                    qstg[:, m * 512:(m + 1) * 512],
                                    PSA[:, (m % 4) * 512:(m % 4) * 512 + 512],
                                    AF.Copy, scale=SCALE_B),
                                    reads=[PSAK[m % 4], "Kc"], writes=[f"qstg:{m}"],
                                    tag=f"{bk0}qe{ch}_{m}")
                            else:
                                p.op("ACT", lambda e, m=m, ch=ch: e.activation(
                                    kT_b[:, m * SEQ + ch * 512:m * SEQ + ch * 512 + 512],
                                    PSA[:, (m % 4) * 512:(m % 4) * 512 + 512],
                                    AF.Copy, scale=SCALE_B),
                                    reads=[PSAK[m % 4], "Kc"], writes=[f"kTb:{m}"],
                                    tag=f"{bk0}ke{ch}_{m}")
                        if wtag == "q":
                            dma("ACT", qTs.rearrange("(k p) n -> p k n", p=128)
                                [:, :, bt0 * 128 + ch * 512:bt0 * 128 + (ch + 1) * 512],
                                qstg[:, :].rearrange("p (k n) -> p k n", k=KC),
                                [f"qstg:{m}" for m in range(8)], ["qstg_out"],
                                f"{bk0}qs{ch}")
                    else:
                        for i in range(4):
                            vt = ch * 4 + i

                            def vmmb(e, i=i, wb=wb):
                                inst = None
                                for k in range(KC):
                                    for nh_ in range(2):
                                        inst = e.matmul(
                                            PSB[:, nh_ * 512:(nh_ + 1) * 512],
                                            xTb[:, k * 512 + i * 128:k * 512 + (i + 1) * 128],
                                            wb[:, k * C + nh_ * 512:k * C + (nh_ + 1) * 512],
                                            start=(k == 0), stop=(k == KC - 1))
                                return inst
                            p.op("PE", vmmb, reads=xbk + [wk], writes=PSBK,
                                 tag=f"{bk0}v{vt}")
                            p.op("DVE", lambda e, vt=vt: e.memset(
                                v65_b[vt][:, :].rearrange("p (h w) -> p h w", h=NH)
                                [:, :, HD:HD + 1], 1.0),
                                writes=[f"v65b:{vt}"], tag=f"{bk0}vo{vt}")
                            p.op("DVE", lambda e, vt=vt: e.tensor_copy(
                                v65_b[vt][:, :].rearrange("p (h w) -> p h w", h=NH)[:, :, 0:HD],
                                PSB[:, :].rearrange("p (h w) -> p h w", h=NH)),
                                reads=PSBK + [f"v65b:{vt}"], writes=[f"v65b:{vt}"],
                                tag=f"{bk0}ve{vt}")
            ar.pop()
            ar.push()
            qTc = [ar.alloc(KC * 512, BF16) for _ in range(2)]
            oTc = [ar.alloc(KC * 512, BF16) for _ in range(2)]
            expt = [ar.alloc(16 * 512, BF16) for _ in range(2)]
            orwB = [ar.alloc(512, F32) for _ in range(2)]
            rrowB = [ar.alloc(512, F32) for _ in range(2)]
            lnb = ar.alloc(1024, F32)
            shft = [ar.alloc(512, BF16) for _ in range(2)]
            J = 4 * NH

            def qload(ic):
                dma("SP", qTc[ic % 2][:, :].rearrange("p (k n) -> p k n", k=KC),
                    qTs.rearrange("(k p) n -> p k n", p=128)
                    [:, :, bt0 * 128 + ic * 512:bt0 * 128 + (ic + 1) * 512],
                    ["qstg_out"], [f"qTc{ic % 2}"], f"{bk0}lq{ic}")

            def S(j):
                ic, h = j // NH, j % NH
                sI = h // 2
                r = (h % 2) * 64
                akt = f"{bk0}i{ic}h{h}"
                qsl = qTc[ic % 2][r:r + 64, sI * 512:(sI + 1) * 512]
                for w in range(8):
                    pk2 = [PSAK[(w % 2) * 2], PSAK[(w % 2) * 2 + 1]]

                    def swv(e, w=w, r=r, sI=sI, qsl=qsl):
                        inst = None
                        for jj in range(2):
                            jt = w * 2 + jj
                            co = (w % 2) * 1024 + jj * 512
                            inst = e.matmul(
                                PSA[:, co:co + 512],
                                kT_b[r:r + 64, sI * SEQ + jt * 128:sI * SEQ + (jt + 1) * 128],
                                qsl, start=True, stop=True)
                        return inst
                    p.op("PE", swv, reads=kkeys + [f"qTc{ic % 2}"], writes=pk2,
                         tag=f"{akt}_s{w}")

                    def ewv(e, w=w, j=j):
                        inst = None
                        for jj in range(2):
                            jt = w * 2 + jj
                            co = (w % 2) * 1024 + jj * 512
                            inst = e.activation(
                                expt[j % 2][:, jt * 512:(jt + 1) * 512],
                                PSA[:, co:co + 512], AF.Exp)
                        return inst
                    p.op("ACT", ewv, reads=pk2, writes=[f"ex{j % 2}:{w}"],
                         tag=f"{akt}_e{w}")

            def O(j):
                ic, h = j // NH, j % NH
                akt = f"{bk0}i{ic}h{h}"
                ek = [f"ex{j % 2}:{w}" for w in range(8)]
                pco = (j % 2) * 512

                def ommb(e, h=h, j=j, pco=pco):
                    inst = None
                    for jt in range(16):
                        inst = e.matmul(PSB[0:65, pco:pco + 512],
                                        v65_b[jt][:, h * 65:h * 65 + 65],
                                        expt[j % 2][:, jt * 512:(jt + 1) * 512],
                                        start=(jt == 0), stop=(jt == 15))
                    return inst
                p.op("PE", ommb, reads=ek + vkeys, writes=[PSBK[j % 2]],
                     tag=f"{akt}_o")
                p.op("DVE", lambda e, pco=pco: e.tensor_copy(
                    lnb[0:1, 0:512], PSB[64:65, pco:pco + 512]),
                    reads=[PSBK[j % 2]], writes=["lnB"], tag=f"{akt}_rc0")
                p.op("ACT", lambda e: e.activation(lnb[0:1, 512:1024],
                                                   lnb[0:1, 0:512], AF.Ln),
                     reads=["lnB"], writes=["lnB2"], tag=f"{akt}_rc1")
                p.op("ACT", lambda e, j=j: e.activation(
                    rrowB[j % 2][0:1, :], lnb[0:1, 512:1024], AF.Exp, scale=-1.0),
                    reads=["lnB2", "Kc"], writes=[f"rrB{j % 2}"], tag=f"{akt}_rc")

            def B(j):
                ic, h = j // NH, j % NH
                sI = h // 2
                akt = f"{bk0}i{ic}h{h}"
                pco = (j % 2) * 512
                p.op("PE", lambda e, j=j: e.matmul(
                    PSC[0:64, :], sb_ones[0:1, 0:64], rrowB[j % 2][0:1, :],
                    start=True, stop=True),
                    reads=[f"rrB{j % 2}", "ones"], writes=["psc"], tag=f"{akt}_bc")
                p.op("ACT", lambda e, j=j, pco=pco: e.activation(
                    orwB[j % 2][0:64, :], PSB[0:64, pco:pco + 512], AF.Copy),
                    reads=[PSBK[j % 2]], writes=[f"orB{j % 2}"], tag=f"{akt}_ev")
                if h % 2 == 0:
                    p.op("DVE", lambda e, sI=sI, j=j, ic=ic: e.tensor_tensor(
                        oTc[ic % 2][0:64, sI * 512:(sI + 1) * 512],
                        orwB[j % 2][0:64, :], PSC[0:64, :], ALU.mult),
                        reads=[f"orB{j % 2}", "psc"], writes=[f"oTc{ic % 2}:{sI}:0"],
                        tag=f"{akt}_n")
                else:
                    p.op("DVE", lambda e, j=j: e.tensor_tensor(
                        shft[j % 2][0:64, :], orwB[j % 2][0:64, :], PSC[0:64, :],
                        ALU.mult),
                        reads=[f"orB{j % 2}", "psc"], writes=[f"shft{j % 2}"],
                        tag=f"{akt}_n")
                    dma("SP", oTc[ic % 2][64:128, sI * 512:(sI + 1) * 512],
                        shft[j % 2][0:64, :], [f"shft{j % 2}"],
                        [f"oTc{ic % 2}:{sI}:1"], f"{akt}_sh")

            for j in range(J + 2):
                if "f2" in stages:
                    emit_gz("f2", 1, "SP")
                if j < J and j % NH == 0:
                    qload(j // NH)
                if j < J:
                    S(j)
                if j >= 1:
                    O(j - 1)
                if j >= 2:
                    B(j - 2)
                    if (j - 2) % NH == NH - 1:
                        ic = (j - 2) // NH
                        okeys = [f"oTc{ic % 2}:{ss}:{par}" for ss in range(8)
                                 for par in range(2)]
                        dma("ACT", oTs.rearrange("(k p) n -> p k n", p=128)
                            [:, :, bt0 * 128 + ic * 512:bt0 * 128 + (ic + 1) * 512],
                            oTc[ic % 2][:, :].rearrange("p (k n) -> p k n", k=KC),
                            okeys, ["oTs_out"], f"{bk0}so{ic}")
            ar.pop()
            ar.push()
            cb_bo = ar.alloc(C, F32)
            mk_bcast(cb_bo, ab_bo.rearrange("(o n) -> o n", o=1), C, "bo")
            wo_sb = ar.alloc(KC * C, BF16)
            oTt = [ar.alloc(KC * 128, BF16) for _ in range(2)]
            zb = [ar.alloc(1024, F32) for _ in range(4)]
            wchunks(wo_sb, "W:bwo", ab_wo, C, C, "abwo")
            for i in range(16):
                ti = bt0 + i
                b = i % 2
                kt = f"{bk0}z{i}"
                dma("SP", oTt[b][:, :].rearrange("p (k n) -> p k n", k=KC),
                    oTs.rearrange("(k p) n -> p k n", p=128)
                    [:, :, ti * 128:(ti + 1) * 128],
                    ["oTs_out"], [f"oTt{b}"], f"{kt}_lo")

                def zmmb(e, b=b):
                    inst = None
                    for k in range(KC):
                        for nh_ in range(2):
                            inst = e.matmul(
                                PSB[:, nh_ * 512:(nh_ + 1) * 512],
                                oTt[b][:, k * 128:(k + 1) * 128],
                                wo_sb[:, k * C + nh_ * 512:k * C + (nh_ + 1) * 512],
                                start=(k == 0), stop=(k == KC - 1))
                    return inst
                p.op("PE", zmmb, reads=[f"oTt{b}", "W:bwo"], writes=PSBK, tag=kt)
                dma("SP", zb[b][:, :], r2[ti * 128:(ti + 1) * 128, :],
                    [f"r2:{ti}"], [f"zbl{b}"], f"{kt}_l")
                p.op("DVE", lambda e, b=b: e.scalar_tensor_tensor(
                    zb[2 + b][:, :], PSB[:, :], 1.0, cb_bo[:, :], ALU.mult, ALU.add),
                    reads=PSBK + ["C:bo"], writes=[f"zbo{b}"], tag=f"{kt}_b")
                p.op("DVE", lambda e, b=b: e.tensor_add(zb[2 + b][:, :],
                                                        zb[2 + b][:, :], zb[b][:, :]),
                     reads=[f"zbo{b}", f"zbl{b}"], writes=[f"zbo{b}"], tag=f"{kt}_r")
                dma("ACT", r3[ti * 128:(ti + 1) * 128, :], zb[2 + b][:, :],
                    [f"zbo{b}"], [f"r3:{ti}"], f"{kt}_st")
            ar.pop()
        ar.pop()

    if "f2" in stages:
        ffn("f2", r3, "r3", out, "out", g2s, g2eo, sh2s)
    else:
        last = {"ab": (r3, "r3"), "f1": (r2, "r2"), "aa": (r1, "r1")}
        for stg_ in ("ab", "f1", "aa"):
            if stg_ in stages:
                src, sname = last[stg_]
                ar.push()
                cp = [ar.alloc(1024, F32) for _ in range(2)]
                for ti in range(NTILES):
                    b = ti % 2
                    dma("SP", cp[b][:, :], src[ti * 128:(ti + 1) * 128, :],
                        [f"{sname}:{ti}"], [f"cp{b}"], f"cp{ti}")
                    dma("ACT", out[ti * 128:(ti + 1) * 128, :], cp[b][:, :],
                        [f"cp{b}"], [f"out:{ti}"], f"cp{ti}o")
                ar.pop()
                break

    p.build(nc, es)
    es.close()
    return p



def make_inputs(inputs, core):
    x = inputs["L0_attn_a_x"]
    shard = np.ascontiguousarray(x[2 * core:2 * core + 2]).reshape(T, C)
    ident_f = np.eye(128, dtype=np.float32)
    ident_b = np.eye(128, dtype=ml_dtypes.bfloat16)
    j = np.arange(64)[:, None]
    i = np.arange(64)[None, :]
    causal = (j <= i).astype(np.float32)
    ones = np.ones((64, 64), np.float32)
    mA = np.concatenate([causal, ones], axis=0)
    mB = np.concatenate([ones, causal], axis=0)
    hmask = np.concatenate([np.tile(mA, (1, NH)), np.tile(mB, (1, NH))], axis=1)
    iota256 = np.tile(np.tile(np.arange(NEXP, dtype=np.float32), 32)[None, :],
                      (128, 1))
    iota32 = np.tile(np.arange(32, dtype=np.float32)[None, :], (128, 1))
    ltri = np.triu(np.ones((128, 128), np.float32), k=1)
    return {
        "xin": shard,
        "aa_wqkv": inputs["L0_attn_a_w_qkv"], "aa_bqkv": inputs["L0_attn_a_b_qkv"],
        "aa_wout": inputs["L0_attn_a_w_out"], "aa_bout": inputs["L0_attn_a_b_out"],
        "aa_g1": inputs["L0_attn_a_ln1_g"], "aa_b1g": inputs["L0_attn_a_ln1_b"],
        "aa_g2": inputs["L0_attn_a_ln2_g"], "aa_b2g": inputs["L0_attn_a_ln2_b"],
        "aa_w1": inputs["L0_attn_a_w1"], "aa_b1": inputs["L0_attn_a_b1"],
        "aa_w2": inputs["L0_attn_a_w2"], "aa_b2": inputs["L0_attn_a_b2"],
        "ab_wq": inputs["L0_attn_b_Wq"], "ab_wk": inputs["L0_attn_b_Wk"],
        "ab_wv": inputs["L0_attn_b_Wv"], "ab_wo": inputs["L0_attn_b_Wo"],
        "ab_bo": inputs["L0_attn_b_bo"],
        "f_rw": inputs["L0_ffn_router_w"],
        "f_eg": inputs["L0_ffn_expert_gate"], "f_eu": inputs["L0_ffn_expert_up"],
        "f_ed": inputs["L0_ffn_expert_down"],
        "f_sg": inputs["L0_ffn_shared_gate"], "f_su": inputs["L0_ffn_shared_up"],
        "f_sd": inputs["L0_ffn_shared_down"],
        "id_f": ident_f, "id_b": ident_b, "hmask": hmask, "iota256": iota256,
        "iota32": iota32, "ltri": ltri,
    }


def kernel(**inputs):
    nc = bass.Bass("TRN2", target_bir_lowering=False)
    build_program(nc)
    nc.finalize()
    in_maps = [make_inputs(inputs, c) for c in range(8)]
    res = run_bass_kernel_spmd(nc, in_maps, core_ids=list(range(8)))
    outs = [np.asarray(res.results[c]["out"]).reshape(2, SEQ, C) for c in range(8)]
    return np.concatenate(outs, axis=0).astype(np.float32)


# version 21
# speedup vs baseline: 1.0481x; 1.0044x over previous
import numpy as np
import ml_dtypes
from contextlib import ExitStack

import concourse.bass as bass
import concourse.mybir as mybir
from concourse.bass_utils import run_bass_kernel_spmd

F32 = mybir.dt.float32
BF16 = mybir.dt.bfloat16
AF = mybir.ActivationFunctionType
ALU = mybir.AluOpType
AX = mybir.AxisListType.X

C = 1024
T = 4096
SEQ = 2048
NH = 16
HD = 64
BUCKET = 64
FF_A = 4096
FF_E = 2048
NEXP = 8
KC = C // 128
NTILES = T // 128
EPS = 1e-5
SCALE_B = float(HD ** -0.25)
SCALE_A = 0.125
V65W = NH * (HD + 1)

ENGINES = ("PE", "DVE", "ACT", "POOL", "SP")


class Op:
    __slots__ = ("idx", "eng", "emit", "reads", "writes", "dma", "deps",
                 "marked", "inc_val", "tag", "_qslot", "_qseq", "fence",
                 "_fence_snap")

    def __init__(self, idx, eng, emit, reads, writes, dma, tag):
        self.idx = idx; self.eng = eng; self.emit = emit
        self.reads = reads; self.writes = writes; self.dma = dma
        self.deps = []; self.marked = False; self.inc_val = 0; self.tag = tag
        self.fence = False; self._fence_snap = None


class Prog:

    def __init__(self, relax_same_engine=False):
        self.ops = []
        self.last_write = {}
        self.readers = {}
        self.relax = relax_same_engine
        self.fence_n = 0

    def op(self, eng, emit, reads=(), writes=(), dma=False, tag=""):
        o = Op(len(self.ops), eng, emit, tuple(reads), tuple(writes), dma, tag)
        deps = set()
        for r in o.reads:
            w = self.last_write.get(r)
            if w is not None:
                deps.add(w)
        for w in o.writes:
            lw = self.last_write.get(w)
            if lw is not None:
                deps.add(lw)
            deps.update(self.readers.get(w, ()))
        deps.discard(o.idx)
        o.deps = sorted(deps)
        self.ops.append(o)
        for r in o.reads:
            self.readers.setdefault(r, []).append(o.idx)
        for w in o.writes:
            self.last_write[w] = o.idx
            self.readers[w] = []
        return o.idx

    def fence(self, tag="fence"):
        n = self.fence_n
        self.fence_n += 1
        for E in ENGINES:
            o = Op(len(self.ops), E, None, (), (), False, f"{tag}{n}_{E}")
            o.fence = True
            self.ops.append(o)

    def _needs_wait(self, ci, pi):
        p, c = self.ops[pi], self.ops[ci]
        if p.eng == c.eng and not p.dma:
            if p.eng == "PE" and not c.dma:
                return False
            if self.relax and not c.dma:
                return False
            return True
        return True

    NQ = 16

    def build(self, nc, es):
        sems = {e: es.enter_context(nc.semaphore(f"s_{e}")) for e in ENGINES}
        qsems = {e: [es.enter_context(nc.semaphore(f"q_{e}{i}"))
                     for i in range(self.NQ)] for e in ("SP", "ACT", "POOL")}

        last_by_eng = {}
        for o in self.ops:
            if o.fence:
                for le in last_by_eng.values():
                    le.marked = True
            else:
                last_by_eng[o.eng] = o
        for o in self.ops:
            if o.dma:
                o.marked = True
            for d in o.deps:
                if self._needs_wait(o.idx, d):
                    self.ops[d].marked = True

        counts = {}
        dma_seq = {}
        for o in self.ops:
            if o.fence:
                snap = dict(counts)
                qsnap = {}
                for e2, nn in dma_seq.items():
                    for i in range(min(nn, self.NQ)):
                        qsnap[(e2, i)] = ((nn - 1 - i) // self.NQ + 1) * 16
                o._fence_snap = (snap, qsnap)
                continue
            if o.marked:
                if o.dma:
                    j = dma_seq.get(o.eng, 0)
                    dma_seq[o.eng] = j + 1
                    o.inc_val = (j // self.NQ + 1) * 16
                    o._qslot = j % self.NQ
                    o._qseq = j
                else:
                    key = (o.eng, False)
                    counts[key] = counts.get(key, 0) + 1
                    o.inc_val = counts[key]

        per_eng = {e: [] for e in ENGINES}
        for o in self.ops:
            per_eng[o.eng].append(o)
        final_q = {}
        for e in ("SP", "ACT", "POOL"):
            n = dma_seq.get(e, 0)
            final_q[e] = [(n - 1 - i) // self.NQ + 1 for i in range(self.NQ)] \
                if n else []

        def body(eng_obj, eng_name):
            waited = {}

            def wait(key, sem, val):
                if waited.get(key, 0) >= val:
                    return
                eng_obj.wait_ge(sem, val)
                waited[key] = val

            for o in per_eng[eng_name]:
                if o.fence:
                    snap, qsnap = o._fence_snap
                    for (e2, dm), v in snap.items():
                        if v and not (e2 == eng_name and not dm):
                            wait(("s", e2), sems[e2], v)
                    for (e2, i), v in qsnap.items():
                        wait(("q", e2, i), qsems[e2][i], v)
                    continue
                if o.dma and o._qseq >= self.NQ:
                    wait(("q", o.eng, o._qslot), qsems[o.eng][o._qslot],
                         o.inc_val - 16)
                for d in o.deps:
                    if not self._needs_wait(o.idx, d):
                        continue
                    pp = self.ops[d]
                    if not pp.marked:
                        continue
                    if pp.dma:
                        wait(("q", pp.eng, pp._qslot), qsems[pp.eng][pp._qslot],
                             pp.inc_val)
                    else:
                        wait(("s", pp.eng), sems[pp.eng], pp.inc_val)
                inst = o.emit(eng_obj)
                if o.marked:
                    if isinstance(inst, (list, tuple)):
                        inst = inst[-1]
                    assert inst is not None, f"op {o.idx} ({o.tag}) returned None"
                    if o.dma:
                        inst.then_inc(qsems[o.eng][o._qslot], 16)
                    else:
                        inst.then_inc(sems[o.eng], 1)
            if eng_name == "POOL":
                for e in ("SP", "ACT", "POOL"):
                    n = dma_seq.get(e, 0)
                    for i in range(min(n, self.NQ)):
                        nslot = (n - 1 - i) // self.NQ + 1
                        wait(("q", e, i), qsems[e][i], nslot * 16)

        blk = es.enter_context(nc.Block())
        blk.sync(lambda e: body(e, "SP"))
        blk.tensor(lambda e: body(e, "PE"))
        blk.vector(lambda e: body(e, "DVE"))
        blk.scalar(lambda e: body(e, "ACT"))
        blk.gpsimd(lambda e: body(e, "POOL"))


class Arena:
    def __init__(self, nc, es, prog, nbytes=204800):
        self.t = es.enter_context(nc.sbuf_tensor("arena", [128, nbytes],
                                                 mybir.dt.uint8))
        self.nbytes = nbytes
        self.off = 0
        self.stack = []
        self.hi = 0
        self.prog = prog

    def alloc(self, cols, dtype):
        size = cols * (2 if dtype == BF16 else 4)
        size = (size + 63) & ~63
        assert self.off + size <= self.nbytes, \
            f"arena overflow: {self.off}+{size} > {self.nbytes}"
        ap = self.t[:][:, self.off:self.off + size].bitcast(dtype)[:, 0:cols]
        self.off += size
        self.hi = max(self.hi, self.off)
        return ap

    def push(self):
        self.stack.append(self.off)

    def pop(self):
        self.off = self.stack.pop()
        self.prog.fence()



def build_program(nc, stages=("aa", "f1", "ab", "f2"), relax=False):
    p = Prog(relax_same_engine=relax)
    es = ExitStack()

    def din(name, shape, dt=F32):
        return nc.dram_tensor(name, shape, dt, kind="ExternalInput").ap()

    xin = din("xin", [T, C])
    aa_wqkv = din("aa_wqkv", [C, 3 * C]); aa_bqkv = din("aa_bqkv", [3 * C])
    aa_wout = din("aa_wout", [C, C]); aa_bout = din("aa_bout", [C])
    aa_g1 = din("aa_g1", [C]); aa_b1g = din("aa_b1g", [C])
    aa_g2 = din("aa_g2", [C]); aa_b2g = din("aa_b2g", [C])
    aa_w1 = din("aa_w1", [C, FF_A]); aa_b1 = din("aa_b1", [FF_A])
    aa_w2 = din("aa_w2", [FF_A, C]); aa_b2 = din("aa_b2", [C])
    ab_wq = din("ab_wq", [C, C]); ab_wk = din("ab_wk", [C, C])
    ab_wv = din("ab_wv", [C, C]); ab_wo = din("ab_wo", [C, C])
    ab_bo = din("ab_bo", [C])
    f_rw = din("f_rw", [NEXP, C])
    f_eg = din("f_eg", [NEXP, C, FF_E]); f_eu = din("f_eu", [NEXP, C, FF_E])
    f_ed = din("f_ed", [NEXP, FF_E, C])
    f_sg = din("f_sg", [C, FF_E]); f_su = din("f_su", [C, FF_E])
    f_sd = din("f_sd", [FF_E, C])
    id_f = din("id_f", [128, 128])
    id_b = din("id_b", [128, 128], BF16)
    hmaskin = din("hmask", [128, 2 * NH * BUCKET])
    iota256 = din("iota256", [128, NTILES * NEXP])
    iota32 = din("iota32", [128, NTILES])
    ltri = din("ltri", [128, 128])

    out = nc.dram_tensor("out", [T, C], F32, kind="ExternalOutput").ap()

    def scr(name, shape, dt=F32):
        return nc.dram_tensor(name, shape, dt, kind="Internal").ap()

    ys = scr("ys", [T, C])
    y2s = scr("y2s", [T, C])
    r1 = scr("r1", [T, C])
    r2 = scr("r2", [T, C])
    r3 = scr("r3", [T, C])
    xTs = scr("xTs", [C, T], BF16)
    CAP = 1024
    NSLOT = NEXP * CAP + NTILES
    g1s = scr("g1s", [NSLOT, C], BF16)
    g1eo = scr("g1eo", [NSLOT, C], BF16)
    g2s = scr("g2s", [NSLOT, C], BF16)
    g2eo = scr("g2eo", [NSLOT, C], BF16)
    sh1s = scr("sh1s", [T, C], BF16)
    sh2s = scr("sh2s", [T, C], BF16)
    qTs = scr("qTs", [C, T], BF16)
    oTs = scr("oTs", [C, T], BF16)

    ar = Arena(nc, es, p)

    PSA = es.enter_context(nc.psum_tensor("psa", [128, 2048], F32))[:]
    PSB = es.enter_context(nc.psum_tensor("psb", [128, 1024], F32))[:]
    PSC = es.enter_context(nc.psum_tensor("psc", [128, 512], F32))[:]
    PST = es.enter_context(nc.psum_tensor("pst", [128, 1024], BF16))[:]
    PSAK = ["psa0", "psa1", "psa2", "psa3"]
    PSBK = ["psb0", "psb1"]

    def dma(eng, dst, src, reads, writes, tag=""):
        p.op(eng, lambda e, d=dst, s=src: e.dma_start(out=d, in_=s),
             reads=reads, writes=writes, dma=True, tag=tag)

    sb_idf = ar.alloc(128, F32)
    sb_idb = ar.alloc(128, BF16)
    sb_ones = ar.alloc(128, F32)
    sb_onesb = ar.alloc(128, BF16)
    sb_hmask = ar.alloc(2 * NH * BUCKET, BF16)
    sb_iota256 = ar.alloc(NTILES * NEXP, F32)
    sb_iota32 = ar.alloc(NTILES, F32)
    rwT = ar.alloc(8 * KC, F32)
    sb_ltri = ar.alloc(128, F32)
    sb_onesf = ar.alloc(4, F32)
    sb_zt = ar.alloc(1024, BF16)
    sb_onesr = ar.alloc(128, BF16)
    ROWBUF = ar.alloc(1024, F32)
    STG = ar.alloc(128, F32)
    STT = ar.alloc(64, F32)

    dma("SP", sb_idf[:, :], id_f, ["H:idf"], ["idf"], "idf")
    dma("SP", sb_idb[:, :], id_b, ["H:idb"], ["idb"], "idb")
    dma("SP", sb_iota256[:, :], iota256, ["H:iota256"], ["iota256"], "iota256")
    dma("SP", sb_iota32[:, :], iota32, ["H:iota32"], ["iota32"], "iota32")
    dma("POOL", sb_hmask[:, :], hmaskin, ["H:hmask"], ["hmask"], "hmask")
    dma("SP", sb_ltri[:, :], ltri, ["H:ltri"], ["ltri"], "ltri")
    p.op("DVE", lambda e: e.memset(sb_onesf[:, 0:1], 1.0), writes=["onesf"], tag="onesf")
    p.op("DVE", lambda e: e.memset(sb_zt[:, :], 0.0), writes=["zt"], tag="zt")
    p.op("DVE", lambda e: e.memset(sb_ones[0:1, :], 1.0), writes=["ones"], tag="ones")
    p.op("DVE", lambda e: e.memset(sb_onesb[:, 0:1], 1.0), writes=["onesb"], tag="onesb")
    p.op("DVE", lambda e: e.memset(sb_onesr[0:1, :], 1.0), writes=["onesr"], tag="onesr")
    KCOL = ar.alloc(8, F32)
    _kvals = [EPS, -0.5, SCALE_A, SCALE_B, -1.0]
    for _i, _v in enumerate(_kvals):
        p.op("DVE", lambda e, _i=_i, _v=_v: e.memset(KCOL[:, _i:_i + 1], _v),
             writes=["Kc"], tag=f"kc{_i}")
        nc.const_aps.register(F32, _v, KCOL[:, _i:_i + 1])

    GZP = {}

    def prep_gz(tag, G, Geo):
        ops = []
        nzi = (NSLOT + 127) // 128
        for zi in range(nzi):
            r0 = zi * 128
            nr = min(128, NSLOT - r0)
            ops.append((G[r0:r0 + nr, :], sb_zt[0:nr, :], f"{tag}gz{zi}",
                        ["Ginit"]))
        ops.append((Geo[NEXP * CAP:NSLOT, :], sb_zt[0:NTILES, :], f"{tag}ez0",
                    ["Geoinit"]))
        GZP[tag] = ops

    def emit_gz(tag, n, queue):
        ops = GZP.get(tag)
        if not ops:
            return
        for _ in range(min(n, len(ops))):
            dst, src, tg, wk = ops.pop(0)
            dma(queue, dst, src, ["zt"], wk, tg)

    prep_gz("f1", g1s, g1eo)
    prep_gz("f2", g2s, g2eo)

    for k in range(KC):
        dma("SP", STG[0:8, 0:128], f_rw[:, k * 128:(k + 1) * 128], ["H:rw"],
            ["stg"], f"ld_rw{k}")
        p.op("PE", lambda e: e.transpose(PSB[0:128, 0:8], STG[0:8, 0:128],
                                         sb_idf[0:8, 0:8]),
             reads=["stg", "idf"], writes=["psb0"], tag=f"tr_rw{k}")
        p.op("DVE", lambda e, k=k: e.tensor_copy(rwT[:, k * 8:(k + 1) * 8], PSB[:, 0:8]),
             reads=["psb0"], writes=["C:rwT", "stg"], tag=f"ev_rw{k}")

    def mk_bcast(dst, src_row_ap, n, tag):
        row = ROWBUF[0:1, 0:n]
        dma("SP", row, src_row_ap, [f"H:{tag}"], ["rowbuf"], f"ld_{tag}")
        nch = (n + 511) // 512

        def mm(e):
            inst = None
            for i in range(nch):
                s = i * 512
                w = min(512, n - s)
                inst = e.matmul(PSA[:, s:s + w], sb_ones[0:1, :], row[:, s:s + w],
                                start=True, stop=True)
            return inst
        p.op("PE", mm, reads=["ones", "rowbuf"],
             writes=[PSAK[i] for i in range(nch)], tag=f"bc_{tag}")
        p.op("DVE", lambda e: e.tensor_copy(dst[:, 0:n], PSA[:, 0:n]),
             reads=[PSAK[i] for i in range(nch)],
             writes=[f"C:{tag}", "rowbuf"], tag=f"ev_{tag}")

    def mk_cols(dst, src_vec, ntile, tag):
        stg = STG[0:ntile, 0:128]
        dma("SP", stg, src_vec.rearrange("(t p) -> t p", p=128),
            [f"H:{tag}"], ["stg"], f"ld_{tag}")
        p.op("PE", lambda e, stg=stg, ntile=ntile: e.transpose(
            PSA[:, 0:ntile], stg, sb_idf[0:ntile, 0:ntile]),
            reads=["stg", "idf"], writes=["psa0"], tag=f"tr_{tag}")
        p.op("DVE", lambda e, ntile=ntile: e.tensor_copy(dst[:, 0:ntile],
                                                         PSA[:, 0:ntile]),
             reads=["psa0"], writes=[f"C:{tag}", "stg"], tag=f"ev_{tag}")

    PK = {}

    def alloc_pack():
        PK["T"] = [ar.alloc(1024, F32) for _ in range(2)]
        PK["T2"] = [ar.alloc(1024, F32) for _ in range(2)]
        PK["Y"] = [ar.alloc(1024, F32) for _ in range(2)]
        PK["H"] = [ar.alloc(1024, BF16) for _ in range(2)]
        PK["SQ"] = [ar.alloc(1024, F32) for _ in range(2)]

    _ln_ctr = [0]

    def ln_core(src_ap, src_keys, y_ap, y_key, kt):
        st = STT
        sc = _ln_ctr[0] % 6
        _ln_ctr[0] += 1
        c0 = sc * 8
        sq = PK["SQ"][sc % 2]
        p.op("ACT", lambda e: e.activation(sq[:, :], src_ap, AF.Square,
                                           accum_out=st[:, c0:c0 + 1]),
             reads=src_keys, writes=[f"sqd{sc % 2}", f"st{c0}"], tag=f"{kt}_sq")
        p.op("DVE", lambda e: e.reduce_sum(st[:, c0 + 1:c0 + 2], src_ap, axis=AX),
             reads=src_keys, writes=[f"st{c0}b"], tag=f"{kt}_sm")
        p.op("DVE", lambda e: e.tensor_scalar(st[:, c0 + 2:c0 + 3],
                                              st[:, c0 + 1:c0 + 2],
                                              1.0 / C, None, ALU.mult),
             reads=[f"st{c0}b"], writes=[f"st{c0}c"], tag=f"{kt}_mu")
        p.op("DVE", lambda e: e.tensor_scalar(st[:, c0 + 3:c0 + 4], st[:, c0:c0 + 1],
                                              1.0 / C, None, ALU.mult),
             reads=[f"st{c0}"], writes=[f"st{c0}d"], tag=f"{kt}_e2")
        p.op("DVE", lambda e: e.tensor_mul(st[:, c0 + 4:c0 + 5], st[:, c0 + 2:c0 + 3],
                                           st[:, c0 + 2:c0 + 3]),
             reads=[f"st{c0}c"], writes=[f"st{c0}e"], tag=f"{kt}_m2")
        p.op("DVE", lambda e: e.tensor_sub(st[:, c0 + 5:c0 + 6], st[:, c0 + 3:c0 + 4],
                                           st[:, c0 + 4:c0 + 5]),
             reads=[f"st{c0}d", f"st{c0}e"], writes=[f"st{c0}f"], tag=f"{kt}_va")
        p.op("ACT", lambda e: e.activation(st[:, c0 + 6:c0 + 7], st[:, c0 + 5:c0 + 6],
                                           AF.Ln, bias=EPS),
             reads=[f"st{c0}f", "Kc"], writes=[f"st{c0}g"], tag=f"{kt}_ln")
        p.op("ACT", lambda e: e.activation(st[:, c0 + 7:c0 + 8], st[:, c0 + 6:c0 + 7],
                                           AF.Exp, scale=-0.5),
             reads=[f"st{c0}g", "Kc"], writes=[f"st{c0}h"], tag=f"{kt}_rs")
        p.op("DVE", lambda e: e.tensor_scalar(y_ap, src_ap, st[:, c0 + 2:c0 + 3],
                                              st[:, c0 + 7:c0 + 8],
                                              ALU.subtract, ALU.mult),
             reads=list(src_keys) + [f"st{c0}c", f"st{c0}h"], writes=[y_key],
             tag=f"{kt}_ap")

    def transpose8(src_ap, src_keys, kt):
        def mm(e):
            inst = None
            for k in range(KC):
                inst = e.transpose(PST[:, k * 128:(k + 1) * 128],
                                   src_ap[:, k * 128:(k + 1) * 128], sb_idb[:, :])
            return inst
        p.op("PE", mm, reads=list(src_keys) + ["idb"], writes=["pst"], tag=kt)

    def wchunks(slot, skey, w_ap, kdim, n, tag):
        kch = kdim // 128
        step = max(1, min(kch, (2 * 1024 * 1024) // (n * 512)))
        for k0 in range(0, kch, step):
            k1 = min(k0 + step, kch)
            dma("POOL",
                slot[:, :].rearrange("p (k n) -> p k n", k=kch)[:, k0:k1, :],
                w_ap.rearrange("(k p) n -> p k n", p=128)[:, k0:k1, :],
                [f"H:{tag}"], [skey], f"{tag}_{k0}")

    if "aa" in stages:
        ar.push()
        alloc_pack()
        cb_g1 = ar.alloc(C, F32)
        cb_b1g = ar.alloc(C, F32)
        cb_bqv = ar.alloc(C, F32)
        cb_bout = ar.alloc(C, F32)
        col_bqkv = ar.alloc(24, F32)
        mk_bcast(cb_g1, aa_g1.rearrange("(o n) -> o n", o=1), C, "g1")
        mk_bcast(cb_b1g, aa_b1g.rearrange("(o n) -> o n", o=1), C, "b1g")
        mk_bcast(cb_bqv, aa_bqkv.rearrange("(o n) -> o n", o=1)[:, 2 * C:3 * C], C, "bqv")
        mk_bcast(cb_bout, aa_bout.rearrange("(o n) -> o n", o=1), C, "bout")
        mk_cols(col_bqkv, aa_bqkv, 24, "bqkv")

        wqkv_sb = ar.alloc(KC * 3 * C, BF16)
        wout_sb = ar.alloc(KC * C, BF16)
        hTc = ar.alloc(KC * 512, BF16)
        HGRP = 512
        NG = T // HGRP
        GT = HGRP // 128
        GB = HGRP // BUCKET
        GPB = SEQ // HGRP
        qT_g = ar.alloc(KC * HGRP, BF16)
        kT_g = ar.alloc(KC * HGRP, BF16)
        kT_c = ar.alloc(KC * BUCKET, BF16)
        v65_g = [ar.alloc(V65W, BF16) for _ in range(GT)]
        v65_c = ar.alloc(V65W, BF16)
        zv65 = ar.alloc(V65W, BF16)
        zkT = ar.alloc(64, BF16)
        oT_g = ar.alloc(KC * HGRP, BF16)
        orw = [ar.alloc(512, F32) for _ in range(2)]
        rrow = [ar.alloc(1024, BF16) for _ in range(3)]
        lna = ar.alloc(2048, F32)
        expb = [ar.alloc(NH * BUCKET, BF16) for _ in range(2)]

        wchunks(wqkv_sb, "W:qkv", aa_wqkv, C, 3 * C, "aaqkv")
        wchunks(wout_sb, "W:aout", aa_wout, C, C, "aaout")
        p.op("DVE", lambda e: e.memset(zv65[:, :], 0.0), writes=["zv65"], tag="zv0")
        p.op("DVE", lambda e: e.memset(
            zv65[:, :].rearrange("p (h w) -> p h w", h=NH)[:, :, HD:HD + 1], 1.0),
            reads=["zv65"], writes=["zv65"], tag="zv1")
        p.op("DVE", lambda e: e.memset(zkT[:, :], 0.0), writes=["zkT"], tag="zk")

        for g in range(NG):
            gk = f"g{g}"
            for i in range(GT):
                ti = g * GT + i
                b = i % 2
                t, t2, y, h = PK["T"][b], PK["T2"][b], PK["Y"][b], PK["H"][b]
                kt = f"aa{ti}"
                dma("SP", t[:, :], xin[ti * 128:(ti + 1) * 128, :],
                    [f"H:xin:{ti}"], [f"tb{b}"], kt)
                ln_core(t[:, :], [f"tb{b}"], y[:, :], f"yb{b}", kt)
                dma("ACT", ys[ti * 128:(ti + 1) * 128, :], y[:, :],
                    [f"yb{b}"], [f"ys:{ti}"], f"{kt}_sy")
                ln_core(y[:, :], [f"yb{b}"], t2[:, :], f"t2b{b}", kt + "i")
                p.op("DVE", lambda e, t2=t2: e.tensor_mul(t2[:, :], t2[:, :],
                                                          cb_g1[:, :]),
                     reads=[f"t2b{b}", "C:g1"], writes=[f"t2b{b}"], tag=f"{kt}_g1")
                p.op("DVE", lambda e, t2=t2, h=h: e.tensor_add(h[:, :], t2[:, :],
                                                               cb_b1g[:, :]),
                     reads=[f"t2b{b}", "C:b1g"], writes=[f"hb{b}"], tag=f"{kt}_b1")
                transpose8(h[:, :], [f"hb{b}"], f"{kt}_tr")
                p.op("DVE", lambda e, i=i: e.tensor_copy(
                    hTc[:, :].rearrange("p (k n) -> p k n", k=KC)[:, :, i * 128:(i + 1) * 128],
                    PST[:, :].rearrange("p (k n) -> p k n", k=KC)),
                    reads=["pst"], writes=[f"hTc:{i}"], tag=f"{kt}_ev")
            hck = [f"hTc:{i}" for i in range(GT)]
            for m in range(16):
                bank = PSAK[m % 4]
                dst = qT_g if m < 8 else kT_g
                dk = ("qT" if m < 8 else "kT") + f"{gk}:{m % 8}"

                def qkmm(e, m=m):
                    inst = None
                    for k in range(KC):
                        inst = e.matmul(
                            PSA[:, (m % 4) * 512:(m % 4) * 512 + 512],
                            wqkv_sb[:, k * 3 * C + m * 128:k * 3 * C + (m + 1) * 128],
                            hTc[:, k * 512:(k + 1) * 512],
                            start=(k == 0), stop=(k == KC - 1))
                    return inst
                p.op("PE", qkmm, reads=hck + ["W:qkv"], writes=[bank], tag=f"{gk}qk{m}")
                p.op("ACT", lambda e, m=m, dst=dst: e.activation(
                    dst[:, (m % 8) * HGRP:(m % 8) * HGRP + 512],
                    PSA[:, (m % 4) * 512:(m % 4) * 512 + 512],
                    AF.Identity, bias=col_bqkv[:, m:m + 1]),
                    reads=[bank, "C:bqkv"], writes=[dk], tag=f"{gk}qke{m}")
            for i in range(GT):
                def vmm(e, i=i):
                    inst = None
                    for k in range(KC):
                        for nh_ in range(2):
                            inst = e.matmul(
                                PSB[:, nh_ * 512:(nh_ + 1) * 512],
                                hTc[:, k * 512 + i * 128:k * 512 + (i + 1) * 128],
                                wqkv_sb[:, k * 3 * C + 2 * C + nh_ * 512:
                                        k * 3 * C + 2 * C + (nh_ + 1) * 512],
                                start=(k == 0), stop=(k == KC - 1))
                    return inst
                p.op("PE", vmm, reads=hck + ["W:qkv"], writes=PSBK, tag=f"{gk}v{i}")
                p.op("DVE", lambda e, i=i: e.memset(
                    v65_g[i][:, :].rearrange("p (h w) -> p h w", h=NH)[:, :, HD:HD + 1],
                    1.0), writes=[f"v65:{i}"], tag=f"{gk}vo{i}")
                p.op("DVE", lambda e, i=i: e.scalar_tensor_tensor(
                    v65_g[i][:, :].rearrange("p (h w) -> p h w", h=NH)[:, :, 0:HD],
                    PSB[:, :].rearrange("p (h w) -> p h w", h=NH),
                    1.0, cb_bqv[:, :].rearrange("p (h w) -> p h w", h=NH),
                    ALU.mult, ALU.add),
                    reads=PSBK + ["C:bqv", f"v65:{i}"],
                    writes=[f"v65:{i}"], tag=f"{gk}ve{i}")
            qkkeys = [f"qT{gk}:{m}" for m in range(8)] + \
                     [f"kT{gk}:{m}" for m in range(8)]
            vkeys = [f"v65:{i}" for i in range(GT)]
            def hcol(h):
                return (h // 2) * 64 + (h % 2) * 512

            def SS(b):
                bi = b
                first = (g % GPB == 0) and bi == 0
                bk = f"{gk}b{bi}"
                pc = (bi % 2) * 64
                pp = 64 - pc

                def smm(e, bi=bi, first=first, pc=pc, pp=pp):
                    inst = None
                    for h in [0, 2, 4, 6, 8, 10, 12, 14, 1, 3, 5, 7, 9, 11, 13, 15]:
                        sI = h // 2
                        r = (h % 2) * 64
                        qsl = qT_g[r:r + 64, sI * HGRP + bi * 64:sI * HGRP + (bi + 1) * 64]
                        if first:
                            kp = zkT[r:r + 64, 0:64]
                        elif bi == 0:
                            kp = kT_c[r:r + 64, sI * BUCKET:(sI + 1) * BUCKET]
                        else:
                            kp = kT_g[r:r + 64, sI * HGRP + (bi - 1) * 64:sI * HGRP + bi * 64]
                        kc = kT_g[r:r + 64, sI * HGRP + bi * 64:sI * HGRP + (bi + 1) * 64]
                        inst = e.matmul(PSB[pp:pp + 64, hcol(h):hcol(h) + 64], kp, qsl,
                                        start=True, stop=True)
                        inst = e.matmul(PSB[pc:pc + 64, hcol(h):hcol(h) + 64], kc, qsl,
                                        start=True, stop=True)
                    return inst
                p.op("PE", smm, reads=qkkeys + ["zkT", "kTc"], writes=PSBK,
                     tag=f"{bk}_s")
                p.op("ACT", lambda e, b=b: e.activation(expb[b % 2][:, :], PSB[:, :],
                                                        AF.Exp, scale=SCALE_A),
                     reads=PSBK + ["Kc"], writes=[f"expb{b % 2}"], tag=f"{bk}_exp")
                p.op("DVE", lambda e, bi=bi, b=b: e.tensor_mul(
                    expb[b % 2][:, :], expb[b % 2][:, :],
                    sb_hmask[:, (bi % 2) * NH * BUCKET:(bi % 2 + 1) * NH * BUCKET]),
                    reads=[f"expb{b % 2}", "hmask"], writes=[f"expb{b % 2}"],
                    tag=f"{bk}_msk")

                def dmm(e, b=b):
                    e.matmul(PSB[0:1, 0:512], sb_onesb[:, 0:1], expb[b % 2][:, 0:512],
                             start=True, stop=True)
                    return e.matmul(PSB[0:1, 512:1024], sb_onesb[:, 0:1],
                                    expb[b % 2][:, 512:1024], start=True, stop=True)
                p.op("PE", dmm, reads=[f"expb{b % 2}", "onesb"], writes=PSBK,
                     tag=f"{bk}_d")
                p.op("DVE", lambda e: e.tensor_copy(lna[0:1, 0:1024], PSB[0:1, :]),
                     reads=PSBK, writes=["lnA"], tag=f"{bk}_rc0")
                p.op("ACT", lambda e: e.activation(lna[0:1, 1024:2048],
                                                   lna[0:1, 0:1024], AF.Ln),
                     reads=["lnA"], writes=["lnA2"], tag=f"{bk}_rc1")

                def rcf(e, b=b):
                    with nc.allow_low_precision(
                            reason="recip feeds a bf16-operand matmul; rounding "
                                   "here equals the matmul's own operand cast"):
                        return e.activation(rrow[b % 3][0:1, :],
                                            lna[0:1, 1024:2048], AF.Exp,
                                            scale=-1.0)
                p.op("ACT", rcf, reads=["lnA2", "Kc"], writes=[f"rrow{b % 3}"],
                     tag=f"{bk}_rc")

            def OM(b):
                bi = b
                first = (g % GPB == 0) and bi == 0
                bk = f"{gk}b{bi}"
                pc = (bi % 2) * 64
                pp = 64 - pc
                bko = (b % 2) * 2

                def omm(e, bi=bi, first=first, pc=pc, pp=pp, b=b, bko=bko):
                    inst = None
                    for grp in range(2):
                        for h in range(NH):
                            sI = h // 2
                            ro = (h % 2) * 64
                            co = (bko + grp) * 512 + sI * 64
                            if grp == 0:
                                if first:
                                    v = zv65[pp:pp + 64, h * 65:h * 65 + 64]
                                elif bi == 0:
                                    v = v65_c[64:128, h * 65:h * 65 + 64]
                                else:
                                    v = v65_g[(bi - 1) // 2][pp:pp + 64,
                                                             h * 65:h * 65 + 64]
                                rh = expb[b % 2][pp:pp + 64, hcol(h):hcol(h) + 64]
                            else:
                                v = v65_g[bi // 2][pc:pc + 64, h * 65:h * 65 + 64]
                                rh = expb[b % 2][pc:pc + 64, hcol(h):hcol(h) + 64]
                            inst = e.matmul(PSA[ro:ro + 64, co:co + 64], v, rh,
                                            start=True, stop=True)
                    return inst
                p.op("PE", omm, reads=[f"expb{b % 2}", "zv65", "v65c"] + vkeys,
                     writes=[PSAK[bko], PSAK[bko + 1]], tag=f"{bk}_o")

            def BM(b):
                bi = b
                bk = f"{gk}b{bi}"
                bko = (b % 2) * 2

                def bmm(e, b=b):
                    e.matmul(PSB[:, 0:512], sb_onesr[0:1, :],
                             rrow[b % 3][0:1, 0:512], start=True, stop=True)
                    return e.matmul(PSB[:, 512:1024], sb_onesr[0:1, :],
                                    rrow[b % 3][0:1, 512:1024], start=True, stop=True)
                p.op("PE", bmm, reads=[f"rrow{b % 3}", "onesr"],
                     writes=PSBK, tag=f"{bk}_bc")
                p.op("ACT", lambda e, b=b, bko=bko: e.activation(
                    orw[b % 2][0:64, 0:512], PSA[0:64, bko * 512:bko * 512 + 512],
                    AF.Copy),
                    reads=[PSAK[bko]], writes=[f"orwE{b % 2}"], tag=f"{bk}_ee")
                p.op("DVE", lambda e, b=b, bko=bko: e.scalar_tensor_tensor(
                    orw[b % 2][0:64, 0:512], PSA[0:64, (bko + 1) * 512:(bko + 2) * 512],
                    1.0, orw[b % 2][0:64, 0:512], ALU.mult, ALU.add),
                    reads=[PSAK[bko + 1], f"orwE{b % 2}"], writes=[f"orwE{b % 2}"],
                    tag=f"{bk}_ee2")
                p.op("ACT", lambda e, b=b, bko=bko: e.activation(
                    orw[b % 2][64:128, 0:512], PSA[64:128, bko * 512:bko * 512 + 512],
                    AF.Copy),
                    reads=[PSAK[bko]], writes=[f"orwO{b % 2}"], tag=f"{bk}_eo")
                p.op("DVE", lambda e, b=b, bko=bko: e.scalar_tensor_tensor(
                    orw[b % 2][64:128, 0:512], PSA[64:128, (bko + 1) * 512:(bko + 2) * 512],
                    1.0, orw[b % 2][64:128, 0:512], ALU.mult, ALU.add),
                    reads=[PSAK[bko + 1], f"orwO{b % 2}"], writes=[f"orwO{b % 2}"],
                    tag=f"{bk}_eo2")
                p.op("DVE", lambda e, bi=bi, b=b: e.tensor_tensor(
                    oT_g[0:64, :].rearrange("p (s n) -> p s n", s=KC)[:, :, bi * 64:(bi + 1) * 64],
                    orw[b % 2][0:64, 0:512].rearrange("p (s n) -> p s n", s=8),
                    PSB[0:64, 0:512].rearrange("p (s n) -> p s n", s=8),
                    ALU.mult),
                    reads=[f"orwE{b % 2}"] + PSBK, writes=[f"oT{gk}"],
                    tag=f"{bk}_ne")
                p.op("DVE", lambda e, bi=bi, b=b: e.tensor_tensor(
                    oT_g[64:128, :].rearrange("p (s n) -> p s n", s=KC)[:, :, bi * 64:(bi + 1) * 64],
                    orw[b % 2][64:128, 0:512].rearrange("p (s n) -> p s n", s=8),
                    PSB[64:128, 512:1024].rearrange("p (s n) -> p s n", s=8),
                    ALU.mult),
                    reads=[f"orwO{b % 2}"] + PSBK, writes=[f"oT{gk}"],
                    tag=f"{bk}_no")

            for b in range(GB + 2):
                if b < GB:
                    SS(b)
                if 1 <= b <= GB:
                    OM(b - 1)
                if b >= 2:
                    BM(b - 2)
            p.op("DVE", lambda e: e.tensor_copy(
                kT_c[:, :].rearrange("p (s n) -> p s n", s=KC),
                kT_g[:, :].rearrange("p (s n) -> p s n", s=KC)[:, :, HGRP - 64:HGRP]),
                reads=[f"kT{gk}:{m}" for m in range(8)], writes=["kTc"],
                tag=f"{gk}_kc")
            p.op("DVE", lambda e: e.tensor_copy(v65_c[64:128, :],
                                                v65_g[GT - 1][64:128, :]),
                 reads=[f"v65:{GT - 1}"], writes=["v65c"], tag=f"{gk}_vc")
            for i in range(GT):
                ti = g * GT + i
                b = i % 2
                kt = f"{gk}z{i}"

                def zmm(e, i=i):
                    inst = None
                    for k in range(KC):
                        for nh_ in range(2):
                            inst = e.matmul(
                                PSB[:, nh_ * 512:(nh_ + 1) * 512],
                                oT_g[:, k * HGRP + i * 128:k * HGRP + (i + 1) * 128],
                                wout_sb[:, k * C + nh_ * 512:k * C + (nh_ + 1) * 512],
                                start=(k == 0), stop=(k == KC - 1))
                    return inst
                p.op("PE", zmm, reads=[f"oT{gk}", "W:aout"], writes=PSBK, tag=kt)
                tb_, t2b_ = PK["T"][b], PK["T2"][b]
                dma("SP", tb_[:, :], ys[ti * 128:(ti + 1) * 128, :],
                    [f"ys:{ti}"], [f"tb{b}"], f"{kt}_ly")
                p.op("DVE", lambda e, t2b_=t2b_: e.scalar_tensor_tensor(
                    t2b_[:, :], PSB[:, :], 1.0, cb_bout[:, :],
                    ALU.mult, ALU.add),
                    reads=PSBK + ["C:bout"], writes=[f"t2b{b}"], tag=f"{kt}_b")
                p.op("DVE", lambda e, t2b_=t2b_, tb_=tb_: e.tensor_add(
                    t2b_[:, :], t2b_[:, :], tb_[:, :]),
                    reads=[f"t2b{b}", f"tb{b}"], writes=[f"t2b{b}"], tag=f"{kt}_y")
                dma("ACT", y2s[ti * 128:(ti + 1) * 128, :], t2b_[:, :],
                    [f"t2b{b}"], [f"y2s:{ti}"], f"{kt}_st")
        ar.pop()

        ar.push()
        alloc_pack()
        cb_g2 = ar.alloc(C, F32)
        cb_b2g = ar.alloc(C, F32)
        cb_b2 = ar.alloc(C, F32)
        col_b1 = ar.alloc(32, F32)
        mk_bcast(cb_g2, aa_g2.rearrange("(o n) -> o n", o=1), C, "g2")
        mk_bcast(cb_b2g, aa_b2g.rearrange("(o n) -> o n", o=1), C, "b2g")
        mk_bcast(cb_b2, aa_b2.rearrange("(o n) -> o n", o=1), C, "b2")
        mk_cols(col_b1, aa_b1, 32, "b1")
        w2_sb = ar.alloc((FF_A // 128) * C, BF16)
        w1p = [ar.alloc(KC * 128, BF16) for _ in range(4)]
        h2T = ar.alloc(KC * 512, BF16)
        m1T = ar.alloc((FF_A // 128) * 512, BF16)
        wchunks(w2_sb, "W:w2", aa_w2, FF_A, C, "aaw2")
        NF = FF_A // 128
        for ch in range(T // 512):
            ck = f"a4c{ch}"
            if "f1" in stages:
                emit_gz("f1", 9, "ACT")
            for i in range(4):
                ti = ch * 4 + i
                b = i % 2
                t, t2, h = PK["T"][b], PK["T2"][b], PK["H"][b]
                kt = f"a4{ti}"
                dma("SP", t[:, :], y2s[ti * 128:(ti + 1) * 128, :],
                    [f"y2s:{ti}"], [f"tb{b}"], kt)
                ln_core(t[:, :], [f"tb{b}"], t2[:, :], f"t2b{b}", kt)
                p.op("DVE", lambda e, t2=t2: e.tensor_mul(t2[:, :], t2[:, :],
                                                          cb_g2[:, :]),
                     reads=[f"t2b{b}", "C:g2"], writes=[f"t2b{b}"], tag=f"{kt}_g2")
                p.op("DVE", lambda e, t2=t2, h=h: e.tensor_add(h[:, :], t2[:, :],
                                                               cb_b2g[:, :]),
                     reads=[f"t2b{b}", "C:b2g"], writes=[f"hb{b}"], tag=f"{kt}_b2")
                transpose8(h[:, :], [f"hb{b}"], f"{kt}_tr")
                p.op("DVE", lambda e, i=i: e.tensor_copy(
                    h2T[:, :].rearrange("p (k n) -> p k n", k=KC)[:, :, i * 128:(i + 1) * 128],
                    PST[:, :].rearrange("p (k n) -> p k n", k=KC)),
                    reads=["pst"], writes=[f"h2T:{i}"], tag=f"{kt}_ev")
            hk = [f"h2T:{i}" for i in range(4)]
            for f in range(NF):
                wb = w1p[f % 4]
                wk = f"w1p{f % 4}"
                dma("POOL", wb[:, :].rearrange("p (k n) -> p k n", k=KC),
                    aa_w1.rearrange("(k p) n -> p k n", p=128)[:, :, f * 128:(f + 1) * 128],
                    ["H:w1"], [wk], f"{ck}w1_{f}")
                bank = PSAK[f % 4]

                def m1mm(e, f=f, wb=wb):
                    inst = None
                    for k in range(KC):
                        inst = e.matmul(PSA[:, (f % 4) * 512:(f % 4) * 512 + 512],
                                        wb[:, k * 128:(k + 1) * 128],
                                        h2T[:, k * 512:(k + 1) * 512],
                                        start=(k == 0), stop=(k == KC - 1))
                    return inst
                p.op("PE", m1mm, reads=hk + [wk], writes=[bank], tag=f"{ck}m1_{f}")
                p.op("ACT", lambda e, f=f: e.activation(
                    m1T[:, f * 512:(f + 1) * 512],
                    PSA[:, (f % 4) * 512:(f % 4) * 512 + 512],
                    AF.Gelu, bias=col_b1[:, f:f + 1]),
                    reads=[bank, "C:b1"], writes=[f"m1T:{f}"], tag=f"{ck}ge_{f}")
            m1k = [f"m1T:{f}" for f in range(NF)]
            for i in range(4):
                ti = ch * 4 + i
                b = i % 2

                def m2mm(e, i=i):
                    inst = None
                    for k in range(NF):
                        for nh_ in range(2):
                            inst = e.matmul(
                                PSB[:, nh_ * 512:(nh_ + 1) * 512],
                                m1T[:, k * 512 + i * 128:k * 512 + (i + 1) * 128],
                                w2_sb[:, k * C + nh_ * 512:k * C + (nh_ + 1) * 512],
                                start=(k == 0), stop=(k == NF - 1))
                    return inst
                kt = f"{ck}m2_{i}"
                p.op("PE", m2mm, reads=m1k + ["W:w2"], writes=PSBK, tag=kt)
                tb_, t2b_, yb_ = PK["T"][b], PK["T2"][b], PK["Y"][b]
                dma("SP", tb_[:, :], y2s[ti * 128:(ti + 1) * 128, :],
                    [f"y2s:{ti}"], [f"tb{b}"], f"{kt}_l2")
                dma("SP", yb_[:, :], xin[ti * 128:(ti + 1) * 128, :],
                    [f"H:xin:{ti}"], [f"yb{b}"], f"{kt}_l0")
                p.op("DVE", lambda e, t2b_=t2b_: e.scalar_tensor_tensor(
                    t2b_[:, :], PSB[:, :], 1.0, cb_b2[:, :],
                    ALU.mult, ALU.add),
                    reads=PSBK + ["C:b2"], writes=[f"t2b{b}"], tag=f"{kt}_b")
                p.op("DVE", lambda e, t2b_=t2b_, tb_=tb_: e.tensor_add(
                    t2b_[:, :], t2b_[:, :], tb_[:, :]),
                    reads=[f"t2b{b}", f"tb{b}"], writes=[f"t2b{b}"], tag=f"{kt}_y2")
                p.op("DVE", lambda e, t2b_=t2b_, yb_=yb_: e.tensor_add(
                    t2b_[:, :], t2b_[:, :], yb_[:, :]),
                    reads=[f"t2b{b}", f"yb{b}"], writes=[f"t2b{b}"], tag=f"{kt}_x0")
                dma("ACT", r1[ti * 128:(ti + 1) * 128, :], t2b_[:, :],
                    [f"t2b{b}"], [f"r1:{ti}"], f"{kt}_st")
        ar.pop()

    def ffn(tag, rin, rin_name, rout, rout_name, G, Geo, SHS):
        ar.push()
        emit_gz(tag, 10 ** 9, "ACT")
        fmask = ar.alloc(NTILES * NEXP, F32)
        rankw = ar.alloc(NTILES * NEXP, F32)
        desti = ar.alloc(NTILES, mybir.dt.int32)
        cntrow = ar.alloc(NTILES * NEXP, F32)
        baserow = ar.alloc(NTILES * NEXP, F32)
        cmat = ar.alloc(NEXP, F32)
        lgtall = ar.alloc(NTILES * NEXP, F32)
        mb0 = ar.alloc(NTILES * NEXP, F32)
        mb8 = ar.alloc(NTILES * NEXP, F32)
        mxsc = ar.alloc(2 * NTILES, F32)
        dtmp = ar.alloc(8 * NTILES, F32)
        PSCK2 = ["psc0", "psc1"]
        sl01 = [ar.alloc(16 * 1024, BF16) for _ in range(2)]
        wchunks(sl01[0], "slot0", f_sg, C, FF_E, f"{tag}shg")
        wchunks(sl01[1], "slot1", f_su, C, FF_E, f"{tag}shu")
        ar.push()
        xsall = ar.alloc(NTILES * C, BF16)
        alloc_pack()
        xstg = [ar.alloc(1024, BF16) for _ in range(2)]
        xtf = [ar.alloc(1024, F32) for _ in range(2)]
        for ti in range(NTILES):
            b = ti % 2
            t, y, h = PK["T"][b], PK["Y"][b], PK["H"][b]
            kt = f"{tag}L{ti}"
            xt = xtf[b]
            po = 1024 * b
            pk2 = [f"psa{2 * b}", f"psa{2 * b + 1}"]
            dma("SP", t[:, :], rin[ti * 128:(ti + 1) * 128, :],
                [f"{rin_name}:{ti}"], [f"tb{b}"], kt)
            ln_core(t[:, :], [f"tb{b}"], y[:, :], f"yb{b}", kt)
            p.op("PE", lambda e, y=y, po=po: [e.transpose(
                PSA[:, po + k * 128:po + (k + 1) * 128],
                y[:, k * 128:(k + 1) * 128], sb_idf[:, :]) for k in range(KC)][-1],
                reads=[f"yb{b}", "idf"], writes=pk2, tag=f"{kt}_ftr")
            p.op("DVE", lambda e, xt=xt, po=po: e.tensor_copy(
                xt[:, :], PSA[:, po:po + 1024]),
                reads=pk2, writes=[f"xtf{b}"], tag=f"{kt}_fev")

            def rmm(e, xt=xt, b=b):
                inst = None
                for k in range(KC):
                    inst = e.matmul(PSC[:, b * 8:b * 8 + 8],
                                    xt[:, k * 128:(k + 1) * 128],
                                    rwT[:, k * 8:(k + 1) * 8],
                                    start=(k == 0), stop=(k == KC - 1))
                return inst
            p.op("PE", rmm, reads=[f"xtf{b}", "C:rwT"], writes=[f"psc{b}"],
                 tag=f"{kt}_rt")
            p.op("DVE", lambda e, ti=ti, b=b: e.tensor_copy(
                lgtall[:, ti * 8:(ti + 1) * 8], PSC[:, b * 8:b * 8 + 8]),
                reads=[f"psc{b}"], writes=["lga"], tag=f"{kt}_lg")
            p.op("ACT", lambda e, y=y, ti=ti: e.activation(
                xsall[:, ti * C:(ti + 1) * C], y[:, :], AF.Copy),
                reads=[f"yb{b}"], writes=[f"xsa:{ti}"], tag=f"{kt}_cx")
            p.op("ACT", lambda e, y=y, h=h: e.activation(h[:, :], y[:, :], AF.Copy),
                 reads=[f"yb{b}"], writes=[f"hb{b}"], tag=f"{kt}_cy")
            transpose8(h[:, :], [f"hb{b}"], f"{kt}_t1")
            p.op("DVE", lambda e, b=b: e.tensor_copy(xstg[b][:, :], PST[:, :]),
                 reads=["pst"], writes=[f"xstg{b}"], tag=f"{kt}_e1")
            dma("ACT", xTs.rearrange("(k p) n -> p k n", p=128)[:, :, ti * 128:(ti + 1) * 128],
                xstg[b][:, :].rearrange("p (k n) -> p k n", k=KC),
                [f"xstg{b}"], [f"xTs:{ti}"], f"{kt}_s1")
        lg3 = lgtall[:, 0:NTILES * NEXP].rearrange("p (t e) -> p t e", e=NEXP)
        mx3 = mxsc[:, 0:NTILES].rearrange("p (t o) -> p t o", o=1)
        p.op("DVE", lambda e: e.reduce_max(mx3, lg3, axis=AX),
             reads=["lga"], writes=["mxc"], tag=f"{tag}mx")
        p.op("ACT", lambda e: e.activation(mxsc[:, NTILES:2 * NTILES],
                                           mxsc[:, 0:NTILES], AF.Sigmoid),
             reads=["mxc"], writes=["scal"], tag=f"{tag}sg")
        for ti in range(NTILES):
            g0, g1 = ti * NEXP, (ti + 1) * NEXP
            p.op("DVE", lambda e, g0=g0, g1=g1, ti=ti: e.tensor_single_scalar(
                mb0[:, g0:g1], lgtall[:, g0:g1], mxsc[:, ti:ti + 1], ALU.is_equal),
                reads=["lga", "mxc"], writes=["m0"], tag=f"{tag}eq{ti}")
            p.op("DVE", lambda e, g0=g0, g1=g1: e.tensor_tensor_scan(
                mb8[:, g0:g1], mb0[:, g0:g1], mb0[:, g0:g1], 0.0,
                ALU.add, ALU.bypass),
                reads=["m0"], writes=["m8"], tag=f"{tag}cs{ti}")
        p.op("DVE", lambda e: e.tensor_scalar(mb8[:, :], mb8[:, :], 1.0, None,
                                              ALU.is_equal),
             reads=["m8"], writes=["m8"], tag=f"{tag}c1")
        p.op("DVE", lambda e: e.tensor_mul(fmask[:, :], mb0[:, :], mb8[:, :]),
             reads=["m0", "m8"], writes=["fma"], tag=f"{tag}fm")
        p.op("PE", lambda e: e.matmul(PSC[:, 256:512], sb_ltri[:, :], fmask[:, :],
                                      start=True, stop=True),
             reads=["fma", "ltri"], writes=PSCK2, tag=f"{tag}rk")
        p.op("PE", lambda e: e.matmul(PSC[0:1, 0:256], sb_onesf[:, 0:1], fmask[:, :],
                                      start=True, stop=True),
             reads=["fma", "onesf"], writes=PSCK2, tag=f"{tag}ct")
        p.op("DVE", lambda e: e.tensor_copy(rankw[:, :], PSC[:, 256:512]),
             reads=PSCK2, writes=["rka"], tag=f"{tag}rke")
        p.op("DVE", lambda e: e.tensor_copy(cntrow[0:1, :], PSC[0:1, 0:256]),
             reads=PSCK2, writes=["cnt"], tag=f"{tag}cte")
        for ti in range(NTILES):
            def xsf(e, ti=ti):
                with nc.allow_low_precision(
                        reason="xs feeds bf16-operand matmuls; rounding here "
                               "equals the matmul's own operand cast"):
                    return e.tensor_scalar(
                        xsall[:, ti * C:(ti + 1) * C], xsall[:, ti * C:(ti + 1) * C],
                        mxsc[:, NTILES + ti:NTILES + ti + 1], None, ALU.mult)
            p.op("DVE", xsf,
                 reads=[f"xsa:{ti}", "scal"], writes=[f"xsa:{ti}"], tag=f"{tag}xs{ti}")
        dma("SP", cmat[0:NTILES, 0:NEXP],
            cntrow[0:1, 0:NTILES * NEXP].rearrange("o (t e) -> o t e", e=NEXP),
            ["cnt"], ["cmat"], f"{tag}cm")
        p.op("PE", lambda e: e.matmul(PSC[0:NTILES, 8:16], sb_ltri[0:NTILES, 0:NTILES],
                                      cmat[0:NTILES, 0:NEXP], start=True, stop=True),
             reads=["cmat", "ltri"], writes=PSCK2, tag=f"{tag}bs")
        p.op("DVE", lambda e: e.tensor_copy(cmat[0:NTILES, 0:NEXP], PSC[0:NTILES, 8:16]),
             reads=PSCK2, writes=["cmat2"], tag=f"{tag}bse")
        dma("SP", baserow[0:1, 0:NTILES * NEXP].rearrange("o (t e) -> o t e", e=NEXP),
            cmat[0:NTILES, 0:NEXP], ["cmat2"], ["baserow"], f"{tag}br")
        p.op("PE", lambda e: e.matmul(PSC[:, 0:256], sb_ones[0:1, :],
                                      baserow[0:1, 0:256], start=True, stop=True),
             reads=["baserow", "ones"], writes=PSCK2, tag=f"{tag}bb")

        def d3(c):
            return dtmp[:, c * NTILES:(c + 1) * NTILES]

        def t3(buf):
            return buf[:, :].rearrange("p (t e) -> p t e", e=NEXP)

        def dr3(c):
            return d3(c).rearrange("p (t o) -> p t o", o=1)
        p.op("DVE", lambda e: e.tensor_mul(mb0[:, :], rankw[:, :], fmask[:, :]),
             reads=["rka", "fma"], writes=["m0"], tag=f"{tag}rs")
        p.op("DVE", lambda e: e.reduce_sum(dr3(0), t3(mb0), axis=AX),
             reads=["m0"], writes=["dR"], tag=f"{tag}rs2")
        p.op("DVE", lambda e: e.tensor_mul(mb8[:, :], PSC[:, 0:256], fmask[:, :]),
             reads=PSCK2 + ["fma"], writes=["m8"], tag=f"{tag}bsl")
        p.op("DVE", lambda e: e.reduce_sum(dr3(1), t3(mb8), axis=AX),
             reads=["m8"], writes=["dB"], tag=f"{tag}bsl2")
        p.op("DVE", lambda e: e.tensor_mul(mb0[:, :], sb_iota256[:, :], fmask[:, :]),
             reads=["iota256", "fma"], writes=["m0"], tag=f"{tag}cp")
        p.op("DVE", lambda e: e.reduce_sum(dr3(2), t3(mb0), axis=AX),
             reads=["m0"], writes=["dE"], tag=f"{tag}cp2")
        p.op("DVE", lambda e: e.tensor_scalar(d3(2), d3(2), float(CAP), None,
                                              ALU.mult),
             reads=["dE"], writes=["dE"], tag=f"{tag}cpm")
        p.op("DVE", lambda e: e.tensor_add(d3(3), d3(0), d3(1)),
             reads=["dR", "dB"], writes=["dS"], tag=f"{tag}rb")
        p.op("DVE", lambda e: e.tensor_single_scalar(d3(4), d3(3), float(CAP),
                                                     ALU.is_ge),
             reads=["dS"], writes=["dO"], tag=f"{tag}ov")
        p.op("DVE", lambda e: e.tensor_add(d3(5), d3(3), d3(2)),
             reads=["dS", "dE"], writes=["dF"], tag=f"{tag}df")
        p.op("DVE", lambda e: e.tensor_scalar(d3(6), d3(5), -1.0, float(NEXP * CAP),
                                              ALU.mult, ALU.add),
             reads=["dF"], writes=["dT"], tag=f"{tag}tr")
        p.op("DVE", lambda e: e.tensor_add(d3(6), d3(6), sb_iota32[:, 0:NTILES]),
             reads=["dT", "iota32"], writes=["dT"], tag=f"{tag}tr2")
        p.op("DVE", lambda e: e.tensor_mul(d3(4), d3(4), d3(6)),
             reads=["dO", "dT"], writes=["dO"], tag=f"{tag}cl")
        p.op("DVE", lambda e: e.tensor_add(d3(7), d3(4), d3(5)),
             reads=["dO", "dF"], writes=["dPre"], tag=f"{tag}cl2")
        p.op("DVE", lambda e: e.tensor_copy(desti[:, 0:NTILES], d3(7)),
             reads=["dPre"], writes=["diall"], tag=f"{tag}ci")
        for ti in range(NTILES):
            p.op("POOL", lambda e, ti=ti: e.indirect_dma_start(
                G, bass.IndirectOffsetOnAxis(ap=desti[:, ti:ti + 1], axis=0),
                xsall[:, ti * C:(ti + 1) * C], None),
                reads=["diall", f"xsa:{ti}", "Ginit"], writes=[f"H:G:{ti}"],
                dma=True, tag=f"{tag}sc{ti}")
        ar.pop()

        slots = sl01 + [ar.alloc(16 * 1024, BF16)]
        hT = ar.alloc(16 * 512, BF16)
        xch = ar.alloc(KC * 512, BF16)
        stile = [ar.alloc(512, BF16) for _ in range(2)]
        ebuf = [ar.alloc(1024, BF16) for _ in range(2)]
        xg = [ar.alloc(1024, BF16) for _ in range(4)]

        def expert_block(en, wg, wu, wd, nch, load_x, out_rows, out_name,
                         skip_gu=False):
            ek = f"{tag}{en}"
            sg, su, sd = slots
            kg, ku, kd = "slot0", "slot1", "slot2"
            if not skip_gu:
                wchunks(sg, kg, wg, C, FF_E, f"{ek}g")
                wchunks(su, ku, wu, C, FF_E, f"{ek}u")
            wchunks(sd, kd, wd, FF_E, C, f"{ek}d")
            for ch in range(nch):
                load_x(ek, ch)
                for f in range(16):
                    st = stile[f % 2]
                    stk = f"stile{f % 2}"
                    gco = (f % 2) * 1024
                    uco = (f % 2) * 1024 + 512
                    gk2 = PSAK[(f % 2) * 2]
                    uk2 = PSAK[(f % 2) * 2 + 1]

                    def gmm(e, f=f, sg=sg, gco=gco):
                        inst = None
                        for k in range(KC):
                            inst = e.matmul(
                                PSA[:, gco:gco + 512],
                                sg[:, k * FF_E + f * 128:k * FF_E + (f + 1) * 128],
                                xch[:, k * 512:(k + 1) * 512],
                                start=(k == 0), stop=(k == KC - 1))
                        return inst

                    def umm(e, f=f, su=su, uco=uco):
                        inst = None
                        for k in range(KC):
                            inst = e.matmul(
                                PSA[:, uco:uco + 512],
                                su[:, k * FF_E + f * 128:k * FF_E + (f + 1) * 128],
                                xch[:, k * 512:(k + 1) * 512],
                                start=(k == 0), stop=(k == KC - 1))
                        return inst
                    p.op("PE", gmm, reads=[kg, "xch"], writes=[gk2], tag=f"{ek}g{ch}_{f}")
                    p.op("PE", umm, reads=[ku, "xch"], writes=[uk2], tag=f"{ek}u{ch}_{f}")
                    p.op("ACT", lambda e, st=st, gco=gco: e.activation(
                        st[:, :], PSA[:, gco:gco + 512], AF.Silu),
                        reads=[gk2], writes=[stk], tag=f"{ek}s{ch}_{f}")
                    p.op("DVE", lambda e, f=f, st=st, uco=uco: e.tensor_tensor(
                        hT[:, f * 512:(f + 1) * 512], st[:, :],
                        PSA[:, uco:uco + 512], ALU.mult),
                        reads=[stk, uk2], writes=[f"hT:{f}"], tag=f"{ek}h{ch}_{f}")
                hk = [f"hT:{f}" for f in range(16)]
                for it in range(4):
                    eb2 = ebuf[it % 2]
                    ebk = f"ebuf{it % 2}"
                    for nh_ in range(2):
                        bkk = PSBK[nh_]

                        def dnmm(e, it=it, sd=sd, nh_=nh_):
                            inst = None
                            for k in range(16):
                                inst = e.matmul(
                                    PSB[:, nh_ * 512:(nh_ + 1) * 512],
                                    hT[:, k * 512 + it * 128:k * 512 + (it + 1) * 128],
                                    sd[:, k * C + nh_ * 512:k * C + (nh_ + 1) * 512],
                                    start=(k == 0), stop=(k == 15))
                            return inst
                        p.op("PE", dnmm, reads=hk + [kd], writes=[bkk],
                             tag=f"{ek}d{ch}_{it}_{nh_}")
                        def dnev(e, nh_=nh_, eb2=eb2):
                            with nc.allow_low_precision(
                                    reason="ffn branch outputs bounce via HBM "
                                           "in bf16; same rounding class as "
                                           "the bf16 matmul operands"):
                                return e.tensor_copy(
                                    eb2[:, nh_ * 512:(nh_ + 1) * 512],
                                    PSB[:, nh_ * 512:(nh_ + 1) * 512])
                        p.op("DVE", dnev,
                             reads=[bkk], writes=[ebk + f":{nh_}"],
                             tag=f"{ek}e{ch}_{it}_{nh_}")
                    r0 = out_rows(ch, it)
                    dma("ACT", r0, eb2[:, :], [ebk + ":0", ebk + ":1"],
                        [f"{out_name}:{en}:{ch}:{it}"], f"{ek}o{ch}_{it}")

        def shared_loadx(ek, ch):
            dma("SP", xch[:, :].rearrange("p (k n) -> p k n", k=KC),
                xTs.rearrange("(k p) n -> p k n", p=128)[:, :, ch * 512:(ch + 1) * 512],
                [f"xTs:{tt}" for tt in range(ch * 4, ch * 4 + 4)],
                ["xch"], f"{ek}x{ch}")
        expert_block("sh", f_sg, f_su, f_sd, 8, shared_loadx,
                     lambda ch, it: SHS[(ch * 4 + it) * 128:(ch * 4 + it + 1) * 128, :],
                     "H:SHS", skip_gu=True)
        shs_keys = [f"H:SHS:sh:{ch}:{it}" for ch in range(8) for it in range(4)]

        for x in range(NEXP):
            def exp_loadx(ek, ch, x=x):
                for i in range(4):
                    b = i % 2
                    row0 = x * CAP + ch * 512 + i * 128
                    dma("SP", xg[b][:, :], G[row0:row0 + 128, :],
                        [f"H:G:{tt}" for tt in range(NTILES)], [f"xg{b}"],
                        f"{ek}xl{ch}_{i}")
                    transpose8(xg[b][:, :], [f"xg{b}"], f"{ek}xt{ch}_{i}")
                    p.op("DVE", lambda e, i=i: e.tensor_copy(
                        xch[:, :].rearrange("p (k n) -> p k n", k=KC)[:, :, i * 128:(i + 1) * 128],
                        PST[:, :].rearrange("p (k n) -> p k n", k=KC)),
                        reads=["pst"], writes=["xch"], tag=f"{ek}xe{ch}_{i}")
            expert_block(f"e{x}", f_eg[x], f_eu[x], f_ed[x], CAP // 512, exp_loadx,
                         lambda ch, it, x=x: Geo[x * CAP + (ch * 4 + it) * 128:
                                                 x * CAP + (ch * 4 + it + 1) * 128, :],
                         "H:Geo")
        geo_keys = [f"H:Geo:e{x}:{ch}:{it}" for x in range(NEXP)
                    for ch in range(CAP // 512) for it in range(4)]

        ar.push()
        eb = [ar.alloc(1024, F32) for _ in range(2)] + \
            [ar.alloc(1024, BF16) for _ in range(2)]
        mg = [ar.alloc(1024, BF16) for _ in range(2)]
        for ti in range(NTILES):
            b = ti % 2
            kt = f"{tag}E{ti}"
            dma("SP", eb[b][:, :], rin[ti * 128:(ti + 1) * 128, :],
                [f"{rin_name}:{ti}"], [f"ebl{b}"], kt)
            dma("SP", eb[2 + b][:, :], SHS[ti * 128:(ti + 1) * 128, :],
                shs_keys, [f"ebs{b}"], f"{kt}_sh")
            p.op("POOL", lambda e, ti=ti, b=b: e.indirect_dma_start(
                mg[b][:, :], None, Geo,
                bass.IndirectOffsetOnAxis(ap=desti[:, ti:ti + 1], axis=0)),
                reads=geo_keys + ["Geoinit", "diall"], writes=[f"mg{b}"],
                dma=True, tag=f"{kt}_gt")
            p.op("DVE", lambda e, b=b: e.tensor_add(eb[b][:, :], eb[b][:, :],
                                                    eb[2 + b][:, :]),
                 reads=[f"ebl{b}", f"ebs{b}"], writes=[f"ebl{b}"], tag=f"{kt}_a1")
            p.op("DVE", lambda e, b=b: e.tensor_add(eb[b][:, :], eb[b][:, :],
                                                    mg[b][:, :]),
                 reads=[f"ebl{b}", f"mg{b}"], writes=[f"ebl{b}"], tag=f"{kt}_a2")
            dma("ACT", rout[ti * 128:(ti + 1) * 128, :], eb[b][:, :],
                [f"ebl{b}"], [f"{rout_name}:{ti}"], f"{kt}_st")
        ar.pop()
        ar.pop()

    if "f1" in stages:
        ffn("f1", r1, "r1", r2, "r2", g1s, g1eo, sh1s)

    if "ab" in stages:
        ar.push()
        kT_b = ar.alloc(KC * SEQ, BF16)
        v65_b = [ar.alloc(V65W, BF16) for _ in range(SEQ // 128)]
        for bb in range(2):
            bt0 = bb * 16
            bk0 = f"ab{bb}"
            vkeys = [f"v65b:{t}" for t in range(16)]
            kkeys = [f"kTb:{m}" for m in range(8)]
            ar.push()
            alloc_pack()
            xTb = ar.alloc(KC * 512, BF16)
            qstg = ar.alloc(KC * 512, BF16)
            wring = [ar.alloc(KC * C, BF16) for _ in range(2)]
            wseq = 0
            for ch in range(4):
                for i in range(4):
                    ti = bt0 + ch * 4 + i
                    b = i % 2
                    t, y, h = PK["T"][b], PK["Y"][b], PK["H"][b]
                    kt = f"{bk0}p{ti}"
                    dma("SP", t[:, :], r2[ti * 128:(ti + 1) * 128, :],
                        [f"r2:{ti}"], [f"tb{b}"], kt)
                    ln_core(t[:, :], [f"tb{b}"], y[:, :], f"yb{b}", kt)
                    p.op("ACT", lambda e, y=y, h=h: e.activation(h[:, :], y[:, :],
                                                                 AF.Copy),
                         reads=[f"yb{b}"], writes=[f"hb{b}"], tag=f"{kt}_c")
                    transpose8(h[:, :], [f"hb{b}"], f"{kt}_tr")
                    p.op("DVE", lambda e, i=i: e.tensor_copy(
                        xTb[:, :].rearrange("p (k n) -> p k n", k=KC)[:, :, i * 128:(i + 1) * 128],
                        PST[:, :].rearrange("p (k n) -> p k n", k=KC)),
                        reads=["pst"], writes=[f"xTb:{i}"], tag=f"{kt}_ev")
                xbk = [f"xTb:{i}" for i in range(4)]
                for wsrc, wtag in [(ab_wq, "q"), (ab_wk, "k"), (ab_wv, "v")]:
                    wb = wring[wseq % 2]
                    wk = f"wring{wseq % 2}"
                    wseq += 1
                    wchunks(wb, wk, wsrc, C, C, f"{bk0}w{wtag}{ch}")
                    if wtag != "v":
                        for m in range(8):
                            def pmm(e, m=m, wb=wb):
                                inst = None
                                for k in range(KC):
                                    inst = e.matmul(
                                        PSA[:, (m % 4) * 512:(m % 4) * 512 + 512],
                                        wb[:, k * C + m * 128:k * C + (m + 1) * 128],
                                        xTb[:, k * 512:(k + 1) * 512],
                                        start=(k == 0), stop=(k == KC - 1))
                                return inst
                            p.op("PE", pmm, reads=xbk + [wk], writes=[PSAK[m % 4]],
                                 tag=f"{bk0}{wtag}{ch}_{m}")
                            if wtag == "q":
                                p.op("ACT", lambda e, m=m: e.activation(
                                    qstg[:, m * 512:(m + 1) * 512],
                                    PSA[:, (m % 4) * 512:(m % 4) * 512 + 512],
                                    AF.Copy, scale=SCALE_B),
                                    reads=[PSAK[m % 4], "Kc"], writes=[f"qstg:{m}"],
                                    tag=f"{bk0}qe{ch}_{m}")
                            else:
                                p.op("ACT", lambda e, m=m, ch=ch: e.activation(
                                    kT_b[:, m * SEQ + ch * 512:m * SEQ + ch * 512 + 512],
                                    PSA[:, (m % 4) * 512:(m % 4) * 512 + 512],
                                    AF.Copy, scale=SCALE_B),
                                    reads=[PSAK[m % 4], "Kc"], writes=[f"kTb:{m}"],
                                    tag=f"{bk0}ke{ch}_{m}")
                        if wtag == "q":
                            dma("ACT", qTs.rearrange("(k p) n -> p k n", p=128)
                                [:, :, bt0 * 128 + ch * 512:bt0 * 128 + (ch + 1) * 512],
                                qstg[:, :].rearrange("p (k n) -> p k n", k=KC),
                                [f"qstg:{m}" for m in range(8)], ["qstg_out"],
                                f"{bk0}qs{ch}")
                    else:
                        for i in range(4):
                            vt = ch * 4 + i

                            def vmmb(e, i=i, wb=wb):
                                inst = None
                                for k in range(KC):
                                    for nh_ in range(2):
                                        inst = e.matmul(
                                            PSB[:, nh_ * 512:(nh_ + 1) * 512],
                                            xTb[:, k * 512 + i * 128:k * 512 + (i + 1) * 128],
                                            wb[:, k * C + nh_ * 512:k * C + (nh_ + 1) * 512],
                                            start=(k == 0), stop=(k == KC - 1))
                                return inst
                            p.op("PE", vmmb, reads=xbk + [wk], writes=PSBK,
                                 tag=f"{bk0}v{vt}")
                            p.op("DVE", lambda e, vt=vt: e.memset(
                                v65_b[vt][:, :].rearrange("p (h w) -> p h w", h=NH)
                                [:, :, HD:HD + 1], 1.0),
                                writes=[f"v65b:{vt}"], tag=f"{bk0}vo{vt}")
                            p.op("DVE", lambda e, vt=vt: e.tensor_copy(
                                v65_b[vt][:, :].rearrange("p (h w) -> p h w", h=NH)[:, :, 0:HD],
                                PSB[:, :].rearrange("p (h w) -> p h w", h=NH)),
                                reads=PSBK + [f"v65b:{vt}"], writes=[f"v65b:{vt}"],
                                tag=f"{bk0}ve{vt}")
            ar.pop()
            ar.push()
            qTc = [ar.alloc(KC * 512, BF16) for _ in range(2)]
            oTc = [ar.alloc(KC * 512, BF16) for _ in range(2)]
            expt = [ar.alloc(16 * 512, BF16) for _ in range(2)]
            orwB = [ar.alloc(512, F32) for _ in range(2)]
            rrowB = [ar.alloc(512, F32) for _ in range(2)]
            lnb = ar.alloc(1024, F32)
            shft = [ar.alloc(512, BF16) for _ in range(2)]
            J = 4 * NH

            def qload(ic):
                dma("SP", qTc[ic % 2][:, :].rearrange("p (k n) -> p k n", k=KC),
                    qTs.rearrange("(k p) n -> p k n", p=128)
                    [:, :, bt0 * 128 + ic * 512:bt0 * 128 + (ic + 1) * 512],
                    ["qstg_out"], [f"qTc{ic % 2}"], f"{bk0}lq{ic}")

            def S(j):
                ic, h = j // NH, j % NH
                sI = h // 2
                r = (h % 2) * 64
                akt = f"{bk0}i{ic}h{h}"
                qsl = qTc[ic % 2][r:r + 64, sI * 512:(sI + 1) * 512]
                for w in range(8):
                    pk2 = [PSAK[(w % 2) * 2], PSAK[(w % 2) * 2 + 1]]

                    def swv(e, w=w, r=r, sI=sI, qsl=qsl):
                        inst = None
                        for jj in range(2):
                            jt = w * 2 + jj
                            co = (w % 2) * 1024 + jj * 512
                            inst = e.matmul(
                                PSA[:, co:co + 512],
                                kT_b[r:r + 64, sI * SEQ + jt * 128:sI * SEQ + (jt + 1) * 128],
                                qsl, start=True, stop=True)
                        return inst
                    p.op("PE", swv, reads=kkeys + [f"qTc{ic % 2}"], writes=pk2,
                         tag=f"{akt}_s{w}")

                    def ewv(e, w=w, j=j):
                        inst = None
                        for jj in range(2):
                            jt = w * 2 + jj
                            co = (w % 2) * 1024 + jj * 512
                            inst = e.activation(
                                expt[j % 2][:, jt * 512:(jt + 1) * 512],
                                PSA[:, co:co + 512], AF.Exp)
                        return inst
                    p.op("ACT", ewv, reads=pk2, writes=[f"ex{j % 2}:{w}"],
                         tag=f"{akt}_e{w}")

            def O(j):
                ic, h = j // NH, j % NH
                akt = f"{bk0}i{ic}h{h}"
                ek = [f"ex{j % 2}:{w}" for w in range(8)]
                pco = (j % 2) * 512

                def ommb(e, h=h, j=j, pco=pco):
                    inst = None
                    for jt in range(16):
                        inst = e.matmul(PSB[0:65, pco:pco + 512],
                                        v65_b[jt][:, h * 65:h * 65 + 65],
                                        expt[j % 2][:, jt * 512:(jt + 1) * 512],
                                        start=(jt == 0), stop=(jt == 15))
                    return inst
                p.op("PE", ommb, reads=ek + vkeys, writes=[PSBK[j % 2]],
                     tag=f"{akt}_o")
                p.op("DVE", lambda e, pco=pco: e.tensor_copy(
                    lnb[0:1, 0:512], PSB[64:65, pco:pco + 512]),
                    reads=[PSBK[j % 2]], writes=["lnB"], tag=f"{akt}_rc0")
                p.op("ACT", lambda e: e.activation(lnb[0:1, 512:1024],
                                                   lnb[0:1, 0:512], AF.Ln),
                     reads=["lnB"], writes=["lnB2"], tag=f"{akt}_rc1")
                p.op("ACT", lambda e, j=j: e.activation(
                    rrowB[j % 2][0:1, :], lnb[0:1, 512:1024], AF.Exp, scale=-1.0),
                    reads=["lnB2", "Kc"], writes=[f"rrB{j % 2}"], tag=f"{akt}_rc")

            def B(j):
                ic, h = j // NH, j % NH
                sI = h // 2
                akt = f"{bk0}i{ic}h{h}"
                pco = (j % 2) * 512
                p.op("PE", lambda e, j=j: e.matmul(
                    PSC[0:64, :], sb_ones[0:1, 0:64], rrowB[j % 2][0:1, :],
                    start=True, stop=True),
                    reads=[f"rrB{j % 2}", "ones"], writes=["psc"], tag=f"{akt}_bc")
                p.op("ACT", lambda e, j=j, pco=pco: e.activation(
                    orwB[j % 2][0:64, :], PSB[0:64, pco:pco + 512], AF.Copy),
                    reads=[PSBK[j % 2]], writes=[f"orB{j % 2}"], tag=f"{akt}_ev")
                if h % 2 == 0:
                    p.op("DVE", lambda e, sI=sI, j=j, ic=ic: e.tensor_tensor(
                        oTc[ic % 2][0:64, sI * 512:(sI + 1) * 512],
                        orwB[j % 2][0:64, :], PSC[0:64, :], ALU.mult),
                        reads=[f"orB{j % 2}", "psc"], writes=[f"oTc{ic % 2}:{sI}:0"],
                        tag=f"{akt}_n")
                else:
                    p.op("DVE", lambda e, j=j: e.tensor_tensor(
                        shft[j % 2][0:64, :], orwB[j % 2][0:64, :], PSC[0:64, :],
                        ALU.mult),
                        reads=[f"orB{j % 2}", "psc"], writes=[f"shft{j % 2}"],
                        tag=f"{akt}_n")
                    dma("SP", oTc[ic % 2][64:128, sI * 512:(sI + 1) * 512],
                        shft[j % 2][0:64, :], [f"shft{j % 2}"],
                        [f"oTc{ic % 2}:{sI}:1"], f"{akt}_sh")

            for j in range(J + 2):
                if "f2" in stages:
                    emit_gz("f2", 1, "SP")
                if j < J and j % NH == 0:
                    qload(j // NH)
                if j < J:
                    S(j)
                if j >= 1:
                    O(j - 1)
                if j >= 2:
                    B(j - 2)
                    if (j - 2) % NH == NH - 1:
                        ic = (j - 2) // NH
                        okeys = [f"oTc{ic % 2}:{ss}:{par}" for ss in range(8)
                                 for par in range(2)]
                        dma("ACT", oTs.rearrange("(k p) n -> p k n", p=128)
                            [:, :, bt0 * 128 + ic * 512:bt0 * 128 + (ic + 1) * 512],
                            oTc[ic % 2][:, :].rearrange("p (k n) -> p k n", k=KC),
                            okeys, ["oTs_out"], f"{bk0}so{ic}")
            ar.pop()
            ar.push()
            cb_bo = ar.alloc(C, F32)
            mk_bcast(cb_bo, ab_bo.rearrange("(o n) -> o n", o=1), C, "bo")
            wo_sb = ar.alloc(KC * C, BF16)
            oTt = [ar.alloc(KC * 128, BF16) for _ in range(2)]
            zb = [ar.alloc(1024, F32) for _ in range(4)]
            wchunks(wo_sb, "W:bwo", ab_wo, C, C, "abwo")
            for i in range(16):
                ti = bt0 + i
                b = i % 2
                kt = f"{bk0}z{i}"
                dma("SP", oTt[b][:, :].rearrange("p (k n) -> p k n", k=KC),
                    oTs.rearrange("(k p) n -> p k n", p=128)
                    [:, :, ti * 128:(ti + 1) * 128],
                    ["oTs_out"], [f"oTt{b}"], f"{kt}_lo")

                def zmmb(e, b=b):
                    inst = None
                    for k in range(KC):
                        for nh_ in range(2):
                            inst = e.matmul(
                                PSB[:, nh_ * 512:(nh_ + 1) * 512],
                                oTt[b][:, k * 128:(k + 1) * 128],
                                wo_sb[:, k * C + nh_ * 512:k * C + (nh_ + 1) * 512],
                                start=(k == 0), stop=(k == KC - 1))
                    return inst
                p.op("PE", zmmb, reads=[f"oTt{b}", "W:bwo"], writes=PSBK, tag=kt)
                dma("SP", zb[b][:, :], r2[ti * 128:(ti + 1) * 128, :],
                    [f"r2:{ti}"], [f"zbl{b}"], f"{kt}_l")
                p.op("DVE", lambda e, b=b: e.scalar_tensor_tensor(
                    zb[2 + b][:, :], PSB[:, :], 1.0, cb_bo[:, :], ALU.mult, ALU.add),
                    reads=PSBK + ["C:bo"], writes=[f"zbo{b}"], tag=f"{kt}_b")
                p.op("DVE", lambda e, b=b: e.tensor_add(zb[2 + b][:, :],
                                                        zb[2 + b][:, :], zb[b][:, :]),
                     reads=[f"zbo{b}", f"zbl{b}"], writes=[f"zbo{b}"], tag=f"{kt}_r")
                dma("ACT", r3[ti * 128:(ti + 1) * 128, :], zb[2 + b][:, :],
                    [f"zbo{b}"], [f"r3:{ti}"], f"{kt}_st")
            ar.pop()
        ar.pop()

    if "f2" in stages:
        ffn("f2", r3, "r3", out, "out", g2s, g2eo, sh2s)
    else:
        last = {"ab": (r3, "r3"), "f1": (r2, "r2"), "aa": (r1, "r1")}
        for stg_ in ("ab", "f1", "aa"):
            if stg_ in stages:
                src, sname = last[stg_]
                ar.push()
                cp = [ar.alloc(1024, F32) for _ in range(2)]
                for ti in range(NTILES):
                    b = ti % 2
                    dma("SP", cp[b][:, :], src[ti * 128:(ti + 1) * 128, :],
                        [f"{sname}:{ti}"], [f"cp{b}"], f"cp{ti}")
                    dma("ACT", out[ti * 128:(ti + 1) * 128, :], cp[b][:, :],
                        [f"cp{b}"], [f"out:{ti}"], f"cp{ti}o")
                ar.pop()
                break

    p.build(nc, es)
    es.close()
    return p



def make_inputs(inputs, core):
    x = inputs["L0_attn_a_x"]
    shard = np.ascontiguousarray(x[2 * core:2 * core + 2]).reshape(T, C)
    ident_f = np.eye(128, dtype=np.float32)
    ident_b = np.eye(128, dtype=ml_dtypes.bfloat16)
    j = np.arange(64)[:, None]
    i = np.arange(64)[None, :]
    causal = (j <= i).astype(np.float32)
    ones = np.ones((64, 64), np.float32)
    mA = np.concatenate([causal, ones], axis=0)
    mB = np.concatenate([ones, causal], axis=0)
    hmask = np.concatenate([np.tile(mA, (1, NH)), np.tile(mB, (1, NH))], axis=1)
    iota256 = np.tile(np.tile(np.arange(NEXP, dtype=np.float32), 32)[None, :],
                      (128, 1))
    iota32 = np.tile(np.arange(32, dtype=np.float32)[None, :], (128, 1))
    ltri = np.triu(np.ones((128, 128), np.float32), k=1)
    return {
        "xin": shard,
        "aa_wqkv": inputs["L0_attn_a_w_qkv"], "aa_bqkv": inputs["L0_attn_a_b_qkv"],
        "aa_wout": inputs["L0_attn_a_w_out"], "aa_bout": inputs["L0_attn_a_b_out"],
        "aa_g1": inputs["L0_attn_a_ln1_g"], "aa_b1g": inputs["L0_attn_a_ln1_b"],
        "aa_g2": inputs["L0_attn_a_ln2_g"], "aa_b2g": inputs["L0_attn_a_ln2_b"],
        "aa_w1": inputs["L0_attn_a_w1"], "aa_b1": inputs["L0_attn_a_b1"],
        "aa_w2": inputs["L0_attn_a_w2"], "aa_b2": inputs["L0_attn_a_b2"],
        "ab_wq": inputs["L0_attn_b_Wq"], "ab_wk": inputs["L0_attn_b_Wk"],
        "ab_wv": inputs["L0_attn_b_Wv"], "ab_wo": inputs["L0_attn_b_Wo"],
        "ab_bo": inputs["L0_attn_b_bo"],
        "f_rw": inputs["L0_ffn_router_w"],
        "f_eg": inputs["L0_ffn_expert_gate"], "f_eu": inputs["L0_ffn_expert_up"],
        "f_ed": inputs["L0_ffn_expert_down"],
        "f_sg": inputs["L0_ffn_shared_gate"], "f_su": inputs["L0_ffn_shared_up"],
        "f_sd": inputs["L0_ffn_shared_down"],
        "id_f": ident_f, "id_b": ident_b, "hmask": hmask, "iota256": iota256,
        "iota32": iota32, "ltri": ltri,
    }


def kernel(**inputs):
    nc = bass.Bass("TRN2", target_bir_lowering=False)
    build_program(nc)
    nc.finalize()
    in_maps = [make_inputs(inputs, c) for c in range(8)]
    res = run_bass_kernel_spmd(nc, in_maps, core_ids=list(range(8)))
    outs = [np.asarray(res.results[c]["out"]).reshape(2, SEQ, C) for c in range(8)]
    return np.concatenate(outs, axis=0).astype(np.float32)


# version 22
# speedup vs baseline: 1.0600x; 1.0114x over previous
import numpy as np
import ml_dtypes
from contextlib import ExitStack

import concourse.bass as bass
import concourse.mybir as mybir
from concourse.bass_utils import run_bass_kernel_spmd

F32 = mybir.dt.float32
BF16 = mybir.dt.bfloat16
AF = mybir.ActivationFunctionType
ALU = mybir.AluOpType
AX = mybir.AxisListType.X

C = 1024
T = 4096
SEQ = 2048
NH = 16
HD = 64
BUCKET = 64
FF_A = 4096
FF_E = 2048
NEXP = 8
KC = C // 128
NTILES = T // 128
EPS = 1e-5
SCALE_B = float(HD ** -0.25)
SCALE_A = 0.125
V65W = NH * (HD + 1)

ENGINES = ("PE", "DVE", "ACT", "POOL", "SP")


class Op:
    __slots__ = ("idx", "eng", "emit", "reads", "writes", "dma", "deps",
                 "marked", "inc_val", "tag", "_qslot", "_qseq", "fence",
                 "_fence_snap")

    def __init__(self, idx, eng, emit, reads, writes, dma, tag):
        self.idx = idx; self.eng = eng; self.emit = emit
        self.reads = reads; self.writes = writes; self.dma = dma
        self.deps = []; self.marked = False; self.inc_val = 0; self.tag = tag
        self.fence = False; self._fence_snap = None


class Prog:

    def __init__(self, relax_same_engine=False):
        self.ops = []
        self.last_write = {}
        self.readers = {}
        self.relax = relax_same_engine
        self.fence_n = 0

    def op(self, eng, emit, reads=(), writes=(), dma=False, tag=""):
        o = Op(len(self.ops), eng, emit, tuple(reads), tuple(writes), dma, tag)
        deps = set()
        for r in o.reads:
            w = self.last_write.get(r)
            if w is not None:
                deps.add(w)
        for w in o.writes:
            lw = self.last_write.get(w)
            if lw is not None:
                deps.add(lw)
            deps.update(self.readers.get(w, ()))
        deps.discard(o.idx)
        o.deps = sorted(deps)
        self.ops.append(o)
        for r in o.reads:
            self.readers.setdefault(r, []).append(o.idx)
        for w in o.writes:
            self.last_write[w] = o.idx
            self.readers[w] = []
        return o.idx

    def fence(self, tag="fence"):
        n = self.fence_n
        self.fence_n += 1
        for E in ENGINES:
            o = Op(len(self.ops), E, None, (), (), False, f"{tag}{n}_{E}")
            o.fence = True
            self.ops.append(o)

    def _needs_wait(self, ci, pi):
        p, c = self.ops[pi], self.ops[ci]
        if p.eng == c.eng and not p.dma:
            if p.eng == "PE" and not c.dma:
                return False
            if self.relax and not c.dma:
                return False
            return True
        return True

    NQ = 12

    def build(self, nc, es):
        sems = {e: es.enter_context(nc.semaphore(f"s_{e}")) for e in ENGINES}
        qsems = {e: [es.enter_context(nc.semaphore(f"q_{e}{i}"))
                     for i in range(self.NQ)] for e in ("SP", "ACT", "POOL")}

        last_by_eng = {}
        for o in self.ops:
            if o.fence:
                for le in last_by_eng.values():
                    le.marked = True
            else:
                last_by_eng[o.eng] = o
        for o in self.ops:
            if o.dma:
                o.marked = True
            for d in o.deps:
                if self._needs_wait(o.idx, d):
                    self.ops[d].marked = True

        counts = {}
        dma_seq = {}
        for o in self.ops:
            if o.fence:
                snap = dict(counts)
                qsnap = {}
                for e2, nn in dma_seq.items():
                    for i in range(min(nn, self.NQ)):
                        qsnap[(e2, i)] = ((nn - 1 - i) // self.NQ + 1) * 16
                o._fence_snap = (snap, qsnap)
                continue
            if o.marked:
                if o.dma:
                    j = dma_seq.get(o.eng, 0)
                    dma_seq[o.eng] = j + 1
                    o.inc_val = (j // self.NQ + 1) * 16
                    o._qslot = j % self.NQ
                    o._qseq = j
                else:
                    key = (o.eng, False)
                    counts[key] = counts.get(key, 0) + 1
                    o.inc_val = counts[key]

        per_eng = {e: [] for e in ENGINES}
        for o in self.ops:
            per_eng[o.eng].append(o)
        final_q = {}
        for e in ("SP", "ACT", "POOL"):
            n = dma_seq.get(e, 0)
            final_q[e] = [(n - 1 - i) // self.NQ + 1 for i in range(self.NQ)] \
                if n else []

        def body(eng_obj, eng_name):
            waited = {}

            def wait(key, sem, val):
                if waited.get(key, 0) >= val:
                    return
                eng_obj.wait_ge(sem, val)
                waited[key] = val

            for o in per_eng[eng_name]:
                if o.fence:
                    snap, qsnap = o._fence_snap
                    for (e2, dm), v in snap.items():
                        if v and not (e2 == eng_name and not dm):
                            wait(("s", e2), sems[e2], v)
                    for (e2, i), v in qsnap.items():
                        wait(("q", e2, i), qsems[e2][i], v)
                    continue
                if o.dma and o._qseq >= self.NQ:
                    wait(("q", o.eng, o._qslot), qsems[o.eng][o._qslot],
                         o.inc_val - 16)
                for d in o.deps:
                    if not self._needs_wait(o.idx, d):
                        continue
                    pp = self.ops[d]
                    if not pp.marked:
                        continue
                    if pp.dma:
                        wait(("q", pp.eng, pp._qslot), qsems[pp.eng][pp._qslot],
                             pp.inc_val)
                    else:
                        wait(("s", pp.eng), sems[pp.eng], pp.inc_val)
                inst = o.emit(eng_obj)
                if o.marked:
                    if isinstance(inst, (list, tuple)):
                        inst = inst[-1]
                    assert inst is not None, f"op {o.idx} ({o.tag}) returned None"
                    if o.dma:
                        inst.then_inc(qsems[o.eng][o._qslot], 16)
                    else:
                        inst.then_inc(sems[o.eng], 1)
            if eng_name == "POOL":
                for e in ("SP", "ACT", "POOL"):
                    n = dma_seq.get(e, 0)
                    for i in range(min(n, self.NQ)):
                        nslot = (n - 1 - i) // self.NQ + 1
                        wait(("q", e, i), qsems[e][i], nslot * 16)

        blk = es.enter_context(nc.Block())
        blk.sync(lambda e: body(e, "SP"))
        blk.tensor(lambda e: body(e, "PE"))
        blk.vector(lambda e: body(e, "DVE"))
        blk.scalar(lambda e: body(e, "ACT"))
        blk.gpsimd(lambda e: body(e, "POOL"))


class Arena:
    def __init__(self, nc, es, prog, nbytes=204800):
        self.t = es.enter_context(nc.sbuf_tensor("arena", [128, nbytes],
                                                 mybir.dt.uint8))
        self.nbytes = nbytes
        self.off = 0
        self.stack = []
        self.hi = 0
        self.prog = prog

    def alloc(self, cols, dtype):
        size = cols * (2 if dtype == BF16 else 4)
        size = (size + 63) & ~63
        assert self.off + size <= self.nbytes, \
            f"arena overflow: {self.off}+{size} > {self.nbytes}"
        ap = self.t[:][:, self.off:self.off + size].bitcast(dtype)[:, 0:cols]
        self.off += size
        self.hi = max(self.hi, self.off)
        return ap

    def push(self):
        self.stack.append(self.off)

    def pop(self):
        self.off = self.stack.pop()
        self.prog.fence()



def build_program(nc, stages=("aa", "f1", "ab", "f2"), relax=False):
    p = Prog(relax_same_engine=relax)
    es = ExitStack()

    def din(name, shape, dt=F32):
        return nc.dram_tensor(name, shape, dt, kind="ExternalInput").ap()

    xin = din("xin", [T, C])
    aa_wqkv = din("aa_wqkv", [C, 3 * C]); aa_bqkv = din("aa_bqkv", [3 * C])
    aa_wout = din("aa_wout", [C, C]); aa_bout = din("aa_bout", [C])
    aa_g1 = din("aa_g1", [C]); aa_b1g = din("aa_b1g", [C])
    aa_g2 = din("aa_g2", [C]); aa_b2g = din("aa_b2g", [C])
    aa_w1 = din("aa_w1", [C, FF_A]); aa_b1 = din("aa_b1", [FF_A])
    aa_w2 = din("aa_w2", [FF_A, C]); aa_b2 = din("aa_b2", [C])
    ab_wq = din("ab_wq", [C, C]); ab_wk = din("ab_wk", [C, C])
    ab_wv = din("ab_wv", [C, C]); ab_wo = din("ab_wo", [C, C])
    ab_bo = din("ab_bo", [C])
    f_rw = din("f_rw", [NEXP, C])
    f_eg = din("f_eg", [NEXP, C, FF_E]); f_eu = din("f_eu", [NEXP, C, FF_E])
    f_ed = din("f_ed", [NEXP, FF_E, C])
    f_sg = din("f_sg", [C, FF_E]); f_su = din("f_su", [C, FF_E])
    f_sd = din("f_sd", [FF_E, C])
    id_f = din("id_f", [128, 128])
    id_b = din("id_b", [128, 128], BF16)
    hmaskin = din("hmask", [128, 2 * NH * BUCKET])
    iota256 = din("iota256", [128, NTILES * NEXP])
    iota32 = din("iota32", [128, NTILES])
    ltri = din("ltri", [128, 128])

    out = nc.dram_tensor("out", [T, C], F32, kind="ExternalOutput").ap()

    def scr(name, shape, dt=F32):
        return nc.dram_tensor(name, shape, dt, kind="Internal").ap()

    ys = scr("ys", [T, C])
    y2s = scr("y2s", [T, C])
    r1 = scr("r1", [T, C])
    r2 = scr("r2", [T, C])
    r3 = scr("r3", [T, C])
    xTs = scr("xTs", [C, T], BF16)
    CAP = 1024
    NSLOT = NEXP * CAP + NTILES
    g1s = scr("g1s", [NSLOT, C], BF16)
    g1eo = scr("g1eo", [NSLOT, C], BF16)
    g2s = scr("g2s", [NSLOT, C], BF16)
    g2eo = scr("g2eo", [NSLOT, C], BF16)
    sh1s = scr("sh1s", [T, C], BF16)
    sh2s = scr("sh2s", [T, C], BF16)
    qTs = scr("qTs", [C, T], BF16)
    oTs = scr("oTs", [C, T], BF16)

    ar = Arena(nc, es, p)

    PSA = es.enter_context(nc.psum_tensor("psa", [128, 2048], F32))[:]
    PSB = es.enter_context(nc.psum_tensor("psb", [128, 1024], F32))[:]
    PSC = es.enter_context(nc.psum_tensor("psc", [128, 512], F32))[:]
    PST = es.enter_context(nc.psum_tensor("pst", [128, 1024], BF16))[:]
    PSAK = ["psa0", "psa1", "psa2", "psa3"]
    PSBK = ["psb0", "psb1"]

    def dma(eng, dst, src, reads, writes, tag=""):
        p.op(eng, lambda e, d=dst, s=src: e.dma_start(out=d, in_=s),
             reads=reads, writes=writes, dma=True, tag=tag)

    sb_idf = ar.alloc(128, F32)
    sb_idb = ar.alloc(128, BF16)
    sb_ones = ar.alloc(128, F32)
    sb_onesb = ar.alloc(128, BF16)
    sb_hmask = ar.alloc(2 * NH * BUCKET, BF16)
    sb_iota256 = ar.alloc(NTILES * NEXP, F32)
    sb_iota32 = ar.alloc(NTILES, F32)
    rwT = ar.alloc(8 * KC, F32)
    sb_ltri = ar.alloc(128, F32)
    sb_onesf = ar.alloc(4, F32)
    sb_zt = ar.alloc(1024, BF16)
    sb_onesr = ar.alloc(128, BF16)
    ROWBUF = ar.alloc(1024, F32)
    STG = ar.alloc(128, F32)
    STT = ar.alloc(64, F32)

    dma("SP", sb_idf[:, :], id_f, ["H:idf"], ["idf"], "idf")
    dma("SP", sb_idb[:, :], id_b, ["H:idb"], ["idb"], "idb")
    dma("SP", sb_iota256[:, :], iota256, ["H:iota256"], ["iota256"], "iota256")
    dma("SP", sb_iota32[:, :], iota32, ["H:iota32"], ["iota32"], "iota32")
    dma("POOL", sb_hmask[:, :], hmaskin, ["H:hmask"], ["hmask"], "hmask")
    dma("SP", sb_ltri[:, :], ltri, ["H:ltri"], ["ltri"], "ltri")
    p.op("DVE", lambda e: e.memset(sb_onesf[:, 0:1], 1.0), writes=["onesf"], tag="onesf")
    p.op("DVE", lambda e: e.memset(sb_zt[:, :], 0.0), writes=["zt"], tag="zt")
    p.op("DVE", lambda e: e.memset(sb_ones[0:1, :], 1.0), writes=["ones"], tag="ones")
    p.op("DVE", lambda e: e.memset(sb_onesb[:, 0:1], 1.0), writes=["onesb"], tag="onesb")
    p.op("DVE", lambda e: e.memset(sb_onesr[0:1, :], 1.0), writes=["onesr"], tag="onesr")
    KCOL = ar.alloc(8, F32)
    _kvals = [EPS, -0.5, SCALE_A, SCALE_B, -1.0]
    for _i, _v in enumerate(_kvals):
        p.op("DVE", lambda e, _i=_i, _v=_v: e.memset(KCOL[:, _i:_i + 1], _v),
             writes=["Kc"], tag=f"kc{_i}")
        nc.const_aps.register(F32, _v, KCOL[:, _i:_i + 1])

    GZP = {}

    def prep_gz(tag, G, Geo):
        ops = []
        nzi = (NSLOT + 127) // 128
        for zi in range(nzi):
            r0 = zi * 128
            nr = min(128, NSLOT - r0)
            ops.append((G[r0:r0 + nr, :], sb_zt[0:nr, :], f"{tag}gz{zi}",
                        ["Ginit"]))
        ops.append((Geo[NEXP * CAP:NSLOT, :], sb_zt[0:NTILES, :], f"{tag}ez0",
                    ["Geoinit"]))
        GZP[tag] = ops

    def emit_gz(tag, n, queue):
        ops = GZP.get(tag)
        if not ops:
            return
        for _ in range(min(n, len(ops))):
            dst, src, tg, wk = ops.pop(0)
            dma(queue, dst, src, ["zt"], wk, tg)

    prep_gz("f1", g1s, g1eo)
    prep_gz("f2", g2s, g2eo)

    for k in range(KC):
        dma("SP", STG[0:8, 0:128], f_rw[:, k * 128:(k + 1) * 128], ["H:rw"],
            ["stg"], f"ld_rw{k}")
        p.op("PE", lambda e: e.transpose(PSB[0:128, 0:8], STG[0:8, 0:128],
                                         sb_idf[0:8, 0:8]),
             reads=["stg", "idf"], writes=["psb0"], tag=f"tr_rw{k}")
        p.op("DVE", lambda e, k=k: e.tensor_copy(rwT[:, k * 8:(k + 1) * 8], PSB[:, 0:8]),
             reads=["psb0"], writes=["C:rwT", "stg"], tag=f"ev_rw{k}")

    def mk_bcast(dst, src_row_ap, n, tag):
        row = ROWBUF[0:1, 0:n]
        dma("SP", row, src_row_ap, [f"H:{tag}"], ["rowbuf"], f"ld_{tag}")
        nch = (n + 511) // 512

        def mm(e):
            inst = None
            for i in range(nch):
                s = i * 512
                w = min(512, n - s)
                inst = e.matmul(PSA[:, s:s + w], sb_ones[0:1, :], row[:, s:s + w],
                                start=True, stop=True)
            return inst
        p.op("PE", mm, reads=["ones", "rowbuf"],
             writes=[PSAK[i] for i in range(nch)], tag=f"bc_{tag}")
        p.op("DVE", lambda e: e.tensor_copy(dst[:, 0:n], PSA[:, 0:n]),
             reads=[PSAK[i] for i in range(nch)],
             writes=[f"C:{tag}", "rowbuf"], tag=f"ev_{tag}")

    def mk_cols(dst, src_vec, ntile, tag):
        stg = STG[0:ntile, 0:128]
        dma("SP", stg, src_vec.rearrange("(t p) -> t p", p=128),
            [f"H:{tag}"], ["stg"], f"ld_{tag}")
        p.op("PE", lambda e, stg=stg, ntile=ntile: e.transpose(
            PSA[:, 0:ntile], stg, sb_idf[0:ntile, 0:ntile]),
            reads=["stg", "idf"], writes=["psa0"], tag=f"tr_{tag}")
        p.op("DVE", lambda e, ntile=ntile: e.tensor_copy(dst[:, 0:ntile],
                                                         PSA[:, 0:ntile]),
             reads=["psa0"], writes=[f"C:{tag}", "stg"], tag=f"ev_{tag}")

    PK = {}

    def alloc_pack():
        PK["T"] = [ar.alloc(1024, F32) for _ in range(2)]
        PK["T2"] = [ar.alloc(1024, F32) for _ in range(2)]
        PK["Y"] = [ar.alloc(1024, F32) for _ in range(2)]
        PK["H"] = [ar.alloc(1024, BF16) for _ in range(2)]
        PK["SQ"] = [ar.alloc(1024, F32) for _ in range(2)]

    _ln_ctr = [0]

    def ln_core(src_ap, src_keys, y_ap, y_key, kt):
        st = STT
        sc = _ln_ctr[0] % 6
        _ln_ctr[0] += 1
        c0 = sc * 8
        sq = PK["SQ"][sc % 2]
        p.op("ACT", lambda e: e.activation(sq[:, :], src_ap, AF.Square,
                                           accum_out=st[:, c0:c0 + 1]),
             reads=src_keys, writes=[f"sqd{sc % 2}", f"st{c0}"], tag=f"{kt}_sq")
        p.op("DVE", lambda e: e.reduce_sum(st[:, c0 + 1:c0 + 2], src_ap, axis=AX),
             reads=src_keys, writes=[f"st{c0}b"], tag=f"{kt}_sm")
        p.op("DVE", lambda e: e.tensor_scalar(st[:, c0 + 2:c0 + 3],
                                              st[:, c0 + 1:c0 + 2],
                                              1.0 / C, None, ALU.mult),
             reads=[f"st{c0}b"], writes=[f"st{c0}c"], tag=f"{kt}_mu")
        p.op("DVE", lambda e: e.tensor_scalar(st[:, c0 + 3:c0 + 4], st[:, c0:c0 + 1],
                                              1.0 / C, None, ALU.mult),
             reads=[f"st{c0}"], writes=[f"st{c0}d"], tag=f"{kt}_e2")
        p.op("DVE", lambda e: e.tensor_mul(st[:, c0 + 4:c0 + 5], st[:, c0 + 2:c0 + 3],
                                           st[:, c0 + 2:c0 + 3]),
             reads=[f"st{c0}c"], writes=[f"st{c0}e"], tag=f"{kt}_m2")
        p.op("DVE", lambda e: e.tensor_sub(st[:, c0 + 5:c0 + 6], st[:, c0 + 3:c0 + 4],
                                           st[:, c0 + 4:c0 + 5]),
             reads=[f"st{c0}d", f"st{c0}e"], writes=[f"st{c0}f"], tag=f"{kt}_va")
        p.op("ACT", lambda e: e.activation(st[:, c0 + 6:c0 + 7], st[:, c0 + 5:c0 + 6],
                                           AF.Ln, bias=EPS),
             reads=[f"st{c0}f", "Kc"], writes=[f"st{c0}g"], tag=f"{kt}_ln")
        p.op("ACT", lambda e: e.activation(st[:, c0 + 7:c0 + 8], st[:, c0 + 6:c0 + 7],
                                           AF.Exp, scale=-0.5),
             reads=[f"st{c0}g", "Kc"], writes=[f"st{c0}h"], tag=f"{kt}_rs")
        p.op("DVE", lambda e: e.tensor_scalar(y_ap, src_ap, st[:, c0 + 2:c0 + 3],
                                              st[:, c0 + 7:c0 + 8],
                                              ALU.subtract, ALU.mult),
             reads=list(src_keys) + [f"st{c0}c", f"st{c0}h"], writes=[y_key],
             tag=f"{kt}_ap")

    def transpose8(src_ap, src_keys, kt):
        def mm(e):
            inst = None
            for k in range(KC):
                inst = e.transpose(PST[:, k * 128:(k + 1) * 128],
                                   src_ap[:, k * 128:(k + 1) * 128], sb_idb[:, :])
            return inst
        p.op("PE", mm, reads=list(src_keys) + ["idb"], writes=["pst"], tag=kt)

    def wchunks(slot, skey, w_ap, kdim, n, tag):
        kch = kdim // 128
        step = max(1, min(kch, (2 * 1024 * 1024) // (n * 512)))
        for k0 in range(0, kch, step):
            k1 = min(k0 + step, kch)
            dma("POOL",
                slot[:, :].rearrange("p (k n) -> p k n", k=kch)[:, k0:k1, :],
                w_ap.rearrange("(k p) n -> p k n", p=128)[:, k0:k1, :],
                [f"H:{tag}"], [skey], f"{tag}_{k0}")

    if "aa" in stages:
        ar.push()
        alloc_pack()
        cb_g1 = ar.alloc(C, F32)
        cb_b1g = ar.alloc(C, F32)
        cb_bqv = ar.alloc(C, F32)
        cb_bout = ar.alloc(C, F32)
        col_bqkv = ar.alloc(24, F32)
        mk_bcast(cb_g1, aa_g1.rearrange("(o n) -> o n", o=1), C, "g1")
        mk_bcast(cb_b1g, aa_b1g.rearrange("(o n) -> o n", o=1), C, "b1g")
        mk_bcast(cb_bqv, aa_bqkv.rearrange("(o n) -> o n", o=1)[:, 2 * C:3 * C], C, "bqv")
        mk_bcast(cb_bout, aa_bout.rearrange("(o n) -> o n", o=1), C, "bout")
        mk_cols(col_bqkv, aa_bqkv, 24, "bqkv")

        wqkv_sb = ar.alloc(KC * 3 * C, BF16)
        wout_sb = ar.alloc(KC * C, BF16)
        hTc = ar.alloc(KC * 512, BF16)
        HGRP = 512
        NG = T // HGRP
        GT = HGRP // 128
        GB = HGRP // BUCKET
        GPB = SEQ // HGRP
        qT_g = ar.alloc(KC * HGRP, BF16)
        kT_g = ar.alloc(KC * HGRP, BF16)
        kT_c = ar.alloc(KC * BUCKET, BF16)
        v65_g = [ar.alloc(V65W, BF16) for _ in range(GT)]
        v65_c = ar.alloc(V65W, BF16)
        zv65 = ar.alloc(V65W, BF16)
        zkT = ar.alloc(64, BF16)
        oT_g = ar.alloc(KC * HGRP, BF16)
        orw = [ar.alloc(512, F32) for _ in range(2)]
        rrow = [ar.alloc(1024, BF16) for _ in range(3)]
        lna = ar.alloc(2048, F32)
        expb = [ar.alloc(NH * BUCKET, BF16) for _ in range(2)]

        wchunks(wqkv_sb, "W:qkv", aa_wqkv, C, 3 * C, "aaqkv")
        wchunks(wout_sb, "W:aout", aa_wout, C, C, "aaout")
        p.op("DVE", lambda e: e.memset(zv65[:, :], 0.0), writes=["zv65"], tag="zv0")
        p.op("DVE", lambda e: e.memset(
            zv65[:, :].rearrange("p (h w) -> p h w", h=NH)[:, :, HD:HD + 1], 1.0),
            reads=["zv65"], writes=["zv65"], tag="zv1")
        p.op("DVE", lambda e: e.memset(zkT[:, :], 0.0), writes=["zkT"], tag="zk")

        for g in range(NG):
            gk = f"g{g}"
            for i in range(GT):
                ti = g * GT + i
                b = i % 2
                t, t2, y, h = PK["T"][b], PK["T2"][b], PK["Y"][b], PK["H"][b]
                kt = f"aa{ti}"
                dma("SP", t[:, :], xin[ti * 128:(ti + 1) * 128, :],
                    [f"H:xin:{ti}"], [f"tb{b}"], kt)
                ln_core(t[:, :], [f"tb{b}"], y[:, :], f"yb{b}", kt)
                dma("ACT", ys[ti * 128:(ti + 1) * 128, :], y[:, :],
                    [f"yb{b}"], [f"ys:{ti}"], f"{kt}_sy")
                ln_core(y[:, :], [f"yb{b}"], t2[:, :], f"t2b{b}", kt + "i")
                p.op("DVE", lambda e, t2=t2: e.tensor_mul(t2[:, :], t2[:, :],
                                                          cb_g1[:, :]),
                     reads=[f"t2b{b}", "C:g1"], writes=[f"t2b{b}"], tag=f"{kt}_g1")
                p.op("DVE", lambda e, t2=t2, h=h: e.tensor_add(h[:, :], t2[:, :],
                                                               cb_b1g[:, :]),
                     reads=[f"t2b{b}", "C:b1g"], writes=[f"hb{b}"], tag=f"{kt}_b1")
                transpose8(h[:, :], [f"hb{b}"], f"{kt}_tr")
                p.op("DVE", lambda e, i=i: e.tensor_copy(
                    hTc[:, :].rearrange("p (k n) -> p k n", k=KC)[:, :, i * 128:(i + 1) * 128],
                    PST[:, :].rearrange("p (k n) -> p k n", k=KC)),
                    reads=["pst"], writes=[f"hTc:{i}"], tag=f"{kt}_ev")
            hck = [f"hTc:{i}" for i in range(GT)]
            for m in range(16):
                bank = PSAK[m % 4]
                dst = qT_g if m < 8 else kT_g
                dk = ("qT" if m < 8 else "kT") + f"{gk}:{m % 8}"

                def qkmm(e, m=m):
                    inst = None
                    for k in range(KC):
                        inst = e.matmul(
                            PSA[:, (m % 4) * 512:(m % 4) * 512 + 512],
                            wqkv_sb[:, k * 3 * C + m * 128:k * 3 * C + (m + 1) * 128],
                            hTc[:, k * 512:(k + 1) * 512],
                            start=(k == 0), stop=(k == KC - 1))
                    return inst
                p.op("PE", qkmm, reads=hck + ["W:qkv"], writes=[bank], tag=f"{gk}qk{m}")
                p.op("ACT", lambda e, m=m, dst=dst: e.activation(
                    dst[:, (m % 8) * HGRP:(m % 8) * HGRP + 512],
                    PSA[:, (m % 4) * 512:(m % 4) * 512 + 512],
                    AF.Identity, bias=col_bqkv[:, m:m + 1]),
                    reads=[bank, "C:bqkv"], writes=[dk], tag=f"{gk}qke{m}")
            for i in range(GT):
                def vmm(e, i=i):
                    inst = None
                    for k in range(KC):
                        for nh_ in range(2):
                            inst = e.matmul(
                                PSB[:, nh_ * 512:(nh_ + 1) * 512],
                                hTc[:, k * 512 + i * 128:k * 512 + (i + 1) * 128],
                                wqkv_sb[:, k * 3 * C + 2 * C + nh_ * 512:
                                        k * 3 * C + 2 * C + (nh_ + 1) * 512],
                                start=(k == 0), stop=(k == KC - 1))
                    return inst
                p.op("PE", vmm, reads=hck + ["W:qkv"], writes=PSBK, tag=f"{gk}v{i}")
                p.op("DVE", lambda e, i=i: e.memset(
                    v65_g[i][:, :].rearrange("p (h w) -> p h w", h=NH)[:, :, HD:HD + 1],
                    1.0), writes=[f"v65:{i}"], tag=f"{gk}vo{i}")
                p.op("DVE", lambda e, i=i: e.scalar_tensor_tensor(
                    v65_g[i][:, :].rearrange("p (h w) -> p h w", h=NH)[:, :, 0:HD],
                    PSB[:, :].rearrange("p (h w) -> p h w", h=NH),
                    1.0, cb_bqv[:, :].rearrange("p (h w) -> p h w", h=NH),
                    ALU.mult, ALU.add),
                    reads=PSBK + ["C:bqv", f"v65:{i}"],
                    writes=[f"v65:{i}"], tag=f"{gk}ve{i}")
            qkkeys = [f"qT{gk}:{m}" for m in range(8)] + \
                     [f"kT{gk}:{m}" for m in range(8)]
            vkeys = [f"v65:{i}" for i in range(GT)]
            def hcol(h):
                return (h // 2) * 64 + (h % 2) * 512

            def SS(b):
                bi = b
                first = (g % GPB == 0) and bi == 0
                bk = f"{gk}b{bi}"
                pc = (bi % 2) * 64
                pp = 64 - pc

                def smm(e, bi=bi, first=first, pc=pc, pp=pp):
                    inst = None
                    for h in [0, 2, 4, 6, 8, 10, 12, 14, 1, 3, 5, 7, 9, 11, 13, 15]:
                        sI = h // 2
                        r = (h % 2) * 64
                        qsl = qT_g[r:r + 64, sI * HGRP + bi * 64:sI * HGRP + (bi + 1) * 64]
                        if first:
                            kp = zkT[r:r + 64, 0:64]
                        elif bi == 0:
                            kp = kT_c[r:r + 64, sI * BUCKET:(sI + 1) * BUCKET]
                        else:
                            kp = kT_g[r:r + 64, sI * HGRP + (bi - 1) * 64:sI * HGRP + bi * 64]
                        kc = kT_g[r:r + 64, sI * HGRP + bi * 64:sI * HGRP + (bi + 1) * 64]
                        inst = e.matmul(PSB[pp:pp + 64, hcol(h):hcol(h) + 64], kp, qsl,
                                        start=True, stop=True)
                        inst = e.matmul(PSB[pc:pc + 64, hcol(h):hcol(h) + 64], kc, qsl,
                                        start=True, stop=True)
                    return inst
                p.op("PE", smm, reads=qkkeys + ["zkT", "kTc"], writes=PSBK,
                     tag=f"{bk}_s")
                p.op("ACT", lambda e, b=b: e.activation(expb[b % 2][:, :], PSB[:, :],
                                                        AF.Exp, scale=SCALE_A),
                     reads=PSBK + ["Kc"], writes=[f"expb{b % 2}"], tag=f"{bk}_exp")
                p.op("DVE", lambda e, bi=bi, b=b: e.tensor_mul(
                    expb[b % 2][:, :], expb[b % 2][:, :],
                    sb_hmask[:, (bi % 2) * NH * BUCKET:(bi % 2 + 1) * NH * BUCKET]),
                    reads=[f"expb{b % 2}", "hmask"], writes=[f"expb{b % 2}"],
                    tag=f"{bk}_msk")

                def dmm(e, b=b):
                    e.matmul(PSB[0:1, 0:512], sb_onesb[:, 0:1], expb[b % 2][:, 0:512],
                             start=True, stop=True)
                    return e.matmul(PSB[0:1, 512:1024], sb_onesb[:, 0:1],
                                    expb[b % 2][:, 512:1024], start=True, stop=True)
                p.op("PE", dmm, reads=[f"expb{b % 2}", "onesb"], writes=PSBK,
                     tag=f"{bk}_d")
                p.op("DVE", lambda e: e.tensor_copy(lna[0:1, 0:1024], PSB[0:1, :]),
                     reads=PSBK, writes=["lnA"], tag=f"{bk}_rc0")
                p.op("ACT", lambda e: e.activation(lna[0:1, 1024:2048],
                                                   lna[0:1, 0:1024], AF.Ln),
                     reads=["lnA"], writes=["lnA2"], tag=f"{bk}_rc1")

                def rcf(e, b=b):
                    with nc.allow_low_precision(
                            reason="recip feeds a bf16-operand matmul; rounding "
                                   "here equals the matmul's own operand cast"):
                        return e.activation(rrow[b % 3][0:1, :],
                                            lna[0:1, 1024:2048], AF.Exp,
                                            scale=-1.0)
                p.op("ACT", rcf, reads=["lnA2", "Kc"], writes=[f"rrow{b % 3}"],
                     tag=f"{bk}_rc")

            def OM(b):
                bi = b
                first = (g % GPB == 0) and bi == 0
                bk = f"{gk}b{bi}"
                pc = (bi % 2) * 64
                pp = 64 - pc
                bko = (b % 2) * 2

                def omm(e, bi=bi, first=first, pc=pc, pp=pp, b=b, bko=bko):
                    inst = None
                    for grp in range(2):
                        for h in range(NH):
                            sI = h // 2
                            ro = (h % 2) * 64
                            co = (bko + grp) * 512 + sI * 64
                            if grp == 0:
                                if first:
                                    v = zv65[pp:pp + 64, h * 65:h * 65 + 64]
                                elif bi == 0:
                                    v = v65_c[64:128, h * 65:h * 65 + 64]
                                else:
                                    v = v65_g[(bi - 1) // 2][pp:pp + 64,
                                                             h * 65:h * 65 + 64]
                                rh = expb[b % 2][pp:pp + 64, hcol(h):hcol(h) + 64]
                            else:
                                v = v65_g[bi // 2][pc:pc + 64, h * 65:h * 65 + 64]
                                rh = expb[b % 2][pc:pc + 64, hcol(h):hcol(h) + 64]
                            inst = e.matmul(PSA[ro:ro + 64, co:co + 64], v, rh,
                                            start=True, stop=True)
                    return inst
                p.op("PE", omm, reads=[f"expb{b % 2}", "zv65", "v65c"] + vkeys,
                     writes=[PSAK[bko], PSAK[bko + 1]], tag=f"{bk}_o")

            def BM(b):
                bi = b
                bk = f"{gk}b{bi}"
                bko = (b % 2) * 2

                def bmm(e, b=b):
                    e.matmul(PSB[:, 0:512], sb_onesr[0:1, :],
                             rrow[b % 3][0:1, 0:512], start=True, stop=True)
                    return e.matmul(PSB[:, 512:1024], sb_onesr[0:1, :],
                                    rrow[b % 3][0:1, 512:1024], start=True, stop=True)
                p.op("PE", bmm, reads=[f"rrow{b % 3}", "onesr"],
                     writes=PSBK, tag=f"{bk}_bc")
                p.op("ACT", lambda e, b=b, bko=bko: e.activation(
                    orw[b % 2][0:64, 0:512], PSA[0:64, bko * 512:bko * 512 + 512],
                    AF.Copy),
                    reads=[PSAK[bko]], writes=[f"orwE{b % 2}"], tag=f"{bk}_ee")
                p.op("DVE", lambda e, b=b, bko=bko: e.scalar_tensor_tensor(
                    orw[b % 2][0:64, 0:512], PSA[0:64, (bko + 1) * 512:(bko + 2) * 512],
                    1.0, orw[b % 2][0:64, 0:512], ALU.mult, ALU.add),
                    reads=[PSAK[bko + 1], f"orwE{b % 2}"], writes=[f"orwE{b % 2}"],
                    tag=f"{bk}_ee2")
                p.op("ACT", lambda e, b=b, bko=bko: e.activation(
                    orw[b % 2][64:128, 0:512], PSA[64:128, bko * 512:bko * 512 + 512],
                    AF.Copy),
                    reads=[PSAK[bko]], writes=[f"orwO{b % 2}"], tag=f"{bk}_eo")
                p.op("DVE", lambda e, b=b, bko=bko: e.scalar_tensor_tensor(
                    orw[b % 2][64:128, 0:512], PSA[64:128, (bko + 1) * 512:(bko + 2) * 512],
                    1.0, orw[b % 2][64:128, 0:512], ALU.mult, ALU.add),
                    reads=[PSAK[bko + 1], f"orwO{b % 2}"], writes=[f"orwO{b % 2}"],
                    tag=f"{bk}_eo2")
                p.op("DVE", lambda e, bi=bi, b=b: e.tensor_tensor(
                    oT_g[0:64, :].rearrange("p (s n) -> p s n", s=KC)[:, :, bi * 64:(bi + 1) * 64],
                    orw[b % 2][0:64, 0:512].rearrange("p (s n) -> p s n", s=8),
                    PSB[0:64, 0:512].rearrange("p (s n) -> p s n", s=8),
                    ALU.mult),
                    reads=[f"orwE{b % 2}"] + PSBK, writes=[f"oT{gk}"],
                    tag=f"{bk}_ne")
                p.op("DVE", lambda e, bi=bi, b=b: e.tensor_tensor(
                    oT_g[64:128, :].rearrange("p (s n) -> p s n", s=KC)[:, :, bi * 64:(bi + 1) * 64],
                    orw[b % 2][64:128, 0:512].rearrange("p (s n) -> p s n", s=8),
                    PSB[64:128, 512:1024].rearrange("p (s n) -> p s n", s=8),
                    ALU.mult),
                    reads=[f"orwO{b % 2}"] + PSBK, writes=[f"oT{gk}"],
                    tag=f"{bk}_no")

            for b in range(GB + 2):
                if b < GB:
                    SS(b)
                if 1 <= b <= GB:
                    OM(b - 1)
                if b >= 2:
                    BM(b - 2)
            p.op("DVE", lambda e: e.tensor_copy(
                kT_c[:, :].rearrange("p (s n) -> p s n", s=KC),
                kT_g[:, :].rearrange("p (s n) -> p s n", s=KC)[:, :, HGRP - 64:HGRP]),
                reads=[f"kT{gk}:{m}" for m in range(8)], writes=["kTc"],
                tag=f"{gk}_kc")
            p.op("DVE", lambda e: e.tensor_copy(v65_c[64:128, :],
                                                v65_g[GT - 1][64:128, :]),
                 reads=[f"v65:{GT - 1}"], writes=["v65c"], tag=f"{gk}_vc")
            for i in range(GT):
                ti = g * GT + i
                b = i % 2
                kt = f"{gk}z{i}"

                def zmm(e, i=i):
                    inst = None
                    for k in range(KC):
                        for nh_ in range(2):
                            inst = e.matmul(
                                PSB[:, nh_ * 512:(nh_ + 1) * 512],
                                oT_g[:, k * HGRP + i * 128:k * HGRP + (i + 1) * 128],
                                wout_sb[:, k * C + nh_ * 512:k * C + (nh_ + 1) * 512],
                                start=(k == 0), stop=(k == KC - 1))
                    return inst
                p.op("PE", zmm, reads=[f"oT{gk}", "W:aout"], writes=PSBK, tag=kt)
                tb_, t2b_ = PK["T"][b], PK["T2"][b]
                dma("SP", tb_[:, :], ys[ti * 128:(ti + 1) * 128, :],
                    [f"ys:{ti}"], [f"tb{b}"], f"{kt}_ly")
                p.op("DVE", lambda e, t2b_=t2b_: e.scalar_tensor_tensor(
                    t2b_[:, :], PSB[:, :], 1.0, cb_bout[:, :],
                    ALU.mult, ALU.add),
                    reads=PSBK + ["C:bout"], writes=[f"t2b{b}"], tag=f"{kt}_b")
                p.op("DVE", lambda e, t2b_=t2b_, tb_=tb_: e.tensor_add(
                    t2b_[:, :], t2b_[:, :], tb_[:, :]),
                    reads=[f"t2b{b}", f"tb{b}"], writes=[f"t2b{b}"], tag=f"{kt}_y")
                dma("ACT", y2s[ti * 128:(ti + 1) * 128, :], t2b_[:, :],
                    [f"t2b{b}"], [f"y2s:{ti}"], f"{kt}_st")
        ar.pop()

        ar.push()
        alloc_pack()
        cb_g2 = ar.alloc(C, F32)
        cb_b2g = ar.alloc(C, F32)
        cb_b2 = ar.alloc(C, F32)
        col_b1 = ar.alloc(32, F32)
        mk_bcast(cb_g2, aa_g2.rearrange("(o n) -> o n", o=1), C, "g2")
        mk_bcast(cb_b2g, aa_b2g.rearrange("(o n) -> o n", o=1), C, "b2g")
        mk_bcast(cb_b2, aa_b2.rearrange("(o n) -> o n", o=1), C, "b2")
        mk_cols(col_b1, aa_b1, 32, "b1")
        w2_sb = ar.alloc((FF_A // 128) * C, BF16)
        w1p = [ar.alloc(KC * 128, BF16) for _ in range(4)]
        h2T = ar.alloc(KC * 512, BF16)
        m1T = ar.alloc((FF_A // 128) * 512, BF16)
        wchunks(w2_sb, "W:w2", aa_w2, FF_A, C, "aaw2")
        NF = FF_A // 128
        for ch in range(T // 512):
            ck = f"a4c{ch}"
            if "f1" in stages:
                emit_gz("f1", 9, "ACT")
            for i in range(4):
                ti = ch * 4 + i
                b = i % 2
                t, t2, h = PK["T"][b], PK["T2"][b], PK["H"][b]
                kt = f"a4{ti}"
                dma("SP", t[:, :], y2s[ti * 128:(ti + 1) * 128, :],
                    [f"y2s:{ti}"], [f"tb{b}"], kt)
                ln_core(t[:, :], [f"tb{b}"], t2[:, :], f"t2b{b}", kt)
                p.op("DVE", lambda e, t2=t2: e.tensor_mul(t2[:, :], t2[:, :],
                                                          cb_g2[:, :]),
                     reads=[f"t2b{b}", "C:g2"], writes=[f"t2b{b}"], tag=f"{kt}_g2")
                p.op("DVE", lambda e, t2=t2, h=h: e.tensor_add(h[:, :], t2[:, :],
                                                               cb_b2g[:, :]),
                     reads=[f"t2b{b}", "C:b2g"], writes=[f"hb{b}"], tag=f"{kt}_b2")
                transpose8(h[:, :], [f"hb{b}"], f"{kt}_tr")
                p.op("DVE", lambda e, i=i: e.tensor_copy(
                    h2T[:, :].rearrange("p (k n) -> p k n", k=KC)[:, :, i * 128:(i + 1) * 128],
                    PST[:, :].rearrange("p (k n) -> p k n", k=KC)),
                    reads=["pst"], writes=[f"h2T:{i}"], tag=f"{kt}_ev")
            hk = [f"h2T:{i}" for i in range(4)]
            for f in range(NF):
                wb = w1p[f % 4]
                wk = f"w1p{f % 4}"
                dma("POOL", wb[:, :].rearrange("p (k n) -> p k n", k=KC),
                    aa_w1.rearrange("(k p) n -> p k n", p=128)[:, :, f * 128:(f + 1) * 128],
                    ["H:w1"], [wk], f"{ck}w1_{f}")
                bank = PSAK[f % 4]

                def m1mm(e, f=f, wb=wb):
                    inst = None
                    for k in range(KC):
                        inst = e.matmul(PSA[:, (f % 4) * 512:(f % 4) * 512 + 512],
                                        wb[:, k * 128:(k + 1) * 128],
                                        h2T[:, k * 512:(k + 1) * 512],
                                        start=(k == 0), stop=(k == KC - 1))
                    return inst
                p.op("PE", m1mm, reads=hk + [wk], writes=[bank], tag=f"{ck}m1_{f}")
                p.op("ACT", lambda e, f=f: e.activation(
                    m1T[:, f * 512:(f + 1) * 512],
                    PSA[:, (f % 4) * 512:(f % 4) * 512 + 512],
                    AF.Gelu, bias=col_b1[:, f:f + 1]),
                    reads=[bank, "C:b1"], writes=[f"m1T:{f}"], tag=f"{ck}ge_{f}")
            m1k = [f"m1T:{f}" for f in range(NF)]
            for i in range(4):
                ti = ch * 4 + i
                b = i % 2

                def m2mm(e, i=i):
                    inst = None
                    for k in range(NF):
                        for nh_ in range(2):
                            inst = e.matmul(
                                PSB[:, nh_ * 512:(nh_ + 1) * 512],
                                m1T[:, k * 512 + i * 128:k * 512 + (i + 1) * 128],
                                w2_sb[:, k * C + nh_ * 512:k * C + (nh_ + 1) * 512],
                                start=(k == 0), stop=(k == NF - 1))
                    return inst
                kt = f"{ck}m2_{i}"
                p.op("PE", m2mm, reads=m1k + ["W:w2"], writes=PSBK, tag=kt)
                tb_, t2b_, yb_ = PK["T"][b], PK["T2"][b], PK["Y"][b]
                dma("SP", tb_[:, :], y2s[ti * 128:(ti + 1) * 128, :],
                    [f"y2s:{ti}"], [f"tb{b}"], f"{kt}_l2")
                dma("SP", yb_[:, :], xin[ti * 128:(ti + 1) * 128, :],
                    [f"H:xin:{ti}"], [f"yb{b}"], f"{kt}_l0")
                p.op("DVE", lambda e, t2b_=t2b_: e.scalar_tensor_tensor(
                    t2b_[:, :], PSB[:, :], 1.0, cb_b2[:, :],
                    ALU.mult, ALU.add),
                    reads=PSBK + ["C:b2"], writes=[f"t2b{b}"], tag=f"{kt}_b")
                p.op("DVE", lambda e, t2b_=t2b_, tb_=tb_: e.tensor_add(
                    t2b_[:, :], t2b_[:, :], tb_[:, :]),
                    reads=[f"t2b{b}", f"tb{b}"], writes=[f"t2b{b}"], tag=f"{kt}_y2")
                p.op("DVE", lambda e, t2b_=t2b_, yb_=yb_: e.tensor_add(
                    t2b_[:, :], t2b_[:, :], yb_[:, :]),
                    reads=[f"t2b{b}", f"yb{b}"], writes=[f"t2b{b}"], tag=f"{kt}_x0")
                dma("ACT", r1[ti * 128:(ti + 1) * 128, :], t2b_[:, :],
                    [f"t2b{b}"], [f"r1:{ti}"], f"{kt}_st")
        ar.pop()

    def ffn(tag, rin, rin_name, rout, rout_name, G, Geo, SHS):
        ar.push()
        emit_gz(tag, 10 ** 9, "ACT")
        fmask = ar.alloc(NTILES * NEXP, F32)
        rankw = ar.alloc(NTILES * NEXP, F32)
        desti = ar.alloc(NTILES, mybir.dt.int32)
        cntrow = ar.alloc(NTILES * NEXP, F32)
        baserow = ar.alloc(NTILES * NEXP, F32)
        cmat = ar.alloc(NEXP, F32)
        lgtall = ar.alloc(NTILES * NEXP, F32)
        mb0 = ar.alloc(NTILES * NEXP, F32)
        mb8 = ar.alloc(NTILES * NEXP, F32)
        mxsc = ar.alloc(2 * NTILES, F32)
        dtmp = ar.alloc(8 * NTILES, F32)
        PSCK2 = ["psc0", "psc1"]
        sl01 = [ar.alloc(16 * 1024, BF16) for _ in range(2)]
        wchunks(sl01[0], "slot0", f_sg, C, FF_E, f"{tag}shg")
        wchunks(sl01[1], "slot1", f_su, C, FF_E, f"{tag}shu")
        ar.push()
        xsall = ar.alloc(NTILES * C, BF16)
        alloc_pack()
        xstg = [ar.alloc(1024, BF16) for _ in range(2)]
        xtf = [ar.alloc(1024, F32) for _ in range(2)]
        for ti in range(NTILES):
            b = ti % 2
            t, y, h = PK["T"][b], PK["Y"][b], PK["H"][b]
            kt = f"{tag}L{ti}"
            xt = xtf[b]
            po = 1024 * b
            pk2 = [f"psa{2 * b}", f"psa{2 * b + 1}"]
            dma("SP", t[:, :], rin[ti * 128:(ti + 1) * 128, :],
                [f"{rin_name}:{ti}"], [f"tb{b}"], kt)
            ln_core(t[:, :], [f"tb{b}"], y[:, :], f"yb{b}", kt)
            p.op("PE", lambda e, y=y, po=po: [e.transpose(
                PSA[:, po + k * 128:po + (k + 1) * 128],
                y[:, k * 128:(k + 1) * 128], sb_idf[:, :]) for k in range(KC)][-1],
                reads=[f"yb{b}", "idf"], writes=pk2, tag=f"{kt}_ftr")
            p.op("DVE", lambda e, xt=xt, po=po: e.tensor_copy(
                xt[:, :], PSA[:, po:po + 1024]),
                reads=pk2, writes=[f"xtf{b}"], tag=f"{kt}_fev")

            def rmm(e, xt=xt, b=b):
                inst = None
                for k in range(KC):
                    inst = e.matmul(PSC[:, b * 8:b * 8 + 8],
                                    xt[:, k * 128:(k + 1) * 128],
                                    rwT[:, k * 8:(k + 1) * 8],
                                    start=(k == 0), stop=(k == KC - 1))
                return inst
            p.op("PE", rmm, reads=[f"xtf{b}", "C:rwT"], writes=[f"psc{b}"],
                 tag=f"{kt}_rt")
            p.op("DVE", lambda e, ti=ti, b=b: e.tensor_copy(
                lgtall[:, ti * 8:(ti + 1) * 8], PSC[:, b * 8:b * 8 + 8]),
                reads=[f"psc{b}"], writes=["lga"], tag=f"{kt}_lg")
            p.op("ACT", lambda e, y=y, ti=ti: e.activation(
                xsall[:, ti * C:(ti + 1) * C], y[:, :], AF.Copy),
                reads=[f"yb{b}"], writes=[f"xsa:{ti}"], tag=f"{kt}_cx")
            p.op("ACT", lambda e, y=y, h=h: e.activation(h[:, :], y[:, :], AF.Copy),
                 reads=[f"yb{b}"], writes=[f"hb{b}"], tag=f"{kt}_cy")
            transpose8(h[:, :], [f"hb{b}"], f"{kt}_t1")
            p.op("DVE", lambda e, b=b: e.tensor_copy(xstg[b][:, :], PST[:, :]),
                 reads=["pst"], writes=[f"xstg{b}"], tag=f"{kt}_e1")
            dma("ACT", xTs.rearrange("(k p) n -> p k n", p=128)[:, :, ti * 128:(ti + 1) * 128],
                xstg[b][:, :].rearrange("p (k n) -> p k n", k=KC),
                [f"xstg{b}"], [f"xTs:{ti}"], f"{kt}_s1")
        lg3 = lgtall[:, 0:NTILES * NEXP].rearrange("p (t e) -> p t e", e=NEXP)
        mx3 = mxsc[:, 0:NTILES].rearrange("p (t o) -> p t o", o=1)
        p.op("DVE", lambda e: e.reduce_max(mx3, lg3, axis=AX),
             reads=["lga"], writes=["mxc"], tag=f"{tag}mx")
        p.op("ACT", lambda e: e.activation(mxsc[:, NTILES:2 * NTILES],
                                           mxsc[:, 0:NTILES], AF.Sigmoid),
             reads=["mxc"], writes=["scal"], tag=f"{tag}sg")
        for ti in range(NTILES):
            g0, g1 = ti * NEXP, (ti + 1) * NEXP
            p.op("DVE", lambda e, g0=g0, g1=g1, ti=ti: e.tensor_single_scalar(
                mb0[:, g0:g1], lgtall[:, g0:g1], mxsc[:, ti:ti + 1], ALU.is_equal),
                reads=["lga", "mxc"], writes=["m0"], tag=f"{tag}eq{ti}")
            p.op("DVE", lambda e, g0=g0, g1=g1: e.tensor_tensor_scan(
                mb8[:, g0:g1], mb0[:, g0:g1], mb0[:, g0:g1], 0.0,
                ALU.add, ALU.bypass),
                reads=["m0"], writes=["m8"], tag=f"{tag}cs{ti}")
        p.op("DVE", lambda e: e.tensor_scalar(mb8[:, :], mb8[:, :], 1.0, None,
                                              ALU.is_equal),
             reads=["m8"], writes=["m8"], tag=f"{tag}c1")
        p.op("DVE", lambda e: e.tensor_mul(fmask[:, :], mb0[:, :], mb8[:, :]),
             reads=["m0", "m8"], writes=["fma"], tag=f"{tag}fm")
        p.op("PE", lambda e: e.matmul(PSC[:, 256:512], sb_ltri[:, :], fmask[:, :],
                                      start=True, stop=True),
             reads=["fma", "ltri"], writes=PSCK2, tag=f"{tag}rk")
        p.op("PE", lambda e: e.matmul(PSC[0:1, 0:256], sb_onesf[:, 0:1], fmask[:, :],
                                      start=True, stop=True),
             reads=["fma", "onesf"], writes=PSCK2, tag=f"{tag}ct")
        p.op("DVE", lambda e: e.tensor_copy(rankw[:, :], PSC[:, 256:512]),
             reads=PSCK2, writes=["rka"], tag=f"{tag}rke")
        p.op("DVE", lambda e: e.tensor_copy(cntrow[0:1, :], PSC[0:1, 0:256]),
             reads=PSCK2, writes=["cnt"], tag=f"{tag}cte")
        for ti in range(NTILES):
            def xsf(e, ti=ti):
                with nc.allow_low_precision(
                        reason="xs feeds bf16-operand matmuls; rounding here "
                               "equals the matmul's own operand cast"):
                    return e.tensor_scalar(
                        xsall[:, ti * C:(ti + 1) * C], xsall[:, ti * C:(ti + 1) * C],
                        mxsc[:, NTILES + ti:NTILES + ti + 1], None, ALU.mult)
            p.op("DVE", xsf,
                 reads=[f"xsa:{ti}", "scal"], writes=[f"xsa:{ti}"], tag=f"{tag}xs{ti}")
        dma("SP", cmat[0:NTILES, 0:NEXP],
            cntrow[0:1, 0:NTILES * NEXP].rearrange("o (t e) -> o t e", e=NEXP),
            ["cnt"], ["cmat"], f"{tag}cm")
        p.op("PE", lambda e: e.matmul(PSC[0:NTILES, 8:16], sb_ltri[0:NTILES, 0:NTILES],
                                      cmat[0:NTILES, 0:NEXP], start=True, stop=True),
             reads=["cmat", "ltri"], writes=PSCK2, tag=f"{tag}bs")
        p.op("DVE", lambda e: e.tensor_copy(cmat[0:NTILES, 0:NEXP], PSC[0:NTILES, 8:16]),
             reads=PSCK2, writes=["cmat2"], tag=f"{tag}bse")
        dma("SP", baserow[0:1, 0:NTILES * NEXP].rearrange("o (t e) -> o t e", e=NEXP),
            cmat[0:NTILES, 0:NEXP], ["cmat2"], ["baserow"], f"{tag}br")
        p.op("PE", lambda e: e.matmul(PSC[:, 0:256], sb_ones[0:1, :],
                                      baserow[0:1, 0:256], start=True, stop=True),
             reads=["baserow", "ones"], writes=PSCK2, tag=f"{tag}bb")

        def d3(c):
            return dtmp[:, c * NTILES:(c + 1) * NTILES]

        def t3(buf):
            return buf[:, :].rearrange("p (t e) -> p t e", e=NEXP)

        def dr3(c):
            return d3(c).rearrange("p (t o) -> p t o", o=1)
        p.op("DVE", lambda e: e.tensor_mul(mb0[:, :], rankw[:, :], fmask[:, :]),
             reads=["rka", "fma"], writes=["m0"], tag=f"{tag}rs")
        p.op("DVE", lambda e: e.reduce_sum(dr3(0), t3(mb0), axis=AX),
             reads=["m0"], writes=["dR"], tag=f"{tag}rs2")
        p.op("DVE", lambda e: e.tensor_mul(mb8[:, :], PSC[:, 0:256], fmask[:, :]),
             reads=PSCK2 + ["fma"], writes=["m8"], tag=f"{tag}bsl")
        p.op("DVE", lambda e: e.reduce_sum(dr3(1), t3(mb8), axis=AX),
             reads=["m8"], writes=["dB"], tag=f"{tag}bsl2")
        p.op("DVE", lambda e: e.tensor_mul(mb0[:, :], sb_iota256[:, :], fmask[:, :]),
             reads=["iota256", "fma"], writes=["m0"], tag=f"{tag}cp")
        p.op("DVE", lambda e: e.reduce_sum(dr3(2), t3(mb0), axis=AX),
             reads=["m0"], writes=["dE"], tag=f"{tag}cp2")
        p.op("DVE", lambda e: e.tensor_scalar(d3(2), d3(2), float(CAP), None,
                                              ALU.mult),
             reads=["dE"], writes=["dE"], tag=f"{tag}cpm")
        p.op("DVE", lambda e: e.tensor_add(d3(3), d3(0), d3(1)),
             reads=["dR", "dB"], writes=["dS"], tag=f"{tag}rb")
        p.op("DVE", lambda e: e.tensor_single_scalar(d3(4), d3(3), float(CAP),
                                                     ALU.is_ge),
             reads=["dS"], writes=["dO"], tag=f"{tag}ov")
        p.op("DVE", lambda e: e.tensor_add(d3(5), d3(3), d3(2)),
             reads=["dS", "dE"], writes=["dF"], tag=f"{tag}df")
        p.op("DVE", lambda e: e.tensor_scalar(d3(6), d3(5), -1.0, float(NEXP * CAP),
                                              ALU.mult, ALU.add),
             reads=["dF"], writes=["dT"], tag=f"{tag}tr")
        p.op("DVE", lambda e: e.tensor_add(d3(6), d3(6), sb_iota32[:, 0:NTILES]),
             reads=["dT", "iota32"], writes=["dT"], tag=f"{tag}tr2")
        p.op("DVE", lambda e: e.tensor_mul(d3(4), d3(4), d3(6)),
             reads=["dO", "dT"], writes=["dO"], tag=f"{tag}cl")
        p.op("DVE", lambda e: e.tensor_add(d3(7), d3(4), d3(5)),
             reads=["dO", "dF"], writes=["dPre"], tag=f"{tag}cl2")
        p.op("DVE", lambda e: e.tensor_copy(desti[:, 0:NTILES], d3(7)),
             reads=["dPre"], writes=["diall"], tag=f"{tag}ci")
        for ti in range(NTILES):
            p.op("POOL", lambda e, ti=ti: e.indirect_dma_start(
                G, bass.IndirectOffsetOnAxis(ap=desti[:, ti:ti + 1], axis=0),
                xsall[:, ti * C:(ti + 1) * C], None),
                reads=["diall", f"xsa:{ti}", "Ginit"], writes=[f"H:G:{ti}"],
                dma=True, tag=f"{tag}sc{ti}")
        ar.pop()

        slots = sl01 + [ar.alloc(16 * 1024, BF16)]
        hT = ar.alloc(16 * 512, BF16)
        xch = ar.alloc(KC * 512, BF16)
        stile = [ar.alloc(512, BF16) for _ in range(2)]
        ebuf = [ar.alloc(1024, BF16) for _ in range(2)]
        xg = [ar.alloc(1024, BF16) for _ in range(4)]

        def expert_block(en, wg, wu, wd, nch, load_x, out_rows, out_name,
                         skip_gu=False):
            ek = f"{tag}{en}"
            sg, su, sd = slots
            kg, ku, kd = "slot0", "slot1", "slot2"
            if not skip_gu:
                wchunks(sg, kg, wg, C, FF_E, f"{ek}g")
                wchunks(su, ku, wu, C, FF_E, f"{ek}u")
            wchunks(sd, kd, wd, FF_E, C, f"{ek}d")
            for ch in range(nch):
                load_x(ek, ch)
                for f in range(16):
                    st = stile[f % 2]
                    stk = f"stile{f % 2}"
                    gco = (f % 2) * 1024
                    uco = (f % 2) * 1024 + 512
                    gk2 = PSAK[(f % 2) * 2]
                    uk2 = PSAK[(f % 2) * 2 + 1]

                    def gmm(e, f=f, sg=sg, gco=gco):
                        inst = None
                        for k in range(KC):
                            inst = e.matmul(
                                PSA[:, gco:gco + 512],
                                sg[:, k * FF_E + f * 128:k * FF_E + (f + 1) * 128],
                                xch[:, k * 512:(k + 1) * 512],
                                start=(k == 0), stop=(k == KC - 1))
                        return inst

                    def umm(e, f=f, su=su, uco=uco):
                        inst = None
                        for k in range(KC):
                            inst = e.matmul(
                                PSA[:, uco:uco + 512],
                                su[:, k * FF_E + f * 128:k * FF_E + (f + 1) * 128],
                                xch[:, k * 512:(k + 1) * 512],
                                start=(k == 0), stop=(k == KC - 1))
                        return inst
                    p.op("PE", gmm, reads=[kg, "xch"], writes=[gk2], tag=f"{ek}g{ch}_{f}")
                    p.op("PE", umm, reads=[ku, "xch"], writes=[uk2], tag=f"{ek}u{ch}_{f}")
                    p.op("ACT", lambda e, st=st, gco=gco: e.activation(
                        st[:, :], PSA[:, gco:gco + 512], AF.Silu),
                        reads=[gk2], writes=[stk], tag=f"{ek}s{ch}_{f}")
                    p.op("DVE", lambda e, f=f, st=st, uco=uco: e.tensor_tensor(
                        hT[:, f * 512:(f + 1) * 512], st[:, :],
                        PSA[:, uco:uco + 512], ALU.mult),
                        reads=[stk, uk2], writes=[f"hT:{f}"], tag=f"{ek}h{ch}_{f}")
                hk = [f"hT:{f}" for f in range(16)]
                for it in range(4):
                    eb2 = ebuf[it % 2]
                    ebk = f"ebuf{it % 2}"
                    for nh_ in range(2):
                        bkk = PSBK[nh_]

                        def dnmm(e, it=it, sd=sd, nh_=nh_):
                            inst = None
                            for k in range(16):
                                inst = e.matmul(
                                    PSB[:, nh_ * 512:(nh_ + 1) * 512],
                                    hT[:, k * 512 + it * 128:k * 512 + (it + 1) * 128],
                                    sd[:, k * C + nh_ * 512:k * C + (nh_ + 1) * 512],
                                    start=(k == 0), stop=(k == 15))
                            return inst
                        p.op("PE", dnmm, reads=hk + [kd], writes=[bkk],
                             tag=f"{ek}d{ch}_{it}_{nh_}")
                        def dnev(e, nh_=nh_, eb2=eb2):
                            with nc.allow_low_precision(
                                    reason="ffn branch outputs bounce via HBM "
                                           "in bf16; same rounding class as "
                                           "the bf16 matmul operands"):
                                return e.tensor_copy(
                                    eb2[:, nh_ * 512:(nh_ + 1) * 512],
                                    PSB[:, nh_ * 512:(nh_ + 1) * 512])
                        p.op("DVE", dnev,
                             reads=[bkk], writes=[ebk + f":{nh_}"],
                             tag=f"{ek}e{ch}_{it}_{nh_}")
                    r0 = out_rows(ch, it)
                    dma("ACT", r0, eb2[:, :], [ebk + ":0", ebk + ":1"],
                        [f"{out_name}:{en}:{ch}:{it}"], f"{ek}o{ch}_{it}")

        def shared_loadx(ek, ch):
            dma("SP", xch[:, :].rearrange("p (k n) -> p k n", k=KC),
                xTs.rearrange("(k p) n -> p k n", p=128)[:, :, ch * 512:(ch + 1) * 512],
                [f"xTs:{tt}" for tt in range(ch * 4, ch * 4 + 4)],
                ["xch"], f"{ek}x{ch}")
        expert_block("sh", f_sg, f_su, f_sd, 8, shared_loadx,
                     lambda ch, it: SHS[(ch * 4 + it) * 128:(ch * 4 + it + 1) * 128, :],
                     "H:SHS", skip_gu=True)
        shs_keys = [f"H:SHS:sh:{ch}:{it}" for ch in range(8) for it in range(4)]

        for x in range(NEXP):
            def exp_loadx(ek, ch, x=x):
                for i in range(4):
                    b = i % 2
                    row0 = x * CAP + ch * 512 + i * 128
                    dma("SP", xg[b][:, :], G[row0:row0 + 128, :],
                        [f"H:G:{tt}" for tt in range(NTILES)], [f"xg{b}"],
                        f"{ek}xl{ch}_{i}")
                    transpose8(xg[b][:, :], [f"xg{b}"], f"{ek}xt{ch}_{i}")
                    p.op("DVE", lambda e, i=i: e.tensor_copy(
                        xch[:, :].rearrange("p (k n) -> p k n", k=KC)[:, :, i * 128:(i + 1) * 128],
                        PST[:, :].rearrange("p (k n) -> p k n", k=KC)),
                        reads=["pst"], writes=["xch"], tag=f"{ek}xe{ch}_{i}")
            expert_block(f"e{x}", f_eg[x], f_eu[x], f_ed[x], CAP // 512, exp_loadx,
                         lambda ch, it, x=x: Geo[x * CAP + (ch * 4 + it) * 128:
                                                 x * CAP + (ch * 4 + it + 1) * 128, :],
                         "H:Geo")
        geo_keys = [f"H:Geo:e{x}:{ch}:{it}" for x in range(NEXP)
                    for ch in range(CAP // 512) for it in range(4)]

        ar.push()
        NE = 4
        eb = [ar.alloc(1024, F32) for _ in range(NE)] + \
            [ar.alloc(1024, BF16) for _ in range(NE)]
        mg = [ar.alloc(1024, BF16) for _ in range(NE)]
        for ti in range(NTILES):
            b = ti % NE
            kt = f"{tag}E{ti}"
            dma("SP", eb[b][:, :], rin[ti * 128:(ti + 1) * 128, :],
                [f"{rin_name}:{ti}"], [f"ebl{b}"], kt)
            dma("SP", eb[NE + b][:, :], SHS[ti * 128:(ti + 1) * 128, :],
                shs_keys, [f"ebs{b}"], f"{kt}_sh")
            p.op("POOL", lambda e, ti=ti, b=b: e.indirect_dma_start(
                mg[b][:, :], None, Geo,
                bass.IndirectOffsetOnAxis(ap=desti[:, ti:ti + 1], axis=0)),
                reads=geo_keys + ["Geoinit", "diall"], writes=[f"mg{b}"],
                dma=True, tag=f"{kt}_gt")
            p.op("DVE", lambda e, b=b: e.tensor_add(eb[b][:, :], eb[b][:, :],
                                                    eb[NE + b][:, :]),
                 reads=[f"ebl{b}", f"ebs{b}"], writes=[f"ebl{b}"], tag=f"{kt}_a1")
            p.op("DVE", lambda e, b=b: e.tensor_add(eb[b][:, :], eb[b][:, :],
                                                    mg[b][:, :]),
                 reads=[f"ebl{b}", f"mg{b}"], writes=[f"ebl{b}"], tag=f"{kt}_a2")
            dma("ACT", rout[ti * 128:(ti + 1) * 128, :], eb[b][:, :],
                [f"ebl{b}"], [f"{rout_name}:{ti}"], f"{kt}_st")
        ar.pop()
        ar.pop()

    if "f1" in stages:
        ffn("f1", r1, "r1", r2, "r2", g1s, g1eo, sh1s)

    if "ab" in stages:
        ar.push()
        kT_b = ar.alloc(KC * SEQ, BF16)
        v65_b = [ar.alloc(V65W, BF16) for _ in range(SEQ // 128)]
        for bb in range(2):
            bt0 = bb * 16
            bk0 = f"ab{bb}"
            vkeys = [f"v65b:{t}" for t in range(16)]
            kkeys = [f"kTb:{m}" for m in range(8)]
            ar.push()
            alloc_pack()
            xTb = ar.alloc(KC * 512, BF16)
            qstg = ar.alloc(KC * 512, BF16)
            wring = [ar.alloc(KC * C, BF16) for _ in range(2)]
            wseq = 0
            for ch in range(4):
                for i in range(4):
                    ti = bt0 + ch * 4 + i
                    b = i % 2
                    t, y, h = PK["T"][b], PK["Y"][b], PK["H"][b]
                    kt = f"{bk0}p{ti}"
                    dma("SP", t[:, :], r2[ti * 128:(ti + 1) * 128, :],
                        [f"r2:{ti}"], [f"tb{b}"], kt)
                    ln_core(t[:, :], [f"tb{b}"], y[:, :], f"yb{b}", kt)
                    p.op("ACT", lambda e, y=y, h=h: e.activation(h[:, :], y[:, :],
                                                                 AF.Copy),
                         reads=[f"yb{b}"], writes=[f"hb{b}"], tag=f"{kt}_c")
                    transpose8(h[:, :], [f"hb{b}"], f"{kt}_tr")
                    p.op("DVE", lambda e, i=i: e.tensor_copy(
                        xTb[:, :].rearrange("p (k n) -> p k n", k=KC)[:, :, i * 128:(i + 1) * 128],
                        PST[:, :].rearrange("p (k n) -> p k n", k=KC)),
                        reads=["pst"], writes=[f"xTb:{i}"], tag=f"{kt}_ev")
                xbk = [f"xTb:{i}" for i in range(4)]
                for wsrc, wtag in [(ab_wq, "q"), (ab_wk, "k"), (ab_wv, "v")]:
                    wb = wring[wseq % 2]
                    wk = f"wring{wseq % 2}"
                    wseq += 1
                    wchunks(wb, wk, wsrc, C, C, f"{bk0}w{wtag}{ch}")
                    if wtag != "v":
                        for m in range(8):
                            def pmm(e, m=m, wb=wb):
                                inst = None
                                for k in range(KC):
                                    inst = e.matmul(
                                        PSA[:, (m % 4) * 512:(m % 4) * 512 + 512],
                                        wb[:, k * C + m * 128:k * C + (m + 1) * 128],
                                        xTb[:, k * 512:(k + 1) * 512],
                                        start=(k == 0), stop=(k == KC - 1))
                                return inst
                            p.op("PE", pmm, reads=xbk + [wk], writes=[PSAK[m % 4]],
                                 tag=f"{bk0}{wtag}{ch}_{m}")
                            if wtag == "q":
                                p.op("ACT", lambda e, m=m: e.activation(
                                    qstg[:, m * 512:(m + 1) * 512],
                                    PSA[:, (m % 4) * 512:(m % 4) * 512 + 512],
                                    AF.Copy, scale=SCALE_B),
                                    reads=[PSAK[m % 4], "Kc"], writes=[f"qstg:{m}"],
                                    tag=f"{bk0}qe{ch}_{m}")
                            else:
                                p.op("ACT", lambda e, m=m, ch=ch: e.activation(
                                    kT_b[:, m * SEQ + ch * 512:m * SEQ + ch * 512 + 512],
                                    PSA[:, (m % 4) * 512:(m % 4) * 512 + 512],
                                    AF.Copy, scale=SCALE_B),
                                    reads=[PSAK[m % 4], "Kc"], writes=[f"kTb:{m}"],
                                    tag=f"{bk0}ke{ch}_{m}")
                        if wtag == "q":
                            dma("ACT", qTs.rearrange("(k p) n -> p k n", p=128)
                                [:, :, bt0 * 128 + ch * 512:bt0 * 128 + (ch + 1) * 512],
                                qstg[:, :].rearrange("p (k n) -> p k n", k=KC),
                                [f"qstg:{m}" for m in range(8)], ["qstg_out"],
                                f"{bk0}qs{ch}")
                    else:
                        for i in range(4):
                            vt = ch * 4 + i

                            def vmmb(e, i=i, wb=wb):
                                inst = None
                                for k in range(KC):
                                    for nh_ in range(2):
                                        inst = e.matmul(
                                            PSB[:, nh_ * 512:(nh_ + 1) * 512],
                                            xTb[:, k * 512 + i * 128:k * 512 + (i + 1) * 128],
                                            wb[:, k * C + nh_ * 512:k * C + (nh_ + 1) * 512],
                                            start=(k == 0), stop=(k == KC - 1))
                                return inst
                            p.op("PE", vmmb, reads=xbk + [wk], writes=PSBK,
                                 tag=f"{bk0}v{vt}")
                            p.op("DVE", lambda e, vt=vt: e.memset(
                                v65_b[vt][:, :].rearrange("p (h w) -> p h w", h=NH)
                                [:, :, HD:HD + 1], 1.0),
                                writes=[f"v65b:{vt}"], tag=f"{bk0}vo{vt}")
                            p.op("DVE", lambda e, vt=vt: e.tensor_copy(
                                v65_b[vt][:, :].rearrange("p (h w) -> p h w", h=NH)[:, :, 0:HD],
                                PSB[:, :].rearrange("p (h w) -> p h w", h=NH)),
                                reads=PSBK + [f"v65b:{vt}"], writes=[f"v65b:{vt}"],
                                tag=f"{bk0}ve{vt}")
            ar.pop()
            ar.push()
            qTc = [ar.alloc(KC * 512, BF16) for _ in range(2)]
            oTc = [ar.alloc(KC * 512, BF16) for _ in range(2)]
            expt = [ar.alloc(16 * 512, BF16) for _ in range(2)]
            orwB = [ar.alloc(512, F32) for _ in range(2)]
            rrowB = [ar.alloc(512, F32) for _ in range(2)]
            lnb = ar.alloc(1024, F32)
            shft = [ar.alloc(512, BF16) for _ in range(2)]
            J = 4 * NH

            def qload(ic):
                dma("SP", qTc[ic % 2][:, :].rearrange("p (k n) -> p k n", k=KC),
                    qTs.rearrange("(k p) n -> p k n", p=128)
                    [:, :, bt0 * 128 + ic * 512:bt0 * 128 + (ic + 1) * 512],
                    ["qstg_out"], [f"qTc{ic % 2}"], f"{bk0}lq{ic}")

            def S(j):
                ic, h = j // NH, j % NH
                sI = h // 2
                r = (h % 2) * 64
                akt = f"{bk0}i{ic}h{h}"
                qsl = qTc[ic % 2][r:r + 64, sI * 512:(sI + 1) * 512]
                for w in range(8):
                    pk2 = [PSAK[(w % 2) * 2], PSAK[(w % 2) * 2 + 1]]

                    def swv(e, w=w, r=r, sI=sI, qsl=qsl):
                        inst = None
                        for jj in range(2):
                            jt = w * 2 + jj
                            co = (w % 2) * 1024 + jj * 512
                            inst = e.matmul(
                                PSA[:, co:co + 512],
                                kT_b[r:r + 64, sI * SEQ + jt * 128:sI * SEQ + (jt + 1) * 128],
                                qsl, start=True, stop=True)
                        return inst
                    p.op("PE", swv, reads=kkeys + [f"qTc{ic % 2}"], writes=pk2,
                         tag=f"{akt}_s{w}")

                    def ewv(e, w=w, j=j):
                        inst = None
                        for jj in range(2):
                            jt = w * 2 + jj
                            co = (w % 2) * 1024 + jj * 512
                            inst = e.activation(
                                expt[j % 2][:, jt * 512:(jt + 1) * 512],
                                PSA[:, co:co + 512], AF.Exp)
                        return inst
                    p.op("ACT", ewv, reads=pk2, writes=[f"ex{j % 2}:{w}"],
                         tag=f"{akt}_e{w}")

            def O(j):
                ic, h = j // NH, j % NH
                akt = f"{bk0}i{ic}h{h}"
                ek = [f"ex{j % 2}:{w}" for w in range(8)]
                pco = (j % 2) * 512

                def ommb(e, h=h, j=j, pco=pco):
                    inst = None
                    for jt in range(16):
                        inst = e.matmul(PSB[0:65, pco:pco + 512],
                                        v65_b[jt][:, h * 65:h * 65 + 65],
                                        expt[j % 2][:, jt * 512:(jt + 1) * 512],
                                        start=(jt == 0), stop=(jt == 15))
                    return inst
                p.op("PE", ommb, reads=ek + vkeys, writes=[PSBK[j % 2]],
                     tag=f"{akt}_o")
                p.op("DVE", lambda e, pco=pco: e.tensor_copy(
                    lnb[0:1, 0:512], PSB[64:65, pco:pco + 512]),
                    reads=[PSBK[j % 2]], writes=["lnB"], tag=f"{akt}_rc0")
                p.op("ACT", lambda e: e.activation(lnb[0:1, 512:1024],
                                                   lnb[0:1, 0:512], AF.Ln),
                     reads=["lnB"], writes=["lnB2"], tag=f"{akt}_rc1")
                p.op("ACT", lambda e, j=j: e.activation(
                    rrowB[j % 2][0:1, :], lnb[0:1, 512:1024], AF.Exp, scale=-1.0),
                    reads=["lnB2", "Kc"], writes=[f"rrB{j % 2}"], tag=f"{akt}_rc")

            def B(j):
                ic, h = j // NH, j % NH
                sI = h // 2
                akt = f"{bk0}i{ic}h{h}"
                pco = (j % 2) * 512
                p.op("PE", lambda e, j=j: e.matmul(
                    PSC[0:64, :], sb_ones[0:1, 0:64], rrowB[j % 2][0:1, :],
                    start=True, stop=True),
                    reads=[f"rrB{j % 2}", "ones"], writes=["psc"], tag=f"{akt}_bc")
                p.op("ACT", lambda e, j=j, pco=pco: e.activation(
                    orwB[j % 2][0:64, :], PSB[0:64, pco:pco + 512], AF.Copy),
                    reads=[PSBK[j % 2]], writes=[f"orB{j % 2}"], tag=f"{akt}_ev")
                if h % 2 == 0:
                    p.op("DVE", lambda e, sI=sI, j=j, ic=ic: e.tensor_tensor(
                        oTc[ic % 2][0:64, sI * 512:(sI + 1) * 512],
                        orwB[j % 2][0:64, :], PSC[0:64, :], ALU.mult),
                        reads=[f"orB{j % 2}", "psc"], writes=[f"oTc{ic % 2}:{sI}:0"],
                        tag=f"{akt}_n")
                else:
                    p.op("DVE", lambda e, j=j: e.tensor_tensor(
                        shft[j % 2][0:64, :], orwB[j % 2][0:64, :], PSC[0:64, :],
                        ALU.mult),
                        reads=[f"orB{j % 2}", "psc"], writes=[f"shft{j % 2}"],
                        tag=f"{akt}_n")
                    dma("SP", oTc[ic % 2][64:128, sI * 512:(sI + 1) * 512],
                        shft[j % 2][0:64, :], [f"shft{j % 2}"],
                        [f"oTc{ic % 2}:{sI}:1"], f"{akt}_sh")

            for j in range(J + 2):
                if "f2" in stages:
                    emit_gz("f2", 1, "SP")
                if j < J and j % NH == 0:
                    qload(j // NH)
                if j < J:
                    S(j)
                if j >= 1:
                    O(j - 1)
                if j >= 2:
                    B(j - 2)
                    if (j - 2) % NH == NH - 1:
                        ic = (j - 2) // NH
                        okeys = [f"oTc{ic % 2}:{ss}:{par}" for ss in range(8)
                                 for par in range(2)]
                        dma("ACT", oTs.rearrange("(k p) n -> p k n", p=128)
                            [:, :, bt0 * 128 + ic * 512:bt0 * 128 + (ic + 1) * 512],
                            oTc[ic % 2][:, :].rearrange("p (k n) -> p k n", k=KC),
                            okeys, ["oTs_out"], f"{bk0}so{ic}")
            ar.pop()
            ar.push()
            cb_bo = ar.alloc(C, F32)
            mk_bcast(cb_bo, ab_bo.rearrange("(o n) -> o n", o=1), C, "bo")
            wo_sb = ar.alloc(KC * C, BF16)
            oTt = [ar.alloc(KC * 128, BF16) for _ in range(2)]
            zb = [ar.alloc(1024, F32) for _ in range(4)]
            wchunks(wo_sb, "W:bwo", ab_wo, C, C, "abwo")
            for i in range(16):
                ti = bt0 + i
                b = i % 2
                kt = f"{bk0}z{i}"
                dma("SP", oTt[b][:, :].rearrange("p (k n) -> p k n", k=KC),
                    oTs.rearrange("(k p) n -> p k n", p=128)
                    [:, :, ti * 128:(ti + 1) * 128],
                    ["oTs_out"], [f"oTt{b}"], f"{kt}_lo")

                def zmmb(e, b=b):
                    inst = None
                    for k in range(KC):
                        for nh_ in range(2):
                            inst = e.matmul(
                                PSB[:, nh_ * 512:(nh_ + 1) * 512],
                                oTt[b][:, k * 128:(k + 1) * 128],
                                wo_sb[:, k * C + nh_ * 512:k * C + (nh_ + 1) * 512],
                                start=(k == 0), stop=(k == KC - 1))
                    return inst
                p.op("PE", zmmb, reads=[f"oTt{b}", "W:bwo"], writes=PSBK, tag=kt)
                dma("SP", zb[b][:, :], r2[ti * 128:(ti + 1) * 128, :],
                    [f"r2:{ti}"], [f"zbl{b}"], f"{kt}_l")
                p.op("DVE", lambda e, b=b: e.scalar_tensor_tensor(
                    zb[2 + b][:, :], PSB[:, :], 1.0, cb_bo[:, :], ALU.mult, ALU.add),
                    reads=PSBK + ["C:bo"], writes=[f"zbo{b}"], tag=f"{kt}_b")
                p.op("DVE", lambda e, b=b: e.tensor_add(zb[2 + b][:, :],
                                                        zb[2 + b][:, :], zb[b][:, :]),
                     reads=[f"zbo{b}", f"zbl{b}"], writes=[f"zbo{b}"], tag=f"{kt}_r")
                dma("ACT", r3[ti * 128:(ti + 1) * 128, :], zb[2 + b][:, :],
                    [f"zbo{b}"], [f"r3:{ti}"], f"{kt}_st")
            ar.pop()
        ar.pop()

    if "f2" in stages:
        ffn("f2", r3, "r3", out, "out", g2s, g2eo, sh2s)
    else:
        last = {"ab": (r3, "r3"), "f1": (r2, "r2"), "aa": (r1, "r1")}
        for stg_ in ("ab", "f1", "aa"):
            if stg_ in stages:
                src, sname = last[stg_]
                ar.push()
                cp = [ar.alloc(1024, F32) for _ in range(2)]
                for ti in range(NTILES):
                    b = ti % 2
                    dma("SP", cp[b][:, :], src[ti * 128:(ti + 1) * 128, :],
                        [f"{sname}:{ti}"], [f"cp{b}"], f"cp{ti}")
                    dma("ACT", out[ti * 128:(ti + 1) * 128, :], cp[b][:, :],
                        [f"cp{b}"], [f"out:{ti}"], f"cp{ti}o")
                ar.pop()
                break

    p.build(nc, es)
    es.close()
    return p



def make_inputs(inputs, core):
    x = inputs["L0_attn_a_x"]
    shard = np.ascontiguousarray(x[2 * core:2 * core + 2]).reshape(T, C)
    ident_f = np.eye(128, dtype=np.float32)
    ident_b = np.eye(128, dtype=ml_dtypes.bfloat16)
    j = np.arange(64)[:, None]
    i = np.arange(64)[None, :]
    causal = (j <= i).astype(np.float32)
    ones = np.ones((64, 64), np.float32)
    mA = np.concatenate([causal, ones], axis=0)
    mB = np.concatenate([ones, causal], axis=0)
    hmask = np.concatenate([np.tile(mA, (1, NH)), np.tile(mB, (1, NH))], axis=1)
    iota256 = np.tile(np.tile(np.arange(NEXP, dtype=np.float32), 32)[None, :],
                      (128, 1))
    iota32 = np.tile(np.arange(32, dtype=np.float32)[None, :], (128, 1))
    ltri = np.triu(np.ones((128, 128), np.float32), k=1)
    return {
        "xin": shard,
        "aa_wqkv": inputs["L0_attn_a_w_qkv"], "aa_bqkv": inputs["L0_attn_a_b_qkv"],
        "aa_wout": inputs["L0_attn_a_w_out"], "aa_bout": inputs["L0_attn_a_b_out"],
        "aa_g1": inputs["L0_attn_a_ln1_g"], "aa_b1g": inputs["L0_attn_a_ln1_b"],
        "aa_g2": inputs["L0_attn_a_ln2_g"], "aa_b2g": inputs["L0_attn_a_ln2_b"],
        "aa_w1": inputs["L0_attn_a_w1"], "aa_b1": inputs["L0_attn_a_b1"],
        "aa_w2": inputs["L0_attn_a_w2"], "aa_b2": inputs["L0_attn_a_b2"],
        "ab_wq": inputs["L0_attn_b_Wq"], "ab_wk": inputs["L0_attn_b_Wk"],
        "ab_wv": inputs["L0_attn_b_Wv"], "ab_wo": inputs["L0_attn_b_Wo"],
        "ab_bo": inputs["L0_attn_b_bo"],
        "f_rw": inputs["L0_ffn_router_w"],
        "f_eg": inputs["L0_ffn_expert_gate"], "f_eu": inputs["L0_ffn_expert_up"],
        "f_ed": inputs["L0_ffn_expert_down"],
        "f_sg": inputs["L0_ffn_shared_gate"], "f_su": inputs["L0_ffn_shared_up"],
        "f_sd": inputs["L0_ffn_shared_down"],
        "id_f": ident_f, "id_b": ident_b, "hmask": hmask, "iota256": iota256,
        "iota32": iota32, "ltri": ltri,
    }


def kernel(**inputs):
    nc = bass.Bass("TRN2", target_bir_lowering=False)
    build_program(nc)
    nc.finalize()
    in_maps = [make_inputs(inputs, c) for c in range(8)]
    res = run_bass_kernel_spmd(nc, in_maps, core_ids=list(range(8)))
    outs = [np.asarray(res.results[c]["out"]).reshape(2, SEQ, C) for c in range(8)]
    return np.concatenate(outs, axis=0).astype(np.float32)
